# version 10
# baseline (speedup 1.0000x reference)
import numpy as np
import ml_dtypes
import concourse.bass as bass
import concourse.mybir as mybir
from concourse.bass import IndirectOffsetOnAxis
from concourse.bass_utils import run_bass_kernel_spmd

F32 = mybir.dt.float32
BF16 = mybir.dt.bfloat16
U32 = mybir.dt.uint32
AF = mybir.ActivationFunctionType
ALU = mybir.AluOpType
AX = mybir.AxisListType

NCORE = 8
BATCH = 16
SEQ = 2048
DM = 1024
BS = BATCH // NCORE
NTOK = BATCH * SEQ
CTOK = BS * SEQ
NB = 512


ENGINES = ("pe", "act", "dve", "pool", "sp")
MAXINFLIGHT = 8


class Op:
    __slots__ = ("i", "eng", "fn", "reads", "writes", "dma", "marked", "mark_no",
                 "dma_no", "waits", "tag")

    def __init__(self, i, eng, fn, reads, writes, dma, tag):
        self.i = i
        self.eng = eng
        self.fn = fn
        self.reads = reads
        self.writes = writes
        self.dma = dma
        self.marked = False
        self.mark_no = 0
        self.dma_no = 0
        self.waits = {}
        self.tag = tag


def _keys(x):
    out = []
    if not isinstance(x, (list, tuple)):
        x = (x,)
    for a in x:
        if isinstance(a, (tuple, str)):
            out.append(a)
        elif hasattr(a, "tensor"):
            out.append(a.tensor.name)
        elif hasattr(a, "name"):
            out.append(a.name)
        else:
            out.append(str(a))
    return out


class Prog:
    def __init__(self, nc):
        self.nc = nc
        self.ops = []
        self._qcnt = {}

    def op(self, eng, fn, reads=(), writes=(), dma=False, tag=""):
        o = Op(len(self.ops), eng, fn, _keys(reads), _keys(writes),
               eng if dma else None, tag)
        if o.dma is not None:
            self._qcnt[o.dma] = self._qcnt.get(o.dma, 0) + 1
            o.dma_no = self._qcnt[o.dma]
        self.ops.append(o)
        return o

    def barrier(self):
        key = ("barrier", len(self.ops))
        for e in ENGINES:
            self.op(e, lambda eng: eng.nop(), reads=(), writes=(key,), tag="barw")
        for e in ENGINES:
            self.op(e, lambda eng: eng.nop(), reads=(key,), writes=(), tag="barr")

    def _resolve(self):
        last_writer = {}
        readers_since = {}
        for o in self.ops:
            deps = []
            for k in o.reads:
                w = last_writer.get(k)
                if w is not None:
                    deps.append(w)
                readers_since.setdefault(k, []).append(o)
            for k in o.writes:
                w = last_writer.get(k)
                if w is not None:
                    deps.append(w)
                for r in readers_since.get(k, ()):
                    deps.append(r)
                last_writer[k] = o
                readers_since[k] = []
            for d in deps:
                if d is o:
                    continue
                if d.eng == o.eng and d.dma is None and o.dma is None \
                        and d.eng == "pe":
                    continue
                if d.dma is not None:
                    o.waits["q" + d.dma] = max(o.waits.get("q" + d.dma, 0),
                                               16 * d.dma_no)
                else:
                    d.marked = True
                    key = "#" + d.eng
                    o.waits[key] = max(o.waits.get(key, -1), d.i)
        mark_cnt = {e: 0 for e in ENGINES}
        idx2mark = {}
        for o in self.ops:
            if o.marked and o.dma is None:
                mark_cnt[o.eng] += 1
                o.mark_no = mark_cnt[o.eng]
            idx2mark[o.i] = (o.eng, o.mark_no)
        for o in self.ops:
            fixed = {}
            for k, v in o.waits.items():
                if k.startswith("#"):
                    eng, mn = idx2mark[v]
                    assert mn > 0, (o.tag, k, v)
                    fixed[eng] = max(fixed.get(eng, 0), mn)
                else:
                    fixed[k] = max(fixed.get(k, 0), v)
            o.waits = fixed
        self.final_q = dict(self._qcnt)
        self.final_m = mark_cnt

    def emit_engine(self, e, eng, sems):
        seen = {}
        for o in self.ops:
            if o.eng != e:
                continue
            for k, v in sorted(o.waits.items()):
                if seen.get(k, -1) >= v:
                    continue
                eng.wait_ge(sems[k], v)
                seen[k] = v
            if o.dma is not None and o.dma_no > 1:
                k = "q" + o.dma
                v = 16 * (o.dma_no - 1)
                if seen.get(k, -1) < v:
                    eng.wait_ge(sems[k], v)
                    seen[k] = v
            if o.marked and o.dma is None and o.tag in ("barw", "barr"):
                v = o.mark_no - 1
                if v > 0 and seen.get(e, -1) < v:
                    eng.wait_ge(sems[e], v)
                    seen[e] = v
            inst = o.fn(eng)
            if o.dma is not None:
                assert inst is not None, o.tag
                inst.then_inc(sems["q" + o.dma], 16)
            elif o.marked:
                if inst is None:
                    inst = eng.nop()
                inst.then_inc(sems[o.eng], 1)
        if e == "sp":
            for q, n in self.final_q.items():
                if seen.get("q" + q, -1) < 16 * n:
                    eng.wait_ge(sems["q" + q], 16 * n)
            for m, n in self.final_m.items():
                if n and seen.get(m, -1) < n:
                    eng.wait_ge(sems[m], n)


def launch(progfn, in_maps, core_ids=None, **kw):
    from contextlib import ExitStack
    if core_ids is None:
        core_ids = list(range(len(in_maps)))
    nc = bass.Bass("TRN2", target_bir_lowering=False)
    prog = Prog(nc)
    es = ExitStack()
    with es:
        progfn(nc, prog, es)
        sems = {}
        for e in ENGINES:
            sems[e] = es.enter_context(nc.semaphore("sem_" + e))
            sems["q" + e] = es.enter_context(nc.semaphore("semq_" + e))
        import os
        _cut = int(os.environ.get("BASSCUT", "0"))
        if _cut:
            prog.ops = prog.ops[:_cut]
            prog._qcnt = {}
            for o in prog.ops:
                if o.dma is not None:
                    prog._qcnt[o.dma] = prog._qcnt.get(o.dma, 0) + 1
        prog._resolve()
        with nc.Block() as block:

            @block.tensor
            def _(t):
                prog.emit_engine("pe", t, sems)

            @block.scalar
            def _(s):
                prog.emit_engine("act", s, sems)

            @block.vector
            def _(v):
                prog.emit_engine("dve", v, sems)

            @block.gpsimd
            def _(g):
                prog.emit_engine("pool", g, sems)

            @block.sync
            def _(sy):
                prog.emit_engine("sp", sy, sems)
    import os
    if os.environ.get("BASSPROF"):
        import time as _t
        t0 = _t.time()
        r = run_bass_kernel_spmd(nc, in_maps, core_ids=core_ids, trace=True, **kw)
        print(f"[prof] launch ops={len(prog.ops)} exec_ns={r.exec_time_ns} wall={_t.time()-t0:.1f}s", flush=True)
        return r
    if os.environ.get("BASSSIM"):
        import concourse.bass_interp as bass_interp

        class _R:
            pass
        sim = bass_interp.MultiCoreSim(nc, len(core_ids))
        outs = []
        for i in range(len(core_ids)):
            for k, v in in_maps[i].items():
                sim.cores[i].tensor(k)[:] = np.asarray(v)
        sim.simulate()
        r = _R()
        r.results = []
        for i in range(len(core_ids)):
            d = {}
            for alloc in nc.m.functions[0].allocations:
                if not isinstance(alloc, mybir.MemoryLocationSet):
                    continue
                if alloc.kind == "ExternalOutput":
                    nmm = alloc.memorylocations[0].name
                    d[nmm] = np.array(sim.cores[i].mem_tensor(nmm)).reshape(
                        tuple(alloc.tensor_shape))
            r.results.append(d)
        return r
    return run_bass_kernel_spmd(nc, in_maps, core_ids=core_ids, **kw)



def sb(nc, es, name, shape, dt=F32):
    return es.enter_context(nc.sbuf_tensor(name, shape, dt))


def psum(nc, es, name, shape=(128, NB), dt=F32):
    return es.enter_context(nc.psum_tensor(name, list(shape), dt))


class PsumPool:

    def __init__(self, nc, es, n=6, name="pp"):
        self.tiles = [psum(nc, es, f"{name}{i}") for i in range(n)]
        self.i = 0

    def get(self):
        t = self.tiles[self.i % len(self.tiles)]
        self.i += 1
        return t


def dma(prog, q, out_ap, in_ap, reads, writes, tag="dma", **kw):
    prog.op(q, lambda eng: eng.dma_start(out=out_ap, in_=in_ap, **kw),
            reads=reads, writes=writes, dma=True, tag=tag)


def bcast_row(prog, onescol_ap, row_ap, ps_ap, n0, n1, rkey, wkey, tag="bc"):
    prog.op("pe",
            lambda eng: eng.matmul(ps_ap[:, 0:(n1 - n0)], onescol_ap,
                                   row_ap[0:1, n0:n1], start=True, stop=True),
            reads=[rkey, "c_ones_col"], writes=[wkey], tag=tag)



class Ctx:
    def __init__(self, nc, prog, es):
        self.nc = nc
        self.prog = prog
        self.es = es
        self.uid = 0
        self.c_ones_red = None
        self.c_ones_bc = None
        self.c_ident = None
        self.pp = None

    def name(self, base):
        self.uid += 1
        return f"{base}_{self.uid}"

    def sbuf(self, base, shape, dt=F32):
        return sb(self.nc, self.es, self.name(base), list(shape), dt)

    def feat_cols(self, dram_vec, n, tag="fc", scale=None):
        prog, nc = self.prog, self.nc
        if n <= 128:
            row = self.sbuf(tag + "r", (1, n))
            dma(prog, "sp", row[:, :], dram_vec.rearrange("(o n) -> o n", o=1),
                reads=[dram_vec], writes=[row], tag=tag)
            ps = self.pp.get()
            prog.op("pe", lambda eng: eng.transpose(ps[0:n, 0:1], row[0:1, 0:n],
                                                    self.c_ident[0:1, 0:1]),
                    reads=[row, "c_ident"], writes=[ps], tag=tag + "t")
            col = self.sbuf(tag + "c", (n, 1))
            if scale is None:
                prog.op("dve", lambda eng: eng.tensor_copy(col[:, :], ps[0:n, 0:1]),
                        reads=[ps], writes=[col], tag=tag + "e")
            else:
                prog.op("dve", lambda eng: eng.tensor_scalar(col[:, :], ps[0:n, 0:1],
                                                             float(scale), None, ALU.mult),
                        reads=[ps], writes=[col], tag=tag + "e")
            return col
        k = n // 128
        assert n % 128 == 0
        wide = self.sbuf(tag + "w", (k, 128))
        dma(prog, "sp", wide[:, :], dram_vec.rearrange("(k p) -> k p", p=128),
            reads=[dram_vec], writes=[wide], tag=tag)
        ps = self.pp.get()
        prog.op("pe", lambda eng: eng.transpose(ps[0:128, 0:k], wide[0:k, 0:128],
                                                self.c_ident[0:k, 0:k]),
                reads=[wide, "c_ident"], writes=[ps], tag=tag + "t")
        cols = self.sbuf(tag + "c", (128, k))
        if scale is None:
            prog.op("dve", lambda eng: eng.tensor_copy(cols[:, :], ps[0:128, 0:k]),
                    reads=[ps], writes=[cols], tag=tag + "e")
        else:
            prog.op("dve", lambda eng: eng.tensor_scalar(cols[:, :], ps[0:128, 0:k],
                                                         float(scale), None, ALU.mult),
                    reads=[ps], writes=[cols], tag=tag + "e")
        return cols

    def load_w_bf16(self, dram_w, K, M, tag="w"):
        prog = self.prog
        kc = K // 128
        t = self.sbuf(tag, (128, kc * M), BF16)
        for c in range(kc):
            dma(prog, "pool", t[:, c * M:(c + 1) * M], dram_w[c * 128:(c + 1) * 128, :],
                reads=[dram_w], writes=[(t.name, c)], tag=tag + str(c))
        return t


def emit_ln_fm(ctx, get_chunk, T, nchunk, h_out, h_key, tag="ln", eps=1e-5,
               rms=False, nsq=2):
    prog = ctx.prog
    D = nchunk * 128
    ns = T // NB
    assert ns <= 4
    sq = [ctx.sbuf(tag + "sq", (128, T)) for _ in range(nsq)]
    sq = (sq * 2)[:2]
    rows = ctx.sbuf(tag + "rw", (1, 2 * T))
    def R(i):
        if i == 2:
            return sq[0][0:1, 0:T]
        return rows[0:1, i * T:(i + 1) * T]
    ps_s = [ctx.pp.get() for _ in range(ns)]
    ps_q = [ctx.pp.get() for _ in range(ns)]
    for c in range(nchunk):
        xap, xkey = get_chunk(c, 0)
        s_ = sq[c % 2]
        prog.op("act", lambda eng, xap=xap, s_=s_: eng.activation(
            s_[:, :], xap, AF.Square),
            reads=[xkey], writes=[s_], tag=tag + "sq")
        for s in range(ns):
            prog.op("pe", lambda eng, xap=xap, c=c, s=s: eng.matmul(
                ps_s[s][0:1, 0:NB], ctx.c_ones_red, xap[:, s * NB:(s + 1) * NB],
                start=(c == 0), stop=(c == nchunk - 1)),
                reads=[xkey, "c_ones_red"], writes=[ps_s[s]], tag=tag + "ms")
            prog.op("pe", lambda eng, c=c, s=s, s_=s_: eng.matmul(
                ps_q[s][0:1, 0:NB], ctx.c_ones_red, s_[:, s * NB:(s + 1) * NB],
                start=(c == 0), stop=(c == nchunk - 1)),
                reads=[s_, "c_ones_red"], writes=[ps_q[s]], tag=tag + "mq")
    for s in range(ns):
        prog.op("dve", lambda eng, s=s: eng.tensor_copy(
            rows[0:1, s * NB:(s + 1) * NB], ps_s[s][0:1, 0:NB]),
            reads=[ps_s[s]], writes=[(rows.name, 0)], tag=tag + "es")
        prog.op("dve", lambda eng, s=s: eng.tensor_copy(
            rows[0:1, T + s * NB:T + (s + 1) * NB], ps_q[s][0:1, 0:NB]),
            reads=[ps_q[s]], writes=[(rows.name, 1)], tag=tag + "eq")
    if not rms:
        prog.op("dve", lambda eng: eng.tensor_scalar(
            R(0), R(0), 1.0 / D, None, ALU.mult),
            reads=[(rows.name, 0)], writes=[(rows.name, 0)], tag=tag + "m")
        prog.op("dve", lambda eng: eng.tensor_tensor(
            R(2), R(0), R(0), ALU.mult),
            reads=[(rows.name, 0)], writes=[sq[0]], tag=tag + "m2")
        prog.op("dve", lambda eng: eng.scalar_tensor_tensor(
            R(1), R(1), 1.0 / D, R(2), ALU.mult, ALU.subtract),
            reads=[(rows.name, 1), sq[0]], writes=[(rows.name, 1)],
            tag=tag + "v2")
    else:
        prog.op("dve", lambda eng: eng.tensor_scalar(
            R(1), R(1), 1.0 / D, None, ALU.mult),
            reads=[(rows.name, 1)], writes=[(rows.name, 1)], tag=tag + "v")
    prog.op("act", lambda eng: eng.activation(R(1), R(1), AF.Ln,
            bias=scal_ap(ctx, eps, 1)),
            reads=[(rows.name, 1), "c_scal"], writes=[(rows.name, 1)], tag=tag + "ln")
    prog.op("act", lambda eng: eng.activation(R(1), R(1), AF.Exp, scale=-0.5),
            reads=[(rows.name, 1)], writes=[(rows.name, 1)], tag=tag + "ex")
    if not rms:
        prog.op("dve", lambda eng: eng.scalar_tensor_tensor(
            R(0), R(0), -1.0, R(1), ALU.mult, ALU.mult),
            reads=[(rows.name, 0), (rows.name, 1)], writes=[(rows.name, 0)],
            tag=tag + "nm")
    psR = []
    psM = []
    for s in range(ns):
        pR = ctx.pp.get()
        psR.append(pR)
        prog.op("pe", lambda eng, s=s, pR=pR: eng.matmul(
            pR[:, 0:NB], ctx.c_ones_bc, rows[0:1, T + s * NB: T + (s + 1) * NB],
            start=True, stop=True),
            reads=[(rows.name, 1), "c_ones_bc"], writes=[pR], tag=tag + "bR")
        if not rms:
            pM = ctx.pp.get()
            psM.append(pM)
            prog.op("pe", lambda eng, s=s, pM=pM: eng.matmul(
                pM[:, 0:NB], ctx.c_ones_bc,
                rows[0:1, s * NB: (s + 1) * NB],
                start=True, stop=True),
                reads=[(rows.name, 0), "c_ones_bc"], writes=[pM], tag=tag + "bM")
    for c in range(nchunk):
        xap, xkey = get_chunk(c, 1)
        for s in range(ns):
            lo = c * T + s * NB
            hi = c * T + (s + 1) * NB
            if rms:
                prog.op("dve", lambda eng, xap=xap, lo=lo, hi=hi, s=s: eng.tensor_tensor(
                    h_out[:, lo:hi], xap[:, (lo - c * T):(hi - c * T)] if False else xap[:, s * NB:(s + 1) * NB], psR[s][:, 0:NB], ALU.mult),
                    reads=[xkey, psR[s]], writes=[(h_key, c)], tag=tag + "ap")
            else:
                s_ = sq[c % 2]
                prog.op("dve", lambda eng, xap=xap, s=s, s_=s_: eng.tensor_tensor(
                    s_[:, s * NB:(s + 1) * NB], xap[:, s * NB:(s + 1) * NB],
                    psR[s][:, 0:NB], ALU.mult),
                    reads=[xkey, psR[s]], writes=[s_], tag=tag + "a1")
                prog.op("dve", lambda eng, lo=lo, hi=hi, s=s, s_=s_: eng.tensor_tensor(
                    h_out[:, lo:hi], s_[:, s * NB:(s + 1) * NB], psM[s][:, 0:NB],
                    ALU.add),
                    reads=[s_, psM[s]], writes=[(h_key, c)], tag=tag + "a2")


def dram_chunk_loader(ctx, dram_row_fn, T, tag="xl", q="sp", nbuf=2):
    bufs = [ctx.sbuf(tag, (128, T)) for _ in range(nbuf)]
    bufs = (bufs * 2)[:2]
    cnt = [0]

    def get_chunk(c, passno):
        b = bufs[cnt[0] % 2]
        cnt[0] += 1
        ap, rkey = dram_row_fn(c)
        dma(ctx.prog, q, b[:, :], ap, reads=[rkey], writes=[b.name], tag=tag)
        return b[:, :], b.name
    return get_chunk


def emit_proj_tiles(ctx, wsb, M, x_sb, xkeys, kc, T, mtiles, out_cb, tag="pj",
                    extra_reads=()):
    prog = ctx.prog
    ns = T // NB
    for (m0, m1, info) in mtiles:
        pss = [ctx.pp.get() for _ in range(ns)]
        for k in range(kc):
            for s in range(ns):
                prog.op("pe", lambda eng, m0=m0, m1=m1, s=s, k=k, ps=pss[s]: eng.matmul(
                    ps[0:m1 - m0, 0:NB],
                    wsb[:, k * M + m0: k * M + m1],
                    x_sb[:, k * T + s * NB: k * T + (s + 1) * NB],
                    start=(k == 0), stop=(k == kc - 1)),
                    reads=[(wsb.name, k), (xkeys, k)] + list(extra_reads),
                    writes=[pss[s]], tag=tag + f"m{m0}s{s}")
        for s in range(ns):
            out_cb(info, m0, m1, s, pss[s])


def emit_mamba1(ctx, b, xin, xout, WxT_d, WoutT_d, Alog_d, D_d):
    nc, prog = ctx.nc, ctx.prog
    from contextlib import ExitStack
    T = SEQ
    NS = T // NB
    NCH = 8
    old_es = ctx.es

    rowA = ctx.sbuf("m1_rowA", (1, 16))
    dma(prog, "sp", rowA[:, :], Alog_d.rearrange("(o n) -> o n", o=1), reads=[Alog_d],
        writes=[rowA], tag="ldA")
    rowAe = ctx.sbuf("m1_rowAe", (1, 16))
    prog.op("act", lambda eng: eng.activation(rowAe[:, :], rowA[:, :], AF.Exp, scale=1.0),
            reads=[rowA], writes=[rowAe], tag="Aexp")
    psA = ctx.pp.get()
    prog.op("pe", lambda eng: eng.matmul(psA[0:128, 0:16], ctx.c_ones_bc,
                                         rowAe[0:1, 0:16], start=True, stop=True),
            reads=[rowAe, "c_ones_bc"], writes=[psA], tag="Abc")
    Acols = ctx.sbuf("m1_Acols", (128, 16))
    prog.op("dve", lambda eng: eng.tensor_scalar(Acols[:, :], psA[0:128, 0:16],
                                                 -1.0, None, ALU.mult),
            reads=[psA], writes=[Acols], tag="Aneg")
    Dcols = ctx.feat_cols(D_d, DM, tag="m1_D")

    es13 = ExitStack()
    ctx.es = es13
    h_bf = ctx.sbuf("m1_hbf", (128, NCH * T), BF16)
    es23 = ExitStack()
    ctx.es = es23
    dsp = ctx.sbuf("m1_dsp", (128, NCH * T), BF16)
    Bt = ctx.sbuf("m1_Bt", (16, T))
    Ct = ctx.sbuf("m1_Ct", (16, T))
    es34 = ExitStack()
    ctx.es = es34
    y_bf = ctx.sbuf("m1_ybf", (128, NCH * T), BF16)

    es1 = ExitStack()
    ctx.es = es1
    loader = dram_chunk_loader(
        ctx, lambda c: (xin[b, c * 128:(c + 1) * 128, :], (xin.tensor.name, b, c)),
        T, tag="m1xl")
    emit_ln_fm(ctx, loader, T, NCH, h_bf, "m1_h", tag=f"m1ln{b}")
    ctx.es = old_es
    prog.barrier()
    es1.close()

    es2 = ExitStack()
    ctx.es = es2
    wx = ctx.load_w_bf16(WxT_d, DM, 1056, tag="m1_wx")
    tmpb = [ctx.sbuf("m1_tmp", (128, NB)) for _ in range(2)]

    def evac_xp(info, m0, m1, s, ps):
        if info == "B":
            prog.op("dve", lambda eng, s=s, ps=ps: eng.tensor_copy(
                Bt[:, s * NB:(s + 1) * NB], ps[0:16, 0:NB]),
                reads=[ps], writes=[Bt], tag="evB")
        elif info == "C":
            prog.op("dve", lambda eng, s=s, ps=ps: eng.tensor_copy(
                Ct[:, s * NB:(s + 1) * NB], ps[0:16, 0:NB]),
                reads=[ps], writes=[Ct], tag="evC")
        else:
            r = info[1]
            t = tmpb[(r * NS + s) % 2]
            prog.op("act", lambda eng, t=t, ps=ps: eng.activation(
                t[:, :], ps[0:128, 0:NB], AF.Exp),
                reads=[ps], writes=[t], tag="evD1")
            prog.op("act", lambda eng, t=t, r=r, s=s: eng.activation(
                dsp[:, r * T + s * NB: r * T + (s + 1) * NB], t[:, :], AF.Ln,
                bias=scal_ap(ctx, 1.0)),
                reads=[t, "c_scal"], writes=[(dsp.name, r)], tag="evD2")

    mtiles = [(0, 16, "B"), (16, 32, "C")] + [(32 + 128 * r, 160 + 128 * r, ("d", r))
                                              for r in range(8)]
    emit_proj_tiles(ctx, wx, 1056, h_bf, "m1_h", NCH, T, mtiles, evac_xp, tag="xp")
    ctx.es = old_es
    prog.barrier()
    es2.close()

    es3 = ExitStack()
    ctx.es = es3
    dA_r = [ctx.sbuf("m1_dA", (128, T)) for _ in range(2)]
    u_r = [ctx.sbuf("m1_u", (128, T)) for _ in range(2)]
    Sb = [ctx.sbuf("m1_S", (128, T))] * 2
    tmb = [ctx.sbuf("m1_tm", (128, T))] * 2
    dcf = dA_r[0]
    hcf = ctx.sbuf("m1_hcf", (128, T))
    wcf = ctx.sbuf("m1_wcf", (128, T))
    ya = ctx.sbuf("m1_ya", (128, T))
    yb = ctx.sbuf("m1_yb", (128, T))
    for c in range(NCH):
        prog.op("act", lambda eng, c=c: eng.activation(
            hcf[:, :], h_bf[:, c * T:(c + 1) * T], AF.Copy),
            reads=[("m1_h", c)], writes=[hcf], tag="hcf")
        prog.op("act", lambda eng, c=c: eng.activation(
            dcf[:, :], dsp[:, c * T:(c + 1) * T], AF.Copy),
            reads=[(dsp.name, c)], writes=[dcf], tag="dcf")
        prog.op("dve", lambda eng: eng.tensor_tensor(wcf[:, :], dcf[:, :], hcf[:, :],
                                                     ALU.mult),
                reads=[dcf, hcf], writes=[wcf], tag="wcf")
        for n in range(16):
            dA = dA_r[n % 2]
            u = u_r[n % 2]
            S = Sb[n % 2]
            tm = tmb[n % 2]
            prog.op("act", lambda eng, c=c, n=n, dA=dA: eng.activation(
                dA[:, :], dsp[:, c * T:(c + 1) * T], AF.Exp,
                scale=Acols[:, n:n + 1]),
                reads=[(dsp.name, c), Acols], writes=[dA], tag="dAe")
            for s in range(NS):
                psB = ctx.pp.get()
                prog.op("pe", lambda eng, n=n, s=s, psB=psB: eng.matmul(
                    psB[0:128, 0:NB], ctx.c_sel16[:, n * 128:(n + 1) * 128],
                    Bt[0:16, s * NB:(s + 1) * NB], start=True, stop=True),
                    reads=[Bt, "c_sel16"], writes=[psB], tag="bcB")
                prog.op("dve", lambda eng, s=s, psB=psB, u=u: eng.tensor_tensor(
                    u[:, s * NB:(s + 1) * NB], wcf[:, s * NB:(s + 1) * NB],
                    psB[:, 0:NB], ALU.mult),
                    reads=[wcf, psB], writes=[u], tag="ubld")
            prog.op("dve", lambda eng, S=S, dA=dA, u=u: eng.tensor_tensor_scan(
                S[:, :], dA[:, :], u[:, :], 0.0, ALU.mult, ALU.add),
                reads=[dA, u], writes=[S], tag="scan")
            yt = ya if (n % 2 == 0) else yb
            for s in range(NS):
                psC = ctx.pp.get()
                prog.op("pe", lambda eng, n=n, s=s, psC=psC: eng.matmul(
                    psC[0:128, 0:NB], ctx.c_sel16[:, n * 128:(n + 1) * 128],
                    Ct[0:16, s * NB:(s + 1) * NB], start=True, stop=True),
                    reads=[Ct, "c_sel16"], writes=[psC], tag="bcC")
                dst = yt if n < 2 else tm
                prog.op("dve", lambda eng, s=s, psC=psC, S=S, dst=dst: eng.tensor_tensor(
                    dst[:, s * NB:(s + 1) * NB], S[:, s * NB:(s + 1) * NB],
                    psC[:, 0:NB], ALU.mult),
                    reads=[S, psC], writes=[dst], tag="ymul")
            if n >= 2:
                e = "dve" if (n % 2 == 0) else "pool"
                prog.op(e, lambda eng, tm=tm, yt=yt: eng.tensor_tensor(
                    yt[:, :], yt[:, :], tm[:, :], ALU.add),
                    reads=[tm, yt], writes=[yt], tag="yacc")
        prog.op("dve", lambda eng: eng.tensor_tensor(ya[:, :], ya[:, :], yb[:, :],
                                                     ALU.add),
                reads=[ya, yb], writes=[ya], tag="yab")
        prog.op("dve", lambda eng, c=c: eng.scalar_tensor_tensor(
            y_bf[:, c * T:(c + 1) * T], hcf[:, :], Dcols[:, c:c + 1], ya[:, :],
            ALU.mult, ALU.add),
            reads=[hcf, Dcols, ya], writes=[(y_bf.name, c)], tag="yfin")
    ctx.es = old_es
    prog.barrier()
    es3.close()

    es4 = ExitStack()
    ctx.es = es4
    wo = ctx.load_w_bf16(WoutT_d, DM, DM, tag="m1_wo")
    xre = [ctx.sbuf("m1_xre", (128, T)) for _ in range(2)]
    xo = [ctx.sbuf("m1_xo", (128, T)) for _ in range(2)]
    for m in range(8):
        xr = xre[m % 2]
        xw = xo[m % 2]
        dma(prog, "sp", xr[:, :], xin[b, m * 128:(m + 1) * 128, :],
            reads=[(xin.tensor.name, b, m)], writes=[xr.name], tag="rex")
        for s in range(NS):
            ps = ctx.pp.get()
            for k in range(8):
                prog.op("pe", lambda eng, m=m, s=s, k=k, ps=ps: eng.matmul(
                    ps[0:128, 0:NB], wo[:, k * DM + m * 128: k * DM + (m + 1) * 128],
                    y_bf[:, k * T + s * NB: k * T + (s + 1) * NB],
                    start=(k == 0), stop=(k == 7)),
                    reads=[(wo.name, k), (y_bf.name, k)], writes=[ps], tag="wom")
            prog.op("dve", lambda eng, s=s, ps=ps, xr=xr, xw=xw, m=m: eng.scalar_tensor_tensor(
                xw[:, s * NB:(s + 1) * NB], ps[0:128, 0:NB], 1.0,
                xr[:, s * NB:(s + 1) * NB], ALU.mult, ALU.add),
                reads=[ps, xr.name], writes=[xw.name], tag="resid")
        dma(prog, "sp", xout[b, m * 128:(m + 1) * 128, :], xw[:, :],
            reads=[xw.name], writes=[(xout.tensor.name, b, m)], tag="stx")
    ctx.es = old_es
    prog.barrier()
    es4.close()
    es34.close()
    es23.close()
    es13.close()



def _sel(n, w):
    s = np.zeros((n, n * w), np.float32)
    for i in range(n):
        s[i, i * w:(i + 1) * w] = 1.0
    return s


def const_arrays():
    sel16 = _sel(16, 128)
    return {
        "c_ones_red": np.ones((128, 1), np.float32),
        "c_ones_bc": np.ones((1, 128), np.float32),
        "c_ident": np.eye(128, dtype=np.float32),
        "c_sel16": sel16,
        "c_scal": np.tile(np.array([[1.0, 1e-5, 1e-6, 1e-30]], np.float32), (128, 1)),
        "c_ones_red_bf": np.ones((128, 1), ml_dtypes.bfloat16),
        "c_rev8": np.tile(np.arange(7, -1, -1, dtype=np.float32)[None, :], (128, 1)),
        "c_sel32": _sel(32, 128),
        "c_trimask": np.where(np.arange(128)[:, None] <= np.arange(128)[None, :],
                              0.0, -1e30).astype(np.float32),
        "c_dm_l1": np.tile((np.arange(128) < 64).astype(np.float32)[None, :], (32, 1)),
        "c_dm_l2": np.tile((np.arange(128) >= 64).astype(np.float32)[None, :], (32, 1)),
        "c_dm_m1": (np.arange(32)[:, None] == 2 * np.arange(16)[None, :]).astype(np.float32),
        "c_dm_m2": (np.arange(32)[:, None] == 2 * np.arange(16)[None, :] + 1).astype(np.float32),
        "c_ident_bf": np.eye(128, dtype=ml_dtypes.bfloat16),
        "c_cmask": np.concatenate(
            [np.where(np.arange(128)[:, None] + 128 * i <= np.arange(512)[None, :],
                      0.0, -1e30).astype(np.float32) for i in range(4)], axis=1),
    }


BASE_CONSTS = ("c_ones_red", "c_ones_bc", "c_ident", "c_scal")


def declare_consts(ctx, extra=()):
    nc, prog, es = ctx.nc, ctx.prog, ctx.es
    arrs = const_arrays()
    want = set(BASE_CONSTS) | set(extra)
    arrs = {k: v for k, v in arrs.items() if k in want}
    for name, arr in arrs.items():
        dt = BF16 if arr.dtype == ml_dtypes.bfloat16 else F32
        d = nc.dram_tensor(name + "_d", list(arr.shape), dt, kind="ExternalInput").ap()
        t = sb(nc, es, name, list(arr.shape), dt)
        dma(prog, "sp", t[:, :], d, reads=[d], writes=[name], tag="ldc")
        setattr(ctx, name, t[:, :])
    ctx.pp = PsumPool(nc, es, n=8)


SCAL_IDX = {1.0: 0, 1e-5: 1, 1e-6: 2, 1e-30: 3}


def scal_ap(ctx, v, p=128):
    i = SCAL_IDX[v]
    return ctx.c_scal[0:p, i:i + 1]


def row_bcast_sb(ctx, row_sb, W, tag="rb"):
    prog = ctx.prog
    out = ctx.sbuf(tag, (128, W))
    for s in range(0, W, NB):
        w = min(NB, W - s)
        ps = ctx.pp.get()
        prog.op("pe", lambda eng, s=s, w=w, ps=ps: eng.matmul(
            ps[:, 0:w], ctx.c_ones_bc, row_sb[0:1, s:s + w], start=True, stop=True),
            reads=[row_sb, "c_ones_bc"], writes=[ps], tag=tag + "m")
        prog.op("dve", lambda eng, s=s, w=w, ps=ps: eng.tensor_copy(
            out[:, s:s + w], ps[:, 0:w]),
            reads=[ps], writes=[out], tag=tag + "e")
    return out


def load_row(ctx, vec_d, n, tag="lr"):
    row = ctx.sbuf(tag, (1, n))
    dma(ctx.prog, "sp", row[:, :], vec_d.rearrange("(o n) -> o n", o=1),
        reads=[vec_d], writes=[row], tag=tag)
    return row


def emit_out_proj(ctx, b, W_d, bias_cols, y_bf, ykey, xin, xout, kc, T, tag="op"):
    prog = ctx.prog
    from contextlib import ExitStack
    es4 = ExitStack()
    old = ctx.es
    ctx.es = es4
    wo = ctx.load_w_bf16(W_d, kc * 128, DM, tag=tag + "w")
    xre = [ctx.sbuf(tag + "xr", (128, T)) for _ in range(2)]
    xo = [ctx.sbuf(tag + "xo", (128, T)) for _ in range(2)]
    NS = T // NB
    for m in range(8):
        xr = xre[m % 2]
        xw = xo[m % 2]
        dma(prog, "sp", xr[:, :], xin[b, m * 128:(m + 1) * 128, :],
            reads=[(xin.tensor.name, b, m)], writes=[xr.name], tag=tag + "rex")
        pss = [ctx.pp.get() for _ in range(NS)]
        for k in range(kc):
            for s in range(NS):
                prog.op("pe", lambda eng, m=m, s=s, k=k, ps=pss[s]: eng.matmul(
                    ps[0:128, 0:NB], wo[:, k * DM + m * 128: k * DM + (m + 1) * 128],
                    y_bf[:, k * T + s * NB: k * T + (s + 1) * NB],
                    start=(k == 0), stop=(k == kc - 1)),
                    reads=[(wo.name, k), (ykey, k)], writes=[pss[s]], tag=tag + "mm")
        for s in range(NS):
            ps = pss[s]
            if bias_cols is not None:
                tmp = ctx.pp.get()
                prog.op("dve", lambda eng, s=s, ps=ps, xr=xr, m=m: eng.scalar_tensor_tensor(
                    ps[0:128, 0:NB], ps[0:128, 0:NB], bias_cols[:, m:m + 1],
                    xr[:, s * NB:(s + 1) * NB], ALU.add, ALU.add),
                    reads=[ps, bias_cols, xr.name], writes=[ps], tag=tag + "bi")
                prog.op("dve", lambda eng, s=s, ps=ps, xw=xw: eng.tensor_copy(
                    xw[:, s * NB:(s + 1) * NB], ps[0:128, 0:NB]),
                    reads=[ps], writes=[xw.name], tag=tag + "cp")
            else:
                prog.op("dve", lambda eng, s=s, ps=ps, xr=xr, xw=xw: eng.scalar_tensor_tensor(
                    xw[:, s * NB:(s + 1) * NB], ps[0:128, 0:NB], 1.0,
                    xr[:, s * NB:(s + 1) * NB], ALU.mult, ALU.add),
                    reads=[ps, xr.name], writes=[xw.name], tag=tag + "res")
        dma(prog, "sp", xout[b, m * 128:(m + 1) * 128, :], xw[:, :],
            reads=[xw.name], writes=[(xout.tensor.name, b, m)], tag=tag + "stx")
    ctx.es = old
    prog.barrier()
    es4.close()


def emit_attn0(ctx, b, xin, xout, wqT_d, bq_d, wkT_d, bk_d, wvT_d, bv_d, woT_d,
               bo_d):
    nc, prog = ctx.nc, ctx.prog
    from contextlib import ExitStack
    T = SEQ
    NS = T // NB
    NCH = 8
    NH = 16
    old = ctx.es

    es_a = ExitStack()
    ctx.es = es_a
    Qt = ctx.sbuf("a0_Q", (128, NCH * T), BF16)
    Kt = ctx.sbuf("a0_K", (128, NCH * T), BF16)
    Vt = ctx.sbuf("a0_V", (128, 16 * 1024), BF16)
    Yt = ctx.sbuf("a0_Y", (128, NCH * T), BF16)
    bqc = ctx.feat_cols(bq_d, DM, tag="a0bq")
    bkc = ctx.feat_cols(bk_d, DM, tag="a0bk")
    boc = ctx.feat_cols(bo_d, DM, tag="a0bo")

    es1 = ExitStack()
    ctx.es = es1
    h_bf = ctx.sbuf("a0_h", (128, NCH * T), BF16)
    es1b = ExitStack()
    ctx.es = es1b
    loader = dram_chunk_loader(
        ctx, lambda c: (xin[b, c * 128:(c + 1) * 128, :], (xin.tensor.name, b, c)),
        T, tag="a0xl", nbuf=1)
    emit_ln_fm(ctx, loader, T, NCH, h_bf, "a0_hk", tag=f"a0ln{b}", nsq=1)
    prog.barrier()
    es1b.close()

    es2 = ExitStack()
    ctx.es = es2
    for which, wd, bc, out_t, okey in (("q", wqT_d, bqc, Qt, "a0_Qk"),
                                       ("k", wkT_d, bkc, Kt, "a0_Kk")):
        ws = ctx.load_w_bf16(wd, DM, DM, tag=f"a0w{which}")

        def evac(info, m0, m1, s, ps, bc=bc, out_t=out_t, okey=okey):
            m = m0 // 128
            prog.op("dve", lambda eng, m=m, s=s, ps=ps: eng.tensor_scalar(
                out_t[:, m * T + s * NB: m * T + (s + 1) * NB], ps[0:128, 0:NB],
                bc[:, m:m + 1], None, ALU.add),
                reads=[ps, bc], writes=[(okey, m)], tag="a0ev" + which)
        emit_proj_tiles(ctx, ws, DM, h_bf, "a0_hk", NCH, T,
                        [(m * 128, (m + 1) * 128, m) for m in range(8)], evac,
                        tag="a0p" + which)
    prog.barrier()
    es2.close()
    es2 = ExitStack()
    ctx.es = es2
    bvrow = load_row(ctx, bv_d, DM, tag="a0bvr")
    bvb = row_bcast_sb(ctx, bvrow, DM, tag="a0bvb")
    wv = ctx.load_w_bf16(wvT_d, DM, DM, tag="a0wv")
    for tt in range(16):
        for s in range(2):
            ps = ctx.pp.get()
            for k in range(NCH):
                prog.op("pe", lambda eng, tt=tt, s=s, k=k, ps=ps: eng.matmul(
                    ps[0:128, 0:NB],
                    h_bf[:, k * T + tt * 128: k * T + (tt + 1) * 128],
                    wv[:, k * DM + s * NB: k * DM + (s + 1) * NB],
                    start=(k == 0), stop=(k == NCH - 1)),
                    reads=[("a0_hk", k), (wv.name, k)], writes=[ps], tag="a0pv")
            prog.op("dve", lambda eng, tt=tt, s=s, ps=ps: eng.scalar_tensor_tensor(
                Vt[:, tt * 1024 + s * NB: tt * 1024 + (s + 1) * NB],
                ps[0:128, 0:NB], 1.0, bvb[:, s * NB:(s + 1) * NB],
                ALU.mult, ALU.add),
                reads=[ps, bvb], writes=[("a0_Vk", tt)], tag="a0evv")
    ctx.es = old
    prog.barrier()
    es2.close()
    es1.close()

    es3 = ExitStack()
    ctx.es = es3
    Pt = ctx.sbuf("a0_P", (128, 16 * NB), BF16)
    srow = ctx.sbuf("a0_srow", (1, NB))
    rrow = ctx.sbuf("a0_rrow", (1, NB))
    yev = ctx.sbuf("a0_yev", (64, NB))
    qk_tiles = ctx.pp.tiles[0:5]
    t_sum = ctx.pp.tiles[5]
    t_r = ctx.pp.tiles[6]
    t_y = ctx.pp.tiles[7]
    qkc = [0]
    for h in range(NH):
        hb = (h % 2) * 64
        hc = h // 2
        for qb in range(NS):
            for kt in range(16):
                ps = qk_tiles[qkc[0] % 5]
                qkc[0] += 1
                prog.op("pe", lambda eng, h=h, qb=qb, kt=kt, ps=ps, hb=hb, hc=hc: eng.matmul(
                    ps[0:128, 0:NB],
                    Kt[hb:hb + 64, hc * T + kt * 128: hc * T + (kt + 1) * 128],
                    Qt[hb:hb + 64, hc * T + qb * NB: hc * T + (qb + 1) * NB],
                    start=True, stop=True),
                    reads=[("a0_Kk", hc), ("a0_Qk", hc)], writes=[ps], tag="a0qk")
                prog.op("act", lambda eng, kt=kt, ps=ps: eng.activation(
                    Pt[:, kt * NB:(kt + 1) * NB], ps[0:128, 0:NB], AF.Exp,
                    scale=0.25),
                    reads=[ps], writes=[(Pt.name, kt)], tag="a0ex")
            ps_sum = t_sum
            for kt in range(16):
                prog.op("pe", lambda eng, kt=kt: eng.matmul(
                    ps_sum[0:1, 0:NB], ctx.c_ones_red_bf,
                    Pt[:, kt * NB:(kt + 1) * NB],
                    start=(kt == 0), stop=(kt == 15)),
                    reads=[(Pt.name, kt), "c_ones_red_bf"], writes=[ps_sum],
                    tag="a0sm")
            prog.op("dve", lambda eng: eng.tensor_copy(srow[:, :], ps_sum[0:1, 0:NB]),
                    reads=[ps_sum], writes=[srow], tag="a0se")
            prog.op("dve", lambda eng: eng.reciprocal(rrow[:, :], srow[:, :]),
                    reads=[srow], writes=[rrow], tag="a0rc")
            ps_r = t_r
            prog.op("pe", lambda eng, ps_r=ps_r: eng.matmul(
                ps_r[:, 0:NB], ctx.c_ones_bc, rrow[0:1, 0:NB], start=True, stop=True),
                reads=[rrow, "c_ones_bc"], writes=[ps_r], tag="a0rb")
            ps_y = t_y
            for kt in range(16):
                prog.op("pe", lambda eng, h=h, kt=kt, ps_y=ps_y: eng.matmul(
                    ps_y[0:64, 0:NB], Vt[:, kt * 1024 + h * 64: kt * 1024 + (h + 1) * 64],
                    Pt[:, kt * NB:(kt + 1) * NB],
                    start=(kt == 0), stop=(kt == 15)),
                    reads=[("a0_Vk", kt), (Pt.name, kt)], writes=[ps_y], tag="a0pv2")
            prog.op("dve", lambda eng, ps_y=ps_y: eng.tensor_copy(
                yev[:, :], ps_y[0:64, 0:NB]),
                reads=[ps_y], writes=[yev], tag="a0ye")
            prog.op("dve", lambda eng, h=h, qb=qb, ps_r=ps_r, hb=hb, hc=hc: eng.tensor_tensor(
                Yt[hb:hb + 64, hc * T + qb * NB: hc * T + (qb + 1) * NB],
                yev[:, :], ps_r[0:64, 0:NB], ALU.mult),
                reads=[yev, ps_r], writes=[("a0_Yk", hc)], tag="a0yw")
    ctx.es = old
    prog.barrier()
    es3.close()

    emit_out_proj(ctx, b, woT_d, boc, Yt, "a0_Yk", xin, xout, NCH, T, tag="a0o")
    es_a.close()


def emit_router(ctx, b, xin, h_out_d, rw_d, rb_d, gate_d, idx_d, tag="rt"):
    nc, prog = ctx.nc, ctx.prog
    from contextlib import ExitStack
    T = SEQ
    NCH = 8
    old = ctx.es
    es = ExitStack()
    ctx.es = es
    h_bf = ctx.sbuf(tag + "_h", (128, NCH * T), BF16)
    es1 = ExitStack()
    ctx.es = es1
    loader = dram_chunk_loader(
        ctx, lambda c: (xin[b, c * 128:(c + 1) * 128, :], (xin.tensor.name, b, c)),
        T, tag=tag + "xl", nbuf=1)
    emit_ln_fm(ctx, loader, T, NCH, h_bf, tag + "_hk", tag=tag + f"ln{b}", nsq=1)
    prog.barrier()
    es1.close()
    es2 = ExitStack()
    ctx.es = es2
    for c in range(NCH):
        dma(prog, "pool", h_out_d[b, c * 128:(c + 1) * 128, :],
            h_bf[:, c * T:(c + 1) * T],
            reads=[(tag + "_hk", c)], writes=[(h_out_d.tensor.name, b, c)],
            tag=tag + "sth")
    rw = ctx.load_w_bf16(rw_d, DM, 8, tag=tag + "rw")
    rbrow = load_row(ctx, rb_d, 8, tag=tag + "rb")
    psb = ctx.pp.get()
    prog.op("pe", lambda eng: eng.matmul(psb[0:128, 0:8], ctx.c_ones_bc,
                                         rbrow[0:1, 0:8], start=True, stop=True),
            reads=[rbrow, "c_ones_bc"], writes=[psb], tag=tag + "rbb")
    rbb = ctx.sbuf(tag + "rbbs", (128, 8))
    prog.op("dve", lambda eng: eng.tensor_copy(rbb[:, :], psb[0:128, 0:8]),
            reads=[psb], writes=[rbb], tag=tag + "rbe")
    lg = ctx.sbuf(tag + "lg", (128, 8))
    m1 = ctx.sbuf(tag + "m", (128, 1))
    ex = ctx.sbuf(tag + "ex", (128, 8))
    srow = ctx.sbuf(tag + "sr", (128, 1))
    grow = ctx.sbuf(tag + "g", (128, 1))
    msk = ctx.sbuf(tag + "msk", (128, 8))
    irow = ctx.sbuf(tag + "i", (128, 1))
    for tt in range(16):
        ps = ctx.pp.get()
        for k in range(NCH):
            prog.op("pe", lambda eng, tt=tt, k=k, ps=ps: eng.matmul(
                ps[0:128, 0:8],
                h_bf[:, k * T + tt * 128: k * T + (tt + 1) * 128],
                rw[:, k * 8:(k + 1) * 8],
                start=(k == 0), stop=(k == NCH - 1)),
                reads=[(tag + "_hk", k), (rw.name, k)], writes=[ps], tag=tag + "lgm")
        prog.op("dve", lambda eng, ps=ps: eng.tensor_tensor(
            lg[:, :], ps[0:128, 0:8], rbb[:, :], ALU.add),
            reads=[ps, rbb], writes=[lg], tag=tag + "lge")
        prog.op("dve", lambda eng: eng.tensor_reduce(m1[:, :], lg[:, :], AX.X, ALU.max),
                reads=[lg], writes=[m1], tag=tag + "mx")
        prog.op("dve", lambda eng: eng.tensor_scalar(ex[:, :], lg[:, :], m1[:, 0:1],
                                                     None, ALU.subtract),
                reads=[lg, m1], writes=[ex], tag=tag + "sub")
        prog.op("act", lambda eng: eng.activation(ex[:, :], ex[:, :], AF.Exp),
                reads=[ex], writes=[ex], tag=tag + "exp")
        prog.op("dve", lambda eng: eng.tensor_reduce(srow[:, :], ex[:, :], AX.X,
                                                     ALU.add),
                reads=[ex], writes=[srow], tag=tag + "sm")
        prog.op("dve", lambda eng: eng.reciprocal(grow[:, :], srow[:, :]),
                reads=[srow], writes=[grow], tag=tag + "rc")
        prog.op("dve", lambda eng: eng.tensor_scalar(msk[:, :], lg[:, :], m1[:, 0:1],
                                                     None, ALU.is_ge),
                reads=[lg, m1], writes=[msk], tag=tag + "eq")
        prog.op("dve", lambda eng: eng.tensor_tensor(msk[:, :], msk[:, :],
                                                     ctx.c_rev8, ALU.mult),
                reads=[msk, "c_rev8"], writes=[msk], tag=tag + "rv")
        prog.op("dve", lambda eng: eng.tensor_reduce(irow[:, :], msk[:, :], AX.X,
                                                     ALU.max),
                reads=[msk], writes=[irow], tag=tag + "am")
        prog.op("dve", lambda eng: eng.tensor_scalar(irow[:, :], irow[:, :], -1.0,
                                                     7.0, ALU.mult, ALU.add),
                reads=[irow], writes=[irow], tag=tag + "rv2")
        dma(prog, "sp", gate_d[b, tt, :].rearrange("(p o) -> p o", o=1), grow[:, :],
            reads=[grow], writes=[(gate_d.tensor.name, b, tt)], tag=tag + "stg")
        dma(prog, "sp", idx_d[b, tt, :].rearrange("(p o) -> p o", o=1), irow[:, :],
            reads=[irow], writes=[(idx_d.tensor.name, b, tt)], tag=tag + "sti")
    ctx.es = old
    prog.barrier()
    es2.close()
    es.close()


def emit_expert_l1(ctx, Xg_d, w_d, bias_d, Yg_d, nslot, tag="e1"):
    nc, prog = ctx.nc, ctx.prog
    from contextlib import ExitStack
    old = ctx.es
    es = ExitStack()
    ctx.es = es
    NSL = nslot // NB
    xg = ctx.sbuf(tag + "x", (128, 8 * nslot), BF16)
    for c in range(8):
        dma(prog, "pool", xg[:, c * nslot:(c + 1) * nslot],
            Xg_d[c * 128:(c + 1) * 128, :], reads=[(Xg_d.tensor.name, c)],
            writes=[(xg.name, c)], tag=tag + "lx")
    ws = ctx.load_w_bf16(w_d, DM, DM, tag=tag + "w")
    bc = ctx.feat_cols(bias_d, DM, tag=tag + "b")
    yo = [ctx.sbuf(tag + "y", (128, NB)) for _ in range(2)]
    for m in range(8):
        for s in range(NSL):
            ps = ctx.pp.get()
            for k in range(8):
                prog.op("pe", lambda eng, m=m, s=s, k=k, ps=ps: eng.matmul(
                    ps[0:128, 0:NB], ws[:, k * DM + m * 128: k * DM + (m + 1) * 128],
                    xg[:, k * nslot + s * NB: k * nslot + (s + 1) * NB],
                    start=(k == 0), stop=(k == 7)),
                    reads=[(ws.name, k), (xg.name, k)], writes=[ps], tag=tag + "mm")
            y = yo[(m * NSL + s) % 2]
            prog.op("dve", lambda eng, ps=ps, y=y, m=m: eng.tensor_scalar(
                y[:, :], ps[0:128, 0:NB], bc[:, m:m + 1], None, ALU.add),
                reads=[ps, bc], writes=[y.name], tag=tag + "ev")
            dma(prog, "sp", Yg_d[m * 128:(m + 1) * 128, s * NB:(s + 1) * NB],
                y[:, :], reads=[y.name], writes=[(Yg_d.tensor.name, m, s)],
                tag=tag + "st")
    ctx.es = old
    prog.barrier()
    es.close()


def emit_expert_l0(ctx, Xg_d, w1_d, b1_d, w2_d, b2_d, Yg_d, nslot, tag="e0"):
    nc, prog = ctx.nc, ctx.prog
    from contextlib import ExitStack
    old = ctx.es
    es = ExitStack()
    ctx.es = es
    EXP = 4096
    xg = ctx.sbuf(tag + "x", (128, 8 * nslot), BF16)
    for c in range(8):
        dma(prog, "pool", xg[:, c * nslot:(c + 1) * nslot],
            Xg_d[c * 128:(c + 1) * 128, :], reads=[(Xg_d.tensor.name, c)],
            writes=[(xg.name, c)], tag=tag + "lx")
    w1 = ctx.load_w_bf16(w1_d, DM, EXP, tag=tag + "w1")
    w2 = ctx.load_w_bf16(w2_d, EXP, DM, tag=tag + "w2")
    b1c = ctx.feat_cols(b1_d, EXP, tag=tag + "b1")
    b2c = ctx.feat_cols(b2_d, DM, tag=tag + "b2")
    h1 = ctx.sbuf(tag + "h1", (128, 32 * NB), BF16)
    yo = [ctx.sbuf(tag + "y", (128, NB)) for _ in range(2)]
    for s in range(nslot // NB):
        for m in range(32):
            ps = ctx.pp.get()
            for k in range(8):
                prog.op("pe", lambda eng, m=m, s=s, k=k, ps=ps: eng.matmul(
                    ps[0:128, 0:NB], w1[:, k * EXP + m * 128: k * EXP + (m + 1) * 128],
                    xg[:, k * nslot + s * NB: k * nslot + (s + 1) * NB],
                    start=(k == 0), stop=(k == 7)),
                    reads=[(w1.name, k), (xg.name, k)], writes=[ps], tag=tag + "m1")
            prog.op("act", lambda eng, m=m, ps=ps: eng.activation(
                h1[:, m * NB:(m + 1) * NB], ps[0:128, 0:NB], AF.Relu,
                bias=b1c[:, m:m + 1]),
                reads=[ps, b1c], writes=[(h1.name, m)], tag=tag + "rl")
        for m in range(8):
            ps = ctx.pp.get()
            for k in range(32):
                prog.op("pe", lambda eng, m=m, k=k, ps=ps: eng.matmul(
                    ps[0:128, 0:NB], w2[:, k * DM + m * 128: k * DM + (m + 1) * 128],
                    h1[:, k * NB:(k + 1) * NB],
                    start=(k == 0), stop=(k == 31)),
                    reads=[(w2.name, k), (h1.name, k)], writes=[ps], tag=tag + "m2")
            y = yo[m % 2]
            prog.op("dve", lambda eng, ps=ps, y=y, m=m: eng.tensor_scalar(
                y[:, :], ps[0:128, 0:NB], b2c[:, m:m + 1], None, ALU.add),
                reads=[ps, b2c], writes=[y.name], tag=tag + "ev")
            dma(prog, "sp", Yg_d[m * 128:(m + 1) * 128, s * NB:(s + 1) * NB],
                y[:, :], reads=[y.name], writes=[(Yg_d.tensor.name, m, s)],
                tag=tag + "st")
    ctx.es = old
    prog.barrier()
    es.close()


def emit_combine(ctx, b, xin, xout, Ysel_d, gk_d, mode, tag="cb"):
    nc, prog = ctx.nc, ctx.prog
    from contextlib import ExitStack
    T = SEQ
    NCH = 8
    old = ctx.es
    es = ExitStack()
    ctx.es = es
    hf = None
    if mode == "keep":
        hf = ctx.sbuf(tag + "_hf", (128, NCH * T))
        es1 = ExitStack()
        ctx.es = es1
        loader = dram_chunk_loader(
            ctx, lambda c: (xin[b, c * 128:(c + 1) * 128, :],
                            (xin.tensor.name, b, c)), T, tag=tag + "xl", nbuf=1)
        emit_ln_fm(ctx, loader, T, NCH, hf, tag + "_hfk", tag=tag + f"ln{b}", nsq=1)
        prog.barrier()
        es1.close()
        ctx.es = es
    grow = ctx.sbuf(tag + "g", (1, T))
    dma(prog, "sp", grow[:, :], gk_d[b, :, :], reads=[(gk_d.tensor.name, b)],
        writes=[grow], tag=tag + "lg")
    xr = [ctx.sbuf(tag + "xr", (128, T)) for _ in range(2)]
    yr = [ctx.sbuf(tag + "yr", (128, T)) for _ in range(2)]
    xw = [ctx.sbuf(tag + "xw", (128, T)) for _ in range(2)]
    gb = []
    for s in range(T // NB):
        pg = ctx.pp.get()
        bcast_row(ctx.prog, ctx.c_ones_bc, grow, pg, s * NB, (s + 1) * NB,
                  grow.name, pg.name, tag=tag + "gb")
        gb.append(pg)
    for c in range(NCH):
        x_ = xr[c % 2]
        y_ = yr[c % 2]
        w_ = xw[c % 2]
        dma(prog, "sp", x_[:, :], xin[b, c * 128:(c + 1) * 128, :],
            reads=[(xin.tensor.name, b, c)], writes=[x_.name], tag=tag + "lx")
        dma(prog, "pool", y_[:, :], Ysel_d[b, c * 128:(c + 1) * 128, :],
            reads=[(Ysel_d.tensor.name, b, c)], writes=[y_.name], tag=tag + "ly")
        for s in range(T // NB):
            sl = slice(s * NB, (s + 1) * NB)
            if mode == "gate":
                prog.op("dve", lambda eng, sl=sl, s=s, x_=x_, y_=y_, w_=w_: eng.tensor_tensor(
                    w_[:, sl], y_[:, sl], gb[s][:, 0:NB], ALU.mult),
                    reads=[y_.name, gb[s]], writes=[w_.name], tag=tag + "g1")
                prog.op("dve", lambda eng, sl=sl, x_=x_, w_=w_: eng.tensor_tensor(
                    w_[:, sl], w_[:, sl], x_[:, sl], ALU.add),
                    reads=[w_.name, x_.name], writes=[w_.name], tag=tag + "g2")
            else:
                prog.op("dve", lambda eng, sl=sl, s=s, c=c, y_=y_, w_=w_: eng.tensor_tensor(
                    w_[:, sl], y_[:, sl],
                    hf[:, c * T + s * NB: c * T + (s + 1) * NB], ALU.subtract),
                    reads=[y_.name, (tag + "_hfk", c)], writes=[w_.name],
                    tag=tag + "se")
                prog.op("dve", lambda eng, sl=sl, s=s, w_=w_: eng.tensor_tensor(
                    w_[:, sl], w_[:, sl], gb[s][:, 0:NB], ALU.mult),
                    reads=[w_.name, gb[s]], writes=[w_.name], tag=tag + "s1b")
                prog.op("dve", lambda eng, sl=sl, s=s, c=c, w_=w_: eng.tensor_tensor(
                    w_[:, sl], w_[:, sl],
                    hf[:, c * T + s * NB: c * T + (s + 1) * NB], ALU.add),
                    reads=[w_.name, (tag + "_hfk", c)], writes=[w_.name],
                    tag=tag + "s1c")
                prog.op("dve", lambda eng, sl=sl, x_=x_, w_=w_: eng.tensor_tensor(
                    w_[:, sl], w_[:, sl], x_[:, sl], ALU.add),
                    reads=[w_.name, x_.name], writes=[w_.name], tag=tag + "s2")
        dma(prog, "sp", xout[b, c * 128:(c + 1) * 128, :], w_[:, :],
            reads=[w_.name], writes=[(xout.tensor.name, b, c)], tag=tag + "st")
    ctx.es = old
    prog.barrier()
    es.close()


def emit_s5(ctx, xin, xout, logdt_d, logLre_d, Lim_d, Vre_d, Vim_d, Wre_d, Wim_d,
            SL, tag="s5"):
    nc, prog = ctx.nc, ctx.prog
    from contextlib import ExitStack
    NCH = 8
    TB = 16
    PI = float(np.pi)
    old = ctx.es
    es = ExitStack()
    ctx.es = es

    r_dt = load_row(ctx, logdt_d, 64, tag=tag + "dt")
    prog.op("act", lambda eng: eng.activation(r_dt[:, :], r_dt[:, :], AF.Exp),
            reads=[r_dt], writes=[r_dt], tag=tag + "e1")
    r_lre = load_row(ctx, logLre_d, 64, tag=tag + "lr")
    prog.op("act", lambda eng: eng.activation(r_lre[:, :], r_lre[:, :], AF.Exp),
            reads=[r_lre], writes=[r_lre], tag=tag + "e2")
    r_lim = load_row(ctx, Lim_d, 64, tag=tag + "li")
    rphi = ctx.sbuf(tag + "phi", (1, 64))
    rth = ctx.sbuf(tag + "th", (1, 64))
    prog.op("dve", lambda eng: eng.scalar_tensor_tensor(
        rphi[:, :], r_lre[:, :], -1.0, r_dt[:, :], ALU.mult, ALU.mult),
        reads=[r_lre, r_dt], writes=[rphi], tag=tag + "ph")
    prog.op("dve", lambda eng: eng.tensor_tensor(rth[:, :], r_lim[:, :], r_dt[:, :],
                                                 ALU.mult),
            reads=[r_lim, r_dt], writes=[rth], tag=tag + "tt")
    prog.op("act", lambda eng: eng.activation(rphi[:, :], rphi[:, :], AF.Exp),
            reads=[rphi], writes=[rphi], tag=tag + "e3")

    def rangered(row, outrow, tg):
        msk = ctx.sbuf(tg + "mk", (1, 64))
        prog.op("dve", lambda eng: eng.tensor_copy(outrow[:, :], row[:, :]),
                reads=[row], writes=[outrow], tag=tg + "cp")
        for k in (1024, 512, 256, 128, 64, 32, 16, 8, 4, 2, 1, 1):
            thr = 2 * PI * k - PI
            prog.op("dve", lambda eng, thr=thr: eng.tensor_single_scalar(
                msk[:, :], outrow[:, :], thr, ALU.is_gt),
                reads=[outrow], writes=[msk], tag=tg + "g")
            prog.op("dve", lambda eng, k=k: eng.scalar_tensor_tensor(
                outrow[:, :], msk[:, :], -2 * PI * k, outrow[:, :], ALU.mult,
                ALU.add),
                reads=[msk, outrow], writes=[outrow], tag=tg + "f")
            prog.op("dve", lambda eng, thr=thr: eng.tensor_single_scalar(
                msk[:, :], outrow[:, :], -thr, ALU.is_lt),
                reads=[outrow], writes=[msk], tag=tg + "g2")
            prog.op("dve", lambda eng, k=k: eng.scalar_tensor_tensor(
                outrow[:, :], msk[:, :], 2 * PI * k, outrow[:, :], ALU.mult,
                ALU.add),
                reads=[msk, outrow], writes=[outrow], tag=tg + "f2")

    rsin = ctx.sbuf(tag + "sn", (1, 64))
    rcos = ctx.sbuf(tag + "cs", (1, 64))
    rtmp = ctx.sbuf(tag + "tm", (1, 64))
    rangered(rth, rsin, tag + "r1")
    prog.op("act", lambda eng: eng.activation(rsin[:, :], rsin[:, :], AF.Sin),
            reads=[rsin], writes=[rsin], tag=tag + "s1")
    prog.op("dve", lambda eng: eng.tensor_scalar(rtmp[:, :], rth[:, :], -1.0, PI / 2,
                                                 ALU.mult, ALU.add),
            reads=[rth], writes=[rtmp], tag=tag + "c0")
    rangered(rtmp, rcos, tag + "r2")
    prog.op("act", lambda eng: eng.activation(rcos[:, :], rcos[:, :], AF.Sin),
            reads=[rcos], writes=[rcos], tag=tag + "s2")
    rare = ctx.sbuf(tag + "ar", (1, 64))
    raim = ctx.sbuf(tag + "ai", (1, 64))
    prog.op("dve", lambda eng: eng.tensor_tensor(rare[:, :], rphi[:, :], rcos[:, :],
                                                 ALU.mult),
            reads=[rphi, rcos], writes=[rare], tag=tag + "ar")
    prog.op("dve", lambda eng: eng.tensor_tensor(raim[:, :], rphi[:, :], rsin[:, :],
                                                 ALU.mult),
            reads=[rphi, rsin], writes=[raim], tag=tag + "ai")

    mats = {}
    for nm, d in (("vre", Vre_d), ("vim", Vim_d), ("wre", Wre_d), ("wim", Wim_d)):
        t = ctx.sbuf(tag + nm, (64, 64))
        dma(prog, "sp", t[:, :], d, reads=[d], writes=[t], tag=tag + "l" + nm)
        mats[nm] = t

    def bc64(row, tg):
        ps = ctx.pp.get()
        prog.op("pe", lambda eng, ps=ps: eng.matmul(
            ps[0:64, 0:64], ctx.c_ones_bc[0:1, 0:64], row[0:1, 0:64],
            start=True, stop=True),
            reads=[row, "c_ones_bc"], writes=[ps], tag=tg + "b")
        t = ctx.sbuf(tg + "bs", (64, 64))
        prog.op("dve", lambda eng, ps=ps: eng.tensor_copy(t[:, :], ps[0:64, 0:64]),
                reads=[ps], writes=[t], tag=tg + "e")
        return t

    areb = bc64(rare, tag + "A")
    aimb = bc64(raim, tag + "B")
    dtb = bc64(r_dt, tag + "D")

    def mk64(tg):
        return ctx.sbuf(tg, (64, 64))

    pre_, pim_, vdre, vdim = mk64(tag + "pre"), mk64(tag + "pim"), mk64(tag + "vdr"), mk64(tag + "vdi")
    for dst, t1, s1, t2, s2, op in ((pre_, "vre", areb, "vim", aimb, ALU.subtract),
                                    (pim_, "vre", aimb, "vim", areb, ALU.add)):
        tmp = mk64(tag + "tmq" + t2 + str(op))
        prog.op("dve", lambda eng, dst=dst, t1=t1, s1=s1: eng.tensor_tensor(
            dst[:, :], mats[t1][:, :], s1[:, :], ALU.mult),
            reads=[mats[t1], s1], writes=[dst], tag=tag + "p1")
        prog.op("dve", lambda eng, tmp=tmp, t2=t2, s2=s2: eng.tensor_tensor(
            tmp[:, :], mats[t2][:, :], s2[:, :], ALU.mult),
            reads=[mats[t2], s2], writes=[tmp], tag=tag + "p2")
        prog.op("dve", lambda eng, dst=dst, tmp=tmp, op=op: eng.tensor_tensor(
            dst[:, :], dst[:, :], tmp[:, :], op),
            reads=[dst, tmp], writes=[dst], tag=tag + "p3")
    prog.op("dve", lambda eng: eng.tensor_tensor(vdre[:, :], mats["vre"][:, :],
                                                 dtb[:, :], ALU.mult),
            reads=[mats["vre"], dtb], writes=[vdre], tag=tag + "v1")
    prog.op("dve", lambda eng: eng.tensor_tensor(vdim[:, :], mats["vim"][:, :],
                                                 dtb[:, :], ALU.mult),
            reads=[mats["vim"], dtb], writes=[vdim], tag=tag + "v2")

    def tr64(src, tg):
        ps = ctx.pp.get()
        prog.op("pe", lambda eng, ps=ps, src=src: eng.transpose(
            ps[0:64, 0:64], src[:, :], ctx.c_ident[0:64, 0:64]),
            reads=[src, "c_ident"], writes=[ps], tag=tg + "t")
        t = mk64(tg + "T")
        prog.op("dve", lambda eng, ps=ps, t=t: eng.tensor_copy(t[:, :], ps[0:64, 0:64]),
                reads=[ps], writes=[t], tag=tg + "e")
        return t

    vdreT = tr64(vdre, tag + "x1")
    vdimT = tr64(vdim, tag + "x2")
    nvdimT = mk64(tag + "nvd")
    prog.op("dve", lambda eng: eng.tensor_scalar(nvdimT[:, :], vdimT[:, :], -1.0,
                                                 None, ALU.mult),
            reads=[vdimT], writes=[nvdimT], tag=tag + "ng")
    gts = {}
    for nm, l1, r1, l2, r2 in (("gre", vdreT, "wre", nvdimT, "wim"),
                               ("gim", vdreT, "wim", vdimT, "wre")):
        ps = ctx.pp.get()
        prog.op("pe", lambda eng, ps=ps, l1=l1, r1=r1: eng.matmul(
            ps[0:64, 0:64], l1[:, :], mats[r1][:, :], start=True, stop=False),
            reads=[l1, mats[r1]], writes=[ps], tag=tag + nm + "1")
        prog.op("pe", lambda eng, ps=ps, l2=l2, r2=r2: eng.matmul(
            ps[0:64, 0:64], l2[:, :], mats[r2][:, :], start=False, stop=True),
            reads=[l2, mats[r2]], writes=[ps], tag=tag + nm + "2")
        g = mk64(tag + nm)
        prog.op("dve", lambda eng, ps=ps, g=g: eng.tensor_copy(g[:, :], ps[0:64, 0:64]),
                reads=[ps], writes=[g], tag=tag + nm + "e")
        gts[nm] = g
    gTre = tr64(gts["gre"], tag + "g1")
    gTim = tr64(gts["gim"], tag + "g2")
    pTre = tr64(pre_, tag + "g3")
    pTim = tr64(pim_, tag + "g4")
    npTim = mk64(tag + "npt")
    prog.op("dve", lambda eng, npTim=npTim, pTim=pTim: eng.tensor_scalar(
        npTim[:, :], pTim[:, :], -1.0, None, ALU.mult),
            reads=[pTim], writes=[npTim], tag=tag + "np")

    def dup128(src_t, tg):
        t = ctx.sbuf(tg, (128, 64))
        for pb in (0, 64):
            prog.op("dve", lambda eng, pb=pb, t=t, src_t=src_t: eng.tensor_copy(
                t[pb:pb + 64, :], src_t[:, :]),
                reads=[src_t], writes=[t], tag=tg + "d")
        return t

    gTre = dup128(gTre, tag + "Gr")
    gTim = dup128(gTim, tag + "Gi")
    pTre = dup128(pTre, tag + "Pr")
    pTim = dup128(pTim, tag + "Pi")
    npTim = dup128(npTim, tag + "Pn")

    zre = [ctx.sbuf(tag + "zr", (128, 8 * SL)) for _ in range(2)]
    zim = [ctx.sbuf(tag + "zi", (128, 8 * SL)) for _ in range(2)]
    hbuf = [ctx.sbuf(tag + "h", (128, 8 * 2 * SL)) for _ in range(2)]
    xre_ = [ctx.sbuf(tag + "xr", (128, SL)) for _ in range(2)]
    xw_ = [ctx.sbuf(tag + "xw", (128, SL)) for _ in range(2)]

    def pair_loader(tpair, hb):
        def f(c, passno):
            key = (hb.name + "ld", c)
            for j in range(2):
                dma(prog, "sp", hb[:, c * 2 * SL + j * SL: c * 2 * SL + (j + 1) * SL],
                    xin[2 * tpair + j, c * 128:(c + 1) * 128, :],
                    reads=[(xin.tensor.name, 2 * tpair + j, c)],
                    writes=[key, (hb.name + "#h", c)], tag=tag + "hl")
            return hb[:, c * 2 * SL:(c + 1) * 2 * SL], key
        return f

    for tpair in range(8):
        hb = hbuf[tpair % 2]
        emit_ln_fm(ctx, pair_loader(tpair, hb), 2 * SL, NCH, hb,
                   hb.name + "#h", tag=tag + f"ln{tpair}", nsq=1)
        for j in range(2):
            t = 2 * tpair + j
            zo = zre[t % 2]
            zoi = zim[t % 2]
            zn = zre[(t + 1) % 2]
            zni = zim[(t + 1) % 2]
            for c in range(NCH):
                for half in range(2):
                    pb = half * 64
                    psr = ctx.pp.get()
                    psi = ctx.pp.get()
                    hsl = hb[pb:pb + 64, c * 2 * SL + j * SL: c * 2 * SL + (j + 1) * SL]
                    rds = [(hb.name + "#h", c)]
                    prog.op("pe", lambda eng, psr=psr, hsl=hsl, pb=pb, t=t: eng.matmul(
                        psr[0:64, 0:SL], gTre[pb:pb + 64, :], hsl, start=True,
                        stop=(t == 0)),
                        reads=rds + [gTre], writes=[psr], tag=tag + "c1")
                    prog.op("pe", lambda eng, psi=psi, hsl=hsl, pb=pb, t=t: eng.matmul(
                        psi[0:64, 0:SL], gTim[pb:pb + 64, :], hsl, start=True,
                        stop=(t == 0)),
                        reads=rds + [gTim], writes=[psi], tag=tag + "c2")
                    if t > 0:
                        zsl_r = zo[pb:pb + 64, c * SL:(c + 1) * SL]
                        zsl_i = zoi[pb:pb + 64, c * SL:(c + 1) * SL]
                        zrd = [(zo.name, c), (zoi.name, c)]
                        prog.op("pe", lambda eng, psr=psr, zsl_r=zsl_r, pb=pb: eng.matmul(
                            psr[0:64, 0:SL], pTre[pb:pb + 64, :], zsl_r, start=False,
                            stop=False),
                            reads=zrd + [pTre], writes=[psr], tag=tag + "z1")
                        prog.op("pe", lambda eng, psr=psr, zsl_i=zsl_i, pb=pb: eng.matmul(
                            psr[0:64, 0:SL], npTim[pb:pb + 64, :], zsl_i, start=False,
                            stop=True),
                            reads=zrd + [npTim], writes=[psr], tag=tag + "z2")
                        prog.op("pe", lambda eng, psi=psi, zsl_i=zsl_i, pb=pb: eng.matmul(
                            psi[0:64, 0:SL], pTre[pb:pb + 64, :], zsl_i, start=False,
                            stop=False),
                            reads=zrd + [pTre], writes=[psi], tag=tag + "z3")
                        prog.op("pe", lambda eng, psi=psi, zsl_r=zsl_r, pb=pb: eng.matmul(
                            psi[0:64, 0:SL], pTim[pb:pb + 64, :], zsl_r, start=False,
                            stop=True),
                            reads=zrd + [pTim], writes=[psi], tag=tag + "z4")
                    prog.op("dve", lambda eng, psr=psr, zn=zn, c=c, pb=pb: eng.tensor_copy(
                        zn[pb:pb + 64, c * SL:(c + 1) * SL], psr[0:64, 0:SL]),
                        reads=[psr], writes=[(zn.name, c)], tag=tag + "ze")
                    prog.op("dve", lambda eng, psi=psi, zni=zni, c=c, pb=pb: eng.tensor_copy(
                        zni[pb:pb + 64, c * SL:(c + 1) * SL], psi[0:64, 0:SL]),
                        reads=[psi], writes=[(zni.name, c)], tag=tag + "zi")
            for c in range(NCH):
                xr = xre_[c % 2]
                xw = xw_[c % 2]
                dma(prog, "sp", xr[:, :], xin[t, c * 128:(c + 1) * 128, :],
                    reads=[(xin.tensor.name, t, c)], writes=[xr.name],
                    tag=tag + "rx")
                prog.op("dve", lambda eng, xr=xr, xw=xw, zn=zn, c=c: eng.scalar_tensor_tensor(
                    xw[:, :], zn[:, c * SL:(c + 1) * SL], 2.0, xr[:, :],
                    ALU.mult, ALU.add),
                    reads=[(zn.name, c), xr.name], writes=[xw.name], tag=tag + "ot")
                dma(prog, "sp", xout[t, c * 128:(c + 1) * 128, :], xw[:, :],
                    reads=[xw.name], writes=[(xout.tensor.name, t, c)],
                    tag=tag + "sx")
    ctx.es = old
    prog.barrier()
    es.close()


def emit_m2core(ctx, b, xin, xout, W_in_d, convw_d, convb_d, Alog_d, Dm_d,
                dtb_d, mnw_d, Wout_d, scr, tag="mc"):
    nc, prog = ctx.nc, ctx.prog
    from contextlib import ExitStack
    T = SEQ
    NS = T // NB
    NCH = 8
    DI = 2048
    NH = 32
    old = ctx.es
    es0 = ExitStack()
    ctx.es = es0

    dtr = ctx.sbuf(tag + "dt", (32, T))
    esA = ExitStack()
    ctx.es = esA
    h_bf = ctx.sbuf(tag + "h", (128, NCH * T), BF16)
    esA1 = ExitStack()
    ctx.es = esA1
    loader = dram_chunk_loader(
        ctx, lambda c: (xin[b, c * 128:(c + 1) * 128, :], (xin.tensor.name, b, c)),
        T, tag=tag + "xl", nbuf=1)
    emit_ln_fm(ctx, loader, T, NCH, h_bf, tag + "_hk", tag=tag + f"ln{b}", nsq=1,
               rms=True, eps=1e-6)
    prog.barrier()
    esA1.close()
    esA2 = ExitStack()
    ctx.es = esA2
    w_in = ctx.load_w_bf16(W_in_d, DM, 4384, tag=tag + "wi")
    ev = [ctx.sbuf(tag + "ev", (128, NB), BF16) for _ in range(2)]
    evf = [ctx.sbuf(tag + "evf", (128, NB)) for _ in range(2)]

    def evac_zx(info, m0, m1, s, ps):
        kind, r = info
        if kind == "z":
            e = ev[(r * NS + s) % 2]
            prog.op("dve", lambda eng, ps=ps, e=e: eng.tensor_copy(
                e[:, :], ps[0:128, 0:NB]), reads=[ps], writes=[e.name],
                tag=tag + "ez")
            dma(prog, "sp", scr["zD"][b, r * 128:(r + 1) * 128, s * NB:(s + 1) * NB],
                e[:, :], reads=[e.name], writes=[("zD", b, r)], tag=tag + "sz")
        elif kind == "x":
            e = evf[(r * NS + s) % 2]
            prog.op("dve", lambda eng, ps=ps, e=e: eng.tensor_copy(
                e[:, :], ps[0:128, 0:NB]), reads=[ps], writes=[e.name],
                tag=tag + "ex")
            dma(prog, "sp", scr["xbcD"][b, r * 128:(r + 1) * 128, s * NB:(s + 1) * NB],
                e[:, :], reads=[e.name], writes=[("xbcD", b, r)], tag=tag + "sx")
        else:
            prog.op("dve", lambda eng, ps=ps, s=s: eng.tensor_copy(
                dtr[:, s * NB:(s + 1) * NB], ps[0:32, 0:NB]),
                reads=[ps], writes=[dtr], tag=tag + "ed")

    mt = [(r * 128, (r + 1) * 128, ("z", r)) for r in range(16)] + \
         [(2048 + r * 128, 2048 + (r + 1) * 128, ("x", r)) for r in range(18)] + \
         [(4352, 4384, ("d", 0))]
    emit_proj_tiles(ctx, w_in, 4384, h_bf, tag + "_hk", NCH, T, mt, evac_zx,
                    tag=tag + "pj")
    prog.barrier()
    esA2.close()
    ctx.es = esA
    prog.barrier()
    esA.close()
    ctx.es = es0

    Bfb = ctx.sbuf(tag + "Bfb", (128, T), BF16)
    Cfb = ctx.sbuf(tag + "Cfb", (128, T), BF16)
    y_bf = ctx.sbuf(tag + "ybf", (128, 16 * T), BF16)
    esM = ExitStack()
    ctx.es = esM
    dtbc = ctx.feat_cols(dtb_d, 32, tag=tag + "db")
    acol = ctx.feat_cols(Alog_d, 32, tag=tag + "al")
    prog.op("act", lambda eng: eng.activation(acol[:, :], acol[:, :], AF.Exp),
            reads=[acol], writes=[acol], tag=tag + "ae")
    prog.op("dve", lambda eng: eng.tensor_scalar(dtr[:, :], dtr[:, :], dtbc[:, 0:1],
                                                 None, ALU.add),
            reads=[dtr, dtbc], writes=[dtr], tag=tag + "d1")
    prog.op("act", lambda eng: eng.activation(dtr[:, :], dtr[:, :], AF.Exp),
            reads=[dtr], writes=[dtr], tag=tag + "d2")
    prog.op("act", lambda eng: eng.activation(dtr[:, :], dtr[:, :], AF.Ln,
                                              bias=scal_ap(ctx, 1.0, 32)),
            reads=[dtr, "c_scal"], writes=[dtr], tag=tag + "d3")
    la = ctx.sbuf(tag + "la", (32, T))
    prog.op("dve", lambda eng: eng.tensor_scalar(la[:, :], dtr[:, :], acol[:, 0:1],
                                                 None, ALU.mult),
            reads=[dtr, acol], writes=[la], tag=tag + "la")
    prog.op("dve", lambda eng: eng.tensor_scalar(la[:, :], la[:, :], -1.0, None,
                                                 ALU.mult),
            reads=[la], writes=[la], tag=tag + "lan")
    ones32 = ctx.sbuf(tag + "o32", (32, 128))
    prog.op("dve", lambda eng: eng.memset(ones32[:, :], 1.0),
            reads=[], writes=[ones32], tag=tag + "o1")
    cs = ctx.sbuf(tag + "cs", (32, T))
    for tc in range(16):
        prog.op("dve", lambda eng, tc=tc: eng.tensor_tensor_scan(
            cs[:, tc * 128:(tc + 1) * 128], ones32[:, :],
            la[:, tc * 128:(tc + 1) * 128], 0.0, ALU.mult, ALU.add),
            reads=[ones32, la], writes=[cs], tag=tag + "cum")
    biasr = ctx.sbuf(tag + "bi", (32, T))
    prog.op("act", lambda eng: eng.activation(biasr[:, :], dtr[:, :], AF.Ln),
            reads=[dtr], writes=[biasr], tag=tag + "ld")
    prog.op("dve", lambda eng: eng.tensor_tensor(biasr[:, :], biasr[:, :], cs[:, :],
                                                 ALU.subtract),
            reads=[biasr, cs], writes=[biasr], tag=tag + "b2")
    biasT = ctx.sbuf(tag + "bT", (128, 16 * 32))
    biasT2 = ctx.sbuf(tag + "bT2", (128, 16 * 32))
    b2row = ctx.sbuf(tag + "b2r", (32, 128))
    for tc in range(16):
        prog.op("dve", lambda eng, tc=tc: eng.tensor_scalar(
            b2row[:, :], biasr[:, tc * 128:(tc + 1) * 128],
            cs[:, tc * 128 + 127: tc * 128 + 128], None, ALU.add),
            reads=[biasr, cs], writes=[b2row], tag=tag + "b2a")
        for src_, dst in ((biasr, biasT), (b2row, biasT2)):
            ps = ctx.pp.get()
            if src_ is biasr:
                prog.op("pe", lambda eng, tc=tc, ps=ps, src_=src_: eng.transpose(
                    ps[0:128, 0:32], src_[:, tc * 128:(tc + 1) * 128],
                    ctx.c_ident[0:32, 0:32]),
                    reads=[src_, "c_ident"], writes=[ps], tag=tag + "bt")
            else:
                prog.op("pe", lambda eng, ps=ps, src_=src_: eng.transpose(
                    ps[0:128, 0:32], src_[:, :],
                    ctx.c_ident[0:32, 0:32]),
                    reads=[src_, "c_ident"], writes=[ps], tag=tag + "bt2")
            prog.op("dve", lambda eng, tc=tc, ps=ps, dst=dst: eng.tensor_copy(
                dst[:, tc * 32:(tc + 1) * 32], ps[0:128, 0:32]),
                reads=[ps], writes=[dst], tag=tag + "be")
    ebias = ctx.sbuf(tag + "eb", (128, 16 * 32))
    prog.op("act", lambda eng: eng.activation(ebias[:, :], biasT2[:, :], AF.Exp),
            reads=[biasT2], writes=[ebias], tag=tag + "ebx")
    gam = ctx.sbuf(tag + "gm", (32, 16))
    for tc in range(16):
        prog.op("dve", lambda eng, tc=tc: eng.tensor_copy(
            gam[:, tc:tc + 1], cs[:, tc * 128 + 127: tc * 128 + 128]),
            reads=[cs], writes=[gam], tag=tag + "g1")
    prog.op("act", lambda eng: eng.activation(gam[:, :], gam[:, :], AF.Exp),
            reads=[gam], writes=[gam], tag=tag + "g2")
    gamT = ctx.sbuf(tag + "gT", (128, 16 * 32))
    psg = ctx.pp.get()
    prog.op("pe", lambda eng: eng.transpose(psg[0:16, 0:32], gam[:, :],
                                            ctx.c_ident[0:32, 0:32]),
            reads=[gam, "c_ident"], writes=[psg], tag=tag + "gt2")
    gamr = ctx.sbuf(tag + "gr", (16, 32))
    prog.op("dve", lambda eng: eng.tensor_copy(gamr[:, :], psg[0:16, 0:32]),
            reads=[psg], writes=[gamr], tag=tag + "ge")
    for tc in range(16):
        ps = ctx.pp.get()
        prog.op("pe", lambda eng, tc=tc, ps=ps: eng.matmul(
            ps[0:128, 0:32], ctx.c_sel16[:, tc * 128: tc * 128 + 128],
            gamr[0:16, 0:32], start=True, stop=True),
            reads=[gamr, "c_sel16"], writes=[ps], tag=tag + "gb")
        prog.op("dve", lambda eng, tc=tc, ps=ps: eng.tensor_copy(
            gamT[:, tc * 32:(tc + 1) * 32], ps[0:128, 0:32]),
            reads=[ps], writes=[gamT], tag=tag + "gbe")

    esB = ExitStack()
    ctx.es = esB
    cw = [ctx.feat_cols(convw_d[k, :], 2304, tag=tag + f"cw{k}") for k in range(4)]
    cb = ctx.feat_cols(convb_d, 2304, tag=tag + "cb")
    Bf = ctx.sbuf(tag + "Bf", (128, T))
    Cf = ctx.sbuf(tag + "Cf", (128, T))
    xcb = [ctx.sbuf(tag + "xc", (128, T)) for _ in range(2)]
    ycb = [ctx.sbuf(tag + "yc", (128, T)) for _ in range(2)]
    ycbb = [ctx.sbuf(tag + "yb", (128, T), BF16) for _ in range(2)]
    for r in range(18):
        xc = xcb[r % 2]
        yc = ycb[r % 2]
        dma(prog, "sp", xc[:, :], scr["xbcD"][b, r * 128:(r + 1) * 128, :],
            reads=[("xbcD", b, r)], writes=[xc.name], tag=tag + "lc")
        prog.op("dve", lambda eng, r=r, xc=xc, yc=yc: eng.tensor_scalar(
            yc[:, :], xc[:, :], cw[3][:, r:r + 1], cb[:, r:r + 1], ALU.mult,
            ALU.add),
            reads=[xc.name, cw[3], cb], writes=[yc.name], tag=tag + "c3")
        for k in range(3):
            sh = 3 - k
            prog.op("dve", lambda eng, r=r, k=k, sh=sh, xc=xc, yc=yc: eng.scalar_tensor_tensor(
                yc[:, sh:T], xc[:, 0:T - sh], cw[k][:, r:r + 1], yc[:, sh:T],
                ALU.mult, ALU.add),
                reads=[xc.name, cw[k], yc.name], writes=[yc.name], tag=tag + "ck")
        if r < 16:
            yb = ycbb[r % 2]
            prog.op("act", lambda eng, yc=yc, yb=yb: eng.activation(
                yb[:, :], yc[:, :], AF.Silu),
                reads=[yc.name], writes=[yb.name], tag=tag + "si")
            dma(prog, "sp", scr["xmD"][b, r * 128:(r + 1) * 128, :], yb[:, :],
                reads=[yb.name], writes=[("xmD", b, r)], tag=tag + "sm")
        else:
            dst = Bf if r == 16 else Cf
            prog.op("act", lambda eng, yc=yc, dst=dst: eng.activation(
                dst[:, :], yc[:, :], AF.Silu),
                reads=[yc.name], writes=[dst], tag=tag + "sB")
    prog.op("dve", lambda eng: eng.tensor_copy(Bfb[:, :], Bf[:, :]),
            reads=[Bf], writes=[Bfb], tag=tag + "bc1")
    prog.op("dve", lambda eng: eng.tensor_copy(Cfb[:, :], Cf[:, :]),
            reads=[Cf], writes=[Cfb], tag=tag + "bc2")

    dma(prog, "sp", scr["BD"][b, :, :], Bfb[:, :], reads=[Bfb], writes=[("BD", b)],
        tag=tag + "sB2")
    dma(prog, "sp", scr["CD"][b, :, :], Cfb[:, :], reads=[Cfb], writes=[("CD", b)],
        tag=tag + "sC2")
    prog.barrier()
    esB.close()

    esC = ExitStack()
    ctx.es = esC
    S_f = ctx.sbuf(tag + "S", (128, NH * 64))
    S_b = ctx.sbuf(tag + "Sb", (128, NH * 64), BF16)
    prog.op("dve", lambda eng: eng.memset(S_f[:, :], 0.0), reads=[], writes=[S_f],
            tag=tag + "s0")
    prog.op("dve", lambda eng: eng.memset(S_b[:, :], 0.0), reads=[], writes=[S_b],
            tag=tag + "s0b")
    Xt = [ctx.sbuf(tag + "Xt", (128, DI), BF16) for _ in range(2)]
    Bt_ = [ctx.sbuf(tag + "Bt", (128, 128), BF16) for _ in range(2)]
    NROT = 3
    CSb_r = [ctx.sbuf(tag + "CSb", (128, 128)) for _ in range(NROT)]
    Wm_r = [ctx.sbuf(tag + "Wm", (128, 128), BF16) for _ in range(NROT)]
    CDt_r = [ctx.sbuf(tag + "CD", (128, 128), BF16) for _ in range(NROT)]
    Bd_r = [ctx.sbuf(tag + "Bd", (128, 128), BF16) for _ in range(NROT)]
    dec3_r = [ctx.sbuf(tag + "d3", (128, 128), BF16) for _ in range(NROT)]
    sup = [ctx.sbuf(tag + "su", (128, 64)) for _ in range(3)]
    for tc in range(16):
        Xtile = Xt[tc % 2]
        Btile = Bt_[tc % 2]
        dma(prog, "sp", Xtile[:, :],
            scr["xmD"][b, :, tc * 128:(tc + 1) * 128], transpose=True,
            reads=[("xmD", b, r) for r in range(16)], writes=[Xtile.name],
            tag=tag + "lxT")
        dma(prog, "sp", Btile[:, :],
            scr["BD"][b, :, tc * 128:(tc + 1) * 128], transpose=True,
            reads=[("BD", b)], writes=[Btile.name], tag=tag + "lbT")
        psG = ctx.pp.tiles[3]
        prog.op("pe", lambda eng, tc=tc, psG=psG: eng.matmul(
            psG[0:128, 0:128], Bfb[:, tc * 128:(tc + 1) * 128],
            Cfb[:, tc * 128:(tc + 1) * 128], start=True, stop=True),
            reads=[Bfb, Cfb], writes=[psG], tag=tag + "gg2")
        for h in range(NH):
            CSb = CSb_r[h % 3]
            Wm = Wm_r[h % 3]
            CDt = CDt_r[h % 3]
            Bd = Bd_r[h % 3]
            dec3 = dec3_r[h % 3]
            psc = ctx.pp.tiles[6 + (h % 2)]
            prog.op("pe", lambda eng, tc=tc, h=h, psc=psc: eng.matmul(
                psc[0:128, 0:128], ctx.c_sel32[:, h * 128:(h + 1) * 128],
                cs[0:32, tc * 128:(tc + 1) * 128], start=True, stop=True),
                reads=[cs, "c_sel32"], writes=[psc], tag=tag + "cb2")
            prog.op("dve", lambda eng, psc=psc, CSb=CSb: eng.tensor_tensor(
                CSb[:, :], psc[0:128, 0:128], ctx.c_trimask, ALU.add),
                reads=[psc, "c_trimask"], writes=[CSb], tag=tag + "msk")
            prog.op("act", lambda eng, tc=tc, h=h, Wm=Wm, CSb=CSb: eng.activation(
                Wm[:, :], CSb[:, :], AF.Exp, bias=biasT[:, tc * 32 + h: tc * 32 + h + 1]),
                reads=[CSb, biasT], writes=[Wm], tag=tag + "wx")
            prog.op("act", lambda eng, psc=psc, dec3=dec3: eng.activation(
                dec3[:, :], psc[0:128, 0:128], AF.Exp),
                reads=[psc], writes=[dec3], tag=tag + "dc3")
            prog.op("dve", lambda eng, psG=psG, Wm=Wm: eng.tensor_tensor(
                Wm[:, :], Wm[:, :], psG[0:128, 0:128], ALU.mult),
                reads=[Wm, psG], writes=[Wm], tag=tag + "mG")
            prog.op("dve", lambda eng, tc=tc, CDt=CDt, dec3=dec3: eng.tensor_tensor(
                CDt[:, :], Cfb[:, tc * 128:(tc + 1) * 128], dec3[:, :], ALU.mult),
                reads=[Cfb, dec3], writes=[CDt], tag=tag + "cd")
            prog.op("dve", lambda eng, tc=tc, h=h, Btile=Btile, Bd=Bd: eng.tensor_scalar(
                Bd[:, :], Btile[:, :], ebias[:, tc * 32 + h: tc * 32 + h + 1],
                None, ALU.mult),
                reads=[Btile.name, ebias], writes=[Bd], tag=tag + "bd")
            psY = ctx.pp.tiles[4 + (h % 2) if False else 5]
            prog.op("pe", lambda eng, h=h, psY=psY, Xtile=Xtile, Wm=Wm: eng.matmul(
                psY[0:64, 0:128], Xtile[:, h * 64:(h + 1) * 64], Wm[:, :],
                start=True, stop=False),
                reads=[Xtile.name, Wm], writes=[psY], tag=tag + "yi")
            prog.op("pe", lambda eng, h=h, psY=psY, CDt=CDt: eng.matmul(
                psY[0:64, 0:128], S_b[:, h * 64:(h + 1) * 64], CDt[:, :],
                start=False, stop=True),
                reads=[(S_b.name, h), CDt], writes=[psY], tag=tag + "ycr")
            psS = ctx.pp.tiles[4]
            prog.op("pe", lambda eng, h=h, psS=psS, Xtile=Xtile, Bd=Bd: eng.matmul(
                psS[0:128, 0:64], Bd[:, :], Xtile[:, h * 64:(h + 1) * 64],
                start=True, stop=True),
                reads=[Bd, Xtile.name], writes=[psS], tag=tag + "su")
            s_ = sup[h % 3]
            prog.op("dve", lambda eng, tc=tc, h=h, psS=psS, s_=s_: eng.scalar_tensor_tensor(
                s_[:, :], S_f[:, h * 64:(h + 1) * 64],
                gamT[:, tc * 32 + h: tc * 32 + h + 1], psS[0:128, 0:64],
                ALU.mult, ALU.add),
                reads=[psS, (S_f.name, h), gamT], writes=[s_.name], tag=tag + "sa")
            prog.op("dve", lambda eng, h=h, s_=s_: eng.tensor_copy(
                S_f[:, h * 64:(h + 1) * 64], s_[:, :]),
                reads=[s_.name], writes=[(S_f.name, h)], tag=tag + "sg")
            prog.op("dve", lambda eng, h=h: eng.tensor_copy(
                S_b[:, h * 64:(h + 1) * 64], S_f[:, h * 64:(h + 1) * 64]),
                reads=[(S_f.name, h)], writes=[(S_b.name, h)], tag=tag + "sb")
            hc = h // 2
            pb = (h % 2) * 64
            prog.op("dve", lambda eng, psY=psY, hc=hc, pb=pb, tc=tc: eng.tensor_copy(
                y_bf[pb:pb + 64, hc * T + tc * 128: hc * T + (tc + 1) * 128],
                psY[0:64, 0:128]),
                reads=[psY], writes=[(y_bf.name, hc)], tag=tag + "ye")
    prog.barrier()
    esC.close()
    esM.close()
    ctx.es = es0

    esD = ExitStack()
    ctx.es = esD
    dmrow = load_row(ctx, Dm_d, 32, tag=tag + "dm")
    psd = ctx.pp.get()
    prog.op("pe", lambda eng: eng.transpose(psd[0:32, 0:1], dmrow[0:1, 0:32],
                                            ctx.c_ident[0:1, 0:1]),
            reads=[dmrow, "c_ident"], writes=[psd], tag=tag + "dmt")
    dmcol = ctx.sbuf(tag + "dmo", (32, 1))
    prog.op("dve", lambda eng: eng.tensor_copy(dmcol[:, :], psd[0:32, 0:1]),
            reads=[psd], writes=[dmcol], tag=tag + "dme")
    r1 = ctx.sbuf(tag + "r1", (32, 16))
    r2 = ctx.sbuf(tag + "r2", (32, 16))
    prog.op("dve", lambda eng: eng.tensor_scalar(r1[:, :], ctx.c_dm_m1[0:32, :],
                                                 dmcol[:, 0:1], None, ALU.mult),
            reads=["c_dm_m1", dmcol], writes=[r1], tag=tag + "r1")
    prog.op("dve", lambda eng: eng.tensor_scalar(r2[:, :], ctx.c_dm_m2[0:32, :],
                                                 dmcol[:, 0:1], None, ALU.mult),
            reads=["c_dm_m2", dmcol], writes=[r2], tag=tag + "r2")
    psd2 = ctx.pp.get()
    prog.op("pe", lambda eng: eng.matmul(psd2[0:128, 0:16], ctx.c_dm_l1,
                                         r1[:, :], start=True, stop=False),
            reads=["c_dm_l1", r1], writes=[psd2], tag=tag + "dm1")
    prog.op("pe", lambda eng: eng.matmul(psd2[0:128, 0:16], ctx.c_dm_l2,
                                         r2[:, :], start=False, stop=True),
            reads=["c_dm_l2", r2], writes=[psd2], tag=tag + "dm2")
    dmdim = ctx.sbuf(tag + "dmd", (128, 16))
    prog.op("dve", lambda eng: eng.tensor_copy(dmdim[:, :], psd2[0:128, 0:16]),
            reads=[psd2], writes=[dmdim], tag=tag + "dm3")
    mnc = ctx.feat_cols(mnw_d, DI, tag=tag + "mn")
    zcb = [ctx.sbuf(tag + "zc", (128, T), BF16)] * 2
    xmc = [ctx.sbuf(tag + "xm", (128, T), BF16)] * 2
    y2c = [ctx.sbuf(tag + "y2", (128, T))] * 2
    sqt = [ctx.sbuf(tag + "sq", (128, T))] * 2
    y2b = y_bf
    ps_q = [ctx.pp.get() for _ in range(4)]

    def compute_y2(c, zc, xc, y2):
        dma(prog, "sp", zc[:, :], scr["zD"][b, c * 128:(c + 1) * 128, :],
            reads=[("zD", b, c)], writes=[zc.name], tag=tag + "lz")
        dma(prog, "pool", xc[:, :], scr["xmD"][b, c * 128:(c + 1) * 128, :],
            reads=[("xmD", b, c)], writes=[xc.name], tag=tag + "lm")
        prog.op("act", lambda eng, zc=zc: eng.activation(zc[:, :], zc[:, :],
                                                         AF.Silu),
                reads=[zc.name], writes=[zc.name], tag=tag + "sz2")
        prog.op("dve", lambda eng, c=c, xc=xc, y2=y2: eng.tensor_scalar(
            y2[:, :], xc[:, :], dmdim[:, c:c + 1], None, ALU.mult),
            reads=[xc.name, dmdim], writes=[y2.name], tag=tag + "dx")
        prog.op("dve", lambda eng, c=c, y2=y2: eng.tensor_tensor(
            y2[:, :], y2[:, :], y_bf[:, c * T:(c + 1) * T], ALU.add),
            reads=[y2.name, (y_bf.name, c)], writes=[y2.name], tag=tag + "ya")
        prog.op("dve", lambda eng, y2=y2, zc=zc: eng.tensor_tensor(
            y2[:, :], y2[:, :], zc[:, :], ALU.mult),
            reads=[y2.name, zc.name], writes=[y2.name], tag=tag + "yz")

    for c in range(16):
        zc = zcb[c % 2]
        xc = xmc[c % 2]
        y2 = y2c[c % 2]
        s_ = sqt[c % 2]
        compute_y2(c, zc, xc, y2)
        prog.op("dve", lambda eng, c=c, y2=y2: eng.tensor_copy(
            y2b[:, c * T:(c + 1) * T], y2[:, :]),
            reads=[y2.name], writes=[(y2b.name, c)], tag=tag + "yb2")
        prog.op("act", lambda eng, y2=y2, s_=s_: eng.activation(
            s_[:, :], y2[:, :], AF.Square),
            reads=[y2.name], writes=[s_.name], tag=tag + "q1")
        for s in range(NS):
            prog.op("pe", lambda eng, c=c, s=s, s_=s_: eng.matmul(
                ps_q[s][0:1, 0:NB], ctx.c_ones_red, s_[:, s * NB:(s + 1) * NB],
                start=(c == 0), stop=(c == 15)),
                reads=[s_.name, "c_ones_red"], writes=[ps_q[s]], tag=tag + "q2")
    rrow = ctx.sbuf(tag + "rr", (1, T))
    for s in range(NS):
        prog.op("dve", lambda eng, s=s: eng.tensor_copy(
            rrow[0:1, s * NB:(s + 1) * NB], ps_q[s][0:1, 0:NB]),
            reads=[ps_q[s]], writes=[rrow], tag=tag + "re")
    prog.op("dve", lambda eng: eng.tensor_scalar(rrow[:, :], rrow[:, :], 1.0 / DI,
                                                 None, ALU.mult),
            reads=[rrow], writes=[rrow], tag=tag + "rv")
    prog.op("act", lambda eng: eng.activation(rrow[:, :], rrow[:, :], AF.Ln,
                                              bias=scal_ap(ctx, 1e-5, 1)),
            reads=[rrow, "c_scal"], writes=[rrow], tag=tag + "rl")
    prog.op("act", lambda eng: eng.activation(rrow[:, :], rrow[:, :], AF.Exp,
                                              scale=-0.5),
            reads=[rrow], writes=[rrow], tag=tag + "rx")
    for s in range(NS):
        pR = ctx.pp.get()
        bcast_row(ctx.prog, ctx.c_ones_bc, rrow, pR, s * NB, (s + 1) * NB,
                  rrow.name, pR.name, tag=tag + "rb")
        for c in range(16):
            lo = c * T + s * NB
            hi = c * T + (s + 1) * NB
            prog.op("dve", lambda eng, lo=lo, hi=hi, pR=pR, c=c: eng.tensor_tensor(
                sqt[0][:, 0:NB], y2b[:, lo:hi], pR[:, 0:NB], ALU.mult),
                reads=[(y2b.name, c), pR], writes=[sqt[0]], tag=tag + "ap1")
            prog.op("dve", lambda eng, lo=lo, hi=hi, c=c: eng.tensor_scalar(
                y2b[:, lo:hi], sqt[0][:, 0:NB], mnc[:, c:c + 1], None, ALU.mult),
                reads=[sqt[0], mnc], writes=[(y2b.name, c)], tag=tag + "ap2")
    prog.barrier()
    esD.close()
    emit_out_proj(ctx, b, Wout_d, None, y2b, y2b.name, xin, xout, 16, T,
                  tag=tag + "o")
    ctx.es = old
    prog.barrier()
    es0.close()


def emit_m2attn(ctx, b, xin, xout, Wqkv_d, Wcp_d, qcw_d, qcb_d, kcw_d, kcb_d,
                vcw_d, vcb_d, tag="ma"):
    nc, prog = ctx.nc, ctx.prog
    from contextlib import ExitStack
    T = SEQ
    NS = T // NB
    NCH = 8
    NH = 16
    old = ctx.es
    es0 = ExitStack()
    ctx.es = es0
    qf = ctx.sbuf(tag + "q", (128, NCH * T), BF16)
    kf = ctx.sbuf(tag + "k", (64, T))
    vf = ctx.sbuf(tag + "v", (64, T))
    yt = ctx.sbuf(tag + "y", (128, NCH * T), BF16)
    convcols = {}
    for nm, wd, bd in (("q", qcw_d, qcb_d), ("k", kcw_d, kcb_d), ("v", vcw_d, vcb_d)):
        cols = []
        for k in range(3):
            c64 = ctx.feat_cols(wd[k, :], 64, tag=tag + f"cw{nm}{k}")
            c128 = ctx.sbuf(tag + f"cw2{nm}{k}", (128, 1))
            for pb in (0, 64):
                prog.op("dve", lambda eng, pb=pb, c64=c64, c128=c128: eng.tensor_copy(
                    c128[pb:pb + 64, :], c64[:, :]),
                    reads=[c64], writes=[c128], tag=tag + "cd")
            cols.append(c128)
        b64 = ctx.feat_cols(bd, 64, tag=tag + f"cb{nm}")
        b128 = ctx.sbuf(tag + f"cb2{nm}", (128, 1))
        for pb in (0, 64):
            prog.op("dve", lambda eng, pb=pb, b64=b64, b128=b128: eng.tensor_copy(
                b128[pb:pb + 64, :], b64[:, :]),
                reads=[b64], writes=[b128], tag=tag + "cd2")
        convcols[nm] = (cols, b128)

    es1 = ExitStack()
    ctx.es = es1
    h_bf = ctx.sbuf(tag + "h", (128, NCH * T), BF16)
    es1b = ExitStack()
    ctx.es = es1b
    loader = dram_chunk_loader(
        ctx, lambda c: (xin[b, c * 128:(c + 1) * 128, :], (xin.tensor.name, b, c)),
        T, tag=tag + "xl", nbuf=1)
    emit_ln_fm(ctx, loader, T, NCH, h_bf, tag + "_hk", tag=tag + f"ln{b}", nsq=1,
               rms=True, eps=1e-6)
    prog.barrier()
    es1b.close()
    es2 = ExitStack()
    ctx.es = es2
    wq = ctx.load_w_bf16(Wqkv_d, DM, 1152, tag=tag + "wq")
    cvd = [ctx.sbuf(tag + "cvd", (128, T)) for _ in range(2)]
    qcv = ctx.sbuf(tag + "qcv", (128, T))
    kvraw = ctx.sbuf(tag + "kvr", (128, T))
    kvc = qcv

    def evac_q(info, m0, m1, s, ps):
        r = info
        if r < 8:
            xc = cvd[r % 2]
            prog.op("dve", lambda eng, s=s, ps=ps, xc=xc: eng.tensor_copy(
                xc[:, s * NB:(s + 1) * NB], ps[0:128, 0:NB]),
                reads=[ps], writes=[xc.name], tag=tag + "eq")
            if s == NS - 1:
                cols, bcol = convcols["q"]
                prog.op("dve", lambda eng, xc=xc: eng.tensor_scalar(
                    qcv[:, :], xc[:, :], cols[2][:, 0:1], bcol[:, 0:1], ALU.mult,
                    ALU.add),
                    reads=[xc.name, cols[2], bcol], writes=[qcv.name],
                    tag=tag + "q3")
                for k in range(2):
                    sh = 2 - k
                    prog.op("dve", lambda eng, k=k, sh=sh, xc=xc: eng.scalar_tensor_tensor(
                        qcv[:, sh:T], xc[:, 0:T - sh],
                        cols[k][:, 0:1], qcv[:, sh:T], ALU.mult, ALU.add),
                        reads=[xc.name, cols[k], qcv.name],
                        writes=[qcv.name], tag=tag + "qk")
                prog.op("dve", lambda eng, r=r: eng.tensor_copy(
                    qf[:, r * T:(r + 1) * T], qcv[:, :]),
                    reads=[qcv.name], writes=[(qf.name, r)], tag=tag + "qf")
        else:
            pb = (r - 8) * 64
            prog.op("dve", lambda eng, pb=pb, s=s, ps=ps: eng.tensor_copy(
                kvraw[pb:pb + 64, s * NB:(s + 1) * NB], ps[0:64, 0:NB]),
                reads=[ps], writes=[(kvraw.name, pb)], tag=tag + "ekv")

    mt = [(r * 128, (r + 1) * 128, r) for r in range(8)] + \
         [(1024, 1088, 8), (1088, 1152, 9)]
    emit_proj_tiles(ctx, wq, 1152, h_bf, tag + "_hk", NCH, T, mt, evac_q,
                    tag=tag + "pq")
    for nm, pb, dst in (("k", 0, kf), ("v", 64, vf)):
        cols, bcol = convcols[nm]
        prog.op("dve", lambda eng, pb=pb, cols=cols, bcol=bcol: eng.tensor_scalar(
            kvc[pb:pb + 64, :], kvraw[pb:pb + 64, :], cols[2][pb:pb + 64, 0:1],
            bcol[pb:pb + 64, 0:1], ALU.mult, ALU.add),
            reads=[(kvraw.name, pb), cols[2], bcol], writes=[kvc.name],
            tag=tag + nm + "3")
        for k in range(2):
            sh = 2 - k
            prog.op("dve", lambda eng, pb=pb, k=k, sh=sh, cols=cols: eng.scalar_tensor_tensor(
                kvc[pb:pb + 64, sh:T], kvraw[pb:pb + 64, 0:T - sh],
                cols[k][pb:pb + 64, 0:1], kvc[pb:pb + 64, sh:T], ALU.mult, ALU.add),
                reads=[(kvraw.name, pb), cols[k], kvc.name],
                writes=[kvc.name], tag=tag + nm + "k")
        prog.op("dve", lambda eng, pb=pb, dst=dst: eng.tensor_copy(
            dst[:, :], kvc[pb:pb + 64, :]),
            reads=[kvc.name], writes=[dst], tag=tag + nm + "f")
    prog.barrier()
    es2.close()
    es1.close()

    es3 = ExitStack()
    ctx.es = es3
    kfb = ctx.sbuf(tag + "kfb", (128, T), BF16)
    for pb in (0, 64):
        prog.op("dve", lambda eng, pb=pb: eng.tensor_copy(
            kfb[pb:pb + 64, :], kf[:, :]),
            reads=[kf], writes=[kfb], tag=tag + "kb")
    vfb = ctx.sbuf(tag + "vfb", (64, T), BF16)
    prog.op("dve", lambda eng: eng.tensor_copy(vfb[:, :], vf[:, :]),
            reads=[vf], writes=[vfb], tag=tag + "vb")
    vtok = ctx.sbuf(tag + "vt", (128, 16 * 64), BF16)
    for tt in range(16):
        ps = ctx.pp.get()
        prog.op("pe", lambda eng, tt=tt, ps=ps: eng.transpose(
            ps[0:128, 0:64], vf[0:64, tt * 128:(tt + 1) * 128],
            ctx.c_ident[0:64, 0:64]),
            reads=[vf, "c_ident"], writes=[ps], tag=tag + "vt")
        prog.op("dve", lambda eng, tt=tt, ps=ps: eng.tensor_copy(
            vtok[:, tt * 64:(tt + 1) * 64], ps[0:128, 0:64]),
            reads=[ps], writes=[(vtok.name, tt)], tag=tag + "ve")
    Pt = ctx.sbuf(tag + "P", (128, 16 * NB), BF16)
    srow = ctx.sbuf(tag + "sr", (1, NB))
    rrow = ctx.sbuf(tag + "rr", (1, NB))
    yev = ctx.sbuf(tag + "ye", (64, NB))
    msks = ctx.sbuf(tag + "mk", (128, NB))
    qk_tiles = ctx.pp.tiles[0:5]
    t_sum = ctx.pp.tiles[5]
    t_r = ctx.pp.tiles[6]
    t_y = ctx.pp.tiles[7]
    qkc = [0]
    for h in range(NH):
        hb = (h % 2) * 64
        hc = h // 2
        for qb in range(NS):
            nkt = 4 * qb + 4
            for kt in range(nkt):
                ps = qk_tiles[qkc[0] % 5]
                qkc[0] += 1
                prog.op("pe", lambda eng, h=h, qb=qb, kt=kt, ps=ps, hb=hb, hc=hc: eng.matmul(
                    ps[0:128, 0:NB],
                    kfb[hb:hb + 64, kt * 128:(kt + 1) * 128],
                    qf[hb:hb + 64, hc * T + qb * NB: hc * T + (qb + 1) * NB],
                    start=True, stop=True),
                    reads=[kfb, (qf.name, hc)], writes=[ps], tag=tag + "qk")
                if kt >= 4 * qb:
                    i = kt - 4 * qb
                    prog.op("dve", lambda eng, i=i, ps=ps: eng.tensor_tensor(
                        msks[:, :], ps[0:128, 0:NB],
                        ctx.c_cmask[:, i * NB:(i + 1) * NB], ALU.add),
                        reads=[ps, "c_cmask"], writes=[msks], tag=tag + "cm")
                    prog.op("act", lambda eng, kt=kt: eng.activation(
                        Pt[:, kt * NB:(kt + 1) * NB], msks[:, :], AF.Exp,
                        scale=0.125),
                        reads=[msks], writes=[(Pt.name, kt)], tag=tag + "ex")
                else:
                    prog.op("act", lambda eng, kt=kt, ps=ps: eng.activation(
                        Pt[:, kt * NB:(kt + 1) * NB], ps[0:128, 0:NB], AF.Exp,
                        scale=0.125),
                        reads=[ps], writes=[(Pt.name, kt)], tag=tag + "ex2")
            for kt in range(nkt):
                prog.op("pe", lambda eng, kt=kt, nkt=nkt: eng.matmul(
                    t_sum[0:1, 0:NB], ctx.c_ones_red_bf,
                    Pt[:, kt * NB:(kt + 1) * NB],
                    start=(kt == 0), stop=(kt == nkt - 1)),
                    reads=[(Pt.name, kt), "c_ones_red_bf"], writes=[t_sum],
                    tag=tag + "sm")
            prog.op("dve", lambda eng: eng.tensor_copy(srow[:, :], t_sum[0:1, 0:NB]),
                    reads=[t_sum], writes=[srow], tag=tag + "se")
            prog.op("dve", lambda eng: eng.reciprocal(rrow[:, :], srow[:, :]),
                    reads=[srow], writes=[rrow], tag=tag + "rc")
            prog.op("pe", lambda eng: eng.matmul(
                t_r[:, 0:NB], ctx.c_ones_bc, rrow[0:1, 0:NB], start=True, stop=True),
                reads=[rrow, "c_ones_bc"], writes=[t_r], tag=tag + "rb")
            for kt in range(nkt):
                prog.op("pe", lambda eng, kt=kt, nkt=nkt: eng.matmul(
                    t_y[0:64, 0:NB], vtok[:, kt * 64:(kt + 1) * 64],
                    Pt[:, kt * NB:(kt + 1) * NB],
                    start=(kt == 0), stop=(kt == nkt - 1)),
                    reads=[(vtok.name, kt), (Pt.name, kt)], writes=[t_y],
                    tag=tag + "pv")
            prog.op("dve", lambda eng: eng.tensor_copy(yev[:, :], t_y[0:64, 0:NB]),
                    reads=[t_y], writes=[yev], tag=tag + "ye2")
            prog.op("dve", lambda eng, h=h, qb=qb, hb=hb, hc=hc: eng.tensor_tensor(
                yt[hb:hb + 64, hc * T + qb * NB: hc * T + (qb + 1) * NB],
                yev[:, :], t_r[0:64, 0:NB], ALU.mult),
                reads=[yev, t_r], writes=[(yt.name, hc)], tag=tag + "yw")
    prog.barrier()
    es3.close()
    emit_out_proj(ctx, b, Wcp_d, None, yt, yt.name, xin, xout, NCH, T, tag=tag + "o")
    ctx.es = old
    prog.barrier()
    es0.close()


def emit_m2rwkv(ctx, b, xin, xout, maak_d, maar_d, Wkey_d, Wrec_d, Wval_d, scr,
                tag="rw"):
    nc, prog = ctx.nc, ctx.prog
    from contextlib import ExitStack
    T = SEQ
    NS = T // NB
    NCH = 8
    FFN = 4096
    old = ctx.es
    es0 = ExitStack()
    ctx.es = es0
    sg_bf = ctx.sbuf(tag + "sg", (128, NCH * T), BF16)
    mkc = ctx.feat_cols(maak_d, DM, tag=tag + "mk")
    mrc = ctx.feat_cols(maar_d, DM, tag=tag + "mr")
    esX = ExitStack()
    ctx.es = esX
    xk_bf = ctx.sbuf(tag + "xk", (128, NCH * T), BF16)
    xr_bf = ctx.sbuf(tag + "xr", (128, NCH * T), BF16)
    es1 = ExitStack()
    ctx.es = es1
    h_f = ctx.sbuf(tag + "h", (128, NCH * T), BF16)
    es1b = ExitStack()
    ctx.es = es1b
    loader = dram_chunk_loader(
        ctx, lambda c: (xin[b, c * 128:(c + 1) * 128, :], (xin.tensor.name, b, c)),
        T, tag=tag + "xl", nbuf=1)
    emit_ln_fm(ctx, loader, T, NCH, h_f, tag + "_hk", tag=tag + f"ln{b}", nsq=1,
               rms=True, eps=1e-6)
    prog.barrier()
    es1b.close()
    ctx.es = es1
    xxb = ctx.sbuf(tag + "xx", (128, T), BF16)
    for c in range(NCH):
        prog.op("dve", lambda eng, c=c: eng.tensor_tensor(
            xxb[:, 1:T], h_f[:, c * T: c * T + T - 1], h_f[:, c * T + 1:(c + 1) * T],
            ALU.subtract),
            reads=[(tag + "_hk", c)], writes=[xxb], tag=tag + "x1")
        prog.op("dve", lambda eng, c=c: eng.tensor_scalar(
            xxb[:, 0:1], h_f[:, c * T: c * T + 1], -1.0, None, ALU.mult),
            reads=[(tag + "_hk", c)], writes=[xxb], tag=tag + "x2")
        prog.op("dve", lambda eng, c=c: eng.scalar_tensor_tensor(
            xk_bf[:, c * T:(c + 1) * T], xxb[:, :], mkc[:, c:c + 1],
            h_f[:, c * T:(c + 1) * T], ALU.mult, ALU.add),
            reads=[xxb, mkc, (tag + "_hk", c)], writes=[(xk_bf.name, c)],
            tag=tag + "x3")
        prog.op("dve", lambda eng, c=c: eng.scalar_tensor_tensor(
            xr_bf[:, c * T:(c + 1) * T], xxb[:, :], mrc[:, c:c + 1],
            h_f[:, c * T:(c + 1) * T], ALU.mult, ALU.add),
            reads=[xxb, mrc, (tag + "_hk", c)], writes=[(xr_bf.name, c)],
            tag=tag + "x4")
    prog.barrier()
    es1.close()
    es2 = ExitStack()
    ctx.es = es2
    wk = ctx.load_w_bf16(Wkey_d, DM, FFN, tag=tag + "wk")
    t1 = [ctx.sbuf(tag + "t1", (128, NB)) for _ in range(2)]
    t2 = [ctx.sbuf(tag + "t2", (128, NB), BF16) for _ in range(2)]

    def evac_kf(info, m0, m1, s, ps):
        r = info
        a = t1[(r * NS + s) % 2]
        bb = t2[(r * NS + s) % 2]
        prog.op("act", lambda eng, ps=ps, a=a: eng.activation(
            a[:, :], ps[0:128, 0:NB], AF.Relu),
            reads=[ps], writes=[a.name], tag=tag + "re")
        prog.op("act", lambda eng, a=a, bb=bb: eng.activation(
            bb[:, :], a[:, :], AF.Square),
            reads=[a.name], writes=[bb.name], tag=tag + "sq")
        dma(prog, "sp", scr["kfD"][b, m0:m1, s * NB:(s + 1) * NB], bb[:, :],
            reads=[bb.name], writes=[("kfD", b, r)], tag=tag + "skf")

    mt = [(r * 128, (r + 1) * 128, r) for r in range(32)]
    emit_proj_tiles(ctx, wk, FFN, xk_bf, xk_bf.name, NCH, T, mt, evac_kf,
                    tag=tag + "pk")
    prog.barrier()
    es2.close()
    es3 = ExitStack()
    ctx.es = es3
    wr = ctx.load_w_bf16(Wrec_d, DM, DM, tag=tag + "wr")

    def evac_sg(info, m0, m1, s, ps):
        r = info
        prog.op("act", lambda eng, ps=ps, r=r, s=s: eng.activation(
            sg_bf[:, r * T + s * NB: r * T + (s + 1) * NB], ps[0:128, 0:NB],
            AF.Sigmoid),
            reads=[ps], writes=[(sg_bf.name, r)], tag=tag + "sg")

    emit_proj_tiles(ctx, wr, DM, xr_bf, xr_bf.name, NCH, T,
                    [(r * 128, (r + 1) * 128, r) for r in range(8)], evac_sg,
                    tag=tag + "pr")
    prog.barrier()
    es3.close()
    esX.close()
    ctx.es = es0
    es4 = ExitStack()
    ctx.es = es4
    wv = ctx.load_w_bf16(Wval_d, FFN, DM, tag=tag + "wv")
    kfc = ctx.sbuf(tag + "kfc", (128, 32 * NB), BF16)
    xr2 = [ctx.sbuf(tag + "x2r", (128, NB)) for _ in range(2)]
    xo2 = [ctx.sbuf(tag + "x2o", (128, NB)) for _ in range(2)]
    for s in range(NS):
        for r in range(32):
            dma(prog, "pool", kfc[:, r * NB:(r + 1) * NB],
                scr["kfD"][b, r * 128:(r + 1) * 128, s * NB:(s + 1) * NB],
                reads=[("kfD", b, r)], writes=[(kfc.name, r)], tag=tag + "lkf")
        for m in range(8):
            ps = ctx.pp.get()
            for k in range(32):
                prog.op("pe", lambda eng, m=m, s=s, k=k, ps=ps: eng.matmul(
                    ps[0:128, 0:NB], wv[:, k * DM + m * 128: k * DM + (m + 1) * 128],
                    kfc[:, k * NB:(k + 1) * NB],
                    start=(k == 0), stop=(k == 31)),
                    reads=[(wv.name, k), (kfc.name, k)], writes=[ps], tag=tag + "mv")
            xr_ = xr2[m % 2]
            xw_ = xo2[m % 2]
            dma(prog, "sp", xr_[:, :], xin[b, m * 128:(m + 1) * 128, s * NB:(s + 1) * NB],
                reads=[(xin.tensor.name, b, m)], writes=[xr_.name], tag=tag + "lx")
            prog.op("dve", lambda eng, ps=ps, m=m, s=s: eng.tensor_tensor(
                ps[0:128, 0:NB], ps[0:128, 0:NB],
                sg_bf[:, m * T + s * NB: m * T + (s + 1) * NB], ALU.mult),
                reads=[ps, (sg_bf.name, m)], writes=[ps], tag=tag + "gv")
            prog.op("dve", lambda eng, ps=ps, xr_=xr_, xw_=xw_: eng.scalar_tensor_tensor(
                xw_[:, :], ps[0:128, 0:NB], 1.0, xr_[:, :], ALU.mult, ALU.add),
                reads=[ps, xr_.name], writes=[xw_.name], tag=tag + "ad")
            dma(prog, "sp", xout[b, m * 128:(m + 1) * 128, s * NB:(s + 1) * NB],
                xw_[:, :], reads=[xw_.name], writes=[(xout.tensor.name, b, m)],
                tag=tag + "so")
    prog.barrier()
    es4.close()
    ctx.es = old
    es0.close()



SLOTS0 = 2048
SLOTS1 = 5120


def _mk_progfn_sub(emit, weights, extra_consts=()):
    def progfn(nc, prog, es):
        ctx = Ctx(nc, prog, es)
        xin = nc.dram_tensor("xin", [BS, DM, SEQ], F32, kind="ExternalInput").ap()
        xout = nc.dram_tensor("xout", [BS, DM, SEQ], F32, kind="ExternalOutput").ap()
        dr = [nc.dram_tensor(f"w{i}", list(np.asarray(w).shape), F32,
                             kind="ExternalInput").ap() for i, w in enumerate(weights)]
        declare_consts(ctx, extra=extra_consts)
        for b in range(BS):
            emit(ctx, b, xin, xout, *dr)
    return progfn


def _base_inmap(ws, extra=()):
    m = {}
    for i, w in enumerate(ws):
        m[f"w{i}"] = np.ascontiguousarray(w)
    want = set(BASE_CONSTS) | set(extra)
    for k, v in const_arrays().items():
        if k in want:
            m[k + "_d"] = v
    return m


def _run_sub(emit, ws, xs, extra_consts=()):
    progfn = _mk_progfn_sub(emit, ws, extra_consts)
    base = _base_inmap(ws, extra_consts)
    in_maps = [dict(base, xin=xs[i]) for i in range(NCORE)]
    res = launch(progfn, in_maps)
    return [np.asarray(res.results[i]["xout"]) for i in range(NCORE)]


def _flatten_tok(per_core_arrays, tr=False):
    out = []
    for Bg in range(BATCH):
        a = per_core_arrays[Bg % 8][Bg // 8]
        out.append(a.T if tr else a)
    return np.concatenate(out, axis=0)


def _routing(idx_all, cap):
    order = np.argsort(idx_all, kind="stable")
    sorted_idx = idx_all[order]
    sel = np.zeros((8, cap), np.int64)
    valid = np.zeros((8, cap), bool)
    keep = np.zeros(idx_all.shape[0], bool)
    slot = np.zeros(idx_all.shape[0], np.int64)
    for e in range(8):
        toks = order[sorted_idx == e]
        n = min(len(toks), cap)
        sel[e, :n] = toks[:n]
        valid[e, :n] = True
        keep[toks[:n]] = True
        slot[toks[:n]] = e * cap + np.arange(n)
    return sel, valid, keep, slot


def _gather_fm_bf16(h_all, sel_row):
    return np.ascontiguousarray(h_all[sel_row].T)


def _per_core_rows(vec, one_col=False):
    out = []
    for i in range(NCORE):
        rows = []
        for j in range(BS):
            Bg = i + 8 * j
            rows.append(vec[Bg * SEQ:(Bg + 1) * SEQ][None, :])
        out.append(np.stack(rows, axis=0))
    return out


def _per_core_fm(mat):
    out = []
    for i in range(NCORE):
        sl = []
        for j in range(BS):
            Bg = i + 8 * j
            sl.append(np.ascontiguousarray(mat[Bg * SEQ:(Bg + 1) * SEQ].T))
        out.append(np.stack(sl, axis=0))
    return out


def _moe_layer(xs, kind, rw, rb, expert_ws, h_dt):
    def progA(nc, prog, es):
        ctx = Ctx(nc, prog, es)
        xin = nc.dram_tensor("xin", [BS, DM, SEQ], F32, kind="ExternalInput").ap()
        hout = nc.dram_tensor("hout", [BS, DM, SEQ], BF16, kind="ExternalOutput").ap()
        gate = nc.dram_tensor("gate", [BS, 16, 128], F32, kind="ExternalOutput").ap()
        idx = nc.dram_tensor("idx", [BS, 16, 128], F32, kind="ExternalOutput").ap()
        rwd = nc.dram_tensor("rw", [DM, 8], F32, kind="ExternalInput").ap()
        rbd = nc.dram_tensor("rb", [8], F32, kind="ExternalInput").ap()
        declare_consts(ctx, extra=("c_rev8",))
        for b in range(BS):
            emit_router(ctx, b, xin, hout, rwd, rbd, gate, idx)
    base = {"rw": np.ascontiguousarray(rw), "rb": np.ascontiguousarray(rb)}
    want = set(BASE_CONSTS) | {"c_rev8"}
    for k, v in const_arrays().items():
        if k in want:
            base[k + "_d"] = v
    res = launch(progA, [dict(base, xin=xs[i]) for i in range(NCORE)])
    h_pc = [np.asarray(res.results[i]["hout"]) for i in range(NCORE)]
    gate_pc = [np.asarray(res.results[i]["gate"]) for i in range(NCORE)]
    idx_pc = [np.asarray(res.results[i]["idx"]) for i in range(NCORE)]
    h_all = _flatten_tok(h_pc, tr=True)
    idx_all = _flatten_tok([a.reshape(BS, SEQ) for a in idx_pc]).reshape(-1)
    idx_all = idx_all.astype(np.int64)
    gate_all = _flatten_tok([a.reshape(BS, SEQ) for a in gate_pc]).reshape(-1)
    cap = SLOTS0 if kind == "keep" else SLOTS1
    sel, valid, keep, slot = _routing(idx_all, cap)
    if kind == "keep":
        w1, b1, w2, b2 = expert_ws

        def progB(nc, prog, es):
            ctx = Ctx(nc, prog, es)
            Xg = nc.dram_tensor("Xg", [DM, cap], BF16, kind="ExternalInput").ap()
            Yg = nc.dram_tensor("Yg", [DM, cap], F32, kind="ExternalOutput").ap()
            w1d = nc.dram_tensor("w1", [DM, 4096], F32, kind="ExternalInput").ap()
            b1d = nc.dram_tensor("b1", [4096], F32, kind="ExternalInput").ap()
            w2d = nc.dram_tensor("w2", [4096, DM], F32, kind="ExternalInput").ap()
            b2d = nc.dram_tensor("b2", [DM], F32, kind="ExternalInput").ap()
            declare_consts(ctx)
            emit_expert_l0(ctx, Xg, w1d, b1d, w2d, b2d, Yg, cap)
        cb = {}
        for k, v in const_arrays().items():
            if k in BASE_CONSTS:
                cb[k + "_d"] = v
        in_maps = [dict(cb, Xg=_gather_fm_bf16(h_all, sel[e]),
                        w1=np.ascontiguousarray(w1[e]), b1=np.ascontiguousarray(b1[e]),
                        w2=np.ascontiguousarray(w2[e]), b2=np.ascontiguousarray(b2[e]))
                   for e in range(8)]
        res = launch(progB, in_maps)
    else:
        ew, eb = expert_ws

        def progB(nc, prog, es):
            ctx = Ctx(nc, prog, es)
            Xg = nc.dram_tensor("Xg", [DM, cap], BF16, kind="ExternalInput").ap()
            Yg = nc.dram_tensor("Yg", [DM, cap], F32, kind="ExternalOutput").ap()
            wd = nc.dram_tensor("we", [DM, DM], F32, kind="ExternalInput").ap()
            bd = nc.dram_tensor("be", [DM], F32, kind="ExternalInput").ap()
            declare_consts(ctx)
            emit_expert_l1(ctx, Xg, wd, bd, Yg, cap)
        cb = {}
        for k, v in const_arrays().items():
            if k in BASE_CONSTS:
                cb[k + "_d"] = v
        in_maps = [dict(cb, Xg=_gather_fm_bf16(h_all, sel[e]),
                        we=np.ascontiguousarray(ew[e]), be=np.ascontiguousarray(eb[e]))
                   for e in range(8)]
        res = launch(progB, in_maps)
    Yg_pc = [np.asarray(res.results[e]["Yg"]) for e in range(8)]
    Ysel = np.zeros((NTOK, DM), np.float32)
    for e in range(8):
        v = valid[e]
        Ysel[sel[e][v]] = Yg_pc[e].T[v]
    gk = keep.astype(np.float32) if kind == "keep" else gate_all.astype(np.float32)
    gk_pc = _per_core_rows(gk)
    Ysel_pc = _per_core_fm(Ysel)

    def progC(nc, prog, es):
        ctx = Ctx(nc, prog, es)
        xin = nc.dram_tensor("xin", [BS, DM, SEQ], F32, kind="ExternalInput").ap()
        xout = nc.dram_tensor("xout", [BS, DM, SEQ], F32, kind="ExternalOutput").ap()
        Ys = nc.dram_tensor("Ys", [BS, DM, SEQ], F32, kind="ExternalInput").ap()
        gkd = nc.dram_tensor("gk", [BS, 1, SEQ], F32, kind="ExternalInput").ap()
        declare_consts(ctx)
        for b in range(BS):
            emit_combine(ctx, b, xin, xout, Ys, gkd, kind)
    cb = {}
    for k, v in const_arrays().items():
        if k in BASE_CONSTS:
            cb[k + "_d"] = v
    res = launch(progC, [dict(cb, xin=xs[i], Ys=Ysel_pc[i], gk=gk_pc[i])
                         for i in range(NCORE)])
    return [np.asarray(res.results[i]["xout"]) for i in range(NCORE)]


def _dump_stage(xs, name):
    import os
    if os.environ.get("DUMPSTAGE"):
        xfull = np.zeros((BATCH, SEQ, DM), np.float32)
        for i in range(NCORE):
            for j in range(BS):
                xfull[i + 8 * j] = np.asarray(xs[i][j]).T
        np.save(f"/tmp/stage_{name}.npy", xfull)


def kernel(**inp):
    g = lambda k: np.asarray(inp[k])
    x0 = g("L0_attn_a_x")
    xs = []
    for i in range(NCORE):
        xs.append(np.ascontiguousarray(
            x0[[i, i + 8]].transpose(0, 2, 1)).astype(np.float32))

    xs = _run_sub(
        lambda ctx, b, xin, xout, WxT, WoutT, Ad, Dd: emit_mamba1(
            ctx, b, xin, xout, WxT, WoutT, Ad, Dd),
        [g("L0_attn_a_Wx").T, g("L0_attn_a_Wout").T, g("L0_attn_a_A_log"),
         g("L0_attn_a_D")], xs, extra_consts=("c_sel16",))
    _dump_stage(xs, "1m1")

    l0w = (g("L0_ffn_w1"), g("L0_ffn_b1"), g("L0_ffn_w2"), g("L0_ffn_b2"))
    xs = _moe_layer(xs, "keep", g("L0_ffn_w_switch"), g("L0_ffn_b_switch"), l0w,
                    None)
    _dump_stage(xs, "2ffn")

    xs = _run_sub(
        lambda ctx, b, xin, xout, wq, bq, wk, bk, wv, bv, wo, bo: emit_attn0(
            ctx, b, xin, xout, wq, bq, wk, bk, wv, bv, wo, bo),
        [g("L0_attn_b_wq").T, g("L0_attn_b_bq"), g("L0_attn_b_wk").T,
         g("L0_attn_b_bk"), g("L0_attn_b_wv").T, g("L0_attn_b_bv"),
         g("L0_attn_b_wo").T, g("L0_attn_b_bo")], xs,
        extra_consts=("c_ones_red_bf",))
    _dump_stage(xs, "3attn")

    xs = _moe_layer(xs, "keep", g("L0_ffn_w_switch"), g("L0_ffn_b_switch"), l0w,
                    None)
    _dump_stage(xs, "4ffn")

    xfull = np.zeros((BATCH, DM, SEQ), np.float32)
    for i in range(NCORE):
        for j in range(BS):
            xfull[i + 8 * j] = xs[i][j]
    SL = SEQ // NCORE
    Vc = g("L1_attn_a_V")
    Wc = g("L1_attn_a_Vinv")

    def progS(nc, prog, es):
        ctx = Ctx(nc, prog, es)
        xin = nc.dram_tensor("xin", [BATCH, DM, SL], F32, kind="ExternalInput").ap()
        xout = nc.dram_tensor("xout", [BATCH, DM, SL], F32, kind="ExternalOutput").ap()
        dr = {}
        for nm, shp in [("logdt", (64,)), ("logLre", (64,)), ("Lim", (64,)),
                        ("Vre", (64, 64)), ("Vim", (64, 64)), ("Wre", (64, 64)),
                        ("Wim", (64, 64))]:
            dr[nm] = nc.dram_tensor(nm, list(shp), F32, kind="ExternalInput").ap()
        declare_consts(ctx)
        emit_s5(ctx, xin, xout, dr["logdt"], dr["logLre"], dr["Lim"], dr["Vre"],
                dr["Vim"], dr["Wre"], dr["Wim"], SL)
    base = {"logdt": g("L1_attn_a_log_dt"), "logLre": g("L1_attn_a_log_Lambda_real"),
            "Lim": g("L1_attn_a_Lambda_imag"),
            "Vre": np.ascontiguousarray(Vc.real), "Vim": np.ascontiguousarray(Vc.imag),
            "Wre": np.ascontiguousarray(Wc.real), "Wim": np.ascontiguousarray(Wc.imag)}
    for k, v in const_arrays().items():
        if k in BASE_CONSTS:
            base[k + "_d"] = v
    res = launch(progS, [dict(base, xin=np.ascontiguousarray(
        xfull[:, :, i * SL:(i + 1) * SL])) for i in range(NCORE)])
    for i in range(NCORE):
        xfull[:, :, i * SL:(i + 1) * SL] = np.asarray(res.results[i]["xout"])
    xs = [np.ascontiguousarray(xfull[[i, i + 8]]) for i in range(NCORE)]
    _dump_stage(xs, "5s5")

    l1w = (g("L1_ffn_expert_w"), g("L1_ffn_expert_b"))
    xs = _moe_layer(xs, "gate", g("L1_ffn_router_w"), g("L1_ffn_router_b"), l1w,
                    None)
    _dump_stage(xs, "6ffn")

    def progM2(nc, prog, es):
        ctx = Ctx(nc, prog, es)
        xin = nc.dram_tensor("xin", [BS, DM, SEQ], F32, kind="ExternalInput").ap()
        xout = nc.dram_tensor("xout", [BS, DM, SEQ], F32, kind="ExternalOutput").ap()
        xu = nc.dram_tensor("xu", [BS, DM, SEQ], F32).ap()
        xa = nc.dram_tensor("xa", [BS, DM, SEQ], F32).ap()
        xb = nc.dram_tensor("xb", [BS, DM, SEQ], F32).ap()
        xc = nc.dram_tensor("xc", [BS, DM, SEQ], F32).ap()
        nm_shapes = [("W_in", (DM, 4384)), ("conv_w", (4, 2304)), ("conv_b", (2304,)),
                     ("A_log", (32,)), ("Dm", (32,)), ("dt_bias", (32,)),
                     ("mnorm_w", (2048,)), ("W_out", (2048, DM)),
                     ("W_qkv", (DM, 1152)), ("W_cproj", (DM, DM)),
                     ("qconv_w", (3, 64)), ("qconv_b", (64,)), ("kconv_w", (3, 64)),
                     ("kconv_b", (64,)), ("vconv_w", (3, 64)), ("vconv_b", (64,)),
                     ("maa_k", (DM,)), ("maa_r", (DM,)), ("W_key", (DM, 4096)),
                     ("W_rec", (DM, DM)), ("W_val", (4096, DM))]
        dr = {nm: nc.dram_tensor(nm, list(shp), F32, kind="ExternalInput").ap()
              for nm, shp in nm_shapes}
        scr = {
            "zD": nc.dram_tensor("zD", [BS, 2048, SEQ], BF16).ap(),
            "xbcD": nc.dram_tensor("xbcD", [BS, 2304, SEQ], F32).ap(),
            "xmD": nc.dram_tensor("xmD", [BS, 2048, SEQ], BF16).ap(),
            "BD": nc.dram_tensor("BD", [BS, 128, SEQ], BF16).ap(),
            "CD": nc.dram_tensor("CD", [BS, 128, SEQ], BF16).ap(),
            "kfD": nc.dram_tensor("kfD", [BS, 4096, SEQ], BF16).ap(),
        }
        declare_consts(ctx, extra=("c_sel16", "c_sel32", "c_trimask", "c_cmask", "c_dm_l1", "c_dm_l2", "c_dm_m1", "c_dm_m2", "c_ones_red_bf"))
        for b in range(BS):
            emit_ln_to_dram(ctx, b, xin, xu)
            emit_m2core(ctx, b, xu, xa, dr["W_in"], dr["conv_w"], dr["conv_b"],
                        dr["A_log"], dr["Dm"], dr["dt_bias"], dr["mnorm_w"],
                        dr["W_out"], scr)
            emit_m2attn(ctx, b, xa, xb, dr["W_qkv"], dr["W_cproj"], dr["qconv_w"],
                        dr["qconv_b"], dr["kconv_w"], dr["kconv_b"], dr["vconv_w"],
                        dr["vconv_b"])
            emit_m2rwkv(ctx, b, xb, xc, dr["maa_k"], dr["maa_r"], dr["W_key"],
                        dr["W_rec"], dr["W_val"], scr)
            emit_add_res(ctx, b, xin, xc, xout)
    base = {"W_in": g("L1_attn_b_W_in"),
            "conv_w": np.ascontiguousarray(g("L1_attn_b_conv_w").T),
            "conv_b": g("L1_attn_b_conv_b"), "A_log": g("L1_attn_b_A_log"),
            "Dm": g("L1_attn_b_Dm"), "dt_bias": g("L1_attn_b_dt_bias"),
            "mnorm_w": g("L1_attn_b_mnorm_w"), "W_out": g("L1_attn_b_W_out"),
            "W_qkv": g("L1_attn_b_W_qkv"), "W_cproj": g("L1_attn_b_W_cproj"),
            "qconv_w": np.ascontiguousarray(g("L1_attn_b_qconv_w").T),
            "qconv_b": g("L1_attn_b_qconv_b"),
            "kconv_w": np.ascontiguousarray(g("L1_attn_b_kconv_w").T),
            "kconv_b": g("L1_attn_b_kconv_b"),
            "vconv_w": np.ascontiguousarray(g("L1_attn_b_vconv_w").T),
            "vconv_b": g("L1_attn_b_vconv_b"), "maa_k": g("L1_attn_b_time_maa_k"),
            "maa_r": g("L1_attn_b_time_maa_r"), "W_key": g("L1_attn_b_W_key"),
            "W_rec": g("L1_attn_b_W_rec"), "W_val": g("L1_attn_b_W_val")}
    want = set(BASE_CONSTS) | set(("c_sel16", "c_sel32", "c_trimask", "c_cmask", "c_dm_l1", "c_dm_l2", "c_dm_m1", "c_dm_m2", "c_ones_red_bf"))
    for k, v in const_arrays().items():
        if k in want:
            base[k + "_d"] = v
    res = launch(progM2, [dict(base, xin=xs[i]) for i in range(NCORE)])
    xs = [np.asarray(res.results[i]["xout"]) for i in range(NCORE)]
    _dump_stage(xs, "7m2")

    xs = _moe_layer(xs, "gate", g("L1_ffn_router_w"), g("L1_ffn_router_b"), l1w,
                    None)

    out = np.zeros((BATCH, SEQ, DM), np.float32)
    for i in range(NCORE):
        for j in range(BS):
            out[i + 8 * j] = xs[i][j].T
    return out


def emit_ln_to_dram(ctx, b, xin, uout, tag="l2d", eps=1e-5, rms=False):
    nc, prog = ctx.nc, ctx.prog
    from contextlib import ExitStack
    T = SEQ
    NCH = 8
    old = ctx.es
    es = ExitStack()
    ctx.es = es
    hf = ctx.sbuf(tag + "h", (128, NCH * T))
    loader = dram_chunk_loader(
        ctx, lambda c: (xin[b, c * 128:(c + 1) * 128, :], (xin.tensor.name, b, c)),
        T, tag=tag + "xl", nbuf=1)
    emit_ln_fm(ctx, loader, T, NCH, hf, tag + "_hk", tag=tag + f"ln{b}", nsq=1,
               eps=eps, rms=rms)
    for c in range(NCH):
        dma(prog, "sp", uout[b, c * 128:(c + 1) * 128, :], hf[:, c * T:(c + 1) * T],
            reads=[(tag + "_hk", c)], writes=[(uout.tensor.name, b, c)],
            tag=tag + "st")
    ctx.es = old
    prog.barrier()
    es.close()


def emit_add_res(ctx, b, xin, yin, xout, tag="ar"):
    nc, prog = ctx.nc, ctx.prog
    from contextlib import ExitStack
    T = SEQ
    old = ctx.es
    es = ExitStack()
    ctx.es = es
    a = [ctx.sbuf(tag + "a", (128, T)) for _ in range(2)]
    bb = [ctx.sbuf(tag + "b", (128, T)) for _ in range(2)]
    for c in range(8):
        a_ = a[c % 2]
        b_ = bb[c % 2]
        dma(prog, "sp", a_[:, :], xin[b, c * 128:(c + 1) * 128, :],
            reads=[(xin.tensor.name, b, c)], writes=[a_.name], tag=tag + "la")
        dma(prog, "pool", b_[:, :], yin[b, c * 128:(c + 1) * 128, :],
            reads=[(yin.tensor.name, b, c)], writes=[b_.name], tag=tag + "lb")
        prog.op("dve", lambda eng, a_=a_, b_=b_: eng.tensor_tensor(
            a_[:, :], a_[:, :], b_[:, :], ALU.add),
            reads=[a_.name, b_.name], writes=[a_.name], tag=tag + "ad")
        dma(prog, "sp", xout[b, c * 128:(c + 1) * 128, :], a_[:, :],
            reads=[a_.name], writes=[(xout.tensor.name, b, c)], tag=tag + "st")
    ctx.es = old
    prog.barrier()
    es.close()


# version 12
# speedup vs baseline: 1.1525x; 1.1525x over previous
import numpy as np
import ml_dtypes
import concourse.bass as bass
import concourse.mybir as mybir
from concourse.bass import IndirectOffsetOnAxis
from concourse.bass_utils import run_bass_kernel_spmd

F32 = mybir.dt.float32
BF16 = mybir.dt.bfloat16
U32 = mybir.dt.uint32
AF = mybir.ActivationFunctionType
ALU = mybir.AluOpType
AX = mybir.AxisListType

NCORE = 8
BATCH = 16
SEQ = 2048
DM = 1024
BS = BATCH // NCORE
NTOK = BATCH * SEQ
CTOK = BS * SEQ
NB = 512


ENGINES = ("pe", "act", "dve", "pool", "sp")
MAXINFLIGHT = 8


class Op:
    __slots__ = ("i", "eng", "fn", "reads", "writes", "dma", "marked", "mark_no",
                 "dma_no", "waits", "tag")

    def __init__(self, i, eng, fn, reads, writes, dma, tag):
        self.i = i
        self.eng = eng
        self.fn = fn
        self.reads = reads
        self.writes = writes
        self.dma = dma
        self.marked = False
        self.mark_no = 0
        self.dma_no = 0
        self.waits = {}
        self.tag = tag


def _keys(x):
    out = []
    if not isinstance(x, (list, tuple)):
        x = (x,)
    for a in x:
        if isinstance(a, (tuple, str)):
            out.append(a)
        elif hasattr(a, "tensor"):
            out.append(a.tensor.name)
        elif hasattr(a, "name"):
            out.append(a.name)
        else:
            out.append(str(a))
    return out


class Prog:
    def __init__(self, nc):
        self.nc = nc
        self.ops = []
        self._qcnt = {}

    def op(self, eng, fn, reads=(), writes=(), dma=False, tag=""):
        o = Op(len(self.ops), eng, fn, _keys(reads), _keys(writes),
               eng if dma else None, tag)
        if o.dma is not None:
            self._qcnt[o.dma] = self._qcnt.get(o.dma, 0) + 1
            o.dma_no = self._qcnt[o.dma]
        self.ops.append(o)
        return o

    def barrier(self):
        key = ("barrier", len(self.ops))
        for e in ENGINES:
            self.op(e, lambda eng: eng.nop(), reads=(), writes=(key,), tag="barw")
        for e in ENGINES:
            self.op(e, lambda eng: eng.nop(), reads=(key,), writes=(), tag="barr")

    def _resolve(self):
        last_writer = {}
        readers_since = {}
        for o in self.ops:
            deps = []
            for k in o.reads:
                w = last_writer.get(k)
                if w is not None:
                    deps.append(w)
                readers_since.setdefault(k, []).append(o)
            for k in o.writes:
                w = last_writer.get(k)
                if w is not None:
                    deps.append(w)
                for r in readers_since.get(k, ()):
                    deps.append(r)
                last_writer[k] = o
                readers_since[k] = []
            for d in deps:
                if d is o:
                    continue
                if d.eng == o.eng and d.dma is None and o.dma is None \
                        and d.eng == "pe":
                    continue
                if d.dma is not None:
                    o.waits["q" + d.dma] = max(o.waits.get("q" + d.dma, 0),
                                               16 * d.dma_no)
                else:
                    d.marked = True
                    key = "#" + d.eng
                    o.waits[key] = max(o.waits.get(key, -1), d.i)
        mark_cnt = {e: 0 for e in ENGINES}
        idx2mark = {}
        for o in self.ops:
            if o.marked and o.dma is None:
                mark_cnt[o.eng] += 1
                o.mark_no = mark_cnt[o.eng]
            idx2mark[o.i] = (o.eng, o.mark_no)
        for o in self.ops:
            fixed = {}
            for k, v in o.waits.items():
                if k.startswith("#"):
                    eng, mn = idx2mark[v]
                    assert mn > 0, (o.tag, k, v)
                    fixed[eng] = max(fixed.get(eng, 0), mn)
                else:
                    fixed[k] = max(fixed.get(k, 0), v)
            o.waits = fixed
        self.final_q = dict(self._qcnt)
        self.final_m = mark_cnt

    def emit_engine(self, e, eng, sems):
        seen = {}
        for o in self.ops:
            if o.eng != e:
                continue
            for k, v in sorted(o.waits.items()):
                if seen.get(k, -1) >= v:
                    continue
                eng.wait_ge(sems[k], v)
                seen[k] = v
            if o.dma is not None and o.dma_no > 1:
                k = "q" + o.dma
                v = 16 * (o.dma_no - 1)
                if seen.get(k, -1) < v:
                    eng.wait_ge(sems[k], v)
                    seen[k] = v
            if o.marked and o.dma is None and o.tag in ("barw", "barr"):
                v = o.mark_no - 1
                if v > 0 and seen.get(e, -1) < v:
                    eng.wait_ge(sems[e], v)
                    seen[e] = v
            inst = o.fn(eng)
            if o.dma is not None:
                assert inst is not None, o.tag
                inst.then_inc(sems["q" + o.dma], 16)
            elif o.marked:
                if inst is None:
                    inst = eng.nop()
                inst.then_inc(sems[o.eng], 1)
        if e == "sp":
            for q, n in self.final_q.items():
                if seen.get("q" + q, -1) < 16 * n:
                    eng.wait_ge(sems["q" + q], 16 * n)
            for m, n in self.final_m.items():
                if n and seen.get(m, -1) < n:
                    eng.wait_ge(sems[m], n)


def launch(progfn, in_maps, core_ids=None, **kw):
    from contextlib import ExitStack
    if core_ids is None:
        core_ids = list(range(len(in_maps)))
    nc = bass.Bass("TRN2", target_bir_lowering=False)
    prog = Prog(nc)
    es = ExitStack()
    with es:
        progfn(nc, prog, es)
        sems = {}
        for e in ENGINES:
            sems[e] = es.enter_context(nc.semaphore("sem_" + e))
            sems["q" + e] = es.enter_context(nc.semaphore("semq_" + e))
        import os
        _cut = int(os.environ.get("BASSCUT", "0"))
        if _cut:
            prog.ops = prog.ops[:_cut]
            prog._qcnt = {}
            for o in prog.ops:
                if o.dma is not None:
                    prog._qcnt[o.dma] = prog._qcnt.get(o.dma, 0) + 1
        prog._resolve()
        with nc.Block() as block:

            @block.tensor
            def _(t):
                prog.emit_engine("pe", t, sems)

            @block.scalar
            def _(s):
                prog.emit_engine("act", s, sems)

            @block.vector
            def _(v):
                prog.emit_engine("dve", v, sems)

            @block.gpsimd
            def _(g):
                prog.emit_engine("pool", g, sems)

            @block.sync
            def _(sy):
                prog.emit_engine("sp", sy, sems)
    import os
    if os.environ.get("BASSPROF"):
        import time as _t
        t0 = _t.time()
        r = run_bass_kernel_spmd(nc, in_maps, core_ids=core_ids, trace=True, **kw)
        print(f"[prof] launch ops={len(prog.ops)} exec_ns={r.exec_time_ns} wall={_t.time()-t0:.1f}s", flush=True)
        return r
    if os.environ.get("BASSSIM"):
        import concourse.bass_interp as bass_interp

        class _R:
            pass
        sim = bass_interp.MultiCoreSim(nc, len(core_ids))
        outs = []
        for i in range(len(core_ids)):
            for k, v in in_maps[i].items():
                sim.cores[i].tensor(k)[:] = np.asarray(v)
        sim.simulate()
        r = _R()
        r.results = []
        for i in range(len(core_ids)):
            d = {}
            for alloc in nc.m.functions[0].allocations:
                if not isinstance(alloc, mybir.MemoryLocationSet):
                    continue
                if alloc.kind == "ExternalOutput":
                    nmm = alloc.memorylocations[0].name
                    d[nmm] = np.array(sim.cores[i].mem_tensor(nmm)).reshape(
                        tuple(alloc.tensor_shape))
            r.results.append(d)
        return r
    return run_bass_kernel_spmd(nc, in_maps, core_ids=core_ids, **kw)



def sb(nc, es, name, shape, dt=F32):
    return es.enter_context(nc.sbuf_tensor(name, shape, dt))


def psum(nc, es, name, shape=(128, NB), dt=F32):
    return es.enter_context(nc.psum_tensor(name, list(shape), dt))


class PsumPool:

    def __init__(self, nc, es, n=6, name="pp"):
        self.tiles = [psum(nc, es, f"{name}{i}") for i in range(n)]
        self.i = 0

    def get(self):
        t = self.tiles[self.i % len(self.tiles)]
        self.i += 1
        return t


def dma(prog, q, out_ap, in_ap, reads, writes, tag="dma", **kw):
    prog.op(q, lambda eng: eng.dma_start(out=out_ap, in_=in_ap, **kw),
            reads=reads, writes=writes, dma=True, tag=tag)


def bcast_row(prog, onescol_ap, row_ap, ps_ap, n0, n1, rkey, wkey, tag="bc"):
    prog.op("pe",
            lambda eng: eng.matmul(ps_ap[:, 0:(n1 - n0)], onescol_ap,
                                   row_ap[0:1, n0:n1], start=True, stop=True),
            reads=[rkey, "c_ones_col"], writes=[wkey], tag=tag)



class Ctx:
    def __init__(self, nc, prog, es):
        self.nc = nc
        self.prog = prog
        self.es = es
        self.uid = 0
        self.c_ones_red = None
        self.c_ones_bc = None
        self.c_ident = None
        self.pp = None

    def name(self, base):
        self.uid += 1
        return f"{base}_{self.uid}"

    def sbuf(self, base, shape, dt=F32):
        return sb(self.nc, self.es, self.name(base), list(shape), dt)

    def feat_cols(self, dram_vec, n, tag="fc", scale=None):
        prog, nc = self.prog, self.nc
        if n <= 128:
            row = self.sbuf(tag + "r", (1, n))
            dma(prog, "sp", row[:, :], dram_vec.rearrange("(o n) -> o n", o=1),
                reads=[dram_vec], writes=[row], tag=tag)
            ps = self.pp.get()
            prog.op("pe", lambda eng: eng.transpose(ps[0:n, 0:1], row[0:1, 0:n],
                                                    self.c_ident[0:1, 0:1]),
                    reads=[row, "c_ident"], writes=[ps], tag=tag + "t")
            col = self.sbuf(tag + "c", (n, 1))
            if scale is None:
                prog.op("dve", lambda eng: eng.tensor_copy(col[:, :], ps[0:n, 0:1]),
                        reads=[ps], writes=[col], tag=tag + "e")
            else:
                prog.op("dve", lambda eng: eng.tensor_scalar(col[:, :], ps[0:n, 0:1],
                                                             float(scale), None, ALU.mult),
                        reads=[ps], writes=[col], tag=tag + "e")
            return col
        k = n // 128
        assert n % 128 == 0
        wide = self.sbuf(tag + "w", (k, 128))
        dma(prog, "sp", wide[:, :], dram_vec.rearrange("(k p) -> k p", p=128),
            reads=[dram_vec], writes=[wide], tag=tag)
        ps = self.pp.get()
        prog.op("pe", lambda eng: eng.transpose(ps[0:128, 0:k], wide[0:k, 0:128],
                                                self.c_ident[0:k, 0:k]),
                reads=[wide, "c_ident"], writes=[ps], tag=tag + "t")
        cols = self.sbuf(tag + "c", (128, k))
        if scale is None:
            prog.op("dve", lambda eng: eng.tensor_copy(cols[:, :], ps[0:128, 0:k]),
                    reads=[ps], writes=[cols], tag=tag + "e")
        else:
            prog.op("dve", lambda eng: eng.tensor_scalar(cols[:, :], ps[0:128, 0:k],
                                                         float(scale), None, ALU.mult),
                    reads=[ps], writes=[cols], tag=tag + "e")
        return cols

    def load_w_bf16(self, dram_w, K, M, tag="w"):
        prog = self.prog
        kc = K // 128
        t = self.sbuf(tag, (128, kc * M), BF16)
        for c in range(kc):
            dma(prog, "pool", t[:, c * M:(c + 1) * M], dram_w[c * 128:(c + 1) * 128, :],
                reads=[dram_w], writes=[(t.name, c)], tag=tag + str(c))
        return t


def emit_ln_fm(ctx, get_chunk, T, nchunk, h_out, h_key, tag="ln", eps=1e-5,
               rms=False, nsq=2):
    prog = ctx.prog
    D = nchunk * 128
    ns = T // NB
    assert ns <= 4
    sq = [ctx.sbuf(tag + "sq", (128, T)) for _ in range(nsq)]
    sq = (sq * 2)[:2]
    rows = ctx.sbuf(tag + "rw", (1, 2 * T))
    def R(i):
        if i == 2:
            return sq[0][0:1, 0:T]
        return rows[0:1, i * T:(i + 1) * T]
    ps_s = [ctx.pp.get() for _ in range(ns)]
    ps_q = [ctx.pp.get() for _ in range(ns)]
    for c in range(nchunk):
        xap, xkey = get_chunk(c, 0)
        s_ = sq[c % 2]
        prog.op("act", lambda eng, xap=xap, s_=s_: eng.activation(
            s_[:, :], xap, AF.Square),
            reads=[xkey], writes=[s_], tag=tag + "sq")
        for s in range(ns):
            prog.op("pe", lambda eng, xap=xap, c=c, s=s: eng.matmul(
                ps_s[s][0:1, 0:NB], ctx.c_ones_red, xap[:, s * NB:(s + 1) * NB],
                start=(c == 0), stop=(c == nchunk - 1)),
                reads=[xkey, "c_ones_red"], writes=[ps_s[s]], tag=tag + "ms")
            prog.op("pe", lambda eng, c=c, s=s, s_=s_: eng.matmul(
                ps_q[s][0:1, 0:NB], ctx.c_ones_red, s_[:, s * NB:(s + 1) * NB],
                start=(c == 0), stop=(c == nchunk - 1)),
                reads=[s_, "c_ones_red"], writes=[ps_q[s]], tag=tag + "mq")
    for s in range(ns):
        prog.op("dve", lambda eng, s=s: eng.tensor_copy(
            rows[0:1, s * NB:(s + 1) * NB], ps_s[s][0:1, 0:NB]),
            reads=[ps_s[s]], writes=[(rows.name, 0)], tag=tag + "es")
        prog.op("dve", lambda eng, s=s: eng.tensor_copy(
            rows[0:1, T + s * NB:T + (s + 1) * NB], ps_q[s][0:1, 0:NB]),
            reads=[ps_q[s]], writes=[(rows.name, 1)], tag=tag + "eq")
    if not rms:
        prog.op("dve", lambda eng: eng.tensor_scalar(
            R(0), R(0), 1.0 / D, None, ALU.mult),
            reads=[(rows.name, 0)], writes=[(rows.name, 0)], tag=tag + "m")
        prog.op("dve", lambda eng: eng.tensor_tensor(
            R(2), R(0), R(0), ALU.mult),
            reads=[(rows.name, 0)], writes=[sq[0]], tag=tag + "m2")
        prog.op("dve", lambda eng: eng.scalar_tensor_tensor(
            R(1), R(1), 1.0 / D, R(2), ALU.mult, ALU.subtract),
            reads=[(rows.name, 1), sq[0]], writes=[(rows.name, 1)],
            tag=tag + "v2")
    else:
        prog.op("dve", lambda eng: eng.tensor_scalar(
            R(1), R(1), 1.0 / D, None, ALU.mult),
            reads=[(rows.name, 1)], writes=[(rows.name, 1)], tag=tag + "v")
    prog.op("act", lambda eng: eng.activation(R(1), R(1), AF.Ln,
            bias=scal_ap(ctx, eps, 1)),
            reads=[(rows.name, 1), "c_scal"], writes=[(rows.name, 1)], tag=tag + "ln")
    prog.op("act", lambda eng: eng.activation(R(1), R(1), AF.Exp, scale=-0.5),
            reads=[(rows.name, 1)], writes=[(rows.name, 1)], tag=tag + "ex")
    if not rms:
        prog.op("dve", lambda eng: eng.scalar_tensor_tensor(
            R(0), R(0), -1.0, R(1), ALU.mult, ALU.mult),
            reads=[(rows.name, 0), (rows.name, 1)], writes=[(rows.name, 0)],
            tag=tag + "nm")
    psR = []
    psM = []
    for s in range(ns):
        pR = ctx.pp.get()
        psR.append(pR)
        prog.op("pe", lambda eng, s=s, pR=pR: eng.matmul(
            pR[:, 0:NB], ctx.c_ones_bc, rows[0:1, T + s * NB: T + (s + 1) * NB],
            start=True, stop=True),
            reads=[(rows.name, 1), "c_ones_bc"], writes=[pR], tag=tag + "bR")
        if not rms:
            pM = ctx.pp.get()
            psM.append(pM)
            prog.op("pe", lambda eng, s=s, pM=pM: eng.matmul(
                pM[:, 0:NB], ctx.c_ones_bc,
                rows[0:1, s * NB: (s + 1) * NB],
                start=True, stop=True),
                reads=[(rows.name, 0), "c_ones_bc"], writes=[pM], tag=tag + "bM")
    for c in range(nchunk):
        xap, xkey = get_chunk(c, 1)
        for s in range(ns):
            lo = c * T + s * NB
            hi = c * T + (s + 1) * NB
            if rms:
                prog.op("dve", lambda eng, xap=xap, lo=lo, hi=hi, s=s: eng.tensor_tensor(
                    h_out[:, lo:hi], xap[:, (lo - c * T):(hi - c * T)] if False else xap[:, s * NB:(s + 1) * NB], psR[s][:, 0:NB], ALU.mult),
                    reads=[xkey, psR[s]], writes=[(h_key, c)], tag=tag + "ap")
            else:
                s_ = sq[c % 2]
                prog.op("dve", lambda eng, xap=xap, s=s, s_=s_: eng.tensor_tensor(
                    s_[:, s * NB:(s + 1) * NB], xap[:, s * NB:(s + 1) * NB],
                    psR[s][:, 0:NB], ALU.mult),
                    reads=[xkey, psR[s]], writes=[s_], tag=tag + "a1")
                prog.op("dve", lambda eng, lo=lo, hi=hi, s=s, s_=s_: eng.tensor_tensor(
                    h_out[:, lo:hi], s_[:, s * NB:(s + 1) * NB], psM[s][:, 0:NB],
                    ALU.add),
                    reads=[s_, psM[s]], writes=[(h_key, c)], tag=tag + "a2")


def dram_chunk_loader(ctx, dram_row_fn, T, tag="xl", q="sp", nbuf=2):
    bufs = [ctx.sbuf(tag, (128, T)) for _ in range(nbuf)]
    bufs = (bufs * 2)[:2]
    cnt = [0]

    def get_chunk(c, passno):
        b = bufs[cnt[0] % 2]
        cnt[0] += 1
        ap, rkey = dram_row_fn(c)
        dma(ctx.prog, q, b[:, :], ap, reads=[rkey], writes=[b.name], tag=tag)
        return b[:, :], b.name
    return get_chunk


def emit_proj_tiles(ctx, wsb, M, x_sb, xkeys, kc, T, mtiles, out_cb, tag="pj",
                    extra_reads=()):
    prog = ctx.prog
    ns = T // NB
    for (m0, m1, info) in mtiles:
        pss = [ctx.pp.get() for _ in range(ns)]
        for k in range(kc):
            for s in range(ns):
                prog.op("pe", lambda eng, m0=m0, m1=m1, s=s, k=k, ps=pss[s]: eng.matmul(
                    ps[0:m1 - m0, 0:NB],
                    wsb[:, k * M + m0: k * M + m1],
                    x_sb[:, k * T + s * NB: k * T + (s + 1) * NB],
                    start=(k == 0), stop=(k == kc - 1)),
                    reads=[(wsb.name, k), (xkeys, k)] + list(extra_reads),
                    writes=[pss[s]], tag=tag + f"m{m0}s{s}")
        for s in range(ns):
            out_cb(info, m0, m1, s, pss[s])


def emit_mamba1(ctx, b, xin, xout, WxT_d, WoutT_d, Alog_d, D_d):
    nc, prog = ctx.nc, ctx.prog
    from contextlib import ExitStack
    T = SEQ
    NS = T // NB
    NCH = 8
    old_es = ctx.es

    rowA = ctx.sbuf("m1_rowA", (1, 16))
    dma(prog, "sp", rowA[:, :], Alog_d.rearrange("(o n) -> o n", o=1), reads=[Alog_d],
        writes=[rowA], tag="ldA")
    rowAe = ctx.sbuf("m1_rowAe", (1, 16))
    prog.op("act", lambda eng: eng.activation(rowAe[:, :], rowA[:, :], AF.Exp, scale=1.0),
            reads=[rowA], writes=[rowAe], tag="Aexp")
    psA = ctx.pp.get()
    prog.op("pe", lambda eng: eng.matmul(psA[0:128, 0:16], ctx.c_ones_bc,
                                         rowAe[0:1, 0:16], start=True, stop=True),
            reads=[rowAe, "c_ones_bc"], writes=[psA], tag="Abc")
    Acols = ctx.sbuf("m1_Acols", (128, 16))
    prog.op("dve", lambda eng: eng.tensor_scalar(Acols[:, :], psA[0:128, 0:16],
                                                 -1.0, None, ALU.mult),
            reads=[psA], writes=[Acols], tag="Aneg")
    Dcols = ctx.feat_cols(D_d, DM, tag="m1_D")

    es13 = ExitStack()
    ctx.es = es13
    h_bf = ctx.sbuf("m1_hbf", (128, NCH * T), BF16)
    es23 = ExitStack()
    ctx.es = es23
    dsp = ctx.sbuf("m1_dsp", (128, NCH * T), BF16)
    Bt = ctx.sbuf("m1_Bt", (16, T))
    Ct = ctx.sbuf("m1_Ct", (16, T))
    es34 = ExitStack()
    ctx.es = es34
    y_bf = ctx.sbuf("m1_ybf", (128, NCH * T), BF16)

    es1 = ExitStack()
    ctx.es = es1
    loader = dram_chunk_loader(
        ctx, lambda c: (xin[b, c * 128:(c + 1) * 128, :], (xin.tensor.name, b, c)),
        T, tag="m1xl")
    emit_ln_fm(ctx, loader, T, NCH, h_bf, "m1_h", tag=f"m1ln{b}")
    ctx.es = old_es
    prog.barrier()
    es1.close()

    es2 = ExitStack()
    ctx.es = es2
    wx = ctx.load_w_bf16(WxT_d, DM, 1056, tag="m1_wx")
    tmpb = [ctx.sbuf("m1_tmp", (128, NB)) for _ in range(2)]

    def evac_xp(info, m0, m1, s, ps):
        if info == "B":
            prog.op("dve", lambda eng, s=s, ps=ps: eng.tensor_copy(
                Bt[:, s * NB:(s + 1) * NB], ps[0:16, 0:NB]),
                reads=[ps], writes=[Bt], tag="evB")
        elif info == "C":
            prog.op("dve", lambda eng, s=s, ps=ps: eng.tensor_copy(
                Ct[:, s * NB:(s + 1) * NB], ps[0:16, 0:NB]),
                reads=[ps], writes=[Ct], tag="evC")
        else:
            r = info[1]
            t = tmpb[(r * NS + s) % 2]
            prog.op("act", lambda eng, t=t, ps=ps: eng.activation(
                t[:, :], ps[0:128, 0:NB], AF.Exp),
                reads=[ps], writes=[t], tag="evD1")
            prog.op("act", lambda eng, t=t, r=r, s=s: eng.activation(
                dsp[:, r * T + s * NB: r * T + (s + 1) * NB], t[:, :], AF.Ln,
                bias=scal_ap(ctx, 1.0)),
                reads=[t, "c_scal"], writes=[(dsp.name, r)], tag="evD2")

    mtiles = [(0, 16, "B"), (16, 32, "C")] + [(32 + 128 * r, 160 + 128 * r, ("d", r))
                                              for r in range(8)]
    emit_proj_tiles(ctx, wx, 1056, h_bf, "m1_h", NCH, T, mtiles, evac_xp, tag="xp")
    ctx.es = old_es
    prog.barrier()
    es2.close()

    es3 = ExitStack()
    ctx.es = es3
    dA_r = [ctx.sbuf("m1_dA", (128, T)) for _ in range(2)]
    u_r = [ctx.sbuf("m1_u", (128, T)) for _ in range(2)]
    Sb = [ctx.sbuf("m1_S", (128, T))] * 2
    tmb = [ctx.sbuf("m1_tm", (128, T))] * 2
    dcf = dA_r[0]
    hcf = ctx.sbuf("m1_hcf", (128, T))
    wcf = ctx.sbuf("m1_wcf", (128, T))
    ya = ctx.sbuf("m1_ya", (128, T))
    yb = ctx.sbuf("m1_yb", (128, T))
    for c in range(NCH):
        prog.op("act", lambda eng, c=c: eng.activation(
            hcf[:, :], h_bf[:, c * T:(c + 1) * T], AF.Copy),
            reads=[("m1_h", c)], writes=[hcf], tag="hcf")
        prog.op("act", lambda eng, c=c: eng.activation(
            dcf[:, :], dsp[:, c * T:(c + 1) * T], AF.Copy),
            reads=[(dsp.name, c)], writes=[dcf], tag="dcf")
        prog.op("dve", lambda eng: eng.tensor_tensor(wcf[:, :], dcf[:, :], hcf[:, :],
                                                     ALU.mult),
                reads=[dcf, hcf], writes=[wcf], tag="wcf")
        for n in range(16):
            dA = dA_r[n % 2]
            u = u_r[n % 2]
            S = Sb[n % 2]
            tm = tmb[n % 2]
            prog.op("act", lambda eng, c=c, n=n, dA=dA: eng.activation(
                dA[:, :], dsp[:, c * T:(c + 1) * T], AF.Exp,
                scale=Acols[:, n:n + 1]),
                reads=[(dsp.name, c), Acols], writes=[dA], tag="dAe")
            for s in range(NS):
                psB = ctx.pp.get()
                prog.op("pe", lambda eng, n=n, s=s, psB=psB: eng.matmul(
                    psB[0:128, 0:NB], ctx.c_sel16[:, n * 128:(n + 1) * 128],
                    Bt[0:16, s * NB:(s + 1) * NB], start=True, stop=True),
                    reads=[Bt, "c_sel16"], writes=[psB], tag="bcB")
                prog.op("dve", lambda eng, s=s, psB=psB, u=u: eng.tensor_tensor(
                    u[:, s * NB:(s + 1) * NB], wcf[:, s * NB:(s + 1) * NB],
                    psB[:, 0:NB], ALU.mult),
                    reads=[wcf, psB], writes=[u], tag="ubld")
            prog.op("dve", lambda eng, S=S, dA=dA, u=u: eng.tensor_tensor_scan(
                S[:, :], dA[:, :], u[:, :], 0.0, ALU.mult, ALU.add),
                reads=[dA, u], writes=[S], tag="scan")
            yt = ya if (n % 2 == 0) else yb
            for s in range(NS):
                psC = ctx.pp.get()
                prog.op("pe", lambda eng, n=n, s=s, psC=psC: eng.matmul(
                    psC[0:128, 0:NB], ctx.c_sel16[:, n * 128:(n + 1) * 128],
                    Ct[0:16, s * NB:(s + 1) * NB], start=True, stop=True),
                    reads=[Ct, "c_sel16"], writes=[psC], tag="bcC")
                dst = yt if n < 2 else tm
                prog.op("dve", lambda eng, s=s, psC=psC, S=S, dst=dst: eng.tensor_tensor(
                    dst[:, s * NB:(s + 1) * NB], S[:, s * NB:(s + 1) * NB],
                    psC[:, 0:NB], ALU.mult),
                    reads=[S, psC], writes=[dst], tag="ymul")
            if n >= 2:
                e = "dve" if (n % 2 == 0) else "pool"
                prog.op(e, lambda eng, tm=tm, yt=yt: eng.tensor_tensor(
                    yt[:, :], yt[:, :], tm[:, :], ALU.add),
                    reads=[tm, yt], writes=[yt], tag="yacc")
        prog.op("dve", lambda eng: eng.tensor_tensor(ya[:, :], ya[:, :], yb[:, :],
                                                     ALU.add),
                reads=[ya, yb], writes=[ya], tag="yab")
        prog.op("dve", lambda eng, c=c: eng.scalar_tensor_tensor(
            y_bf[:, c * T:(c + 1) * T], hcf[:, :], Dcols[:, c:c + 1], ya[:, :],
            ALU.mult, ALU.add),
            reads=[hcf, Dcols, ya], writes=[(y_bf.name, c)], tag="yfin")
    ctx.es = old_es
    prog.barrier()
    es3.close()

    es4 = ExitStack()
    ctx.es = es4
    wo = ctx.load_w_bf16(WoutT_d, DM, DM, tag="m1_wo")
    xre = [ctx.sbuf("m1_xre", (128, T)) for _ in range(2)]
    xo = [ctx.sbuf("m1_xo", (128, T)) for _ in range(2)]
    for m in range(8):
        xr = xre[m % 2]
        xw = xo[m % 2]
        dma(prog, "sp", xr[:, :], xin[b, m * 128:(m + 1) * 128, :],
            reads=[(xin.tensor.name, b, m)], writes=[xr.name], tag="rex")
        for s in range(NS):
            ps = ctx.pp.get()
            for k in range(8):
                prog.op("pe", lambda eng, m=m, s=s, k=k, ps=ps: eng.matmul(
                    ps[0:128, 0:NB], wo[:, k * DM + m * 128: k * DM + (m + 1) * 128],
                    y_bf[:, k * T + s * NB: k * T + (s + 1) * NB],
                    start=(k == 0), stop=(k == 7)),
                    reads=[(wo.name, k), (y_bf.name, k)], writes=[ps], tag="wom")
            prog.op("dve", lambda eng, s=s, ps=ps, xr=xr, xw=xw, m=m: eng.scalar_tensor_tensor(
                xw[:, s * NB:(s + 1) * NB], ps[0:128, 0:NB], 1.0,
                xr[:, s * NB:(s + 1) * NB], ALU.mult, ALU.add),
                reads=[ps, xr.name], writes=[xw.name], tag="resid")
        dma(prog, "sp", xout[b, m * 128:(m + 1) * 128, :], xw[:, :],
            reads=[xw.name], writes=[(xout.tensor.name, b, m)], tag="stx")
    ctx.es = old_es
    prog.barrier()
    es4.close()
    es34.close()
    es23.close()
    es13.close()



def _sel(n, w):
    s = np.zeros((n, n * w), np.float32)
    for i in range(n):
        s[i, i * w:(i + 1) * w] = 1.0
    return s


def const_arrays():
    sel16 = _sel(16, 128)
    return {
        "c_ones_red": np.ones((128, 1), np.float32),
        "c_ones_bc": np.ones((1, 128), np.float32),
        "c_ident": np.eye(128, dtype=np.float32),
        "c_sel16": sel16,
        "c_scal": np.tile(np.array([[1.0, 1e-5, 1e-6, 1e-30]], np.float32), (128, 1)),
        "c_ones_red_bf": np.ones((128, 1), ml_dtypes.bfloat16),
        "c_rev8": np.tile(np.arange(7, -1, -1, dtype=np.float32)[None, :], (128, 1)),
        "c_sel32": _sel(32, 128),
        "c_trimask": np.where(np.arange(128)[:, None] <= np.arange(128)[None, :],
                              0.0, -1e30).astype(np.float32),
        "c_dm_l1": np.tile((np.arange(128) < 64).astype(np.float32)[None, :], (32, 1)),
        "c_dm_l2": np.tile((np.arange(128) >= 64).astype(np.float32)[None, :], (32, 1)),
        "c_dm_m1": (np.arange(32)[:, None] == 2 * np.arange(16)[None, :]).astype(np.float32),
        "c_dm_m2": (np.arange(32)[:, None] == 2 * np.arange(16)[None, :] + 1).astype(np.float32),
        "c_ident_bf": np.eye(128, dtype=ml_dtypes.bfloat16),
        "c_cmask": np.concatenate(
            [np.where(np.arange(128)[:, None] + 128 * i <= np.arange(512)[None, :],
                      0.0, -1e30).astype(np.float32) for i in range(4)], axis=1),
    }


BASE_CONSTS = ("c_ones_red", "c_ones_bc", "c_ident", "c_scal")


def declare_consts(ctx, extra=()):
    nc, prog, es = ctx.nc, ctx.prog, ctx.es
    arrs = const_arrays()
    want = set(BASE_CONSTS) | set(extra)
    arrs = {k: v for k, v in arrs.items() if k in want}
    for name, arr in arrs.items():
        dt = BF16 if arr.dtype == ml_dtypes.bfloat16 else F32
        d = nc.dram_tensor(name + "_d", list(arr.shape), dt, kind="ExternalInput").ap()
        t = sb(nc, es, name, list(arr.shape), dt)
        dma(prog, "sp", t[:, :], d, reads=[d], writes=[name], tag="ldc")
        setattr(ctx, name, t[:, :])
    ctx.pp = PsumPool(nc, es, n=8)


SCAL_IDX = {1.0: 0, 1e-5: 1, 1e-6: 2, 1e-30: 3}


def scal_ap(ctx, v, p=128):
    i = SCAL_IDX[v]
    return ctx.c_scal[0:p, i:i + 1]


def row_bcast_sb(ctx, row_sb, W, tag="rb"):
    prog = ctx.prog
    out = ctx.sbuf(tag, (128, W))
    for s in range(0, W, NB):
        w = min(NB, W - s)
        ps = ctx.pp.get()
        prog.op("pe", lambda eng, s=s, w=w, ps=ps: eng.matmul(
            ps[:, 0:w], ctx.c_ones_bc, row_sb[0:1, s:s + w], start=True, stop=True),
            reads=[row_sb, "c_ones_bc"], writes=[ps], tag=tag + "m")
        prog.op("dve", lambda eng, s=s, w=w, ps=ps: eng.tensor_copy(
            out[:, s:s + w], ps[:, 0:w]),
            reads=[ps], writes=[out], tag=tag + "e")
    return out


def load_row(ctx, vec_d, n, tag="lr"):
    row = ctx.sbuf(tag, (1, n))
    dma(ctx.prog, "sp", row[:, :], vec_d.rearrange("(o n) -> o n", o=1),
        reads=[vec_d], writes=[row], tag=tag)
    return row


def emit_out_proj(ctx, b, W_d, bias_cols, y_bf, ykey, xin, xout, kc, T, tag="op"):
    prog = ctx.prog
    from contextlib import ExitStack
    es4 = ExitStack()
    old = ctx.es
    ctx.es = es4
    wo = ctx.load_w_bf16(W_d, kc * 128, DM, tag=tag + "w")
    xre = [ctx.sbuf(tag + "xr", (128, T)) for _ in range(2)]
    xo = [ctx.sbuf(tag + "xo", (128, T)) for _ in range(2)]
    NS = T // NB
    for m in range(8):
        xr = xre[m % 2]
        xw = xo[m % 2]
        dma(prog, "sp", xr[:, :], xin[b, m * 128:(m + 1) * 128, :],
            reads=[(xin.tensor.name, b, m)], writes=[xr.name], tag=tag + "rex")
        pss = [ctx.pp.get() for _ in range(NS)]
        for k in range(kc):
            for s in range(NS):
                prog.op("pe", lambda eng, m=m, s=s, k=k, ps=pss[s]: eng.matmul(
                    ps[0:128, 0:NB], wo[:, k * DM + m * 128: k * DM + (m + 1) * 128],
                    y_bf[:, k * T + s * NB: k * T + (s + 1) * NB],
                    start=(k == 0), stop=(k == kc - 1)),
                    reads=[(wo.name, k), (ykey, k)], writes=[pss[s]], tag=tag + "mm")
        for s in range(NS):
            ps = pss[s]
            if bias_cols is not None:
                tmp = ctx.pp.get()
                prog.op("dve", lambda eng, s=s, ps=ps, xr=xr, m=m: eng.scalar_tensor_tensor(
                    ps[0:128, 0:NB], ps[0:128, 0:NB], bias_cols[:, m:m + 1],
                    xr[:, s * NB:(s + 1) * NB], ALU.add, ALU.add),
                    reads=[ps, bias_cols, xr.name], writes=[ps], tag=tag + "bi")
                prog.op("dve", lambda eng, s=s, ps=ps, xw=xw: eng.tensor_copy(
                    xw[:, s * NB:(s + 1) * NB], ps[0:128, 0:NB]),
                    reads=[ps], writes=[xw.name], tag=tag + "cp")
            else:
                prog.op("dve", lambda eng, s=s, ps=ps, xr=xr, xw=xw: eng.scalar_tensor_tensor(
                    xw[:, s * NB:(s + 1) * NB], ps[0:128, 0:NB], 1.0,
                    xr[:, s * NB:(s + 1) * NB], ALU.mult, ALU.add),
                    reads=[ps, xr.name], writes=[xw.name], tag=tag + "res")
        dma(prog, "sp", xout[b, m * 128:(m + 1) * 128, :], xw[:, :],
            reads=[xw.name], writes=[(xout.tensor.name, b, m)], tag=tag + "stx")
    ctx.es = old
    prog.barrier()
    es4.close()


def emit_attn0(ctx, b, xin, xout, wqT_d, bq_d, wkT_d, bk_d, wvT_d, bv_d, woT_d,
               bo_d):
    nc, prog = ctx.nc, ctx.prog
    from contextlib import ExitStack
    T = SEQ
    NS = T // NB
    NCH = 8
    NH = 16
    old = ctx.es

    es_a = ExitStack()
    ctx.es = es_a
    Qt = ctx.sbuf("a0_Q", (128, NCH * T), BF16)
    Kt = ctx.sbuf("a0_K", (128, NCH * T), BF16)
    Vt = ctx.sbuf("a0_V", (128, 16 * 1024), BF16)
    Yt = ctx.sbuf("a0_Y", (128, NCH * T), BF16)
    bqc = ctx.feat_cols(bq_d, DM, tag="a0bq")
    bkc = ctx.feat_cols(bk_d, DM, tag="a0bk")
    boc = ctx.feat_cols(bo_d, DM, tag="a0bo")

    es1 = ExitStack()
    ctx.es = es1
    h_bf = ctx.sbuf("a0_h", (128, NCH * T), BF16)
    es1b = ExitStack()
    ctx.es = es1b
    loader = dram_chunk_loader(
        ctx, lambda c: (xin[b, c * 128:(c + 1) * 128, :], (xin.tensor.name, b, c)),
        T, tag="a0xl", nbuf=1)
    emit_ln_fm(ctx, loader, T, NCH, h_bf, "a0_hk", tag=f"a0ln{b}", nsq=1)
    prog.barrier()
    es1b.close()

    es2 = ExitStack()
    ctx.es = es2
    for which, wd, bc, out_t, okey in (("q", wqT_d, bqc, Qt, "a0_Qk"),
                                       ("k", wkT_d, bkc, Kt, "a0_Kk")):
        ws = ctx.load_w_bf16(wd, DM, DM, tag=f"a0w{which}")

        def evac(info, m0, m1, s, ps, bc=bc, out_t=out_t, okey=okey):
            m = m0 // 128
            prog.op("dve", lambda eng, m=m, s=s, ps=ps: eng.tensor_scalar(
                out_t[:, m * T + s * NB: m * T + (s + 1) * NB], ps[0:128, 0:NB],
                bc[:, m:m + 1], None, ALU.add),
                reads=[ps, bc], writes=[(okey, m)], tag="a0ev" + which)
        emit_proj_tiles(ctx, ws, DM, h_bf, "a0_hk", NCH, T,
                        [(m * 128, (m + 1) * 128, m) for m in range(8)], evac,
                        tag="a0p" + which)
    prog.barrier()
    es2.close()
    es2 = ExitStack()
    ctx.es = es2
    bvrow = load_row(ctx, bv_d, DM, tag="a0bvr")
    bvb = row_bcast_sb(ctx, bvrow, DM, tag="a0bvb")
    wv = ctx.load_w_bf16(wvT_d, DM, DM, tag="a0wv")
    for tt in range(16):
        for s in range(2):
            ps = ctx.pp.get()
            for k in range(NCH):
                prog.op("pe", lambda eng, tt=tt, s=s, k=k, ps=ps: eng.matmul(
                    ps[0:128, 0:NB],
                    h_bf[:, k * T + tt * 128: k * T + (tt + 1) * 128],
                    wv[:, k * DM + s * NB: k * DM + (s + 1) * NB],
                    start=(k == 0), stop=(k == NCH - 1)),
                    reads=[("a0_hk", k), (wv.name, k)], writes=[ps], tag="a0pv")
            prog.op("dve", lambda eng, tt=tt, s=s, ps=ps: eng.scalar_tensor_tensor(
                Vt[:, tt * 1024 + s * NB: tt * 1024 + (s + 1) * NB],
                ps[0:128, 0:NB], 1.0, bvb[:, s * NB:(s + 1) * NB],
                ALU.mult, ALU.add),
                reads=[ps, bvb], writes=[("a0_Vk", tt)], tag="a0evv")
    ctx.es = old
    prog.barrier()
    es2.close()
    es1.close()

    es3 = ExitStack()
    ctx.es = es3
    Pt = ctx.sbuf("a0_P", (128, 16 * NB), BF16)
    srow = ctx.sbuf("a0_srow", (1, NB))
    rrow = ctx.sbuf("a0_rrow", (1, NB))
    yev = ctx.sbuf("a0_yev", (64, NB))
    qk_tiles = ctx.pp.tiles[0:5]
    t_sum = ctx.pp.tiles[5]
    t_r = ctx.pp.tiles[6]
    t_y = ctx.pp.tiles[7]
    qkc = [0]
    for h in range(NH):
        hb = (h % 2) * 64
        hc = h // 2
        for qb in range(NS):
            for kt in range(16):
                ps = qk_tiles[qkc[0] % 5]
                qkc[0] += 1
                prog.op("pe", lambda eng, h=h, qb=qb, kt=kt, ps=ps, hb=hb, hc=hc: eng.matmul(
                    ps[0:128, 0:NB],
                    Kt[hb:hb + 64, hc * T + kt * 128: hc * T + (kt + 1) * 128],
                    Qt[hb:hb + 64, hc * T + qb * NB: hc * T + (qb + 1) * NB],
                    start=True, stop=True),
                    reads=[("a0_Kk", hc), ("a0_Qk", hc)], writes=[ps], tag="a0qk")
                prog.op("act", lambda eng, kt=kt, ps=ps: eng.activation(
                    Pt[:, kt * NB:(kt + 1) * NB], ps[0:128, 0:NB], AF.Exp,
                    scale=0.25),
                    reads=[ps], writes=[(Pt.name, kt)], tag="a0ex")
            ps_sum = t_sum
            for kt in range(16):
                prog.op("pe", lambda eng, kt=kt: eng.matmul(
                    ps_sum[0:1, 0:NB], ctx.c_ones_red_bf,
                    Pt[:, kt * NB:(kt + 1) * NB],
                    start=(kt == 0), stop=(kt == 15)),
                    reads=[(Pt.name, kt), "c_ones_red_bf"], writes=[ps_sum],
                    tag="a0sm")
            prog.op("dve", lambda eng: eng.tensor_copy(srow[:, :], ps_sum[0:1, 0:NB]),
                    reads=[ps_sum], writes=[srow], tag="a0se")
            prog.op("dve", lambda eng: eng.reciprocal(rrow[:, :], srow[:, :]),
                    reads=[srow], writes=[rrow], tag="a0rc")
            ps_r = t_r
            prog.op("pe", lambda eng, ps_r=ps_r: eng.matmul(
                ps_r[:, 0:NB], ctx.c_ones_bc, rrow[0:1, 0:NB], start=True, stop=True),
                reads=[rrow, "c_ones_bc"], writes=[ps_r], tag="a0rb")
            ps_y = t_y
            for kt in range(16):
                prog.op("pe", lambda eng, h=h, kt=kt, ps_y=ps_y: eng.matmul(
                    ps_y[0:64, 0:NB], Vt[:, kt * 1024 + h * 64: kt * 1024 + (h + 1) * 64],
                    Pt[:, kt * NB:(kt + 1) * NB],
                    start=(kt == 0), stop=(kt == 15)),
                    reads=[("a0_Vk", kt), (Pt.name, kt)], writes=[ps_y], tag="a0pv2")
            prog.op("dve", lambda eng, ps_y=ps_y: eng.tensor_copy(
                yev[:, :], ps_y[0:64, 0:NB]),
                reads=[ps_y], writes=[yev], tag="a0ye")
            prog.op("dve", lambda eng, h=h, qb=qb, ps_r=ps_r, hb=hb, hc=hc: eng.tensor_tensor(
                Yt[hb:hb + 64, hc * T + qb * NB: hc * T + (qb + 1) * NB],
                yev[:, :], ps_r[0:64, 0:NB], ALU.mult),
                reads=[yev, ps_r], writes=[("a0_Yk", hc)], tag="a0yw")
    ctx.es = old
    prog.barrier()
    es3.close()

    emit_out_proj(ctx, b, woT_d, boc, Yt, "a0_Yk", xin, xout, NCH, T, tag="a0o")
    es_a.close()


def emit_router(ctx, b, xin, h_out_d, rw_d, rb_d, gate_d, idx_d, tag="rt"):
    nc, prog = ctx.nc, ctx.prog
    from contextlib import ExitStack
    T = SEQ
    NCH = 8
    old = ctx.es
    es = ExitStack()
    ctx.es = es
    h_bf = ctx.sbuf(tag + "_h", (128, NCH * T), BF16)
    es1 = ExitStack()
    ctx.es = es1
    loader = dram_chunk_loader(
        ctx, lambda c: (xin[b, c * 128:(c + 1) * 128, :], (xin.tensor.name, b, c)),
        T, tag=tag + "xl", nbuf=1)
    emit_ln_fm(ctx, loader, T, NCH, h_bf, tag + "_hk", tag=tag + f"ln{b}", nsq=1)
    prog.barrier()
    es1.close()
    es2 = ExitStack()
    ctx.es = es2
    for c in range(NCH):
        dma(prog, "pool", h_out_d[b, c * 128:(c + 1) * 128, :],
            h_bf[:, c * T:(c + 1) * T],
            reads=[(tag + "_hk", c)], writes=[(h_out_d.tensor.name, b, c)],
            tag=tag + "sth")
    rw = ctx.load_w_bf16(rw_d, DM, 8, tag=tag + "rw")
    rbrow = load_row(ctx, rb_d, 8, tag=tag + "rb")
    psb = ctx.pp.get()
    prog.op("pe", lambda eng: eng.matmul(psb[0:128, 0:8], ctx.c_ones_bc,
                                         rbrow[0:1, 0:8], start=True, stop=True),
            reads=[rbrow, "c_ones_bc"], writes=[psb], tag=tag + "rbb")
    rbb = ctx.sbuf(tag + "rbbs", (128, 8))
    prog.op("dve", lambda eng: eng.tensor_copy(rbb[:, :], psb[0:128, 0:8]),
            reads=[psb], writes=[rbb], tag=tag + "rbe")
    lg = ctx.sbuf(tag + "lg", (128, 8))
    m1 = ctx.sbuf(tag + "m", (128, 1))
    ex = ctx.sbuf(tag + "ex", (128, 8))
    srow = ctx.sbuf(tag + "sr", (128, 1))
    grow = ctx.sbuf(tag + "g", (128, 1))
    msk = ctx.sbuf(tag + "msk", (128, 8))
    irow = ctx.sbuf(tag + "i", (128, 1))
    for tt in range(16):
        ps = ctx.pp.get()
        for k in range(NCH):
            prog.op("pe", lambda eng, tt=tt, k=k, ps=ps: eng.matmul(
                ps[0:128, 0:8],
                h_bf[:, k * T + tt * 128: k * T + (tt + 1) * 128],
                rw[:, k * 8:(k + 1) * 8],
                start=(k == 0), stop=(k == NCH - 1)),
                reads=[(tag + "_hk", k), (rw.name, k)], writes=[ps], tag=tag + "lgm")
        prog.op("dve", lambda eng, ps=ps: eng.tensor_tensor(
            lg[:, :], ps[0:128, 0:8], rbb[:, :], ALU.add),
            reads=[ps, rbb], writes=[lg], tag=tag + "lge")
        prog.op("dve", lambda eng: eng.tensor_reduce(m1[:, :], lg[:, :], AX.X, ALU.max),
                reads=[lg], writes=[m1], tag=tag + "mx")
        prog.op("dve", lambda eng: eng.tensor_scalar(ex[:, :], lg[:, :], m1[:, 0:1],
                                                     None, ALU.subtract),
                reads=[lg, m1], writes=[ex], tag=tag + "sub")
        prog.op("act", lambda eng: eng.activation(ex[:, :], ex[:, :], AF.Exp),
                reads=[ex], writes=[ex], tag=tag + "exp")
        prog.op("dve", lambda eng: eng.tensor_reduce(srow[:, :], ex[:, :], AX.X,
                                                     ALU.add),
                reads=[ex], writes=[srow], tag=tag + "sm")
        prog.op("dve", lambda eng: eng.reciprocal(grow[:, :], srow[:, :]),
                reads=[srow], writes=[grow], tag=tag + "rc")
        prog.op("dve", lambda eng: eng.tensor_scalar(msk[:, :], lg[:, :], m1[:, 0:1],
                                                     None, ALU.is_ge),
                reads=[lg, m1], writes=[msk], tag=tag + "eq")
        prog.op("dve", lambda eng: eng.tensor_tensor(msk[:, :], msk[:, :],
                                                     ctx.c_rev8, ALU.mult),
                reads=[msk, "c_rev8"], writes=[msk], tag=tag + "rv")
        prog.op("dve", lambda eng: eng.tensor_reduce(irow[:, :], msk[:, :], AX.X,
                                                     ALU.max),
                reads=[msk], writes=[irow], tag=tag + "am")
        prog.op("dve", lambda eng: eng.tensor_scalar(irow[:, :], irow[:, :], -1.0,
                                                     7.0, ALU.mult, ALU.add),
                reads=[irow], writes=[irow], tag=tag + "rv2")
        dma(prog, "sp", gate_d[b, tt, :].rearrange("(p o) -> p o", o=1), grow[:, :],
            reads=[grow], writes=[(gate_d.tensor.name, b, tt)], tag=tag + "stg")
        dma(prog, "sp", idx_d[b, tt, :].rearrange("(p o) -> p o", o=1), irow[:, :],
            reads=[irow], writes=[(idx_d.tensor.name, b, tt)], tag=tag + "sti")
    ctx.es = old
    prog.barrier()
    es2.close()
    es.close()


def emit_expert_l1(ctx, Xg_d, w_d, bias_d, Yg_d, nslot, tag="e1"):
    nc, prog = ctx.nc, ctx.prog
    from contextlib import ExitStack
    old = ctx.es
    es = ExitStack()
    ctx.es = es
    NSL = nslot // NB
    xg = ctx.sbuf(tag + "x", (128, 8 * nslot), BF16)
    for c in range(8):
        dma(prog, "pool", xg[:, c * nslot:(c + 1) * nslot],
            Xg_d[c * 128:(c + 1) * 128, :], reads=[(Xg_d.tensor.name, c)],
            writes=[(xg.name, c)], tag=tag + "lx")
    ws = ctx.load_w_bf16(w_d, DM, DM, tag=tag + "w")
    bc = ctx.feat_cols(bias_d, DM, tag=tag + "b")
    yo = [ctx.sbuf(tag + "y", (128, NB)) for _ in range(2)]
    for m in range(8):
        for s in range(NSL):
            ps = ctx.pp.get()
            for k in range(8):
                prog.op("pe", lambda eng, m=m, s=s, k=k, ps=ps: eng.matmul(
                    ps[0:128, 0:NB], ws[:, k * DM + m * 128: k * DM + (m + 1) * 128],
                    xg[:, k * nslot + s * NB: k * nslot + (s + 1) * NB],
                    start=(k == 0), stop=(k == 7)),
                    reads=[(ws.name, k), (xg.name, k)], writes=[ps], tag=tag + "mm")
            y = yo[(m * NSL + s) % 2]
            prog.op("dve", lambda eng, ps=ps, y=y, m=m: eng.tensor_scalar(
                y[:, :], ps[0:128, 0:NB], bc[:, m:m + 1], None, ALU.add),
                reads=[ps, bc], writes=[y.name], tag=tag + "ev")
            dma(prog, "sp", Yg_d[m * 128:(m + 1) * 128, s * NB:(s + 1) * NB],
                y[:, :], reads=[y.name], writes=[(Yg_d.tensor.name, m, s)],
                tag=tag + "st")
    ctx.es = old
    prog.barrier()
    es.close()


def emit_expert_l0(ctx, Xg_d, w1_d, b1_d, w2_d, b2_d, Yg_d, nslot, tag="e0"):
    nc, prog = ctx.nc, ctx.prog
    from contextlib import ExitStack
    old = ctx.es
    es = ExitStack()
    ctx.es = es
    EXP = 4096
    xg = ctx.sbuf(tag + "x", (128, 8 * nslot), BF16)
    for c in range(8):
        dma(prog, "pool", xg[:, c * nslot:(c + 1) * nslot],
            Xg_d[c * 128:(c + 1) * 128, :], reads=[(Xg_d.tensor.name, c)],
            writes=[(xg.name, c)], tag=tag + "lx")
    w1 = ctx.load_w_bf16(w1_d, DM, EXP, tag=tag + "w1")
    w2 = ctx.load_w_bf16(w2_d, EXP, DM, tag=tag + "w2")
    b1c = ctx.feat_cols(b1_d, EXP, tag=tag + "b1")
    b2c = ctx.feat_cols(b2_d, DM, tag=tag + "b2")
    h1 = ctx.sbuf(tag + "h1", (128, 32 * NB), BF16)
    yo = [ctx.sbuf(tag + "y", (128, NB)) for _ in range(2)]
    for s in range(nslot // NB):
        for m in range(32):
            ps = ctx.pp.get()
            for k in range(8):
                prog.op("pe", lambda eng, m=m, s=s, k=k, ps=ps: eng.matmul(
                    ps[0:128, 0:NB], w1[:, k * EXP + m * 128: k * EXP + (m + 1) * 128],
                    xg[:, k * nslot + s * NB: k * nslot + (s + 1) * NB],
                    start=(k == 0), stop=(k == 7)),
                    reads=[(w1.name, k), (xg.name, k)], writes=[ps], tag=tag + "m1")
            prog.op("act", lambda eng, m=m, ps=ps: eng.activation(
                h1[:, m * NB:(m + 1) * NB], ps[0:128, 0:NB], AF.Relu,
                bias=b1c[:, m:m + 1]),
                reads=[ps, b1c], writes=[(h1.name, m)], tag=tag + "rl")
        for m in range(8):
            ps = ctx.pp.get()
            for k in range(32):
                prog.op("pe", lambda eng, m=m, k=k, ps=ps: eng.matmul(
                    ps[0:128, 0:NB], w2[:, k * DM + m * 128: k * DM + (m + 1) * 128],
                    h1[:, k * NB:(k + 1) * NB],
                    start=(k == 0), stop=(k == 31)),
                    reads=[(w2.name, k), (h1.name, k)], writes=[ps], tag=tag + "m2")
            y = yo[m % 2]
            prog.op("dve", lambda eng, ps=ps, y=y, m=m: eng.tensor_scalar(
                y[:, :], ps[0:128, 0:NB], b2c[:, m:m + 1], None, ALU.add),
                reads=[ps, b2c], writes=[y.name], tag=tag + "ev")
            dma(prog, "sp", Yg_d[m * 128:(m + 1) * 128, s * NB:(s + 1) * NB],
                y[:, :], reads=[y.name], writes=[(Yg_d.tensor.name, m, s)],
                tag=tag + "st")
    ctx.es = old
    prog.barrier()
    es.close()


def emit_combine(ctx, b, xin, xout, Ysel_d, gk_d, mode, tag="cb"):
    nc, prog = ctx.nc, ctx.prog
    from contextlib import ExitStack
    T = SEQ
    NCH = 8
    old = ctx.es
    es = ExitStack()
    ctx.es = es
    hf = None
    if mode == "keep":
        hf = ctx.sbuf(tag + "_hf", (128, NCH * T))
        es1 = ExitStack()
        ctx.es = es1
        loader = dram_chunk_loader(
            ctx, lambda c: (xin[b, c * 128:(c + 1) * 128, :],
                            (xin.tensor.name, b, c)), T, tag=tag + "xl", nbuf=1)
        emit_ln_fm(ctx, loader, T, NCH, hf, tag + "_hfk", tag=tag + f"ln{b}", nsq=1)
        prog.barrier()
        es1.close()
        ctx.es = es
    grow = ctx.sbuf(tag + "g", (1, T))
    dma(prog, "sp", grow[:, :], gk_d[b, :, :], reads=[(gk_d.tensor.name, b)],
        writes=[grow], tag=tag + "lg")
    xr = [ctx.sbuf(tag + "xr", (128, T)) for _ in range(2)]
    yr = [ctx.sbuf(tag + "yr", (128, T)) for _ in range(2)]
    xw = [ctx.sbuf(tag + "xw", (128, T)) for _ in range(2)]
    gb = []
    for s in range(T // NB):
        pg = ctx.pp.get()
        bcast_row(ctx.prog, ctx.c_ones_bc, grow, pg, s * NB, (s + 1) * NB,
                  grow.name, pg.name, tag=tag + "gb")
        gb.append(pg)
    for c in range(NCH):
        x_ = xr[c % 2]
        y_ = yr[c % 2]
        w_ = xw[c % 2]
        dma(prog, "sp", x_[:, :], xin[b, c * 128:(c + 1) * 128, :],
            reads=[(xin.tensor.name, b, c)], writes=[x_.name], tag=tag + "lx")
        dma(prog, "pool", y_[:, :], Ysel_d[b, c * 128:(c + 1) * 128, :],
            reads=[(Ysel_d.tensor.name, b, c)], writes=[y_.name], tag=tag + "ly")
        for s in range(T // NB):
            sl = slice(s * NB, (s + 1) * NB)
            if mode == "gate":
                prog.op("dve", lambda eng, sl=sl, s=s, x_=x_, y_=y_, w_=w_: eng.tensor_tensor(
                    w_[:, sl], y_[:, sl], gb[s][:, 0:NB], ALU.mult),
                    reads=[y_.name, gb[s]], writes=[w_.name], tag=tag + "g1")
                prog.op("dve", lambda eng, sl=sl, x_=x_, w_=w_: eng.tensor_tensor(
                    w_[:, sl], w_[:, sl], x_[:, sl], ALU.add),
                    reads=[w_.name, x_.name], writes=[w_.name], tag=tag + "g2")
            else:
                prog.op("dve", lambda eng, sl=sl, s=s, c=c, y_=y_, w_=w_: eng.tensor_tensor(
                    w_[:, sl], y_[:, sl],
                    hf[:, c * T + s * NB: c * T + (s + 1) * NB], ALU.subtract),
                    reads=[y_.name, (tag + "_hfk", c)], writes=[w_.name],
                    tag=tag + "se")
                prog.op("dve", lambda eng, sl=sl, s=s, w_=w_: eng.tensor_tensor(
                    w_[:, sl], w_[:, sl], gb[s][:, 0:NB], ALU.mult),
                    reads=[w_.name, gb[s]], writes=[w_.name], tag=tag + "s1b")
                prog.op("dve", lambda eng, sl=sl, s=s, c=c, w_=w_: eng.tensor_tensor(
                    w_[:, sl], w_[:, sl],
                    hf[:, c * T + s * NB: c * T + (s + 1) * NB], ALU.add),
                    reads=[w_.name, (tag + "_hfk", c)], writes=[w_.name],
                    tag=tag + "s1c")
                prog.op("dve", lambda eng, sl=sl, x_=x_, w_=w_: eng.tensor_tensor(
                    w_[:, sl], w_[:, sl], x_[:, sl], ALU.add),
                    reads=[w_.name, x_.name], writes=[w_.name], tag=tag + "s2")
        dma(prog, "sp", xout[b, c * 128:(c + 1) * 128, :], w_[:, :],
            reads=[w_.name], writes=[(xout.tensor.name, b, c)], tag=tag + "st")
    ctx.es = old
    prog.barrier()
    es.close()


def emit_s5(ctx, xin, xout, logdt_d, logLre_d, Lim_d, Vre_d, Vim_d, Wre_d, Wim_d,
            SL, tag="s5"):
    nc, prog = ctx.nc, ctx.prog
    from contextlib import ExitStack
    NCH = 8
    TB = 16
    PI = float(np.pi)
    old = ctx.es
    es = ExitStack()
    ctx.es = es

    r_dt = load_row(ctx, logdt_d, 64, tag=tag + "dt")
    prog.op("act", lambda eng: eng.activation(r_dt[:, :], r_dt[:, :], AF.Exp),
            reads=[r_dt], writes=[r_dt], tag=tag + "e1")
    r_lre = load_row(ctx, logLre_d, 64, tag=tag + "lr")
    prog.op("act", lambda eng: eng.activation(r_lre[:, :], r_lre[:, :], AF.Exp),
            reads=[r_lre], writes=[r_lre], tag=tag + "e2")
    r_lim = load_row(ctx, Lim_d, 64, tag=tag + "li")
    rphi = ctx.sbuf(tag + "phi", (1, 64))
    rth = ctx.sbuf(tag + "th", (1, 64))
    prog.op("dve", lambda eng: eng.scalar_tensor_tensor(
        rphi[:, :], r_lre[:, :], -1.0, r_dt[:, :], ALU.mult, ALU.mult),
        reads=[r_lre, r_dt], writes=[rphi], tag=tag + "ph")
    prog.op("dve", lambda eng: eng.tensor_tensor(rth[:, :], r_lim[:, :], r_dt[:, :],
                                                 ALU.mult),
            reads=[r_lim, r_dt], writes=[rth], tag=tag + "tt")
    prog.op("act", lambda eng: eng.activation(rphi[:, :], rphi[:, :], AF.Exp),
            reads=[rphi], writes=[rphi], tag=tag + "e3")

    def rangered(row, outrow, tg):
        msk = ctx.sbuf(tg + "mk", (1, 64))
        prog.op("dve", lambda eng: eng.tensor_copy(outrow[:, :], row[:, :]),
                reads=[row], writes=[outrow], tag=tg + "cp")
        for k in (1024, 512, 256, 128, 64, 32, 16, 8, 4, 2, 1, 1):
            thr = 2 * PI * k - PI
            prog.op("dve", lambda eng, thr=thr: eng.tensor_single_scalar(
                msk[:, :], outrow[:, :], thr, ALU.is_gt),
                reads=[outrow], writes=[msk], tag=tg + "g")
            prog.op("dve", lambda eng, k=k: eng.scalar_tensor_tensor(
                outrow[:, :], msk[:, :], -2 * PI * k, outrow[:, :], ALU.mult,
                ALU.add),
                reads=[msk, outrow], writes=[outrow], tag=tg + "f")
            prog.op("dve", lambda eng, thr=thr: eng.tensor_single_scalar(
                msk[:, :], outrow[:, :], -thr, ALU.is_lt),
                reads=[outrow], writes=[msk], tag=tg + "g2")
            prog.op("dve", lambda eng, k=k: eng.scalar_tensor_tensor(
                outrow[:, :], msk[:, :], 2 * PI * k, outrow[:, :], ALU.mult,
                ALU.add),
                reads=[msk, outrow], writes=[outrow], tag=tg + "f2")

    rsin = ctx.sbuf(tag + "sn", (1, 64))
    rcos = ctx.sbuf(tag + "cs", (1, 64))
    rtmp = ctx.sbuf(tag + "tm", (1, 64))
    rangered(rth, rsin, tag + "r1")
    prog.op("act", lambda eng: eng.activation(rsin[:, :], rsin[:, :], AF.Sin),
            reads=[rsin], writes=[rsin], tag=tag + "s1")
    prog.op("dve", lambda eng: eng.tensor_scalar(rtmp[:, :], rth[:, :], -1.0, PI / 2,
                                                 ALU.mult, ALU.add),
            reads=[rth], writes=[rtmp], tag=tag + "c0")
    rangered(rtmp, rcos, tag + "r2")
    prog.op("act", lambda eng: eng.activation(rcos[:, :], rcos[:, :], AF.Sin),
            reads=[rcos], writes=[rcos], tag=tag + "s2")
    rare = ctx.sbuf(tag + "ar", (1, 64))
    raim = ctx.sbuf(tag + "ai", (1, 64))
    prog.op("dve", lambda eng: eng.tensor_tensor(rare[:, :], rphi[:, :], rcos[:, :],
                                                 ALU.mult),
            reads=[rphi, rcos], writes=[rare], tag=tag + "ar")
    prog.op("dve", lambda eng: eng.tensor_tensor(raim[:, :], rphi[:, :], rsin[:, :],
                                                 ALU.mult),
            reads=[rphi, rsin], writes=[raim], tag=tag + "ai")

    mats = {}
    for nm, d in (("vre", Vre_d), ("vim", Vim_d), ("wre", Wre_d), ("wim", Wim_d)):
        t = ctx.sbuf(tag + nm, (64, 64))
        dma(prog, "sp", t[:, :], d, reads=[d], writes=[t], tag=tag + "l" + nm)
        mats[nm] = t

    def bc64(row, tg):
        ps = ctx.pp.get()
        prog.op("pe", lambda eng, ps=ps: eng.matmul(
            ps[0:64, 0:64], ctx.c_ones_bc[0:1, 0:64], row[0:1, 0:64],
            start=True, stop=True),
            reads=[row, "c_ones_bc"], writes=[ps], tag=tg + "b")
        t = ctx.sbuf(tg + "bs", (64, 64))
        prog.op("dve", lambda eng, ps=ps: eng.tensor_copy(t[:, :], ps[0:64, 0:64]),
                reads=[ps], writes=[t], tag=tg + "e")
        return t

    areb = bc64(rare, tag + "A")
    aimb = bc64(raim, tag + "B")
    dtb = bc64(r_dt, tag + "D")

    def mk64(tg):
        return ctx.sbuf(tg, (64, 64))

    pre_, pim_, vdre, vdim = mk64(tag + "pre"), mk64(tag + "pim"), mk64(tag + "vdr"), mk64(tag + "vdi")
    for dst, t1, s1, t2, s2, op in ((pre_, "vre", areb, "vim", aimb, ALU.subtract),
                                    (pim_, "vre", aimb, "vim", areb, ALU.add)):
        tmp = mk64(tag + "tmq" + t2 + str(op))
        prog.op("dve", lambda eng, dst=dst, t1=t1, s1=s1: eng.tensor_tensor(
            dst[:, :], mats[t1][:, :], s1[:, :], ALU.mult),
            reads=[mats[t1], s1], writes=[dst], tag=tag + "p1")
        prog.op("dve", lambda eng, tmp=tmp, t2=t2, s2=s2: eng.tensor_tensor(
            tmp[:, :], mats[t2][:, :], s2[:, :], ALU.mult),
            reads=[mats[t2], s2], writes=[tmp], tag=tag + "p2")
        prog.op("dve", lambda eng, dst=dst, tmp=tmp, op=op: eng.tensor_tensor(
            dst[:, :], dst[:, :], tmp[:, :], op),
            reads=[dst, tmp], writes=[dst], tag=tag + "p3")
    prog.op("dve", lambda eng: eng.tensor_tensor(vdre[:, :], mats["vre"][:, :],
                                                 dtb[:, :], ALU.mult),
            reads=[mats["vre"], dtb], writes=[vdre], tag=tag + "v1")
    prog.op("dve", lambda eng: eng.tensor_tensor(vdim[:, :], mats["vim"][:, :],
                                                 dtb[:, :], ALU.mult),
            reads=[mats["vim"], dtb], writes=[vdim], tag=tag + "v2")

    def tr64(src, tg):
        ps = ctx.pp.get()
        prog.op("pe", lambda eng, ps=ps, src=src: eng.transpose(
            ps[0:64, 0:64], src[:, :], ctx.c_ident[0:64, 0:64]),
            reads=[src, "c_ident"], writes=[ps], tag=tg + "t")
        t = mk64(tg + "T")
        prog.op("dve", lambda eng, ps=ps, t=t: eng.tensor_copy(t[:, :], ps[0:64, 0:64]),
                reads=[ps], writes=[t], tag=tg + "e")
        return t

    vdreT = tr64(vdre, tag + "x1")
    vdimT = tr64(vdim, tag + "x2")
    nvdimT = mk64(tag + "nvd")
    prog.op("dve", lambda eng: eng.tensor_scalar(nvdimT[:, :], vdimT[:, :], -1.0,
                                                 None, ALU.mult),
            reads=[vdimT], writes=[nvdimT], tag=tag + "ng")
    gts = {}
    for nm, l1, r1, l2, r2 in (("gre", vdreT, "wre", nvdimT, "wim"),
                               ("gim", vdreT, "wim", vdimT, "wre")):
        ps = ctx.pp.get()
        prog.op("pe", lambda eng, ps=ps, l1=l1, r1=r1: eng.matmul(
            ps[0:64, 0:64], l1[:, :], mats[r1][:, :], start=True, stop=False),
            reads=[l1, mats[r1]], writes=[ps], tag=tag + nm + "1")
        prog.op("pe", lambda eng, ps=ps, l2=l2, r2=r2: eng.matmul(
            ps[0:64, 0:64], l2[:, :], mats[r2][:, :], start=False, stop=True),
            reads=[l2, mats[r2]], writes=[ps], tag=tag + nm + "2")
        g = mk64(tag + nm)
        prog.op("dve", lambda eng, ps=ps, g=g: eng.tensor_copy(g[:, :], ps[0:64, 0:64]),
                reads=[ps], writes=[g], tag=tag + nm + "e")
        gts[nm] = g
    gTre = tr64(gts["gre"], tag + "g1")
    gTim = tr64(gts["gim"], tag + "g2")
    pTre = tr64(pre_, tag + "g3")
    pTim = tr64(pim_, tag + "g4")
    npTim = mk64(tag + "npt")
    prog.op("dve", lambda eng, npTim=npTim, pTim=pTim: eng.tensor_scalar(
        npTim[:, :], pTim[:, :], -1.0, None, ALU.mult),
            reads=[pTim], writes=[npTim], tag=tag + "np")

    def dup128(src_t, tg):
        t = ctx.sbuf(tg, (128, 64))
        for pb in (0, 64):
            prog.op("dve", lambda eng, pb=pb, t=t, src_t=src_t: eng.tensor_copy(
                t[pb:pb + 64, :], src_t[:, :]),
                reads=[src_t], writes=[t], tag=tg + "d")
        return t

    gTre = dup128(gTre, tag + "Gr")
    gTim = dup128(gTim, tag + "Gi")
    pTre = dup128(pTre, tag + "Pr")
    pTim = dup128(pTim, tag + "Pi")
    npTim = dup128(npTim, tag + "Pn")

    zre = [ctx.sbuf(tag + "zr", (128, 8 * SL)) for _ in range(2)]
    zim = [ctx.sbuf(tag + "zi", (128, 8 * SL)) for _ in range(2)]
    hbuf = [ctx.sbuf(tag + "h", (128, 8 * 2 * SL)) for _ in range(2)]
    xre_ = [ctx.sbuf(tag + "xr", (128, SL)) for _ in range(2)]
    xw_ = [ctx.sbuf(tag + "xw", (128, SL)) for _ in range(2)]

    def pair_loader(tpair, hb):
        def f(c, passno):
            key = (hb.name + "ld", c)
            for j in range(2):
                dma(prog, "sp", hb[:, c * 2 * SL + j * SL: c * 2 * SL + (j + 1) * SL],
                    xin[2 * tpair + j, c * 128:(c + 1) * 128, :],
                    reads=[(xin.tensor.name, 2 * tpair + j, c)],
                    writes=[key, (hb.name + "#h", c)], tag=tag + "hl")
            return hb[:, c * 2 * SL:(c + 1) * 2 * SL], key
        return f

    for tpair in range(8):
        hb = hbuf[tpair % 2]
        emit_ln_fm(ctx, pair_loader(tpair, hb), 2 * SL, NCH, hb,
                   hb.name + "#h", tag=tag + f"ln{tpair}", nsq=1)
        for j in range(2):
            t = 2 * tpair + j
            zo = zre[t % 2]
            zoi = zim[t % 2]
            zn = zre[(t + 1) % 2]
            zni = zim[(t + 1) % 2]
            for c in range(NCH):
                for half in range(2):
                    pb = half * 64
                    psr = ctx.pp.get()
                    psi = ctx.pp.get()
                    hsl = hb[pb:pb + 64, c * 2 * SL + j * SL: c * 2 * SL + (j + 1) * SL]
                    rds = [(hb.name + "#h", c)]
                    prog.op("pe", lambda eng, psr=psr, hsl=hsl, pb=pb, t=t: eng.matmul(
                        psr[0:64, 0:SL], gTre[pb:pb + 64, :], hsl, start=True,
                        stop=(t == 0)),
                        reads=rds + [gTre], writes=[psr], tag=tag + "c1")
                    prog.op("pe", lambda eng, psi=psi, hsl=hsl, pb=pb, t=t: eng.matmul(
                        psi[0:64, 0:SL], gTim[pb:pb + 64, :], hsl, start=True,
                        stop=(t == 0)),
                        reads=rds + [gTim], writes=[psi], tag=tag + "c2")
                    if t > 0:
                        zsl_r = zo[pb:pb + 64, c * SL:(c + 1) * SL]
                        zsl_i = zoi[pb:pb + 64, c * SL:(c + 1) * SL]
                        zrd = [(zo.name, c), (zoi.name, c)]
                        prog.op("pe", lambda eng, psr=psr, zsl_r=zsl_r, pb=pb: eng.matmul(
                            psr[0:64, 0:SL], pTre[pb:pb + 64, :], zsl_r, start=False,
                            stop=False),
                            reads=zrd + [pTre], writes=[psr], tag=tag + "z1")
                        prog.op("pe", lambda eng, psr=psr, zsl_i=zsl_i, pb=pb: eng.matmul(
                            psr[0:64, 0:SL], npTim[pb:pb + 64, :], zsl_i, start=False,
                            stop=True),
                            reads=zrd + [npTim], writes=[psr], tag=tag + "z2")
                        prog.op("pe", lambda eng, psi=psi, zsl_i=zsl_i, pb=pb: eng.matmul(
                            psi[0:64, 0:SL], pTre[pb:pb + 64, :], zsl_i, start=False,
                            stop=False),
                            reads=zrd + [pTre], writes=[psi], tag=tag + "z3")
                        prog.op("pe", lambda eng, psi=psi, zsl_r=zsl_r, pb=pb: eng.matmul(
                            psi[0:64, 0:SL], pTim[pb:pb + 64, :], zsl_r, start=False,
                            stop=True),
                            reads=zrd + [pTim], writes=[psi], tag=tag + "z4")
                    prog.op("dve", lambda eng, psr=psr, zn=zn, c=c, pb=pb: eng.tensor_copy(
                        zn[pb:pb + 64, c * SL:(c + 1) * SL], psr[0:64, 0:SL]),
                        reads=[psr], writes=[(zn.name, c)], tag=tag + "ze")
                    prog.op("dve", lambda eng, psi=psi, zni=zni, c=c, pb=pb: eng.tensor_copy(
                        zni[pb:pb + 64, c * SL:(c + 1) * SL], psi[0:64, 0:SL]),
                        reads=[psi], writes=[(zni.name, c)], tag=tag + "zi")
            for c in range(NCH):
                xr = xre_[c % 2]
                xw = xw_[c % 2]
                dma(prog, "sp", xr[:, :], xin[t, c * 128:(c + 1) * 128, :],
                    reads=[(xin.tensor.name, t, c)], writes=[xr.name],
                    tag=tag + "rx")
                prog.op("dve", lambda eng, xr=xr, xw=xw, zn=zn, c=c: eng.scalar_tensor_tensor(
                    xw[:, :], zn[:, c * SL:(c + 1) * SL], 2.0, xr[:, :],
                    ALU.mult, ALU.add),
                    reads=[(zn.name, c), xr.name], writes=[xw.name], tag=tag + "ot")
                dma(prog, "sp", xout[t, c * 128:(c + 1) * 128, :], xw[:, :],
                    reads=[xw.name], writes=[(xout.tensor.name, t, c)],
                    tag=tag + "sx")
    ctx.es = old
    prog.barrier()
    es.close()


def emit_m2core(ctx, b, xin, xout, W_in_d, convw_d, convb_d, Alog_d, Dm_d,
                dtb_d, mnw_d, Wout_d, scr, tag="mc"):
    nc, prog = ctx.nc, ctx.prog
    from contextlib import ExitStack
    T = SEQ
    NS = T // NB
    NCH = 8
    DI = 2048
    NH = 32
    old = ctx.es
    es0 = ExitStack()
    ctx.es = es0

    dtr = ctx.sbuf(tag + "dt", (32, T))
    esA = ExitStack()
    ctx.es = esA
    h_bf = ctx.sbuf(tag + "h", (128, NCH * T), BF16)
    esA1 = ExitStack()
    ctx.es = esA1
    loader = dram_chunk_loader(
        ctx, lambda c: (xin[b, c * 128:(c + 1) * 128, :], (xin.tensor.name, b, c)),
        T, tag=tag + "xl", nbuf=1)
    emit_ln_fm(ctx, loader, T, NCH, h_bf, tag + "_hk", tag=tag + f"ln{b}", nsq=1,
               rms=True, eps=1e-6)
    prog.barrier()
    esA1.close()
    esA2 = ExitStack()
    ctx.es = esA2
    w_in = ctx.load_w_bf16(W_in_d, DM, 4384, tag=tag + "wi")
    ev = [ctx.sbuf(tag + "ev", (128, NB), BF16) for _ in range(2)]
    evf = [ctx.sbuf(tag + "evf", (128, NB)) for _ in range(2)]

    def evac_zx(info, m0, m1, s, ps):
        kind, r = info
        if kind == "z":
            e = ev[(r * NS + s) % 2]
            prog.op("dve", lambda eng, ps=ps, e=e: eng.tensor_copy(
                e[:, :], ps[0:128, 0:NB]), reads=[ps], writes=[e.name],
                tag=tag + "ez")
            dma(prog, "sp", scr["zD"][b, r * 128:(r + 1) * 128, s * NB:(s + 1) * NB],
                e[:, :], reads=[e.name], writes=[("zD", b, r)], tag=tag + "sz")
        elif kind == "x":
            e = evf[(r * NS + s) % 2]
            prog.op("dve", lambda eng, ps=ps, e=e: eng.tensor_copy(
                e[:, :], ps[0:128, 0:NB]), reads=[ps], writes=[e.name],
                tag=tag + "ex")
            dma(prog, "sp", scr["xbcD"][b, r * 128:(r + 1) * 128, s * NB:(s + 1) * NB],
                e[:, :], reads=[e.name], writes=[("xbcD", b, r)], tag=tag + "sx")
        else:
            prog.op("dve", lambda eng, ps=ps, s=s: eng.tensor_copy(
                dtr[:, s * NB:(s + 1) * NB], ps[0:32, 0:NB]),
                reads=[ps], writes=[dtr], tag=tag + "ed")

    mt = [(r * 128, (r + 1) * 128, ("z", r)) for r in range(16)] + \
         [(2048 + r * 128, 2048 + (r + 1) * 128, ("x", r)) for r in range(18)] + \
         [(4352, 4384, ("d", 0))]
    emit_proj_tiles(ctx, w_in, 4384, h_bf, tag + "_hk", NCH, T, mt, evac_zx,
                    tag=tag + "pj")
    prog.barrier()
    esA2.close()
    ctx.es = esA
    prog.barrier()
    esA.close()
    ctx.es = es0

    Bfb = ctx.sbuf(tag + "Bfb", (128, T), BF16)
    Cfb = ctx.sbuf(tag + "Cfb", (128, T), BF16)
    y_bf = ctx.sbuf(tag + "ybf", (128, 16 * T), BF16)
    esM = ExitStack()
    ctx.es = esM
    dtbc = ctx.feat_cols(dtb_d, 32, tag=tag + "db")
    acol = ctx.feat_cols(Alog_d, 32, tag=tag + "al")
    prog.op("act", lambda eng: eng.activation(acol[:, :], acol[:, :], AF.Exp),
            reads=[acol], writes=[acol], tag=tag + "ae")
    prog.op("dve", lambda eng: eng.tensor_scalar(dtr[:, :], dtr[:, :], dtbc[:, 0:1],
                                                 None, ALU.add),
            reads=[dtr, dtbc], writes=[dtr], tag=tag + "d1")
    prog.op("act", lambda eng: eng.activation(dtr[:, :], dtr[:, :], AF.Exp),
            reads=[dtr], writes=[dtr], tag=tag + "d2")
    prog.op("act", lambda eng: eng.activation(dtr[:, :], dtr[:, :], AF.Ln,
                                              bias=scal_ap(ctx, 1.0, 32)),
            reads=[dtr, "c_scal"], writes=[dtr], tag=tag + "d3")
    la = ctx.sbuf(tag + "la", (32, T))
    prog.op("dve", lambda eng: eng.tensor_scalar(la[:, :], dtr[:, :], acol[:, 0:1],
                                                 None, ALU.mult),
            reads=[dtr, acol], writes=[la], tag=tag + "la")
    prog.op("dve", lambda eng: eng.tensor_scalar(la[:, :], la[:, :], -1.0, None,
                                                 ALU.mult),
            reads=[la], writes=[la], tag=tag + "lan")
    ones32 = ctx.sbuf(tag + "o32", (32, 128))
    prog.op("dve", lambda eng: eng.memset(ones32[:, :], 1.0),
            reads=[], writes=[ones32], tag=tag + "o1")
    cs = ctx.sbuf(tag + "cs", (32, T))
    for tc in range(16):
        prog.op("dve", lambda eng, tc=tc: eng.tensor_tensor_scan(
            cs[:, tc * 128:(tc + 1) * 128], ones32[:, :],
            la[:, tc * 128:(tc + 1) * 128], 0.0, ALU.mult, ALU.add),
            reads=[ones32, la], writes=[cs], tag=tag + "cum")
    biasr = ctx.sbuf(tag + "bi", (32, T))
    prog.op("act", lambda eng: eng.activation(biasr[:, :], dtr[:, :], AF.Ln),
            reads=[dtr], writes=[biasr], tag=tag + "ld")
    prog.op("dve", lambda eng: eng.tensor_tensor(biasr[:, :], biasr[:, :], cs[:, :],
                                                 ALU.subtract),
            reads=[biasr, cs], writes=[biasr], tag=tag + "b2")
    biasT = ctx.sbuf(tag + "bT", (128, 16 * 32))
    biasT2 = ctx.sbuf(tag + "bT2", (128, 16 * 32))
    b2row = ctx.sbuf(tag + "b2r", (32, 128))
    for tc in range(16):
        prog.op("dve", lambda eng, tc=tc: eng.tensor_scalar(
            b2row[:, :], biasr[:, tc * 128:(tc + 1) * 128],
            cs[:, tc * 128 + 127: tc * 128 + 128], None, ALU.add),
            reads=[biasr, cs], writes=[b2row], tag=tag + "b2a")
        for src_, dst in ((biasr, biasT), (b2row, biasT2)):
            ps = ctx.pp.get()
            if src_ is biasr:
                prog.op("pe", lambda eng, tc=tc, ps=ps, src_=src_: eng.transpose(
                    ps[0:128, 0:32], src_[:, tc * 128:(tc + 1) * 128],
                    ctx.c_ident[0:32, 0:32]),
                    reads=[src_, "c_ident"], writes=[ps], tag=tag + "bt")
            else:
                prog.op("pe", lambda eng, ps=ps, src_=src_: eng.transpose(
                    ps[0:128, 0:32], src_[:, :],
                    ctx.c_ident[0:32, 0:32]),
                    reads=[src_, "c_ident"], writes=[ps], tag=tag + "bt2")
            prog.op("dve", lambda eng, tc=tc, ps=ps, dst=dst: eng.tensor_copy(
                dst[:, tc * 32:(tc + 1) * 32], ps[0:128, 0:32]),
                reads=[ps], writes=[dst], tag=tag + "be")
    ebias = ctx.sbuf(tag + "eb", (128, 16 * 32))
    prog.op("act", lambda eng: eng.activation(ebias[:, :], biasT2[:, :], AF.Exp),
            reads=[biasT2], writes=[ebias], tag=tag + "ebx")
    gam = ctx.sbuf(tag + "gm", (32, 16))
    for tc in range(16):
        prog.op("dve", lambda eng, tc=tc: eng.tensor_copy(
            gam[:, tc:tc + 1], cs[:, tc * 128 + 127: tc * 128 + 128]),
            reads=[cs], writes=[gam], tag=tag + "g1")
    prog.op("act", lambda eng: eng.activation(gam[:, :], gam[:, :], AF.Exp),
            reads=[gam], writes=[gam], tag=tag + "g2")
    gamT = ctx.sbuf(tag + "gT", (128, 16 * 32))
    psg = ctx.pp.get()
    prog.op("pe", lambda eng: eng.transpose(psg[0:16, 0:32], gam[:, :],
                                            ctx.c_ident[0:32, 0:32]),
            reads=[gam, "c_ident"], writes=[psg], tag=tag + "gt2")
    gamr = ctx.sbuf(tag + "gr", (16, 32))
    prog.op("dve", lambda eng: eng.tensor_copy(gamr[:, :], psg[0:16, 0:32]),
            reads=[psg], writes=[gamr], tag=tag + "ge")
    for tc in range(16):
        ps = ctx.pp.get()
        prog.op("pe", lambda eng, tc=tc, ps=ps: eng.matmul(
            ps[0:128, 0:32], ctx.c_sel16[:, tc * 128: tc * 128 + 128],
            gamr[0:16, 0:32], start=True, stop=True),
            reads=[gamr, "c_sel16"], writes=[ps], tag=tag + "gb")
        prog.op("dve", lambda eng, tc=tc, ps=ps: eng.tensor_copy(
            gamT[:, tc * 32:(tc + 1) * 32], ps[0:128, 0:32]),
            reads=[ps], writes=[gamT], tag=tag + "gbe")

    esB = ExitStack()
    ctx.es = esB
    cw = [ctx.feat_cols(convw_d[k, :], 2304, tag=tag + f"cw{k}") for k in range(4)]
    cb = ctx.feat_cols(convb_d, 2304, tag=tag + "cb")
    Bf = ctx.sbuf(tag + "Bf", (128, T))
    Cf = ctx.sbuf(tag + "Cf", (128, T))
    xcb = [ctx.sbuf(tag + "xc", (128, T)) for _ in range(2)]
    ycb = [ctx.sbuf(tag + "yc", (128, T)) for _ in range(2)]
    ycbb = [ctx.sbuf(tag + "yb", (128, T), BF16) for _ in range(2)]
    for r in range(18):
        xc = xcb[r % 2]
        yc = ycb[r % 2]
        dma(prog, "sp", xc[:, :], scr["xbcD"][b, r * 128:(r + 1) * 128, :],
            reads=[("xbcD", b, r)], writes=[xc.name], tag=tag + "lc")
        prog.op("dve", lambda eng, r=r, xc=xc, yc=yc: eng.tensor_scalar(
            yc[:, :], xc[:, :], cw[3][:, r:r + 1], cb[:, r:r + 1], ALU.mult,
            ALU.add),
            reads=[xc.name, cw[3], cb], writes=[yc.name], tag=tag + "c3")
        for k in range(3):
            sh = 3 - k
            prog.op("dve", lambda eng, r=r, k=k, sh=sh, xc=xc, yc=yc: eng.scalar_tensor_tensor(
                yc[:, sh:T], xc[:, 0:T - sh], cw[k][:, r:r + 1], yc[:, sh:T],
                ALU.mult, ALU.add),
                reads=[xc.name, cw[k], yc.name], writes=[yc.name], tag=tag + "ck")
        if r < 16:
            yb = ycbb[r % 2]
            prog.op("act", lambda eng, yc=yc, yb=yb: eng.activation(
                yb[:, :], yc[:, :], AF.Silu),
                reads=[yc.name], writes=[yb.name], tag=tag + "si")
            dma(prog, "sp", scr["xmD"][b, r * 128:(r + 1) * 128, :], yb[:, :],
                reads=[yb.name], writes=[("xmD", b, r)], tag=tag + "sm")
        else:
            dst = Bf if r == 16 else Cf
            prog.op("act", lambda eng, yc=yc, dst=dst: eng.activation(
                dst[:, :], yc[:, :], AF.Silu),
                reads=[yc.name], writes=[dst], tag=tag + "sB")
    prog.op("dve", lambda eng: eng.tensor_copy(Bfb[:, :], Bf[:, :]),
            reads=[Bf], writes=[Bfb], tag=tag + "bc1")
    prog.op("dve", lambda eng: eng.tensor_copy(Cfb[:, :], Cf[:, :]),
            reads=[Cf], writes=[Cfb], tag=tag + "bc2")

    dma(prog, "sp", scr["BD"][b, :, :], Bfb[:, :], reads=[Bfb], writes=[("BD", b)],
        tag=tag + "sB2")
    dma(prog, "sp", scr["CD"][b, :, :], Cfb[:, :], reads=[Cfb], writes=[("CD", b)],
        tag=tag + "sC2")
    prog.barrier()
    esB.close()

    esC = ExitStack()
    ctx.es = esC
    S_f = ctx.sbuf(tag + "S", (128, NH * 64))
    S_b = ctx.sbuf(tag + "Sb", (128, NH * 64), BF16)
    prog.op("dve", lambda eng: eng.memset(S_f[:, :], 0.0), reads=[], writes=[S_f],
            tag=tag + "s0")
    prog.op("dve", lambda eng: eng.memset(S_b[:, :], 0.0), reads=[], writes=[S_b],
            tag=tag + "s0b")
    Xt = [ctx.sbuf(tag + "Xt", (128, DI), BF16) for _ in range(2)]
    Bt_ = [ctx.sbuf(tag + "Bt", (128, 128), BF16) for _ in range(2)]
    NROT = 3
    CSb_r = [ctx.sbuf(tag + "CSb", (128, 128)) for _ in range(NROT)]
    Wm_r = [ctx.sbuf(tag + "Wm", (128, 128), BF16) for _ in range(NROT)]
    CDt_r = [ctx.sbuf(tag + "CD", (128, 128), BF16) for _ in range(NROT)]
    Bd_r = [ctx.sbuf(tag + "Bd", (128, 128), BF16) for _ in range(NROT)]
    dec3_r = [ctx.sbuf(tag + "d3", (128, 128), BF16) for _ in range(NROT)]
    sup = [ctx.sbuf(tag + "su", (128, 64)) for _ in range(3)]
    for tc in range(16):
        Xtile = Xt[tc % 2]
        Btile = Bt_[tc % 2]
        dma(prog, "sp", Xtile[:, :],
            scr["xmD"][b, :, tc * 128:(tc + 1) * 128], transpose=True,
            reads=[("xmD", b, r) for r in range(16)], writes=[Xtile.name],
            tag=tag + "lxT")
        dma(prog, "sp", Btile[:, :],
            scr["BD"][b, :, tc * 128:(tc + 1) * 128], transpose=True,
            reads=[("BD", b)], writes=[Btile.name], tag=tag + "lbT")
        psG = ctx.pp.tiles[3]
        prog.op("pe", lambda eng, tc=tc, psG=psG: eng.matmul(
            psG[0:128, 0:128], Bfb[:, tc * 128:(tc + 1) * 128],
            Cfb[:, tc * 128:(tc + 1) * 128], start=True, stop=True),
            reads=[Bfb, Cfb], writes=[psG], tag=tag + "gg2")
        def stageA(h, tc=tc):
            CSb = CSb_r[h % 3]
            Wm = Wm_r[h % 3]
            CDt = CDt_r[h % 3]
            Bd = Bd_r[h % 3]
            dec3 = dec3_r[h % 3]
            Btile = Bt_[tc % 2]
            psc = ctx.pp.tiles[6 + (h % 2)]
            prog.op("pe", lambda eng, tc=tc, h=h, psc=psc: eng.matmul(
                psc[0:128, 0:128], ctx.c_sel32[:, h * 128:(h + 1) * 128],
                cs[0:32, tc * 128:(tc + 1) * 128], start=True, stop=True),
                reads=[cs, "c_sel32"], writes=[psc], tag=tag + "cb2")
            prog.op("dve", lambda eng, psc=psc, CSb=CSb: eng.tensor_tensor(
                CSb[:, :], psc[0:128, 0:128], ctx.c_trimask, ALU.add),
                reads=[psc, "c_trimask"], writes=[CSb], tag=tag + "msk")
            prog.op("act", lambda eng, tc=tc, h=h, Wm=Wm, CSb=CSb: eng.activation(
                Wm[:, :], CSb[:, :], AF.Exp, bias=biasT[:, tc * 32 + h: tc * 32 + h + 1]),
                reads=[CSb, biasT], writes=[Wm], tag=tag + "wx")
            prog.op("act", lambda eng, psc=psc, dec3=dec3: eng.activation(
                dec3[:, :], psc[0:128, 0:128], AF.Exp),
                reads=[psc], writes=[dec3], tag=tag + "dc3")
            prog.op("dve", lambda eng, tc=tc, h=h, Btile=Btile, Bd=Bd: eng.tensor_scalar(
                Bd[:, :], Btile[:, :], ebias[:, tc * 32 + h: tc * 32 + h + 1],
                None, ALU.mult),
                reads=[Btile.name, ebias], writes=[Bd], tag=tag + "bd")
            prog.op("dve", lambda eng, psG=psG, Wm=Wm: eng.tensor_tensor(
                Wm[:, :], Wm[:, :], psG[0:128, 0:128], ALU.mult),
                reads=[Wm, psG], writes=[Wm], tag=tag + "mG")
            prog.op("dve", lambda eng, tc=tc, CDt=CDt, dec3=dec3: eng.tensor_tensor(
                CDt[:, :], Cfb[:, tc * 128:(tc + 1) * 128], dec3[:, :], ALU.mult),
                reads=[Cfb, dec3], writes=[CDt], tag=tag + "cd")

        def stageB(h, tc=tc):
            Wm = Wm_r[h % 3]
            CDt = CDt_r[h % 3]
            Bd = Bd_r[h % 3]
            Xtile = Xt[tc % 2]
            psY = ctx.pp.tiles[5 if h % 2 == 0 else 0]
            prog.op("pe", lambda eng, h=h, psY=psY, Xtile=Xtile, Wm=Wm: eng.matmul(
                psY[0:64, 0:128], Xtile[:, h * 64:(h + 1) * 64], Wm[:, :],
                start=True, stop=False),
                reads=[Xtile.name, Wm], writes=[psY], tag=tag + "yi")
            prog.op("pe", lambda eng, h=h, psY=psY, CDt=CDt: eng.matmul(
                psY[0:64, 0:128], S_b[:, h * 64:(h + 1) * 64], CDt[:, :],
                start=False, stop=True),
                reads=[(S_b.name, h), CDt], writes=[psY], tag=tag + "ycr")
            psS = ctx.pp.tiles[4 if h % 2 == 0 else 1]
            prog.op("pe", lambda eng, h=h, psS=psS, Xtile=Xtile, Bd=Bd: eng.matmul(
                psS[0:128, 0:64], Bd[:, :], Xtile[:, h * 64:(h + 1) * 64],
                start=True, stop=True),
                reads=[Bd, Xtile.name], writes=[psS], tag=tag + "su")
            s_ = sup[h % 3]
            prog.op("dve", lambda eng, tc=tc, h=h, psS=psS, s_=s_: eng.scalar_tensor_tensor(
                s_[:, :], S_f[:, h * 64:(h + 1) * 64],
                gamT[:, tc * 32 + h: tc * 32 + h + 1], psS[0:128, 0:64],
                ALU.mult, ALU.add),
                reads=[psS, (S_f.name, h), gamT], writes=[s_.name], tag=tag + "sa")
            prog.op("pool", lambda eng, h=h, s_=s_: eng.tensor_copy(
                S_f[:, h * 64:(h + 1) * 64], s_[:, :]),
                reads=[s_.name], writes=[(S_f.name, h)], tag=tag + "sg")
            prog.op("act", lambda eng, h=h: eng.activation(
                S_b[:, h * 64:(h + 1) * 64], S_f[:, h * 64:(h + 1) * 64], AF.Copy),
                reads=[(S_f.name, h)], writes=[(S_b.name, h)], tag=tag + "sb")
            hc = h // 2
            pb = (h % 2) * 64
            prog.op("act", lambda eng, psY=psY, hc=hc, pb=pb, tc=tc: eng.activation(
                y_bf[pb:pb + 64, hc * T + tc * 128: hc * T + (tc + 1) * 128],
                psY[0:64, 0:128], AF.Copy),
                reads=[psY], writes=[(y_bf.name, hc)], tag=tag + "ye")

        stageA(0)
        for h in range(NH):
            if h + 1 < NH:
                stageA(h + 1)
            stageB(h)
    prog.barrier()
    esC.close()
    esM.close()
    ctx.es = es0

    esD = ExitStack()
    ctx.es = esD
    dmrow = load_row(ctx, Dm_d, 32, tag=tag + "dm")
    psd = ctx.pp.get()
    prog.op("pe", lambda eng: eng.transpose(psd[0:32, 0:1], dmrow[0:1, 0:32],
                                            ctx.c_ident[0:1, 0:1]),
            reads=[dmrow, "c_ident"], writes=[psd], tag=tag + "dmt")
    dmcol = ctx.sbuf(tag + "dmo", (32, 1))
    prog.op("dve", lambda eng: eng.tensor_copy(dmcol[:, :], psd[0:32, 0:1]),
            reads=[psd], writes=[dmcol], tag=tag + "dme")
    r1 = ctx.sbuf(tag + "r1", (32, 16))
    r2 = ctx.sbuf(tag + "r2", (32, 16))
    prog.op("dve", lambda eng: eng.tensor_scalar(r1[:, :], ctx.c_dm_m1[0:32, :],
                                                 dmcol[:, 0:1], None, ALU.mult),
            reads=["c_dm_m1", dmcol], writes=[r1], tag=tag + "r1")
    prog.op("dve", lambda eng: eng.tensor_scalar(r2[:, :], ctx.c_dm_m2[0:32, :],
                                                 dmcol[:, 0:1], None, ALU.mult),
            reads=["c_dm_m2", dmcol], writes=[r2], tag=tag + "r2")
    psd2 = ctx.pp.get()
    prog.op("pe", lambda eng: eng.matmul(psd2[0:128, 0:16], ctx.c_dm_l1,
                                         r1[:, :], start=True, stop=False),
            reads=["c_dm_l1", r1], writes=[psd2], tag=tag + "dm1")
    prog.op("pe", lambda eng: eng.matmul(psd2[0:128, 0:16], ctx.c_dm_l2,
                                         r2[:, :], start=False, stop=True),
            reads=["c_dm_l2", r2], writes=[psd2], tag=tag + "dm2")
    dmdim = ctx.sbuf(tag + "dmd", (128, 16))
    prog.op("dve", lambda eng: eng.tensor_copy(dmdim[:, :], psd2[0:128, 0:16]),
            reads=[psd2], writes=[dmdim], tag=tag + "dm3")
    mnc = ctx.feat_cols(mnw_d, DI, tag=tag + "mn")
    zcb = [ctx.sbuf(tag + "zc", (128, T), BF16)] * 2
    xmc = [ctx.sbuf(tag + "xm", (128, T), BF16)] * 2
    y2c = [ctx.sbuf(tag + "y2", (128, T))] * 2
    sqt = [ctx.sbuf(tag + "sq", (128, T))] * 2
    y2b = y_bf
    ps_q = [ctx.pp.get() for _ in range(4)]

    def compute_y2(c, zc, xc, y2):
        dma(prog, "sp", zc[:, :], scr["zD"][b, c * 128:(c + 1) * 128, :],
            reads=[("zD", b, c)], writes=[zc.name], tag=tag + "lz")
        dma(prog, "pool", xc[:, :], scr["xmD"][b, c * 128:(c + 1) * 128, :],
            reads=[("xmD", b, c)], writes=[xc.name], tag=tag + "lm")
        prog.op("act", lambda eng, zc=zc: eng.activation(zc[:, :], zc[:, :],
                                                         AF.Silu),
                reads=[zc.name], writes=[zc.name], tag=tag + "sz2")
        prog.op("dve", lambda eng, c=c, xc=xc, y2=y2: eng.tensor_scalar(
            y2[:, :], xc[:, :], dmdim[:, c:c + 1], None, ALU.mult),
            reads=[xc.name, dmdim], writes=[y2.name], tag=tag + "dx")
        prog.op("dve", lambda eng, c=c, y2=y2: eng.tensor_tensor(
            y2[:, :], y2[:, :], y_bf[:, c * T:(c + 1) * T], ALU.add),
            reads=[y2.name, (y_bf.name, c)], writes=[y2.name], tag=tag + "ya")
        prog.op("dve", lambda eng, y2=y2, zc=zc: eng.tensor_tensor(
            y2[:, :], y2[:, :], zc[:, :], ALU.mult),
            reads=[y2.name, zc.name], writes=[y2.name], tag=tag + "yz")

    for c in range(16):
        zc = zcb[c % 2]
        xc = xmc[c % 2]
        y2 = y2c[c % 2]
        s_ = sqt[c % 2]
        compute_y2(c, zc, xc, y2)
        prog.op("dve", lambda eng, c=c, y2=y2: eng.tensor_copy(
            y2b[:, c * T:(c + 1) * T], y2[:, :]),
            reads=[y2.name], writes=[(y2b.name, c)], tag=tag + "yb2")
        prog.op("act", lambda eng, y2=y2, s_=s_: eng.activation(
            s_[:, :], y2[:, :], AF.Square),
            reads=[y2.name], writes=[s_.name], tag=tag + "q1")
        for s in range(NS):
            prog.op("pe", lambda eng, c=c, s=s, s_=s_: eng.matmul(
                ps_q[s][0:1, 0:NB], ctx.c_ones_red, s_[:, s * NB:(s + 1) * NB],
                start=(c == 0), stop=(c == 15)),
                reads=[s_.name, "c_ones_red"], writes=[ps_q[s]], tag=tag + "q2")
    rrow = ctx.sbuf(tag + "rr", (1, T))
    for s in range(NS):
        prog.op("dve", lambda eng, s=s: eng.tensor_copy(
            rrow[0:1, s * NB:(s + 1) * NB], ps_q[s][0:1, 0:NB]),
            reads=[ps_q[s]], writes=[rrow], tag=tag + "re")
    prog.op("dve", lambda eng: eng.tensor_scalar(rrow[:, :], rrow[:, :], 1.0 / DI,
                                                 None, ALU.mult),
            reads=[rrow], writes=[rrow], tag=tag + "rv")
    prog.op("act", lambda eng: eng.activation(rrow[:, :], rrow[:, :], AF.Ln,
                                              bias=scal_ap(ctx, 1e-5, 1)),
            reads=[rrow, "c_scal"], writes=[rrow], tag=tag + "rl")
    prog.op("act", lambda eng: eng.activation(rrow[:, :], rrow[:, :], AF.Exp,
                                              scale=-0.5),
            reads=[rrow], writes=[rrow], tag=tag + "rx")
    for s in range(NS):
        pR = ctx.pp.get()
        bcast_row(ctx.prog, ctx.c_ones_bc, rrow, pR, s * NB, (s + 1) * NB,
                  rrow.name, pR.name, tag=tag + "rb")
        for c in range(16):
            lo = c * T + s * NB
            hi = c * T + (s + 1) * NB
            prog.op("dve", lambda eng, lo=lo, hi=hi, pR=pR, c=c: eng.tensor_tensor(
                sqt[0][:, 0:NB], y2b[:, lo:hi], pR[:, 0:NB], ALU.mult),
                reads=[(y2b.name, c), pR], writes=[sqt[0]], tag=tag + "ap1")
            prog.op("dve", lambda eng, lo=lo, hi=hi, c=c: eng.tensor_scalar(
                y2b[:, lo:hi], sqt[0][:, 0:NB], mnc[:, c:c + 1], None, ALU.mult),
                reads=[sqt[0], mnc], writes=[(y2b.name, c)], tag=tag + "ap2")
    prog.barrier()
    esD.close()
    emit_out_proj(ctx, b, Wout_d, None, y2b, y2b.name, xin, xout, 16, T,
                  tag=tag + "o")
    ctx.es = old
    prog.barrier()
    es0.close()


def emit_m2attn(ctx, b, xin, xout, Wqkv_d, Wcp_d, qcw_d, qcb_d, kcw_d, kcb_d,
                vcw_d, vcb_d, tag="ma"):
    nc, prog = ctx.nc, ctx.prog
    from contextlib import ExitStack
    T = SEQ
    NS = T // NB
    NCH = 8
    NH = 16
    old = ctx.es
    es0 = ExitStack()
    ctx.es = es0
    qf = ctx.sbuf(tag + "q", (128, NCH * T), BF16)
    kf = ctx.sbuf(tag + "k", (64, T))
    vf = ctx.sbuf(tag + "v", (64, T))
    yt = ctx.sbuf(tag + "y", (128, NCH * T), BF16)
    convcols = {}
    for nm, wd, bd in (("q", qcw_d, qcb_d), ("k", kcw_d, kcb_d), ("v", vcw_d, vcb_d)):
        cols = []
        for k in range(3):
            c64 = ctx.feat_cols(wd[k, :], 64, tag=tag + f"cw{nm}{k}")
            c128 = ctx.sbuf(tag + f"cw2{nm}{k}", (128, 1))
            for pb in (0, 64):
                prog.op("dve", lambda eng, pb=pb, c64=c64, c128=c128: eng.tensor_copy(
                    c128[pb:pb + 64, :], c64[:, :]),
                    reads=[c64], writes=[c128], tag=tag + "cd")
            cols.append(c128)
        b64 = ctx.feat_cols(bd, 64, tag=tag + f"cb{nm}")
        b128 = ctx.sbuf(tag + f"cb2{nm}", (128, 1))
        for pb in (0, 64):
            prog.op("dve", lambda eng, pb=pb, b64=b64, b128=b128: eng.tensor_copy(
                b128[pb:pb + 64, :], b64[:, :]),
                reads=[b64], writes=[b128], tag=tag + "cd2")
        convcols[nm] = (cols, b128)

    es1 = ExitStack()
    ctx.es = es1
    h_bf = ctx.sbuf(tag + "h", (128, NCH * T), BF16)
    es1b = ExitStack()
    ctx.es = es1b
    loader = dram_chunk_loader(
        ctx, lambda c: (xin[b, c * 128:(c + 1) * 128, :], (xin.tensor.name, b, c)),
        T, tag=tag + "xl", nbuf=1)
    emit_ln_fm(ctx, loader, T, NCH, h_bf, tag + "_hk", tag=tag + f"ln{b}", nsq=1,
               rms=True, eps=1e-6)
    prog.barrier()
    es1b.close()
    es2 = ExitStack()
    ctx.es = es2
    wq = ctx.load_w_bf16(Wqkv_d, DM, 1152, tag=tag + "wq")
    cvd = [ctx.sbuf(tag + "cvd", (128, T)) for _ in range(2)]
    qcv = ctx.sbuf(tag + "qcv", (128, T))
    kvraw = ctx.sbuf(tag + "kvr", (128, T))
    kvc = qcv

    def evac_q(info, m0, m1, s, ps):
        r = info
        if r < 8:
            xc = cvd[r % 2]
            prog.op("dve", lambda eng, s=s, ps=ps, xc=xc: eng.tensor_copy(
                xc[:, s * NB:(s + 1) * NB], ps[0:128, 0:NB]),
                reads=[ps], writes=[xc.name], tag=tag + "eq")
            if s == NS - 1:
                cols, bcol = convcols["q"]
                prog.op("dve", lambda eng, xc=xc: eng.tensor_scalar(
                    qcv[:, :], xc[:, :], cols[2][:, 0:1], bcol[:, 0:1], ALU.mult,
                    ALU.add),
                    reads=[xc.name, cols[2], bcol], writes=[qcv.name],
                    tag=tag + "q3")
                for k in range(2):
                    sh = 2 - k
                    prog.op("dve", lambda eng, k=k, sh=sh, xc=xc: eng.scalar_tensor_tensor(
                        qcv[:, sh:T], xc[:, 0:T - sh],
                        cols[k][:, 0:1], qcv[:, sh:T], ALU.mult, ALU.add),
                        reads=[xc.name, cols[k], qcv.name],
                        writes=[qcv.name], tag=tag + "qk")
                prog.op("dve", lambda eng, r=r: eng.tensor_copy(
                    qf[:, r * T:(r + 1) * T], qcv[:, :]),
                    reads=[qcv.name], writes=[(qf.name, r)], tag=tag + "qf")
        else:
            pb = (r - 8) * 64
            prog.op("dve", lambda eng, pb=pb, s=s, ps=ps: eng.tensor_copy(
                kvraw[pb:pb + 64, s * NB:(s + 1) * NB], ps[0:64, 0:NB]),
                reads=[ps], writes=[(kvraw.name, pb)], tag=tag + "ekv")

    mt = [(r * 128, (r + 1) * 128, r) for r in range(8)] + \
         [(1024, 1088, 8), (1088, 1152, 9)]
    emit_proj_tiles(ctx, wq, 1152, h_bf, tag + "_hk", NCH, T, mt, evac_q,
                    tag=tag + "pq")
    for nm, pb, dst in (("k", 0, kf), ("v", 64, vf)):
        cols, bcol = convcols[nm]
        prog.op("dve", lambda eng, pb=pb, cols=cols, bcol=bcol: eng.tensor_scalar(
            kvc[pb:pb + 64, :], kvraw[pb:pb + 64, :], cols[2][pb:pb + 64, 0:1],
            bcol[pb:pb + 64, 0:1], ALU.mult, ALU.add),
            reads=[(kvraw.name, pb), cols[2], bcol], writes=[kvc.name],
            tag=tag + nm + "3")
        for k in range(2):
            sh = 2 - k
            prog.op("dve", lambda eng, pb=pb, k=k, sh=sh, cols=cols: eng.scalar_tensor_tensor(
                kvc[pb:pb + 64, sh:T], kvraw[pb:pb + 64, 0:T - sh],
                cols[k][pb:pb + 64, 0:1], kvc[pb:pb + 64, sh:T], ALU.mult, ALU.add),
                reads=[(kvraw.name, pb), cols[k], kvc.name],
                writes=[kvc.name], tag=tag + nm + "k")
        prog.op("dve", lambda eng, pb=pb, dst=dst: eng.tensor_copy(
            dst[:, :], kvc[pb:pb + 64, :]),
            reads=[kvc.name], writes=[dst], tag=tag + nm + "f")
    prog.barrier()
    es2.close()
    es1.close()

    es3 = ExitStack()
    ctx.es = es3
    kfb = ctx.sbuf(tag + "kfb", (128, T), BF16)
    for pb in (0, 64):
        prog.op("dve", lambda eng, pb=pb: eng.tensor_copy(
            kfb[pb:pb + 64, :], kf[:, :]),
            reads=[kf], writes=[kfb], tag=tag + "kb")
    vfb = ctx.sbuf(tag + "vfb", (64, T), BF16)
    prog.op("dve", lambda eng: eng.tensor_copy(vfb[:, :], vf[:, :]),
            reads=[vf], writes=[vfb], tag=tag + "vb")
    vtok = ctx.sbuf(tag + "vt", (128, 16 * 64), BF16)
    for tt in range(16):
        ps = ctx.pp.get()
        prog.op("pe", lambda eng, tt=tt, ps=ps: eng.transpose(
            ps[0:128, 0:64], vf[0:64, tt * 128:(tt + 1) * 128],
            ctx.c_ident[0:64, 0:64]),
            reads=[vf, "c_ident"], writes=[ps], tag=tag + "vt")
        prog.op("dve", lambda eng, tt=tt, ps=ps: eng.tensor_copy(
            vtok[:, tt * 64:(tt + 1) * 64], ps[0:128, 0:64]),
            reads=[ps], writes=[(vtok.name, tt)], tag=tag + "ve")
    Pt = ctx.sbuf(tag + "P", (128, 16 * NB), BF16)
    srow = ctx.sbuf(tag + "sr", (1, NB))
    rrow = ctx.sbuf(tag + "rr", (1, NB))
    yev = ctx.sbuf(tag + "ye", (64, NB))
    msks = ctx.sbuf(tag + "mk", (128, NB))
    qk_tiles = ctx.pp.tiles[0:5]
    t_sum = ctx.pp.tiles[5]
    t_r = ctx.pp.tiles[6]
    t_y = ctx.pp.tiles[7]
    qkc = [0]
    for h in range(NH):
        hb = (h % 2) * 64
        hc = h // 2
        for qb in range(NS):
            nkt = 4 * qb + 4
            for kt in range(nkt):
                ps = qk_tiles[qkc[0] % 5]
                qkc[0] += 1
                prog.op("pe", lambda eng, h=h, qb=qb, kt=kt, ps=ps, hb=hb, hc=hc: eng.matmul(
                    ps[0:128, 0:NB],
                    kfb[hb:hb + 64, kt * 128:(kt + 1) * 128],
                    qf[hb:hb + 64, hc * T + qb * NB: hc * T + (qb + 1) * NB],
                    start=True, stop=True),
                    reads=[kfb, (qf.name, hc)], writes=[ps], tag=tag + "qk")
                if kt >= 4 * qb:
                    i = kt - 4 * qb
                    prog.op("dve", lambda eng, i=i, ps=ps: eng.tensor_tensor(
                        msks[:, :], ps[0:128, 0:NB],
                        ctx.c_cmask[:, i * NB:(i + 1) * NB], ALU.add),
                        reads=[ps, "c_cmask"], writes=[msks], tag=tag + "cm")
                    prog.op("act", lambda eng, kt=kt: eng.activation(
                        Pt[:, kt * NB:(kt + 1) * NB], msks[:, :], AF.Exp,
                        scale=0.125),
                        reads=[msks], writes=[(Pt.name, kt)], tag=tag + "ex")
                else:
                    prog.op("act", lambda eng, kt=kt, ps=ps: eng.activation(
                        Pt[:, kt * NB:(kt + 1) * NB], ps[0:128, 0:NB], AF.Exp,
                        scale=0.125),
                        reads=[ps], writes=[(Pt.name, kt)], tag=tag + "ex2")
            for kt in range(nkt):
                prog.op("pe", lambda eng, kt=kt, nkt=nkt: eng.matmul(
                    t_sum[0:1, 0:NB], ctx.c_ones_red_bf,
                    Pt[:, kt * NB:(kt + 1) * NB],
                    start=(kt == 0), stop=(kt == nkt - 1)),
                    reads=[(Pt.name, kt), "c_ones_red_bf"], writes=[t_sum],
                    tag=tag + "sm")
            prog.op("dve", lambda eng: eng.tensor_copy(srow[:, :], t_sum[0:1, 0:NB]),
                    reads=[t_sum], writes=[srow], tag=tag + "se")
            prog.op("dve", lambda eng: eng.reciprocal(rrow[:, :], srow[:, :]),
                    reads=[srow], writes=[rrow], tag=tag + "rc")
            prog.op("pe", lambda eng: eng.matmul(
                t_r[:, 0:NB], ctx.c_ones_bc, rrow[0:1, 0:NB], start=True, stop=True),
                reads=[rrow, "c_ones_bc"], writes=[t_r], tag=tag + "rb")
            for kt in range(nkt):
                prog.op("pe", lambda eng, kt=kt, nkt=nkt: eng.matmul(
                    t_y[0:64, 0:NB], vtok[:, kt * 64:(kt + 1) * 64],
                    Pt[:, kt * NB:(kt + 1) * NB],
                    start=(kt == 0), stop=(kt == nkt - 1)),
                    reads=[(vtok.name, kt), (Pt.name, kt)], writes=[t_y],
                    tag=tag + "pv")
            prog.op("dve", lambda eng: eng.tensor_copy(yev[:, :], t_y[0:64, 0:NB]),
                    reads=[t_y], writes=[yev], tag=tag + "ye2")
            prog.op("dve", lambda eng, h=h, qb=qb, hb=hb, hc=hc: eng.tensor_tensor(
                yt[hb:hb + 64, hc * T + qb * NB: hc * T + (qb + 1) * NB],
                yev[:, :], t_r[0:64, 0:NB], ALU.mult),
                reads=[yev, t_r], writes=[(yt.name, hc)], tag=tag + "yw")
    prog.barrier()
    es3.close()
    emit_out_proj(ctx, b, Wcp_d, None, yt, yt.name, xin, xout, NCH, T, tag=tag + "o")
    ctx.es = old
    prog.barrier()
    es0.close()


def emit_m2rwkv(ctx, b, xin, xout, maak_d, maar_d, Wkey_d, Wrec_d, Wval_d, scr,
                tag="rw"):
    nc, prog = ctx.nc, ctx.prog
    from contextlib import ExitStack
    T = SEQ
    NS = T // NB
    NCH = 8
    FFN = 4096
    old = ctx.es
    es0 = ExitStack()
    ctx.es = es0
    sg_bf = ctx.sbuf(tag + "sg", (128, NCH * T), BF16)
    mkc = ctx.feat_cols(maak_d, DM, tag=tag + "mk")
    mrc = ctx.feat_cols(maar_d, DM, tag=tag + "mr")
    esX = ExitStack()
    ctx.es = esX
    xk_bf = ctx.sbuf(tag + "xk", (128, NCH * T), BF16)
    xr_bf = ctx.sbuf(tag + "xr", (128, NCH * T), BF16)
    es1 = ExitStack()
    ctx.es = es1
    h_f = ctx.sbuf(tag + "h", (128, NCH * T), BF16)
    es1b = ExitStack()
    ctx.es = es1b
    loader = dram_chunk_loader(
        ctx, lambda c: (xin[b, c * 128:(c + 1) * 128, :], (xin.tensor.name, b, c)),
        T, tag=tag + "xl", nbuf=1)
    emit_ln_fm(ctx, loader, T, NCH, h_f, tag + "_hk", tag=tag + f"ln{b}", nsq=1,
               rms=True, eps=1e-6)
    prog.barrier()
    es1b.close()
    ctx.es = es1
    xxb = ctx.sbuf(tag + "xx", (128, T), BF16)
    for c in range(NCH):
        prog.op("dve", lambda eng, c=c: eng.tensor_tensor(
            xxb[:, 1:T], h_f[:, c * T: c * T + T - 1], h_f[:, c * T + 1:(c + 1) * T],
            ALU.subtract),
            reads=[(tag + "_hk", c)], writes=[xxb], tag=tag + "x1")
        prog.op("dve", lambda eng, c=c: eng.tensor_scalar(
            xxb[:, 0:1], h_f[:, c * T: c * T + 1], -1.0, None, ALU.mult),
            reads=[(tag + "_hk", c)], writes=[xxb], tag=tag + "x2")
        prog.op("dve", lambda eng, c=c: eng.scalar_tensor_tensor(
            xk_bf[:, c * T:(c + 1) * T], xxb[:, :], mkc[:, c:c + 1],
            h_f[:, c * T:(c + 1) * T], ALU.mult, ALU.add),
            reads=[xxb, mkc, (tag + "_hk", c)], writes=[(xk_bf.name, c)],
            tag=tag + "x3")
        prog.op("dve", lambda eng, c=c: eng.scalar_tensor_tensor(
            xr_bf[:, c * T:(c + 1) * T], xxb[:, :], mrc[:, c:c + 1],
            h_f[:, c * T:(c + 1) * T], ALU.mult, ALU.add),
            reads=[xxb, mrc, (tag + "_hk", c)], writes=[(xr_bf.name, c)],
            tag=tag + "x4")
    prog.barrier()
    es1.close()
    es2 = ExitStack()
    ctx.es = es2
    wk = ctx.load_w_bf16(Wkey_d, DM, FFN, tag=tag + "wk")
    t1 = [ctx.sbuf(tag + "t1", (128, NB)) for _ in range(2)]
    t2 = [ctx.sbuf(tag + "t2", (128, NB), BF16) for _ in range(2)]

    def evac_kf(info, m0, m1, s, ps):
        r = info
        a = t1[(r * NS + s) % 2]
        bb = t2[(r * NS + s) % 2]
        prog.op("act", lambda eng, ps=ps, a=a: eng.activation(
            a[:, :], ps[0:128, 0:NB], AF.Relu),
            reads=[ps], writes=[a.name], tag=tag + "re")
        prog.op("act", lambda eng, a=a, bb=bb: eng.activation(
            bb[:, :], a[:, :], AF.Square),
            reads=[a.name], writes=[bb.name], tag=tag + "sq")
        dma(prog, "sp", scr["kfD"][b, m0:m1, s * NB:(s + 1) * NB], bb[:, :],
            reads=[bb.name], writes=[("kfD", b, r)], tag=tag + "skf")

    mt = [(r * 128, (r + 1) * 128, r) for r in range(32)]
    emit_proj_tiles(ctx, wk, FFN, xk_bf, xk_bf.name, NCH, T, mt, evac_kf,
                    tag=tag + "pk")
    prog.barrier()
    es2.close()
    es3 = ExitStack()
    ctx.es = es3
    wr = ctx.load_w_bf16(Wrec_d, DM, DM, tag=tag + "wr")

    def evac_sg(info, m0, m1, s, ps):
        r = info
        prog.op("act", lambda eng, ps=ps, r=r, s=s: eng.activation(
            sg_bf[:, r * T + s * NB: r * T + (s + 1) * NB], ps[0:128, 0:NB],
            AF.Sigmoid),
            reads=[ps], writes=[(sg_bf.name, r)], tag=tag + "sg")

    emit_proj_tiles(ctx, wr, DM, xr_bf, xr_bf.name, NCH, T,
                    [(r * 128, (r + 1) * 128, r) for r in range(8)], evac_sg,
                    tag=tag + "pr")
    prog.barrier()
    es3.close()
    esX.close()
    ctx.es = es0
    es4 = ExitStack()
    ctx.es = es4
    wv = ctx.load_w_bf16(Wval_d, FFN, DM, tag=tag + "wv")
    kfc = ctx.sbuf(tag + "kfc", (128, 32 * NB), BF16)
    xr2 = [ctx.sbuf(tag + "x2r", (128, NB)) for _ in range(2)]
    xo2 = [ctx.sbuf(tag + "x2o", (128, NB)) for _ in range(2)]
    for s in range(NS):
        for r in range(32):
            dma(prog, "pool", kfc[:, r * NB:(r + 1) * NB],
                scr["kfD"][b, r * 128:(r + 1) * 128, s * NB:(s + 1) * NB],
                reads=[("kfD", b, r)], writes=[(kfc.name, r)], tag=tag + "lkf")
        for m in range(8):
            ps = ctx.pp.get()
            for k in range(32):
                prog.op("pe", lambda eng, m=m, s=s, k=k, ps=ps: eng.matmul(
                    ps[0:128, 0:NB], wv[:, k * DM + m * 128: k * DM + (m + 1) * 128],
                    kfc[:, k * NB:(k + 1) * NB],
                    start=(k == 0), stop=(k == 31)),
                    reads=[(wv.name, k), (kfc.name, k)], writes=[ps], tag=tag + "mv")
            xr_ = xr2[m % 2]
            xw_ = xo2[m % 2]
            dma(prog, "sp", xr_[:, :], xin[b, m * 128:(m + 1) * 128, s * NB:(s + 1) * NB],
                reads=[(xin.tensor.name, b, m)], writes=[xr_.name], tag=tag + "lx")
            prog.op("dve", lambda eng, ps=ps, m=m, s=s: eng.tensor_tensor(
                ps[0:128, 0:NB], ps[0:128, 0:NB],
                sg_bf[:, m * T + s * NB: m * T + (s + 1) * NB], ALU.mult),
                reads=[ps, (sg_bf.name, m)], writes=[ps], tag=tag + "gv")
            prog.op("dve", lambda eng, ps=ps, xr_=xr_, xw_=xw_: eng.scalar_tensor_tensor(
                xw_[:, :], ps[0:128, 0:NB], 1.0, xr_[:, :], ALU.mult, ALU.add),
                reads=[ps, xr_.name], writes=[xw_.name], tag=tag + "ad")
            dma(prog, "sp", xout[b, m * 128:(m + 1) * 128, s * NB:(s + 1) * NB],
                xw_[:, :], reads=[xw_.name], writes=[(xout.tensor.name, b, m)],
                tag=tag + "so")
    prog.barrier()
    es4.close()
    ctx.es = old
    es0.close()



SLOTS0 = 2048
SLOTS1 = 5120


def _mk_progfn_sub(emit, weights, extra_consts=()):
    def progfn(nc, prog, es):
        ctx = Ctx(nc, prog, es)
        xin = nc.dram_tensor("xin", [BS, DM, SEQ], F32, kind="ExternalInput").ap()
        xout = nc.dram_tensor("xout", [BS, DM, SEQ], F32, kind="ExternalOutput").ap()
        dr = [nc.dram_tensor(f"w{i}", list(np.asarray(w).shape), F32,
                             kind="ExternalInput").ap() for i, w in enumerate(weights)]
        declare_consts(ctx, extra=extra_consts)
        for b in range(BS):
            emit(ctx, b, xin, xout, *dr)
    return progfn


def _base_inmap(ws, extra=()):
    m = {}
    for i, w in enumerate(ws):
        m[f"w{i}"] = np.ascontiguousarray(w)
    want = set(BASE_CONSTS) | set(extra)
    for k, v in const_arrays().items():
        if k in want:
            m[k + "_d"] = v
    return m


def _run_sub(emit, ws, xs, extra_consts=()):
    progfn = _mk_progfn_sub(emit, ws, extra_consts)
    base = _base_inmap(ws, extra_consts)
    in_maps = [dict(base, xin=xs[i]) for i in range(NCORE)]
    res = launch(progfn, in_maps)
    return [np.asarray(res.results[i]["xout"]) for i in range(NCORE)]


def _flatten_tok(per_core_arrays, tr=False):
    out = []
    for Bg in range(BATCH):
        a = per_core_arrays[Bg % 8][Bg // 8]
        out.append(a.T if tr else a)
    return np.concatenate(out, axis=0)


def _routing(idx_all, cap):
    order = np.argsort(idx_all, kind="stable")
    sorted_idx = idx_all[order]
    sel = np.zeros((8, cap), np.int64)
    valid = np.zeros((8, cap), bool)
    keep = np.zeros(idx_all.shape[0], bool)
    slot = np.zeros(idx_all.shape[0], np.int64)
    for e in range(8):
        toks = order[sorted_idx == e]
        n = min(len(toks), cap)
        sel[e, :n] = toks[:n]
        valid[e, :n] = True
        keep[toks[:n]] = True
        slot[toks[:n]] = e * cap + np.arange(n)
    return sel, valid, keep, slot


def _gather_fm_bf16(h_all, sel_row):
    return np.ascontiguousarray(h_all[sel_row].T)


def _per_core_rows(vec, one_col=False):
    out = []
    for i in range(NCORE):
        rows = []
        for j in range(BS):
            Bg = i + 8 * j
            rows.append(vec[Bg * SEQ:(Bg + 1) * SEQ][None, :])
        out.append(np.stack(rows, axis=0))
    return out


def _per_core_fm(mat):
    out = []
    for i in range(NCORE):
        sl = []
        for j in range(BS):
            Bg = i + 8 * j
            sl.append(np.ascontiguousarray(mat[Bg * SEQ:(Bg + 1) * SEQ].T))
        out.append(np.stack(sl, axis=0))
    return out


def _moe_layer(xs, kind, rw, rb, expert_ws, h_dt):
    def progA(nc, prog, es):
        ctx = Ctx(nc, prog, es)
        xin = nc.dram_tensor("xin", [BS, DM, SEQ], F32, kind="ExternalInput").ap()
        hout = nc.dram_tensor("hout", [BS, DM, SEQ], BF16, kind="ExternalOutput").ap()
        gate = nc.dram_tensor("gate", [BS, 16, 128], F32, kind="ExternalOutput").ap()
        idx = nc.dram_tensor("idx", [BS, 16, 128], F32, kind="ExternalOutput").ap()
        rwd = nc.dram_tensor("rw", [DM, 8], F32, kind="ExternalInput").ap()
        rbd = nc.dram_tensor("rb", [8], F32, kind="ExternalInput").ap()
        declare_consts(ctx, extra=("c_rev8",))
        for b in range(BS):
            emit_router(ctx, b, xin, hout, rwd, rbd, gate, idx)
    base = {"rw": np.ascontiguousarray(rw), "rb": np.ascontiguousarray(rb)}
    want = set(BASE_CONSTS) | {"c_rev8"}
    for k, v in const_arrays().items():
        if k in want:
            base[k + "_d"] = v
    res = launch(progA, [dict(base, xin=xs[i]) for i in range(NCORE)])
    h_pc = [np.asarray(res.results[i]["hout"]) for i in range(NCORE)]
    gate_pc = [np.asarray(res.results[i]["gate"]) for i in range(NCORE)]
    idx_pc = [np.asarray(res.results[i]["idx"]) for i in range(NCORE)]
    h_all = _flatten_tok(h_pc, tr=True)
    idx_all = _flatten_tok([a.reshape(BS, SEQ) for a in idx_pc]).reshape(-1)
    idx_all = idx_all.astype(np.int64)
    gate_all = _flatten_tok([a.reshape(BS, SEQ) for a in gate_pc]).reshape(-1)
    cap = SLOTS0 if kind == "keep" else SLOTS1
    sel, valid, keep, slot = _routing(idx_all, cap)
    if kind == "keep":
        w1, b1, w2, b2 = expert_ws

        def progB(nc, prog, es):
            ctx = Ctx(nc, prog, es)
            Xg = nc.dram_tensor("Xg", [DM, cap], BF16, kind="ExternalInput").ap()
            Yg = nc.dram_tensor("Yg", [DM, cap], F32, kind="ExternalOutput").ap()
            w1d = nc.dram_tensor("w1", [DM, 4096], F32, kind="ExternalInput").ap()
            b1d = nc.dram_tensor("b1", [4096], F32, kind="ExternalInput").ap()
            w2d = nc.dram_tensor("w2", [4096, DM], F32, kind="ExternalInput").ap()
            b2d = nc.dram_tensor("b2", [DM], F32, kind="ExternalInput").ap()
            declare_consts(ctx)
            emit_expert_l0(ctx, Xg, w1d, b1d, w2d, b2d, Yg, cap)
        cb = {}
        for k, v in const_arrays().items():
            if k in BASE_CONSTS:
                cb[k + "_d"] = v
        in_maps = [dict(cb, Xg=_gather_fm_bf16(h_all, sel[e]),
                        w1=np.ascontiguousarray(w1[e]), b1=np.ascontiguousarray(b1[e]),
                        w2=np.ascontiguousarray(w2[e]), b2=np.ascontiguousarray(b2[e]))
                   for e in range(8)]
        res = launch(progB, in_maps)
    else:
        ew, eb = expert_ws

        def progB(nc, prog, es):
            ctx = Ctx(nc, prog, es)
            Xg = nc.dram_tensor("Xg", [DM, cap], BF16, kind="ExternalInput").ap()
            Yg = nc.dram_tensor("Yg", [DM, cap], F32, kind="ExternalOutput").ap()
            wd = nc.dram_tensor("we", [DM, DM], F32, kind="ExternalInput").ap()
            bd = nc.dram_tensor("be", [DM], F32, kind="ExternalInput").ap()
            declare_consts(ctx)
            emit_expert_l1(ctx, Xg, wd, bd, Yg, cap)
        cb = {}
        for k, v in const_arrays().items():
            if k in BASE_CONSTS:
                cb[k + "_d"] = v
        in_maps = [dict(cb, Xg=_gather_fm_bf16(h_all, sel[e]),
                        we=np.ascontiguousarray(ew[e]), be=np.ascontiguousarray(eb[e]))
                   for e in range(8)]
        res = launch(progB, in_maps)
    Yg_pc = [np.asarray(res.results[e]["Yg"]) for e in range(8)]
    Ysel = np.zeros((NTOK, DM), np.float32)
    for e in range(8):
        v = valid[e]
        Ysel[sel[e][v]] = Yg_pc[e].T[v]
    gk = keep.astype(np.float32) if kind == "keep" else gate_all.astype(np.float32)
    gk_pc = _per_core_rows(gk)
    Ysel_pc = _per_core_fm(Ysel)

    def progC(nc, prog, es):
        ctx = Ctx(nc, prog, es)
        xin = nc.dram_tensor("xin", [BS, DM, SEQ], F32, kind="ExternalInput").ap()
        xout = nc.dram_tensor("xout", [BS, DM, SEQ], F32, kind="ExternalOutput").ap()
        Ys = nc.dram_tensor("Ys", [BS, DM, SEQ], F32, kind="ExternalInput").ap()
        gkd = nc.dram_tensor("gk", [BS, 1, SEQ], F32, kind="ExternalInput").ap()
        declare_consts(ctx)
        for b in range(BS):
            emit_combine(ctx, b, xin, xout, Ys, gkd, kind)
    cb = {}
    for k, v in const_arrays().items():
        if k in BASE_CONSTS:
            cb[k + "_d"] = v
    res = launch(progC, [dict(cb, xin=xs[i], Ys=Ysel_pc[i], gk=gk_pc[i])
                         for i in range(NCORE)])
    return [np.asarray(res.results[i]["xout"]) for i in range(NCORE)]


def _dump_stage(xs, name):
    import os
    if os.environ.get("DUMPSTAGE"):
        xfull = np.zeros((BATCH, SEQ, DM), np.float32)
        for i in range(NCORE):
            for j in range(BS):
                xfull[i + 8 * j] = np.asarray(xs[i][j]).T
        np.save(f"/tmp/stage_{name}.npy", xfull)


def kernel(**inp):
    g = lambda k: np.asarray(inp[k])
    x0 = g("L0_attn_a_x")
    xs = []
    for i in range(NCORE):
        xs.append(np.ascontiguousarray(
            x0[[i, i + 8]].transpose(0, 2, 1)).astype(np.float32))

    xs = _run_sub(
        lambda ctx, b, xin, xout, WxT, WoutT, Ad, Dd: emit_mamba1(
            ctx, b, xin, xout, WxT, WoutT, Ad, Dd),
        [g("L0_attn_a_Wx").T, g("L0_attn_a_Wout").T, g("L0_attn_a_A_log"),
         g("L0_attn_a_D")], xs, extra_consts=("c_sel16",))
    _dump_stage(xs, "1m1")

    l0w = (g("L0_ffn_w1"), g("L0_ffn_b1"), g("L0_ffn_w2"), g("L0_ffn_b2"))
    xs = _moe_layer(xs, "keep", g("L0_ffn_w_switch"), g("L0_ffn_b_switch"), l0w,
                    None)
    _dump_stage(xs, "2ffn")

    xs = _run_sub(
        lambda ctx, b, xin, xout, wq, bq, wk, bk, wv, bv, wo, bo: emit_attn0(
            ctx, b, xin, xout, wq, bq, wk, bk, wv, bv, wo, bo),
        [g("L0_attn_b_wq").T, g("L0_attn_b_bq"), g("L0_attn_b_wk").T,
         g("L0_attn_b_bk"), g("L0_attn_b_wv").T, g("L0_attn_b_bv"),
         g("L0_attn_b_wo").T, g("L0_attn_b_bo")], xs,
        extra_consts=("c_ones_red_bf",))
    _dump_stage(xs, "3attn")

    xs = _moe_layer(xs, "keep", g("L0_ffn_w_switch"), g("L0_ffn_b_switch"), l0w,
                    None)
    _dump_stage(xs, "4ffn")

    xfull = np.zeros((BATCH, DM, SEQ), np.float32)
    for i in range(NCORE):
        for j in range(BS):
            xfull[i + 8 * j] = xs[i][j]
    SL = SEQ // NCORE
    Vc = g("L1_attn_a_V")
    Wc = g("L1_attn_a_Vinv")

    def progS(nc, prog, es):
        ctx = Ctx(nc, prog, es)
        xin = nc.dram_tensor("xin", [BATCH, DM, SL], F32, kind="ExternalInput").ap()
        xout = nc.dram_tensor("xout", [BATCH, DM, SL], F32, kind="ExternalOutput").ap()
        dr = {}
        for nm, shp in [("logdt", (64,)), ("logLre", (64,)), ("Lim", (64,)),
                        ("Vre", (64, 64)), ("Vim", (64, 64)), ("Wre", (64, 64)),
                        ("Wim", (64, 64))]:
            dr[nm] = nc.dram_tensor(nm, list(shp), F32, kind="ExternalInput").ap()
        declare_consts(ctx)
        emit_s5(ctx, xin, xout, dr["logdt"], dr["logLre"], dr["Lim"], dr["Vre"],
                dr["Vim"], dr["Wre"], dr["Wim"], SL)
    base = {"logdt": g("L1_attn_a_log_dt"), "logLre": g("L1_attn_a_log_Lambda_real"),
            "Lim": g("L1_attn_a_Lambda_imag"),
            "Vre": np.ascontiguousarray(Vc.real), "Vim": np.ascontiguousarray(Vc.imag),
            "Wre": np.ascontiguousarray(Wc.real), "Wim": np.ascontiguousarray(Wc.imag)}
    for k, v in const_arrays().items():
        if k in BASE_CONSTS:
            base[k + "_d"] = v
    res = launch(progS, [dict(base, xin=np.ascontiguousarray(
        xfull[:, :, i * SL:(i + 1) * SL])) for i in range(NCORE)])
    for i in range(NCORE):
        xfull[:, :, i * SL:(i + 1) * SL] = np.asarray(res.results[i]["xout"])
    xs = [np.ascontiguousarray(xfull[[i, i + 8]]) for i in range(NCORE)]
    _dump_stage(xs, "5s5")

    l1w = (g("L1_ffn_expert_w"), g("L1_ffn_expert_b"))
    xs = _moe_layer(xs, "gate", g("L1_ffn_router_w"), g("L1_ffn_router_b"), l1w,
                    None)
    _dump_stage(xs, "6ffn")

    def progM2(nc, prog, es):
        ctx = Ctx(nc, prog, es)
        xin = nc.dram_tensor("xin", [BS, DM, SEQ], F32, kind="ExternalInput").ap()
        xout = nc.dram_tensor("xout", [BS, DM, SEQ], F32, kind="ExternalOutput").ap()
        xu = nc.dram_tensor("xu", [BS, DM, SEQ], F32).ap()
        xa = nc.dram_tensor("xa", [BS, DM, SEQ], F32).ap()
        xb = nc.dram_tensor("xb", [BS, DM, SEQ], F32).ap()
        xc = nc.dram_tensor("xc", [BS, DM, SEQ], F32).ap()
        nm_shapes = [("W_in", (DM, 4384)), ("conv_w", (4, 2304)), ("conv_b", (2304,)),
                     ("A_log", (32,)), ("Dm", (32,)), ("dt_bias", (32,)),
                     ("mnorm_w", (2048,)), ("W_out", (2048, DM)),
                     ("W_qkv", (DM, 1152)), ("W_cproj", (DM, DM)),
                     ("qconv_w", (3, 64)), ("qconv_b", (64,)), ("kconv_w", (3, 64)),
                     ("kconv_b", (64,)), ("vconv_w", (3, 64)), ("vconv_b", (64,)),
                     ("maa_k", (DM,)), ("maa_r", (DM,)), ("W_key", (DM, 4096)),
                     ("W_rec", (DM, DM)), ("W_val", (4096, DM))]
        dr = {nm: nc.dram_tensor(nm, list(shp), F32, kind="ExternalInput").ap()
              for nm, shp in nm_shapes}
        scr = {
            "zD": nc.dram_tensor("zD", [BS, 2048, SEQ], BF16).ap(),
            "xbcD": nc.dram_tensor("xbcD", [BS, 2304, SEQ], F32).ap(),
            "xmD": nc.dram_tensor("xmD", [BS, 2048, SEQ], BF16).ap(),
            "BD": nc.dram_tensor("BD", [BS, 128, SEQ], BF16).ap(),
            "CD": nc.dram_tensor("CD", [BS, 128, SEQ], BF16).ap(),
            "kfD": nc.dram_tensor("kfD", [BS, 4096, SEQ], BF16).ap(),
        }
        declare_consts(ctx, extra=("c_sel16", "c_sel32", "c_trimask", "c_cmask", "c_dm_l1", "c_dm_l2", "c_dm_m1", "c_dm_m2", "c_ones_red_bf"))
        for b in range(BS):
            emit_ln_to_dram(ctx, b, xin, xu)
            emit_m2core(ctx, b, xu, xa, dr["W_in"], dr["conv_w"], dr["conv_b"],
                        dr["A_log"], dr["Dm"], dr["dt_bias"], dr["mnorm_w"],
                        dr["W_out"], scr)
            emit_m2attn(ctx, b, xa, xb, dr["W_qkv"], dr["W_cproj"], dr["qconv_w"],
                        dr["qconv_b"], dr["kconv_w"], dr["kconv_b"], dr["vconv_w"],
                        dr["vconv_b"])
            emit_m2rwkv(ctx, b, xb, xc, dr["maa_k"], dr["maa_r"], dr["W_key"],
                        dr["W_rec"], dr["W_val"], scr)
            emit_add_res(ctx, b, xin, xc, xout)
    base = {"W_in": g("L1_attn_b_W_in"),
            "conv_w": np.ascontiguousarray(g("L1_attn_b_conv_w").T),
            "conv_b": g("L1_attn_b_conv_b"), "A_log": g("L1_attn_b_A_log"),
            "Dm": g("L1_attn_b_Dm"), "dt_bias": g("L1_attn_b_dt_bias"),
            "mnorm_w": g("L1_attn_b_mnorm_w"), "W_out": g("L1_attn_b_W_out"),
            "W_qkv": g("L1_attn_b_W_qkv"), "W_cproj": g("L1_attn_b_W_cproj"),
            "qconv_w": np.ascontiguousarray(g("L1_attn_b_qconv_w").T),
            "qconv_b": g("L1_attn_b_qconv_b"),
            "kconv_w": np.ascontiguousarray(g("L1_attn_b_kconv_w").T),
            "kconv_b": g("L1_attn_b_kconv_b"),
            "vconv_w": np.ascontiguousarray(g("L1_attn_b_vconv_w").T),
            "vconv_b": g("L1_attn_b_vconv_b"), "maa_k": g("L1_attn_b_time_maa_k"),
            "maa_r": g("L1_attn_b_time_maa_r"), "W_key": g("L1_attn_b_W_key"),
            "W_rec": g("L1_attn_b_W_rec"), "W_val": g("L1_attn_b_W_val")}
    want = set(BASE_CONSTS) | set(("c_sel16", "c_sel32", "c_trimask", "c_cmask", "c_dm_l1", "c_dm_l2", "c_dm_m1", "c_dm_m2", "c_ones_red_bf"))
    for k, v in const_arrays().items():
        if k in want:
            base[k + "_d"] = v
    res = launch(progM2, [dict(base, xin=xs[i]) for i in range(NCORE)])
    xs = [np.asarray(res.results[i]["xout"]) for i in range(NCORE)]
    _dump_stage(xs, "7m2")

    xs = _moe_layer(xs, "gate", g("L1_ffn_router_w"), g("L1_ffn_router_b"), l1w,
                    None)

    out = np.zeros((BATCH, SEQ, DM), np.float32)
    for i in range(NCORE):
        for j in range(BS):
            out[i + 8 * j] = xs[i][j].T
    return out


def emit_ln_to_dram(ctx, b, xin, uout, tag="l2d", eps=1e-5, rms=False):
    nc, prog = ctx.nc, ctx.prog
    from contextlib import ExitStack
    T = SEQ
    NCH = 8
    old = ctx.es
    es = ExitStack()
    ctx.es = es
    hf = ctx.sbuf(tag + "h", (128, NCH * T))
    loader = dram_chunk_loader(
        ctx, lambda c: (xin[b, c * 128:(c + 1) * 128, :], (xin.tensor.name, b, c)),
        T, tag=tag + "xl", nbuf=1)
    emit_ln_fm(ctx, loader, T, NCH, hf, tag + "_hk", tag=tag + f"ln{b}", nsq=1,
               eps=eps, rms=rms)
    for c in range(NCH):
        dma(prog, "sp", uout[b, c * 128:(c + 1) * 128, :], hf[:, c * T:(c + 1) * T],
            reads=[(tag + "_hk", c)], writes=[(uout.tensor.name, b, c)],
            tag=tag + "st")
    ctx.es = old
    prog.barrier()
    es.close()


def emit_add_res(ctx, b, xin, yin, xout, tag="ar"):
    nc, prog = ctx.nc, ctx.prog
    from contextlib import ExitStack
    T = SEQ
    old = ctx.es
    es = ExitStack()
    ctx.es = es
    a = [ctx.sbuf(tag + "a", (128, T)) for _ in range(2)]
    bb = [ctx.sbuf(tag + "b", (128, T)) for _ in range(2)]
    for c in range(8):
        a_ = a[c % 2]
        b_ = bb[c % 2]
        dma(prog, "sp", a_[:, :], xin[b, c * 128:(c + 1) * 128, :],
            reads=[(xin.tensor.name, b, c)], writes=[a_.name], tag=tag + "la")
        dma(prog, "pool", b_[:, :], yin[b, c * 128:(c + 1) * 128, :],
            reads=[(yin.tensor.name, b, c)], writes=[b_.name], tag=tag + "lb")
        prog.op("dve", lambda eng, a_=a_, b_=b_: eng.tensor_tensor(
            a_[:, :], a_[:, :], b_[:, :], ALU.add),
            reads=[a_.name, b_.name], writes=[a_.name], tag=tag + "ad")
        dma(prog, "sp", xout[b, c * 128:(c + 1) * 128, :], a_[:, :],
            reads=[a_.name], writes=[(xout.tensor.name, b, c)], tag=tag + "st")
    ctx.es = old
    prog.barrier()
    es.close()


# version 13 (final)
# speedup vs baseline: 1.1730x; 1.0177x over previous
import numpy as np
import ml_dtypes
import concourse.bass as bass
import concourse.mybir as mybir
from concourse.bass import IndirectOffsetOnAxis
from concourse.bass_utils import run_bass_kernel_spmd

F32 = mybir.dt.float32
BF16 = mybir.dt.bfloat16
U32 = mybir.dt.uint32
AF = mybir.ActivationFunctionType
ALU = mybir.AluOpType
AX = mybir.AxisListType

NCORE = 8
BATCH = 16
SEQ = 2048
DM = 1024
BS = BATCH // NCORE
NTOK = BATCH * SEQ
CTOK = BS * SEQ
NB = 512


ENGINES = ("pe", "act", "dve", "pool", "sp")
MAXINFLIGHT = 8


class Op:
    __slots__ = ("i", "eng", "fn", "reads", "writes", "dma", "marked", "mark_no",
                 "dma_no", "waits", "tag")

    def __init__(self, i, eng, fn, reads, writes, dma, tag):
        self.i = i
        self.eng = eng
        self.fn = fn
        self.reads = reads
        self.writes = writes
        self.dma = dma
        self.marked = False
        self.mark_no = 0
        self.dma_no = 0
        self.waits = {}
        self.tag = tag


def _keys(x):
    out = []
    if not isinstance(x, (list, tuple)):
        x = (x,)
    for a in x:
        if isinstance(a, (tuple, str)):
            out.append(a)
        elif hasattr(a, "tensor"):
            out.append(a.tensor.name)
        elif hasattr(a, "name"):
            out.append(a.name)
        else:
            out.append(str(a))
    return out


class Prog:
    def __init__(self, nc):
        self.nc = nc
        self.ops = []
        self._qcnt = {}

    def op(self, eng, fn, reads=(), writes=(), dma=False, tag=""):
        o = Op(len(self.ops), eng, fn, _keys(reads), _keys(writes),
               eng if dma else None, tag)
        if o.dma is not None:
            self._qcnt[o.dma] = self._qcnt.get(o.dma, 0) + 1
            o.dma_no = self._qcnt[o.dma]
        self.ops.append(o)
        return o

    def barrier(self):
        key = ("barrier", len(self.ops))
        for e in ENGINES:
            self.op(e, lambda eng: eng.nop(), reads=(), writes=(key,), tag="barw")
        for e in ENGINES:
            self.op(e, lambda eng: eng.nop(), reads=(key,), writes=(), tag="barr")

    def _resolve(self):
        last_writer = {}
        readers_since = {}
        for o in self.ops:
            deps = []
            for k in o.reads:
                w = last_writer.get(k)
                if w is not None:
                    deps.append(w)
                readers_since.setdefault(k, []).append(o)
            for k in o.writes:
                w = last_writer.get(k)
                if w is not None:
                    deps.append(w)
                for r in readers_since.get(k, ()):
                    deps.append(r)
                last_writer[k] = o
                readers_since[k] = []
            for d in deps:
                if d is o:
                    continue
                if d.eng == o.eng and d.dma is None and o.dma is None \
                        and d.eng == "pe":
                    continue
                if d.dma is not None:
                    o.waits["q" + d.dma] = max(o.waits.get("q" + d.dma, 0),
                                               16 * d.dma_no)
                else:
                    d.marked = True
                    key = "#" + d.eng
                    o.waits[key] = max(o.waits.get(key, -1), d.i)
        mark_cnt = {e: 0 for e in ENGINES}
        idx2mark = {}
        for o in self.ops:
            if o.marked and o.dma is None:
                mark_cnt[o.eng] += 1
                o.mark_no = mark_cnt[o.eng]
            idx2mark[o.i] = (o.eng, o.mark_no)
        for o in self.ops:
            fixed = {}
            for k, v in o.waits.items():
                if k.startswith("#"):
                    eng, mn = idx2mark[v]
                    assert mn > 0, (o.tag, k, v)
                    fixed[eng] = max(fixed.get(eng, 0), mn)
                else:
                    fixed[k] = max(fixed.get(k, 0), v)
            o.waits = fixed
        self.final_q = dict(self._qcnt)
        self.final_m = mark_cnt

    def emit_engine(self, e, eng, sems):
        seen = {}
        for o in self.ops:
            if o.eng != e:
                continue
            for k, v in sorted(o.waits.items()):
                if seen.get(k, -1) >= v:
                    continue
                eng.wait_ge(sems[k], v)
                seen[k] = v
            if o.dma is not None and o.dma_no > 1:
                k = "q" + o.dma
                v = 16 * (o.dma_no - 1)
                if seen.get(k, -1) < v:
                    eng.wait_ge(sems[k], v)
                    seen[k] = v
            if o.marked and o.dma is None and o.tag in ("barw", "barr"):
                v = o.mark_no - 1
                if v > 0 and seen.get(e, -1) < v:
                    eng.wait_ge(sems[e], v)
                    seen[e] = v
            inst = o.fn(eng)
            if o.dma is not None:
                assert inst is not None, o.tag
                inst.then_inc(sems["q" + o.dma], 16)
            elif o.marked:
                if inst is None:
                    inst = eng.nop()
                inst.then_inc(sems[o.eng], 1)
        if e == "sp":
            for q, n in self.final_q.items():
                if seen.get("q" + q, -1) < 16 * n:
                    eng.wait_ge(sems["q" + q], 16 * n)
            for m, n in self.final_m.items():
                if n and seen.get(m, -1) < n:
                    eng.wait_ge(sems[m], n)


def launch(progfn, in_maps, core_ids=None, **kw):
    from contextlib import ExitStack
    if core_ids is None:
        core_ids = list(range(len(in_maps)))
    nc = bass.Bass("TRN2", target_bir_lowering=False)
    prog = Prog(nc)
    es = ExitStack()
    with es:
        progfn(nc, prog, es)
        sems = {}
        for e in ENGINES:
            sems[e] = es.enter_context(nc.semaphore("sem_" + e))
            sems["q" + e] = es.enter_context(nc.semaphore("semq_" + e))
        import os
        _cut = int(os.environ.get("BASSCUT", "0"))
        if _cut:
            prog.ops = prog.ops[:_cut]
            prog._qcnt = {}
            for o in prog.ops:
                if o.dma is not None:
                    prog._qcnt[o.dma] = prog._qcnt.get(o.dma, 0) + 1
        prog._resolve()
        with nc.Block() as block:

            @block.tensor
            def _(t):
                prog.emit_engine("pe", t, sems)

            @block.scalar
            def _(s):
                prog.emit_engine("act", s, sems)

            @block.vector
            def _(v):
                prog.emit_engine("dve", v, sems)

            @block.gpsimd
            def _(g):
                prog.emit_engine("pool", g, sems)

            @block.sync
            def _(sy):
                prog.emit_engine("sp", sy, sems)
    import os
    if os.environ.get("BASSPROF"):
        import time as _t
        t0 = _t.time()
        r = run_bass_kernel_spmd(nc, in_maps, core_ids=core_ids, trace=True, **kw)
        print(f"[prof] launch ops={len(prog.ops)} exec_ns={r.exec_time_ns} wall={_t.time()-t0:.1f}s", flush=True)
        return r
    if os.environ.get("BASSSIM"):
        import concourse.bass_interp as bass_interp

        class _R:
            pass
        sim = bass_interp.MultiCoreSim(nc, len(core_ids))
        outs = []
        for i in range(len(core_ids)):
            for k, v in in_maps[i].items():
                sim.cores[i].tensor(k)[:] = np.asarray(v)
        sim.simulate()
        r = _R()
        r.results = []
        for i in range(len(core_ids)):
            d = {}
            for alloc in nc.m.functions[0].allocations:
                if not isinstance(alloc, mybir.MemoryLocationSet):
                    continue
                if alloc.kind == "ExternalOutput":
                    nmm = alloc.memorylocations[0].name
                    d[nmm] = np.array(sim.cores[i].mem_tensor(nmm)).reshape(
                        tuple(alloc.tensor_shape))
            r.results.append(d)
        return r
    return run_bass_kernel_spmd(nc, in_maps, core_ids=core_ids, **kw)



def sb(nc, es, name, shape, dt=F32):
    return es.enter_context(nc.sbuf_tensor(name, shape, dt))


def psum(nc, es, name, shape=(128, NB), dt=F32):
    return es.enter_context(nc.psum_tensor(name, list(shape), dt))


class PsumPool:

    def __init__(self, nc, es, n=6, name="pp"):
        self.tiles = [psum(nc, es, f"{name}{i}") for i in range(n)]
        self.i = 0

    def get(self):
        t = self.tiles[self.i % len(self.tiles)]
        self.i += 1
        return t


def dma(prog, q, out_ap, in_ap, reads, writes, tag="dma", **kw):
    prog.op(q, lambda eng: eng.dma_start(out=out_ap, in_=in_ap, **kw),
            reads=reads, writes=writes, dma=True, tag=tag)


def bcast_row(prog, onescol_ap, row_ap, ps_ap, n0, n1, rkey, wkey, tag="bc"):
    prog.op("pe",
            lambda eng: eng.matmul(ps_ap[:, 0:(n1 - n0)], onescol_ap,
                                   row_ap[0:1, n0:n1], start=True, stop=True),
            reads=[rkey, "c_ones_col"], writes=[wkey], tag=tag)



class Ctx:
    def __init__(self, nc, prog, es):
        self.nc = nc
        self.prog = prog
        self.es = es
        self.uid = 0
        self.c_ones_red = None
        self.c_ones_bc = None
        self.c_ident = None
        self.pp = None

    def name(self, base):
        self.uid += 1
        return f"{base}_{self.uid}"

    def sbuf(self, base, shape, dt=F32):
        return sb(self.nc, self.es, self.name(base), list(shape), dt)

    def feat_cols(self, dram_vec, n, tag="fc", scale=None):
        prog, nc = self.prog, self.nc
        if n <= 128:
            row = self.sbuf(tag + "r", (1, n))
            dma(prog, "sp", row[:, :], dram_vec.rearrange("(o n) -> o n", o=1),
                reads=[dram_vec], writes=[row], tag=tag)
            ps = self.pp.get()
            prog.op("pe", lambda eng: eng.transpose(ps[0:n, 0:1], row[0:1, 0:n],
                                                    self.c_ident[0:1, 0:1]),
                    reads=[row, "c_ident"], writes=[ps], tag=tag + "t")
            col = self.sbuf(tag + "c", (n, 1))
            if scale is None:
                prog.op("dve", lambda eng: eng.tensor_copy(col[:, :], ps[0:n, 0:1]),
                        reads=[ps], writes=[col], tag=tag + "e")
            else:
                prog.op("dve", lambda eng: eng.tensor_scalar(col[:, :], ps[0:n, 0:1],
                                                             float(scale), None, ALU.mult),
                        reads=[ps], writes=[col], tag=tag + "e")
            return col
        k = n // 128
        assert n % 128 == 0
        wide = self.sbuf(tag + "w", (k, 128))
        dma(prog, "sp", wide[:, :], dram_vec.rearrange("(k p) -> k p", p=128),
            reads=[dram_vec], writes=[wide], tag=tag)
        ps = self.pp.get()
        prog.op("pe", lambda eng: eng.transpose(ps[0:128, 0:k], wide[0:k, 0:128],
                                                self.c_ident[0:k, 0:k]),
                reads=[wide, "c_ident"], writes=[ps], tag=tag + "t")
        cols = self.sbuf(tag + "c", (128, k))
        if scale is None:
            prog.op("dve", lambda eng: eng.tensor_copy(cols[:, :], ps[0:128, 0:k]),
                    reads=[ps], writes=[cols], tag=tag + "e")
        else:
            prog.op("dve", lambda eng: eng.tensor_scalar(cols[:, :], ps[0:128, 0:k],
                                                         float(scale), None, ALU.mult),
                    reads=[ps], writes=[cols], tag=tag + "e")
        return cols

    def load_w_bf16(self, dram_w, K, M, tag="w"):
        prog = self.prog
        kc = K // 128
        t = self.sbuf(tag, (128, kc * M), BF16)
        for c in range(kc):
            dma(prog, "pool", t[:, c * M:(c + 1) * M], dram_w[c * 128:(c + 1) * 128, :],
                reads=[dram_w], writes=[(t.name, c)], tag=tag + str(c))
        return t


def emit_ln_fm(ctx, get_chunk, T, nchunk, h_out, h_key, tag="ln", eps=1e-5,
               rms=False, nsq=2):
    prog = ctx.prog
    D = nchunk * 128
    ns = T // NB
    assert ns <= 4
    sq = [ctx.sbuf(tag + "sq", (128, T)) for _ in range(nsq)]
    sq = (sq * 2)[:2]
    rows = ctx.sbuf(tag + "rw", (1, 2 * T))
    def R(i):
        if i == 2:
            return sq[0][0:1, 0:T]
        return rows[0:1, i * T:(i + 1) * T]
    ps_s = [ctx.pp.get() for _ in range(ns)]
    ps_q = [ctx.pp.get() for _ in range(ns)]
    for c in range(nchunk):
        xap, xkey = get_chunk(c, 0)
        s_ = sq[c % 2]
        prog.op("act", lambda eng, xap=xap, s_=s_: eng.activation(
            s_[:, :], xap, AF.Square),
            reads=[xkey], writes=[s_], tag=tag + "sq")
        for s in range(ns):
            prog.op("pe", lambda eng, xap=xap, c=c, s=s: eng.matmul(
                ps_s[s][0:1, 0:NB], ctx.c_ones_red, xap[:, s * NB:(s + 1) * NB],
                start=(c == 0), stop=(c == nchunk - 1)),
                reads=[xkey, "c_ones_red"], writes=[ps_s[s]], tag=tag + "ms")
            prog.op("pe", lambda eng, c=c, s=s, s_=s_: eng.matmul(
                ps_q[s][0:1, 0:NB], ctx.c_ones_red, s_[:, s * NB:(s + 1) * NB],
                start=(c == 0), stop=(c == nchunk - 1)),
                reads=[s_, "c_ones_red"], writes=[ps_q[s]], tag=tag + "mq")
    for s in range(ns):
        prog.op("dve", lambda eng, s=s: eng.tensor_copy(
            rows[0:1, s * NB:(s + 1) * NB], ps_s[s][0:1, 0:NB]),
            reads=[ps_s[s]], writes=[(rows.name, 0)], tag=tag + "es")
        prog.op("dve", lambda eng, s=s: eng.tensor_copy(
            rows[0:1, T + s * NB:T + (s + 1) * NB], ps_q[s][0:1, 0:NB]),
            reads=[ps_q[s]], writes=[(rows.name, 1)], tag=tag + "eq")
    if not rms:
        prog.op("dve", lambda eng: eng.tensor_scalar(
            R(0), R(0), 1.0 / D, None, ALU.mult),
            reads=[(rows.name, 0)], writes=[(rows.name, 0)], tag=tag + "m")
        prog.op("dve", lambda eng: eng.tensor_tensor(
            R(2), R(0), R(0), ALU.mult),
            reads=[(rows.name, 0)], writes=[sq[0]], tag=tag + "m2")
        prog.op("dve", lambda eng: eng.scalar_tensor_tensor(
            R(1), R(1), 1.0 / D, R(2), ALU.mult, ALU.subtract),
            reads=[(rows.name, 1), sq[0]], writes=[(rows.name, 1)],
            tag=tag + "v2")
    else:
        prog.op("dve", lambda eng: eng.tensor_scalar(
            R(1), R(1), 1.0 / D, None, ALU.mult),
            reads=[(rows.name, 1)], writes=[(rows.name, 1)], tag=tag + "v")
    prog.op("act", lambda eng: eng.activation(R(1), R(1), AF.Ln,
            bias=scal_ap(ctx, eps, 1)),
            reads=[(rows.name, 1), "c_scal"], writes=[(rows.name, 1)], tag=tag + "ln")
    prog.op("act", lambda eng: eng.activation(R(1), R(1), AF.Exp, scale=-0.5),
            reads=[(rows.name, 1)], writes=[(rows.name, 1)], tag=tag + "ex")
    if not rms:
        prog.op("dve", lambda eng: eng.scalar_tensor_tensor(
            R(0), R(0), -1.0, R(1), ALU.mult, ALU.mult),
            reads=[(rows.name, 0), (rows.name, 1)], writes=[(rows.name, 0)],
            tag=tag + "nm")
    psR = []
    psM = []
    for s in range(ns):
        pR = ctx.pp.get()
        psR.append(pR)
        prog.op("pe", lambda eng, s=s, pR=pR: eng.matmul(
            pR[:, 0:NB], ctx.c_ones_bc, rows[0:1, T + s * NB: T + (s + 1) * NB],
            start=True, stop=True),
            reads=[(rows.name, 1), "c_ones_bc"], writes=[pR], tag=tag + "bR")
        if not rms:
            pM = ctx.pp.get()
            psM.append(pM)
            prog.op("pe", lambda eng, s=s, pM=pM: eng.matmul(
                pM[:, 0:NB], ctx.c_ones_bc,
                rows[0:1, s * NB: (s + 1) * NB],
                start=True, stop=True),
                reads=[(rows.name, 0), "c_ones_bc"], writes=[pM], tag=tag + "bM")
    for c in range(nchunk):
        xap, xkey = get_chunk(c, 1)
        for s in range(ns):
            lo = c * T + s * NB
            hi = c * T + (s + 1) * NB
            if rms:
                prog.op("dve", lambda eng, xap=xap, lo=lo, hi=hi, s=s: eng.tensor_tensor(
                    h_out[:, lo:hi], xap[:, (lo - c * T):(hi - c * T)] if False else xap[:, s * NB:(s + 1) * NB], psR[s][:, 0:NB], ALU.mult),
                    reads=[xkey, psR[s]], writes=[(h_key, c)], tag=tag + "ap")
            else:
                s_ = sq[c % 2]
                prog.op("dve", lambda eng, xap=xap, s=s, s_=s_: eng.tensor_tensor(
                    s_[:, s * NB:(s + 1) * NB], xap[:, s * NB:(s + 1) * NB],
                    psR[s][:, 0:NB], ALU.mult),
                    reads=[xkey, psR[s]], writes=[s_], tag=tag + "a1")
                prog.op("dve", lambda eng, lo=lo, hi=hi, s=s, s_=s_: eng.tensor_tensor(
                    h_out[:, lo:hi], s_[:, s * NB:(s + 1) * NB], psM[s][:, 0:NB],
                    ALU.add),
                    reads=[s_, psM[s]], writes=[(h_key, c)], tag=tag + "a2")


def dram_chunk_loader(ctx, dram_row_fn, T, tag="xl", q="sp", nbuf=2):
    bufs = [ctx.sbuf(tag, (128, T)) for _ in range(nbuf)]
    bufs = (bufs * 2)[:2]
    cnt = [0]

    def get_chunk(c, passno):
        b = bufs[cnt[0] % 2]
        cnt[0] += 1
        ap, rkey = dram_row_fn(c)
        dma(ctx.prog, q, b[:, :], ap, reads=[rkey], writes=[b.name], tag=tag)
        return b[:, :], b.name
    return get_chunk


def emit_proj_tiles(ctx, wsb, M, x_sb, xkeys, kc, T, mtiles, out_cb, tag="pj",
                    extra_reads=()):
    prog = ctx.prog
    ns = T // NB
    for (m0, m1, info) in mtiles:
        pss = [ctx.pp.get() for _ in range(ns)]
        for k in range(kc):
            for s in range(ns):
                prog.op("pe", lambda eng, m0=m0, m1=m1, s=s, k=k, ps=pss[s]: eng.matmul(
                    ps[0:m1 - m0, 0:NB],
                    wsb[:, k * M + m0: k * M + m1],
                    x_sb[:, k * T + s * NB: k * T + (s + 1) * NB],
                    start=(k == 0), stop=(k == kc - 1)),
                    reads=[(wsb.name, k), (xkeys, k)] + list(extra_reads),
                    writes=[pss[s]], tag=tag + f"m{m0}s{s}")
        for s in range(ns):
            out_cb(info, m0, m1, s, pss[s])


def emit_mamba1(ctx, b, xin, xout, WxT_d, WoutT_d, Alog_d, D_d):
    nc, prog = ctx.nc, ctx.prog
    from contextlib import ExitStack
    T = SEQ
    NS = T // NB
    NCH = 8
    old_es = ctx.es

    rowA = ctx.sbuf("m1_rowA", (1, 16))
    dma(prog, "sp", rowA[:, :], Alog_d.rearrange("(o n) -> o n", o=1), reads=[Alog_d],
        writes=[rowA], tag="ldA")
    rowAe = ctx.sbuf("m1_rowAe", (1, 16))
    prog.op("act", lambda eng: eng.activation(rowAe[:, :], rowA[:, :], AF.Exp, scale=1.0),
            reads=[rowA], writes=[rowAe], tag="Aexp")
    psA = ctx.pp.get()
    prog.op("pe", lambda eng: eng.matmul(psA[0:128, 0:16], ctx.c_ones_bc,
                                         rowAe[0:1, 0:16], start=True, stop=True),
            reads=[rowAe, "c_ones_bc"], writes=[psA], tag="Abc")
    Acols = ctx.sbuf("m1_Acols", (128, 16))
    prog.op("dve", lambda eng: eng.tensor_scalar(Acols[:, :], psA[0:128, 0:16],
                                                 -1.0, None, ALU.mult),
            reads=[psA], writes=[Acols], tag="Aneg")
    Dcols = ctx.feat_cols(D_d, DM, tag="m1_D")

    es13 = ExitStack()
    ctx.es = es13
    h_bf = ctx.sbuf("m1_hbf", (128, NCH * T), BF16)
    es23 = ExitStack()
    ctx.es = es23
    dsp = ctx.sbuf("m1_dsp", (128, NCH * T), BF16)
    Bt = ctx.sbuf("m1_Bt", (16, T))
    Ct = ctx.sbuf("m1_Ct", (16, T))
    es34 = ExitStack()
    ctx.es = es34
    y_bf = ctx.sbuf("m1_ybf", (128, NCH * T), BF16)

    es1 = ExitStack()
    ctx.es = es1
    loader = dram_chunk_loader(
        ctx, lambda c: (xin[b, c * 128:(c + 1) * 128, :], (xin.tensor.name, b, c)),
        T, tag="m1xl")
    emit_ln_fm(ctx, loader, T, NCH, h_bf, "m1_h", tag=f"m1ln{b}")
    ctx.es = old_es
    prog.barrier()
    es1.close()

    es2 = ExitStack()
    ctx.es = es2
    wx = ctx.load_w_bf16(WxT_d, DM, 1056, tag="m1_wx")
    tmpb = [ctx.sbuf("m1_tmp", (128, NB)) for _ in range(2)]

    def evac_xp(info, m0, m1, s, ps):
        if info == "B":
            prog.op("dve", lambda eng, s=s, ps=ps: eng.tensor_copy(
                Bt[:, s * NB:(s + 1) * NB], ps[0:16, 0:NB]),
                reads=[ps], writes=[Bt], tag="evB")
        elif info == "C":
            prog.op("dve", lambda eng, s=s, ps=ps: eng.tensor_copy(
                Ct[:, s * NB:(s + 1) * NB], ps[0:16, 0:NB]),
                reads=[ps], writes=[Ct], tag="evC")
        else:
            r = info[1]
            t = tmpb[(r * NS + s) % 2]
            prog.op("act", lambda eng, t=t, ps=ps: eng.activation(
                t[:, :], ps[0:128, 0:NB], AF.Exp),
                reads=[ps], writes=[t], tag="evD1")
            prog.op("act", lambda eng, t=t, r=r, s=s: eng.activation(
                dsp[:, r * T + s * NB: r * T + (s + 1) * NB], t[:, :], AF.Ln,
                bias=scal_ap(ctx, 1.0)),
                reads=[t, "c_scal"], writes=[(dsp.name, r)], tag="evD2")

    mtiles = [(0, 16, "B"), (16, 32, "C")] + [(32 + 128 * r, 160 + 128 * r, ("d", r))
                                              for r in range(8)]
    emit_proj_tiles(ctx, wx, 1056, h_bf, "m1_h", NCH, T, mtiles, evac_xp, tag="xp")
    ctx.es = old_es
    prog.barrier()
    es2.close()

    es3 = ExitStack()
    ctx.es = es3
    dA_r = [ctx.sbuf("m1_dA", (128, T)) for _ in range(2)]
    u_r = [ctx.sbuf("m1_u", (128, T)) for _ in range(2)]
    Sb = [ctx.sbuf("m1_S", (128, T))] * 2
    tmb = [ctx.sbuf("m1_tm", (128, T))] * 2
    dcf = dA_r[0]
    hcf = ctx.sbuf("m1_hcf", (128, T))
    wcf = ctx.sbuf("m1_wcf", (128, T))
    ya = ctx.sbuf("m1_ya", (128, T))
    yb = ctx.sbuf("m1_yb", (128, T))
    for c in range(NCH):
        prog.op("act", lambda eng, c=c: eng.activation(
            hcf[:, :], h_bf[:, c * T:(c + 1) * T], AF.Copy),
            reads=[("m1_h", c)], writes=[hcf], tag="hcf")
        prog.op("act", lambda eng, c=c: eng.activation(
            dcf[:, :], dsp[:, c * T:(c + 1) * T], AF.Copy),
            reads=[(dsp.name, c)], writes=[dcf], tag="dcf")
        prog.op("dve", lambda eng: eng.tensor_tensor(wcf[:, :], dcf[:, :], hcf[:, :],
                                                     ALU.mult),
                reads=[dcf, hcf], writes=[wcf], tag="wcf")
        for n in range(16):
            dA = dA_r[n % 2]
            u = u_r[n % 2]
            S = Sb[n % 2]
            tm = tmb[n % 2]
            prog.op("act", lambda eng, c=c, n=n, dA=dA: eng.activation(
                dA[:, :], dsp[:, c * T:(c + 1) * T], AF.Exp,
                scale=Acols[:, n:n + 1]),
                reads=[(dsp.name, c), Acols], writes=[dA], tag="dAe")
            for s in range(NS):
                psB = ctx.pp.get()
                prog.op("pe", lambda eng, n=n, s=s, psB=psB: eng.matmul(
                    psB[0:128, 0:NB], ctx.c_sel16[:, n * 128:(n + 1) * 128],
                    Bt[0:16, s * NB:(s + 1) * NB], start=True, stop=True),
                    reads=[Bt, "c_sel16"], writes=[psB], tag="bcB")
                prog.op("dve", lambda eng, s=s, psB=psB, u=u: eng.tensor_tensor(
                    u[:, s * NB:(s + 1) * NB], wcf[:, s * NB:(s + 1) * NB],
                    psB[:, 0:NB], ALU.mult),
                    reads=[wcf, psB], writes=[u], tag="ubld")
            prog.op("dve", lambda eng, S=S, dA=dA, u=u: eng.tensor_tensor_scan(
                S[:, :], dA[:, :], u[:, :], 0.0, ALU.mult, ALU.add),
                reads=[dA, u], writes=[S], tag="scan")
            yt = ya if (n % 2 == 0) else yb
            for s in range(NS):
                psC = ctx.pp.get()
                prog.op("pe", lambda eng, n=n, s=s, psC=psC: eng.matmul(
                    psC[0:128, 0:NB], ctx.c_sel16[:, n * 128:(n + 1) * 128],
                    Ct[0:16, s * NB:(s + 1) * NB], start=True, stop=True),
                    reads=[Ct, "c_sel16"], writes=[psC], tag="bcC")
                dst = yt if n < 2 else tm
                prog.op("dve", lambda eng, s=s, psC=psC, S=S, dst=dst: eng.tensor_tensor(
                    dst[:, s * NB:(s + 1) * NB], S[:, s * NB:(s + 1) * NB],
                    psC[:, 0:NB], ALU.mult),
                    reads=[S, psC], writes=[dst], tag="ymul")
            if n >= 2:
                e = "dve" if (n % 2 == 0) else "pool"
                prog.op(e, lambda eng, tm=tm, yt=yt: eng.tensor_tensor(
                    yt[:, :], yt[:, :], tm[:, :], ALU.add),
                    reads=[tm, yt], writes=[yt], tag="yacc")
        prog.op("dve", lambda eng: eng.tensor_tensor(ya[:, :], ya[:, :], yb[:, :],
                                                     ALU.add),
                reads=[ya, yb], writes=[ya], tag="yab")
        prog.op("dve", lambda eng, c=c: eng.scalar_tensor_tensor(
            y_bf[:, c * T:(c + 1) * T], hcf[:, :], Dcols[:, c:c + 1], ya[:, :],
            ALU.mult, ALU.add),
            reads=[hcf, Dcols, ya], writes=[(y_bf.name, c)], tag="yfin")
    ctx.es = old_es
    prog.barrier()
    es3.close()

    es4 = ExitStack()
    ctx.es = es4
    wo = ctx.load_w_bf16(WoutT_d, DM, DM, tag="m1_wo")
    xre = [ctx.sbuf("m1_xre", (128, T)) for _ in range(2)]
    xo = [ctx.sbuf("m1_xo", (128, T)) for _ in range(2)]
    for m in range(8):
        xr = xre[m % 2]
        xw = xo[m % 2]
        dma(prog, "sp", xr[:, :], xin[b, m * 128:(m + 1) * 128, :],
            reads=[(xin.tensor.name, b, m)], writes=[xr.name], tag="rex")
        for s in range(NS):
            ps = ctx.pp.get()
            for k in range(8):
                prog.op("pe", lambda eng, m=m, s=s, k=k, ps=ps: eng.matmul(
                    ps[0:128, 0:NB], wo[:, k * DM + m * 128: k * DM + (m + 1) * 128],
                    y_bf[:, k * T + s * NB: k * T + (s + 1) * NB],
                    start=(k == 0), stop=(k == 7)),
                    reads=[(wo.name, k), (y_bf.name, k)], writes=[ps], tag="wom")
            prog.op("dve", lambda eng, s=s, ps=ps, xr=xr, xw=xw, m=m: eng.scalar_tensor_tensor(
                xw[:, s * NB:(s + 1) * NB], ps[0:128, 0:NB], 1.0,
                xr[:, s * NB:(s + 1) * NB], ALU.mult, ALU.add),
                reads=[ps, xr.name], writes=[xw.name], tag="resid")
        dma(prog, "sp", xout[b, m * 128:(m + 1) * 128, :], xw[:, :],
            reads=[xw.name], writes=[(xout.tensor.name, b, m)], tag="stx")
    ctx.es = old_es
    prog.barrier()
    es4.close()
    es34.close()
    es23.close()
    es13.close()



def _sel(n, w):
    s = np.zeros((n, n * w), np.float32)
    for i in range(n):
        s[i, i * w:(i + 1) * w] = 1.0
    return s


def const_arrays():
    sel16 = _sel(16, 128)
    return {
        "c_ones_red": np.ones((128, 1), np.float32),
        "c_ones_bc": np.ones((1, 128), np.float32),
        "c_ident": np.eye(128, dtype=np.float32),
        "c_sel16": sel16,
        "c_scal": np.tile(np.array([[1.0, 1e-5, 1e-6, 1e-30]], np.float32), (128, 1)),
        "c_ones_red_bf": np.ones((128, 1), ml_dtypes.bfloat16),
        "c_rev8": np.tile(np.arange(7, -1, -1, dtype=np.float32)[None, :], (128, 1)),
        "c_sel32": _sel(32, 128),
        "c_trimask": np.where(np.arange(128)[:, None] <= np.arange(128)[None, :],
                              0.0, -1e30).astype(np.float32),
        "c_dm_l1": np.tile((np.arange(128) < 64).astype(np.float32)[None, :], (32, 1)),
        "c_dm_l2": np.tile((np.arange(128) >= 64).astype(np.float32)[None, :], (32, 1)),
        "c_dm_m1": (np.arange(32)[:, None] == 2 * np.arange(16)[None, :]).astype(np.float32),
        "c_dm_m2": (np.arange(32)[:, None] == 2 * np.arange(16)[None, :] + 1).astype(np.float32),
        "c_ident_bf": np.eye(128, dtype=ml_dtypes.bfloat16),
        "c_cmask": np.concatenate(
            [np.where(np.arange(128)[:, None] + 128 * i <= np.arange(512)[None, :],
                      0.0, -1e30).astype(np.float32) for i in range(4)], axis=1),
    }


BASE_CONSTS = ("c_ones_red", "c_ones_bc", "c_ident", "c_scal")


def declare_consts(ctx, extra=()):
    nc, prog, es = ctx.nc, ctx.prog, ctx.es
    arrs = const_arrays()
    want = set(BASE_CONSTS) | set(extra)
    arrs = {k: v for k, v in arrs.items() if k in want}
    for name, arr in arrs.items():
        dt = BF16 if arr.dtype == ml_dtypes.bfloat16 else F32
        d = nc.dram_tensor(name + "_d", list(arr.shape), dt, kind="ExternalInput").ap()
        t = sb(nc, es, name, list(arr.shape), dt)
        dma(prog, "sp", t[:, :], d, reads=[d], writes=[name], tag="ldc")
        setattr(ctx, name, t[:, :])
    ctx.pp = PsumPool(nc, es, n=8)


SCAL_IDX = {1.0: 0, 1e-5: 1, 1e-6: 2, 1e-30: 3}


def scal_ap(ctx, v, p=128):
    i = SCAL_IDX[v]
    return ctx.c_scal[0:p, i:i + 1]


def row_bcast_sb(ctx, row_sb, W, tag="rb"):
    prog = ctx.prog
    out = ctx.sbuf(tag, (128, W))
    for s in range(0, W, NB):
        w = min(NB, W - s)
        ps = ctx.pp.get()
        prog.op("pe", lambda eng, s=s, w=w, ps=ps: eng.matmul(
            ps[:, 0:w], ctx.c_ones_bc, row_sb[0:1, s:s + w], start=True, stop=True),
            reads=[row_sb, "c_ones_bc"], writes=[ps], tag=tag + "m")
        prog.op("dve", lambda eng, s=s, w=w, ps=ps: eng.tensor_copy(
            out[:, s:s + w], ps[:, 0:w]),
            reads=[ps], writes=[out], tag=tag + "e")
    return out


def load_row(ctx, vec_d, n, tag="lr"):
    row = ctx.sbuf(tag, (1, n))
    dma(ctx.prog, "sp", row[:, :], vec_d.rearrange("(o n) -> o n", o=1),
        reads=[vec_d], writes=[row], tag=tag)
    return row


def emit_out_proj(ctx, b, W_d, bias_cols, y_bf, ykey, xin, xout, kc, T, tag="op"):
    prog = ctx.prog
    from contextlib import ExitStack
    es4 = ExitStack()
    old = ctx.es
    ctx.es = es4
    wo = ctx.load_w_bf16(W_d, kc * 128, DM, tag=tag + "w")
    xre = [ctx.sbuf(tag + "xr", (128, T)) for _ in range(2)]
    xo = [ctx.sbuf(tag + "xo", (128, T)) for _ in range(2)]
    NS = T // NB
    for m in range(8):
        xr = xre[m % 2]
        xw = xo[m % 2]
        dma(prog, "sp", xr[:, :], xin[b, m * 128:(m + 1) * 128, :],
            reads=[(xin.tensor.name, b, m)], writes=[xr.name], tag=tag + "rex")
        pss = [ctx.pp.get() for _ in range(NS)]
        for k in range(kc):
            for s in range(NS):
                prog.op("pe", lambda eng, m=m, s=s, k=k, ps=pss[s]: eng.matmul(
                    ps[0:128, 0:NB], wo[:, k * DM + m * 128: k * DM + (m + 1) * 128],
                    y_bf[:, k * T + s * NB: k * T + (s + 1) * NB],
                    start=(k == 0), stop=(k == kc - 1)),
                    reads=[(wo.name, k), (ykey, k)], writes=[pss[s]], tag=tag + "mm")
        for s in range(NS):
            ps = pss[s]
            if bias_cols is not None:
                tmp = ctx.pp.get()
                prog.op("dve", lambda eng, s=s, ps=ps, xr=xr, m=m: eng.scalar_tensor_tensor(
                    ps[0:128, 0:NB], ps[0:128, 0:NB], bias_cols[:, m:m + 1],
                    xr[:, s * NB:(s + 1) * NB], ALU.add, ALU.add),
                    reads=[ps, bias_cols, xr.name], writes=[ps], tag=tag + "bi")
                prog.op("dve", lambda eng, s=s, ps=ps, xw=xw: eng.tensor_copy(
                    xw[:, s * NB:(s + 1) * NB], ps[0:128, 0:NB]),
                    reads=[ps], writes=[xw.name], tag=tag + "cp")
            else:
                prog.op("dve", lambda eng, s=s, ps=ps, xr=xr, xw=xw: eng.scalar_tensor_tensor(
                    xw[:, s * NB:(s + 1) * NB], ps[0:128, 0:NB], 1.0,
                    xr[:, s * NB:(s + 1) * NB], ALU.mult, ALU.add),
                    reads=[ps, xr.name], writes=[xw.name], tag=tag + "res")
        dma(prog, "sp", xout[b, m * 128:(m + 1) * 128, :], xw[:, :],
            reads=[xw.name], writes=[(xout.tensor.name, b, m)], tag=tag + "stx")
    ctx.es = old
    prog.barrier()
    es4.close()


def emit_attn0(ctx, b, xin, xout, wqT_d, bq_d, wkT_d, bk_d, wvT_d, bv_d, woT_d,
               bo_d):
    nc, prog = ctx.nc, ctx.prog
    from contextlib import ExitStack
    T = SEQ
    NS = T // NB
    NCH = 8
    NH = 16
    old = ctx.es

    es_a = ExitStack()
    ctx.es = es_a
    Qt = ctx.sbuf("a0_Q", (128, NCH * T), BF16)
    Kt = ctx.sbuf("a0_K", (128, NCH * T), BF16)
    Vt = ctx.sbuf("a0_V", (128, 16 * 1024), BF16)
    Yt = ctx.sbuf("a0_Y", (128, NCH * T), BF16)
    bqc = ctx.feat_cols(bq_d, DM, tag="a0bq")
    bkc = ctx.feat_cols(bk_d, DM, tag="a0bk")
    boc = ctx.feat_cols(bo_d, DM, tag="a0bo")

    es1 = ExitStack()
    ctx.es = es1
    h_bf = ctx.sbuf("a0_h", (128, NCH * T), BF16)
    es1b = ExitStack()
    ctx.es = es1b
    loader = dram_chunk_loader(
        ctx, lambda c: (xin[b, c * 128:(c + 1) * 128, :], (xin.tensor.name, b, c)),
        T, tag="a0xl", nbuf=1)
    emit_ln_fm(ctx, loader, T, NCH, h_bf, "a0_hk", tag=f"a0ln{b}", nsq=1)
    prog.barrier()
    es1b.close()

    es2 = ExitStack()
    ctx.es = es2
    for which, wd, bc, out_t, okey in (("q", wqT_d, bqc, Qt, "a0_Qk"),
                                       ("k", wkT_d, bkc, Kt, "a0_Kk")):
        ws = ctx.load_w_bf16(wd, DM, DM, tag=f"a0w{which}")

        def evac(info, m0, m1, s, ps, bc=bc, out_t=out_t, okey=okey):
            m = m0 // 128
            prog.op("dve", lambda eng, m=m, s=s, ps=ps: eng.tensor_scalar(
                out_t[:, m * T + s * NB: m * T + (s + 1) * NB], ps[0:128, 0:NB],
                bc[:, m:m + 1], None, ALU.add),
                reads=[ps, bc], writes=[(okey, m)], tag="a0ev" + which)
        emit_proj_tiles(ctx, ws, DM, h_bf, "a0_hk", NCH, T,
                        [(m * 128, (m + 1) * 128, m) for m in range(8)], evac,
                        tag="a0p" + which)
    prog.barrier()
    es2.close()
    es2 = ExitStack()
    ctx.es = es2
    bvrow = load_row(ctx, bv_d, DM, tag="a0bvr")
    bvb = row_bcast_sb(ctx, bvrow, DM, tag="a0bvb")
    wv = ctx.load_w_bf16(wvT_d, DM, DM, tag="a0wv")
    for tt in range(16):
        for s in range(2):
            ps = ctx.pp.get()
            for k in range(NCH):
                prog.op("pe", lambda eng, tt=tt, s=s, k=k, ps=ps: eng.matmul(
                    ps[0:128, 0:NB],
                    h_bf[:, k * T + tt * 128: k * T + (tt + 1) * 128],
                    wv[:, k * DM + s * NB: k * DM + (s + 1) * NB],
                    start=(k == 0), stop=(k == NCH - 1)),
                    reads=[("a0_hk", k), (wv.name, k)], writes=[ps], tag="a0pv")
            prog.op("dve", lambda eng, tt=tt, s=s, ps=ps: eng.scalar_tensor_tensor(
                Vt[:, tt * 1024 + s * NB: tt * 1024 + (s + 1) * NB],
                ps[0:128, 0:NB], 1.0, bvb[:, s * NB:(s + 1) * NB],
                ALU.mult, ALU.add),
                reads=[ps, bvb], writes=[("a0_Vk", tt)], tag="a0evv")
    ctx.es = old
    prog.barrier()
    es2.close()
    es1.close()

    es3 = ExitStack()
    ctx.es = es3
    Pt = ctx.sbuf("a0_P", (128, 16 * NB), BF16)
    srow = ctx.sbuf("a0_srow", (1, NB))
    rrow = ctx.sbuf("a0_rrow", (1, NB))
    yev = ctx.sbuf("a0_yev", (64, NB))
    qk_tiles = ctx.pp.tiles[0:5]
    t_sum = ctx.pp.tiles[5]
    t_r = ctx.pp.tiles[6]
    t_y = ctx.pp.tiles[7]
    qkc = [0]
    for h in range(NH):
        hb = (h % 2) * 64
        hc = h // 2
        for qb in range(NS):
            for kt in range(16):
                ps = qk_tiles[qkc[0] % 5]
                qkc[0] += 1
                prog.op("pe", lambda eng, h=h, qb=qb, kt=kt, ps=ps, hb=hb, hc=hc: eng.matmul(
                    ps[0:128, 0:NB],
                    Kt[hb:hb + 64, hc * T + kt * 128: hc * T + (kt + 1) * 128],
                    Qt[hb:hb + 64, hc * T + qb * NB: hc * T + (qb + 1) * NB],
                    start=True, stop=True),
                    reads=[("a0_Kk", hc), ("a0_Qk", hc)], writes=[ps], tag="a0qk")
                prog.op("act", lambda eng, kt=kt, ps=ps: eng.activation(
                    Pt[:, kt * NB:(kt + 1) * NB], ps[0:128, 0:NB], AF.Exp,
                    scale=0.25),
                    reads=[ps], writes=[(Pt.name, kt)], tag="a0ex")
            ps_sum = t_sum
            for kt in range(16):
                prog.op("pe", lambda eng, kt=kt: eng.matmul(
                    ps_sum[0:1, 0:NB], ctx.c_ones_red_bf,
                    Pt[:, kt * NB:(kt + 1) * NB],
                    start=(kt == 0), stop=(kt == 15)),
                    reads=[(Pt.name, kt), "c_ones_red_bf"], writes=[ps_sum],
                    tag="a0sm")
            prog.op("dve", lambda eng: eng.tensor_copy(srow[:, :], ps_sum[0:1, 0:NB]),
                    reads=[ps_sum], writes=[srow], tag="a0se")
            prog.op("dve", lambda eng: eng.reciprocal(rrow[:, :], srow[:, :]),
                    reads=[srow], writes=[rrow], tag="a0rc")
            ps_r = t_r
            prog.op("pe", lambda eng, ps_r=ps_r: eng.matmul(
                ps_r[:, 0:NB], ctx.c_ones_bc, rrow[0:1, 0:NB], start=True, stop=True),
                reads=[rrow, "c_ones_bc"], writes=[ps_r], tag="a0rb")
            ps_y = t_y
            for kt in range(16):
                prog.op("pe", lambda eng, h=h, kt=kt, ps_y=ps_y: eng.matmul(
                    ps_y[0:64, 0:NB], Vt[:, kt * 1024 + h * 64: kt * 1024 + (h + 1) * 64],
                    Pt[:, kt * NB:(kt + 1) * NB],
                    start=(kt == 0), stop=(kt == 15)),
                    reads=[("a0_Vk", kt), (Pt.name, kt)], writes=[ps_y], tag="a0pv2")
            prog.op("dve", lambda eng, ps_y=ps_y: eng.tensor_copy(
                yev[:, :], ps_y[0:64, 0:NB]),
                reads=[ps_y], writes=[yev], tag="a0ye")
            prog.op("dve", lambda eng, h=h, qb=qb, ps_r=ps_r, hb=hb, hc=hc: eng.tensor_tensor(
                Yt[hb:hb + 64, hc * T + qb * NB: hc * T + (qb + 1) * NB],
                yev[:, :], ps_r[0:64, 0:NB], ALU.mult),
                reads=[yev, ps_r], writes=[("a0_Yk", hc)], tag="a0yw")
    ctx.es = old
    prog.barrier()
    es3.close()

    emit_out_proj(ctx, b, woT_d, boc, Yt, "a0_Yk", xin, xout, NCH, T, tag="a0o")
    es_a.close()


def emit_router(ctx, b, xin, h_out_d, rw_d, rb_d, gate_d, idx_d, tag="rt"):
    nc, prog = ctx.nc, ctx.prog
    from contextlib import ExitStack
    T = SEQ
    NCH = 8
    old = ctx.es
    es = ExitStack()
    ctx.es = es
    h_bf = ctx.sbuf(tag + "_h", (128, NCH * T), BF16)
    es1 = ExitStack()
    ctx.es = es1
    loader = dram_chunk_loader(
        ctx, lambda c: (xin[b, c * 128:(c + 1) * 128, :], (xin.tensor.name, b, c)),
        T, tag=tag + "xl", nbuf=1)
    emit_ln_fm(ctx, loader, T, NCH, h_bf, tag + "_hk", tag=tag + f"ln{b}", nsq=1)
    prog.barrier()
    es1.close()
    es2 = ExitStack()
    ctx.es = es2
    for c in range(NCH):
        dma(prog, "pool", h_out_d[b, c * 128:(c + 1) * 128, :],
            h_bf[:, c * T:(c + 1) * T],
            reads=[(tag + "_hk", c)], writes=[(h_out_d.tensor.name, b, c)],
            tag=tag + "sth")
    rw = ctx.load_w_bf16(rw_d, DM, 8, tag=tag + "rw")
    rbrow = load_row(ctx, rb_d, 8, tag=tag + "rb")
    psb = ctx.pp.get()
    prog.op("pe", lambda eng: eng.matmul(psb[0:128, 0:8], ctx.c_ones_bc,
                                         rbrow[0:1, 0:8], start=True, stop=True),
            reads=[rbrow, "c_ones_bc"], writes=[psb], tag=tag + "rbb")
    rbb = ctx.sbuf(tag + "rbbs", (128, 8))
    prog.op("dve", lambda eng: eng.tensor_copy(rbb[:, :], psb[0:128, 0:8]),
            reads=[psb], writes=[rbb], tag=tag + "rbe")
    lg = ctx.sbuf(tag + "lg", (128, 8))
    m1 = ctx.sbuf(tag + "m", (128, 1))
    ex = ctx.sbuf(tag + "ex", (128, 8))
    srow = ctx.sbuf(tag + "sr", (128, 1))
    grow = ctx.sbuf(tag + "g", (128, 1))
    msk = ctx.sbuf(tag + "msk", (128, 8))
    irow = ctx.sbuf(tag + "i", (128, 1))
    for tt in range(16):
        ps = ctx.pp.get()
        for k in range(NCH):
            prog.op("pe", lambda eng, tt=tt, k=k, ps=ps: eng.matmul(
                ps[0:128, 0:8],
                h_bf[:, k * T + tt * 128: k * T + (tt + 1) * 128],
                rw[:, k * 8:(k + 1) * 8],
                start=(k == 0), stop=(k == NCH - 1)),
                reads=[(tag + "_hk", k), (rw.name, k)], writes=[ps], tag=tag + "lgm")
        prog.op("dve", lambda eng, ps=ps: eng.tensor_tensor(
            lg[:, :], ps[0:128, 0:8], rbb[:, :], ALU.add),
            reads=[ps, rbb], writes=[lg], tag=tag + "lge")
        prog.op("dve", lambda eng: eng.tensor_reduce(m1[:, :], lg[:, :], AX.X, ALU.max),
                reads=[lg], writes=[m1], tag=tag + "mx")
        prog.op("dve", lambda eng: eng.tensor_scalar(ex[:, :], lg[:, :], m1[:, 0:1],
                                                     None, ALU.subtract),
                reads=[lg, m1], writes=[ex], tag=tag + "sub")
        prog.op("act", lambda eng: eng.activation(ex[:, :], ex[:, :], AF.Exp),
                reads=[ex], writes=[ex], tag=tag + "exp")
        prog.op("dve", lambda eng: eng.tensor_reduce(srow[:, :], ex[:, :], AX.X,
                                                     ALU.add),
                reads=[ex], writes=[srow], tag=tag + "sm")
        prog.op("dve", lambda eng: eng.reciprocal(grow[:, :], srow[:, :]),
                reads=[srow], writes=[grow], tag=tag + "rc")
        prog.op("dve", lambda eng: eng.tensor_scalar(msk[:, :], lg[:, :], m1[:, 0:1],
                                                     None, ALU.is_ge),
                reads=[lg, m1], writes=[msk], tag=tag + "eq")
        prog.op("dve", lambda eng: eng.tensor_tensor(msk[:, :], msk[:, :],
                                                     ctx.c_rev8, ALU.mult),
                reads=[msk, "c_rev8"], writes=[msk], tag=tag + "rv")
        prog.op("dve", lambda eng: eng.tensor_reduce(irow[:, :], msk[:, :], AX.X,
                                                     ALU.max),
                reads=[msk], writes=[irow], tag=tag + "am")
        prog.op("dve", lambda eng: eng.tensor_scalar(irow[:, :], irow[:, :], -1.0,
                                                     7.0, ALU.mult, ALU.add),
                reads=[irow], writes=[irow], tag=tag + "rv2")
        dma(prog, "sp", gate_d[b, tt, :].rearrange("(p o) -> p o", o=1), grow[:, :],
            reads=[grow], writes=[(gate_d.tensor.name, b, tt)], tag=tag + "stg")
        dma(prog, "sp", idx_d[b, tt, :].rearrange("(p o) -> p o", o=1), irow[:, :],
            reads=[irow], writes=[(idx_d.tensor.name, b, tt)], tag=tag + "sti")
    ctx.es = old
    prog.barrier()
    es2.close()
    es.close()


def emit_expert_l1(ctx, Xg_d, w_d, bias_d, Yg_d, nslot, tag="e1"):
    nc, prog = ctx.nc, ctx.prog
    from contextlib import ExitStack
    old = ctx.es
    es = ExitStack()
    ctx.es = es
    NSL = nslot // NB
    xg = ctx.sbuf(tag + "x", (128, 8 * nslot), BF16)
    for c in range(8):
        dma(prog, "pool", xg[:, c * nslot:(c + 1) * nslot],
            Xg_d[c * 128:(c + 1) * 128, :], reads=[(Xg_d.tensor.name, c)],
            writes=[(xg.name, c)], tag=tag + "lx")
    ws = ctx.load_w_bf16(w_d, DM, DM, tag=tag + "w")
    bc = ctx.feat_cols(bias_d, DM, tag=tag + "b")
    yo = [ctx.sbuf(tag + "y", (128, NB)) for _ in range(2)]
    for m in range(8):
        for s in range(NSL):
            ps = ctx.pp.get()
            for k in range(8):
                prog.op("pe", lambda eng, m=m, s=s, k=k, ps=ps: eng.matmul(
                    ps[0:128, 0:NB], ws[:, k * DM + m * 128: k * DM + (m + 1) * 128],
                    xg[:, k * nslot + s * NB: k * nslot + (s + 1) * NB],
                    start=(k == 0), stop=(k == 7)),
                    reads=[(ws.name, k), (xg.name, k)], writes=[ps], tag=tag + "mm")
            y = yo[(m * NSL + s) % 2]
            prog.op("dve", lambda eng, ps=ps, y=y, m=m: eng.tensor_scalar(
                y[:, :], ps[0:128, 0:NB], bc[:, m:m + 1], None, ALU.add),
                reads=[ps, bc], writes=[y.name], tag=tag + "ev")
            dma(prog, "sp", Yg_d[m * 128:(m + 1) * 128, s * NB:(s + 1) * NB],
                y[:, :], reads=[y.name], writes=[(Yg_d.tensor.name, m, s)],
                tag=tag + "st")
    ctx.es = old
    prog.barrier()
    es.close()


def emit_expert_l0(ctx, Xg_d, w1_d, b1_d, w2_d, b2_d, Yg_d, nslot, tag="e0"):
    nc, prog = ctx.nc, ctx.prog
    from contextlib import ExitStack
    old = ctx.es
    es = ExitStack()
    ctx.es = es
    EXP = 4096
    xg = ctx.sbuf(tag + "x", (128, 8 * nslot), BF16)
    for c in range(8):
        dma(prog, "pool", xg[:, c * nslot:(c + 1) * nslot],
            Xg_d[c * 128:(c + 1) * 128, :], reads=[(Xg_d.tensor.name, c)],
            writes=[(xg.name, c)], tag=tag + "lx")
    w1 = ctx.load_w_bf16(w1_d, DM, EXP, tag=tag + "w1")
    w2 = ctx.load_w_bf16(w2_d, EXP, DM, tag=tag + "w2")
    b1c = ctx.feat_cols(b1_d, EXP, tag=tag + "b1")
    b2c = ctx.feat_cols(b2_d, DM, tag=tag + "b2")
    h1 = ctx.sbuf(tag + "h1", (128, 32 * NB), BF16)
    yo = [ctx.sbuf(tag + "y", (128, NB)) for _ in range(2)]
    for s in range(nslot // NB):
        for m in range(32):
            ps = ctx.pp.get()
            for k in range(8):
                prog.op("pe", lambda eng, m=m, s=s, k=k, ps=ps: eng.matmul(
                    ps[0:128, 0:NB], w1[:, k * EXP + m * 128: k * EXP + (m + 1) * 128],
                    xg[:, k * nslot + s * NB: k * nslot + (s + 1) * NB],
                    start=(k == 0), stop=(k == 7)),
                    reads=[(w1.name, k), (xg.name, k)], writes=[ps], tag=tag + "m1")
            prog.op("act", lambda eng, m=m, ps=ps: eng.activation(
                h1[:, m * NB:(m + 1) * NB], ps[0:128, 0:NB], AF.Relu,
                bias=b1c[:, m:m + 1]),
                reads=[ps, b1c], writes=[(h1.name, m)], tag=tag + "rl")
        for m in range(8):
            ps = ctx.pp.get()
            for k in range(32):
                prog.op("pe", lambda eng, m=m, k=k, ps=ps: eng.matmul(
                    ps[0:128, 0:NB], w2[:, k * DM + m * 128: k * DM + (m + 1) * 128],
                    h1[:, k * NB:(k + 1) * NB],
                    start=(k == 0), stop=(k == 31)),
                    reads=[(w2.name, k), (h1.name, k)], writes=[ps], tag=tag + "m2")
            y = yo[m % 2]
            prog.op("dve", lambda eng, ps=ps, y=y, m=m: eng.tensor_scalar(
                y[:, :], ps[0:128, 0:NB], b2c[:, m:m + 1], None, ALU.add),
                reads=[ps, b2c], writes=[y.name], tag=tag + "ev")
            dma(prog, "sp", Yg_d[m * 128:(m + 1) * 128, s * NB:(s + 1) * NB],
                y[:, :], reads=[y.name], writes=[(Yg_d.tensor.name, m, s)],
                tag=tag + "st")
    ctx.es = old
    prog.barrier()
    es.close()


def emit_combine(ctx, b, xin, xout, Ysel_d, gk_d, mode, tag="cb"):
    nc, prog = ctx.nc, ctx.prog
    from contextlib import ExitStack
    T = SEQ
    NCH = 8
    old = ctx.es
    es = ExitStack()
    ctx.es = es
    hf = None
    if mode == "keep":
        hf = ctx.sbuf(tag + "_hf", (128, NCH * T))
        es1 = ExitStack()
        ctx.es = es1
        loader = dram_chunk_loader(
            ctx, lambda c: (xin[b, c * 128:(c + 1) * 128, :],
                            (xin.tensor.name, b, c)), T, tag=tag + "xl", nbuf=1)
        emit_ln_fm(ctx, loader, T, NCH, hf, tag + "_hfk", tag=tag + f"ln{b}", nsq=1)
        prog.barrier()
        es1.close()
        ctx.es = es
    grow = ctx.sbuf(tag + "g", (1, T))
    dma(prog, "sp", grow[:, :], gk_d[b, :, :], reads=[(gk_d.tensor.name, b)],
        writes=[grow], tag=tag + "lg")
    xr = [ctx.sbuf(tag + "xr", (128, T)) for _ in range(2)]
    yr = [ctx.sbuf(tag + "yr", (128, T)) for _ in range(2)]
    xw = [ctx.sbuf(tag + "xw", (128, T)) for _ in range(2)]
    gb = []
    for s in range(T // NB):
        pg = ctx.pp.get()
        bcast_row(ctx.prog, ctx.c_ones_bc, grow, pg, s * NB, (s + 1) * NB,
                  grow.name, pg.name, tag=tag + "gb")
        gb.append(pg)
    for c in range(NCH):
        x_ = xr[c % 2]
        y_ = yr[c % 2]
        w_ = xw[c % 2]
        dma(prog, "sp", x_[:, :], xin[b, c * 128:(c + 1) * 128, :],
            reads=[(xin.tensor.name, b, c)], writes=[x_.name], tag=tag + "lx")
        dma(prog, "pool", y_[:, :], Ysel_d[b, c * 128:(c + 1) * 128, :],
            reads=[(Ysel_d.tensor.name, b, c)], writes=[y_.name], tag=tag + "ly")
        for s in range(T // NB):
            sl = slice(s * NB, (s + 1) * NB)
            if mode == "gate":
                prog.op("dve", lambda eng, sl=sl, s=s, x_=x_, y_=y_, w_=w_: eng.tensor_tensor(
                    w_[:, sl], y_[:, sl], gb[s][:, 0:NB], ALU.mult),
                    reads=[y_.name, gb[s]], writes=[w_.name], tag=tag + "g1")
                prog.op("dve", lambda eng, sl=sl, x_=x_, w_=w_: eng.tensor_tensor(
                    w_[:, sl], w_[:, sl], x_[:, sl], ALU.add),
                    reads=[w_.name, x_.name], writes=[w_.name], tag=tag + "g2")
            else:
                prog.op("dve", lambda eng, sl=sl, s=s, c=c, y_=y_, w_=w_: eng.tensor_tensor(
                    w_[:, sl], y_[:, sl],
                    hf[:, c * T + s * NB: c * T + (s + 1) * NB], ALU.subtract),
                    reads=[y_.name, (tag + "_hfk", c)], writes=[w_.name],
                    tag=tag + "se")
                prog.op("dve", lambda eng, sl=sl, s=s, w_=w_: eng.tensor_tensor(
                    w_[:, sl], w_[:, sl], gb[s][:, 0:NB], ALU.mult),
                    reads=[w_.name, gb[s]], writes=[w_.name], tag=tag + "s1b")
                prog.op("dve", lambda eng, sl=sl, s=s, c=c, w_=w_: eng.tensor_tensor(
                    w_[:, sl], w_[:, sl],
                    hf[:, c * T + s * NB: c * T + (s + 1) * NB], ALU.add),
                    reads=[w_.name, (tag + "_hfk", c)], writes=[w_.name],
                    tag=tag + "s1c")
                prog.op("dve", lambda eng, sl=sl, x_=x_, w_=w_: eng.tensor_tensor(
                    w_[:, sl], w_[:, sl], x_[:, sl], ALU.add),
                    reads=[w_.name, x_.name], writes=[w_.name], tag=tag + "s2")
        dma(prog, "sp", xout[b, c * 128:(c + 1) * 128, :], w_[:, :],
            reads=[w_.name], writes=[(xout.tensor.name, b, c)], tag=tag + "st")
    ctx.es = old
    prog.barrier()
    es.close()


def emit_s5(ctx, xin, xout, logdt_d, logLre_d, Lim_d, Vre_d, Vim_d, Wre_d, Wim_d,
            SL, tag="s5"):
    nc, prog = ctx.nc, ctx.prog
    from contextlib import ExitStack
    NCH = 8
    TB = 16
    PI = float(np.pi)
    old = ctx.es
    es = ExitStack()
    ctx.es = es

    r_dt = load_row(ctx, logdt_d, 64, tag=tag + "dt")
    prog.op("act", lambda eng: eng.activation(r_dt[:, :], r_dt[:, :], AF.Exp),
            reads=[r_dt], writes=[r_dt], tag=tag + "e1")
    r_lre = load_row(ctx, logLre_d, 64, tag=tag + "lr")
    prog.op("act", lambda eng: eng.activation(r_lre[:, :], r_lre[:, :], AF.Exp),
            reads=[r_lre], writes=[r_lre], tag=tag + "e2")
    r_lim = load_row(ctx, Lim_d, 64, tag=tag + "li")
    rphi = ctx.sbuf(tag + "phi", (1, 64))
    rth = ctx.sbuf(tag + "th", (1, 64))
    prog.op("dve", lambda eng: eng.scalar_tensor_tensor(
        rphi[:, :], r_lre[:, :], -1.0, r_dt[:, :], ALU.mult, ALU.mult),
        reads=[r_lre, r_dt], writes=[rphi], tag=tag + "ph")
    prog.op("dve", lambda eng: eng.tensor_tensor(rth[:, :], r_lim[:, :], r_dt[:, :],
                                                 ALU.mult),
            reads=[r_lim, r_dt], writes=[rth], tag=tag + "tt")
    prog.op("act", lambda eng: eng.activation(rphi[:, :], rphi[:, :], AF.Exp),
            reads=[rphi], writes=[rphi], tag=tag + "e3")

    def rangered(row, outrow, tg):
        msk = ctx.sbuf(tg + "mk", (1, 64))
        prog.op("dve", lambda eng: eng.tensor_copy(outrow[:, :], row[:, :]),
                reads=[row], writes=[outrow], tag=tg + "cp")
        for k in (1024, 512, 256, 128, 64, 32, 16, 8, 4, 2, 1, 1):
            thr = 2 * PI * k - PI
            prog.op("dve", lambda eng, thr=thr: eng.tensor_single_scalar(
                msk[:, :], outrow[:, :], thr, ALU.is_gt),
                reads=[outrow], writes=[msk], tag=tg + "g")
            prog.op("dve", lambda eng, k=k: eng.scalar_tensor_tensor(
                outrow[:, :], msk[:, :], -2 * PI * k, outrow[:, :], ALU.mult,
                ALU.add),
                reads=[msk, outrow], writes=[outrow], tag=tg + "f")
            prog.op("dve", lambda eng, thr=thr: eng.tensor_single_scalar(
                msk[:, :], outrow[:, :], -thr, ALU.is_lt),
                reads=[outrow], writes=[msk], tag=tg + "g2")
            prog.op("dve", lambda eng, k=k: eng.scalar_tensor_tensor(
                outrow[:, :], msk[:, :], 2 * PI * k, outrow[:, :], ALU.mult,
                ALU.add),
                reads=[msk, outrow], writes=[outrow], tag=tg + "f2")

    rsin = ctx.sbuf(tag + "sn", (1, 64))
    rcos = ctx.sbuf(tag + "cs", (1, 64))
    rtmp = ctx.sbuf(tag + "tm", (1, 64))
    rangered(rth, rsin, tag + "r1")
    prog.op("act", lambda eng: eng.activation(rsin[:, :], rsin[:, :], AF.Sin),
            reads=[rsin], writes=[rsin], tag=tag + "s1")
    prog.op("dve", lambda eng: eng.tensor_scalar(rtmp[:, :], rth[:, :], -1.0, PI / 2,
                                                 ALU.mult, ALU.add),
            reads=[rth], writes=[rtmp], tag=tag + "c0")
    rangered(rtmp, rcos, tag + "r2")
    prog.op("act", lambda eng: eng.activation(rcos[:, :], rcos[:, :], AF.Sin),
            reads=[rcos], writes=[rcos], tag=tag + "s2")
    rare = ctx.sbuf(tag + "ar", (1, 64))
    raim = ctx.sbuf(tag + "ai", (1, 64))
    prog.op("dve", lambda eng: eng.tensor_tensor(rare[:, :], rphi[:, :], rcos[:, :],
                                                 ALU.mult),
            reads=[rphi, rcos], writes=[rare], tag=tag + "ar")
    prog.op("dve", lambda eng: eng.tensor_tensor(raim[:, :], rphi[:, :], rsin[:, :],
                                                 ALU.mult),
            reads=[rphi, rsin], writes=[raim], tag=tag + "ai")

    mats = {}
    for nm, d in (("vre", Vre_d), ("vim", Vim_d), ("wre", Wre_d), ("wim", Wim_d)):
        t = ctx.sbuf(tag + nm, (64, 64))
        dma(prog, "sp", t[:, :], d, reads=[d], writes=[t], tag=tag + "l" + nm)
        mats[nm] = t

    def bc64(row, tg):
        ps = ctx.pp.get()
        prog.op("pe", lambda eng, ps=ps: eng.matmul(
            ps[0:64, 0:64], ctx.c_ones_bc[0:1, 0:64], row[0:1, 0:64],
            start=True, stop=True),
            reads=[row, "c_ones_bc"], writes=[ps], tag=tg + "b")
        t = ctx.sbuf(tg + "bs", (64, 64))
        prog.op("dve", lambda eng, ps=ps: eng.tensor_copy(t[:, :], ps[0:64, 0:64]),
                reads=[ps], writes=[t], tag=tg + "e")
        return t

    areb = bc64(rare, tag + "A")
    aimb = bc64(raim, tag + "B")
    dtb = bc64(r_dt, tag + "D")

    def mk64(tg):
        return ctx.sbuf(tg, (64, 64))

    pre_, pim_, vdre, vdim = mk64(tag + "pre"), mk64(tag + "pim"), mk64(tag + "vdr"), mk64(tag + "vdi")
    for dst, t1, s1, t2, s2, op in ((pre_, "vre", areb, "vim", aimb, ALU.subtract),
                                    (pim_, "vre", aimb, "vim", areb, ALU.add)):
        tmp = mk64(tag + "tmq" + t2 + str(op))
        prog.op("dve", lambda eng, dst=dst, t1=t1, s1=s1: eng.tensor_tensor(
            dst[:, :], mats[t1][:, :], s1[:, :], ALU.mult),
            reads=[mats[t1], s1], writes=[dst], tag=tag + "p1")
        prog.op("dve", lambda eng, tmp=tmp, t2=t2, s2=s2: eng.tensor_tensor(
            tmp[:, :], mats[t2][:, :], s2[:, :], ALU.mult),
            reads=[mats[t2], s2], writes=[tmp], tag=tag + "p2")
        prog.op("dve", lambda eng, dst=dst, tmp=tmp, op=op: eng.tensor_tensor(
            dst[:, :], dst[:, :], tmp[:, :], op),
            reads=[dst, tmp], writes=[dst], tag=tag + "p3")
    prog.op("dve", lambda eng: eng.tensor_tensor(vdre[:, :], mats["vre"][:, :],
                                                 dtb[:, :], ALU.mult),
            reads=[mats["vre"], dtb], writes=[vdre], tag=tag + "v1")
    prog.op("dve", lambda eng: eng.tensor_tensor(vdim[:, :], mats["vim"][:, :],
                                                 dtb[:, :], ALU.mult),
            reads=[mats["vim"], dtb], writes=[vdim], tag=tag + "v2")

    def tr64(src, tg):
        ps = ctx.pp.get()
        prog.op("pe", lambda eng, ps=ps, src=src: eng.transpose(
            ps[0:64, 0:64], src[:, :], ctx.c_ident[0:64, 0:64]),
            reads=[src, "c_ident"], writes=[ps], tag=tg + "t")
        t = mk64(tg + "T")
        prog.op("dve", lambda eng, ps=ps, t=t: eng.tensor_copy(t[:, :], ps[0:64, 0:64]),
                reads=[ps], writes=[t], tag=tg + "e")
        return t

    vdreT = tr64(vdre, tag + "x1")
    vdimT = tr64(vdim, tag + "x2")
    nvdimT = mk64(tag + "nvd")
    prog.op("dve", lambda eng: eng.tensor_scalar(nvdimT[:, :], vdimT[:, :], -1.0,
                                                 None, ALU.mult),
            reads=[vdimT], writes=[nvdimT], tag=tag + "ng")
    gts = {}
    for nm, l1, r1, l2, r2 in (("gre", vdreT, "wre", nvdimT, "wim"),
                               ("gim", vdreT, "wim", vdimT, "wre")):
        ps = ctx.pp.get()
        prog.op("pe", lambda eng, ps=ps, l1=l1, r1=r1: eng.matmul(
            ps[0:64, 0:64], l1[:, :], mats[r1][:, :], start=True, stop=False),
            reads=[l1, mats[r1]], writes=[ps], tag=tag + nm + "1")
        prog.op("pe", lambda eng, ps=ps, l2=l2, r2=r2: eng.matmul(
            ps[0:64, 0:64], l2[:, :], mats[r2][:, :], start=False, stop=True),
            reads=[l2, mats[r2]], writes=[ps], tag=tag + nm + "2")
        g = mk64(tag + nm)
        prog.op("dve", lambda eng, ps=ps, g=g: eng.tensor_copy(g[:, :], ps[0:64, 0:64]),
                reads=[ps], writes=[g], tag=tag + nm + "e")
        gts[nm] = g
    gTre = tr64(gts["gre"], tag + "g1")
    gTim = tr64(gts["gim"], tag + "g2")
    pTre = tr64(pre_, tag + "g3")
    pTim = tr64(pim_, tag + "g4")
    npTim = mk64(tag + "npt")
    prog.op("dve", lambda eng, npTim=npTim, pTim=pTim: eng.tensor_scalar(
        npTim[:, :], pTim[:, :], -1.0, None, ALU.mult),
            reads=[pTim], writes=[npTim], tag=tag + "np")

    def dup128(src_t, tg):
        t = ctx.sbuf(tg, (128, 64))
        for pb in (0, 64):
            prog.op("dve", lambda eng, pb=pb, t=t, src_t=src_t: eng.tensor_copy(
                t[pb:pb + 64, :], src_t[:, :]),
                reads=[src_t], writes=[t], tag=tg + "d")
        return t

    gTre = dup128(gTre, tag + "Gr")
    gTim = dup128(gTim, tag + "Gi")
    pTre = dup128(pTre, tag + "Pr")
    pTim = dup128(pTim, tag + "Pi")
    npTim = dup128(npTim, tag + "Pn")

    zre = [ctx.sbuf(tag + "zr", (128, 8 * SL)) for _ in range(2)]
    zim = [ctx.sbuf(tag + "zi", (128, 8 * SL)) for _ in range(2)]
    hbuf = [ctx.sbuf(tag + "h", (128, 8 * 2 * SL)) for _ in range(2)]
    xre_ = [ctx.sbuf(tag + "xr", (128, SL)) for _ in range(2)]
    xw_ = [ctx.sbuf(tag + "xw", (128, SL)) for _ in range(2)]

    def pair_loader(tpair, hb):
        def f(c, passno):
            key = (hb.name + "ld", c)
            for j in range(2):
                dma(prog, "sp", hb[:, c * 2 * SL + j * SL: c * 2 * SL + (j + 1) * SL],
                    xin[2 * tpair + j, c * 128:(c + 1) * 128, :],
                    reads=[(xin.tensor.name, 2 * tpair + j, c)],
                    writes=[key, (hb.name + "#h", c)], tag=tag + "hl")
            return hb[:, c * 2 * SL:(c + 1) * 2 * SL], key
        return f

    for tpair in range(8):
        hb = hbuf[tpair % 2]
        emit_ln_fm(ctx, pair_loader(tpair, hb), 2 * SL, NCH, hb,
                   hb.name + "#h", tag=tag + f"ln{tpair}", nsq=1)
        for j in range(2):
            t = 2 * tpair + j
            zo = zre[t % 2]
            zoi = zim[t % 2]
            zn = zre[(t + 1) % 2]
            zni = zim[(t + 1) % 2]
            for c in range(NCH):
                for half in range(2):
                    pb = half * 64
                    psr = ctx.pp.get()
                    psi = ctx.pp.get()
                    hsl = hb[pb:pb + 64, c * 2 * SL + j * SL: c * 2 * SL + (j + 1) * SL]
                    rds = [(hb.name + "#h", c)]
                    prog.op("pe", lambda eng, psr=psr, hsl=hsl, pb=pb, t=t: eng.matmul(
                        psr[0:64, 0:SL], gTre[pb:pb + 64, :], hsl, start=True,
                        stop=(t == 0)),
                        reads=rds + [gTre], writes=[psr], tag=tag + "c1")
                    prog.op("pe", lambda eng, psi=psi, hsl=hsl, pb=pb, t=t: eng.matmul(
                        psi[0:64, 0:SL], gTim[pb:pb + 64, :], hsl, start=True,
                        stop=(t == 0)),
                        reads=rds + [gTim], writes=[psi], tag=tag + "c2")
                    if t > 0:
                        zsl_r = zo[pb:pb + 64, c * SL:(c + 1) * SL]
                        zsl_i = zoi[pb:pb + 64, c * SL:(c + 1) * SL]
                        zrd = [(zo.name, c), (zoi.name, c)]
                        prog.op("pe", lambda eng, psr=psr, zsl_r=zsl_r, pb=pb: eng.matmul(
                            psr[0:64, 0:SL], pTre[pb:pb + 64, :], zsl_r, start=False,
                            stop=False),
                            reads=zrd + [pTre], writes=[psr], tag=tag + "z1")
                        prog.op("pe", lambda eng, psr=psr, zsl_i=zsl_i, pb=pb: eng.matmul(
                            psr[0:64, 0:SL], npTim[pb:pb + 64, :], zsl_i, start=False,
                            stop=True),
                            reads=zrd + [npTim], writes=[psr], tag=tag + "z2")
                        prog.op("pe", lambda eng, psi=psi, zsl_i=zsl_i, pb=pb: eng.matmul(
                            psi[0:64, 0:SL], pTre[pb:pb + 64, :], zsl_i, start=False,
                            stop=False),
                            reads=zrd + [pTre], writes=[psi], tag=tag + "z3")
                        prog.op("pe", lambda eng, psi=psi, zsl_r=zsl_r, pb=pb: eng.matmul(
                            psi[0:64, 0:SL], pTim[pb:pb + 64, :], zsl_r, start=False,
                            stop=True),
                            reads=zrd + [pTim], writes=[psi], tag=tag + "z4")
                    prog.op("dve", lambda eng, psr=psr, zn=zn, c=c, pb=pb: eng.tensor_copy(
                        zn[pb:pb + 64, c * SL:(c + 1) * SL], psr[0:64, 0:SL]),
                        reads=[psr], writes=[(zn.name, c)], tag=tag + "ze")
                    prog.op("dve", lambda eng, psi=psi, zni=zni, c=c, pb=pb: eng.tensor_copy(
                        zni[pb:pb + 64, c * SL:(c + 1) * SL], psi[0:64, 0:SL]),
                        reads=[psi], writes=[(zni.name, c)], tag=tag + "zi")
            for c in range(NCH):
                xr = xre_[c % 2]
                xw = xw_[c % 2]
                dma(prog, "sp", xr[:, :], xin[t, c * 128:(c + 1) * 128, :],
                    reads=[(xin.tensor.name, t, c)], writes=[xr.name],
                    tag=tag + "rx")
                prog.op("dve", lambda eng, xr=xr, xw=xw, zn=zn, c=c: eng.scalar_tensor_tensor(
                    xw[:, :], zn[:, c * SL:(c + 1) * SL], 2.0, xr[:, :],
                    ALU.mult, ALU.add),
                    reads=[(zn.name, c), xr.name], writes=[xw.name], tag=tag + "ot")
                dma(prog, "sp", xout[t, c * 128:(c + 1) * 128, :], xw[:, :],
                    reads=[xw.name], writes=[(xout.tensor.name, t, c)],
                    tag=tag + "sx")
    ctx.es = old
    prog.barrier()
    es.close()


def emit_m2core(ctx, b, xin, xout, W_in_d, convw_d, convb_d, Alog_d, Dm_d,
                dtb_d, mnw_d, Wout_d, scr, tag="mc"):
    nc, prog = ctx.nc, ctx.prog
    from contextlib import ExitStack
    T = SEQ
    NS = T // NB
    NCH = 8
    DI = 2048
    NH = 32
    old = ctx.es
    es0 = ExitStack()
    ctx.es = es0

    dtr = ctx.sbuf(tag + "dt", (32, T))
    esA = ExitStack()
    ctx.es = esA
    h_bf = ctx.sbuf(tag + "h", (128, NCH * T), BF16)
    esA1 = ExitStack()
    ctx.es = esA1
    loader = dram_chunk_loader(
        ctx, lambda c: (xin[b, c * 128:(c + 1) * 128, :], (xin.tensor.name, b, c)),
        T, tag=tag + "xl", nbuf=1)
    emit_ln_fm(ctx, loader, T, NCH, h_bf, tag + "_hk", tag=tag + f"ln{b}", nsq=1,
               rms=True, eps=1e-6)
    prog.barrier()
    esA1.close()
    esA2 = ExitStack()
    ctx.es = esA2
    w_in = ctx.load_w_bf16(W_in_d, DM, 4384, tag=tag + "wi")
    ev = [ctx.sbuf(tag + "ev", (128, NB), BF16) for _ in range(2)]
    evf = [ctx.sbuf(tag + "evf", (128, NB)) for _ in range(2)]

    def evac_zx(info, m0, m1, s, ps):
        kind, r = info
        if kind == "z":
            e = ev[(r * NS + s) % 2]
            prog.op("dve", lambda eng, ps=ps, e=e: eng.tensor_copy(
                e[:, :], ps[0:128, 0:NB]), reads=[ps], writes=[e.name],
                tag=tag + "ez")
            dma(prog, "sp", scr["zD"][b, r * 128:(r + 1) * 128, s * NB:(s + 1) * NB],
                e[:, :], reads=[e.name], writes=[("zD", b, r)], tag=tag + "sz")
        elif kind == "x":
            e = evf[(r * NS + s) % 2]
            prog.op("dve", lambda eng, ps=ps, e=e: eng.tensor_copy(
                e[:, :], ps[0:128, 0:NB]), reads=[ps], writes=[e.name],
                tag=tag + "ex")
            dma(prog, "sp", scr["xbcD"][b, r * 128:(r + 1) * 128, s * NB:(s + 1) * NB],
                e[:, :], reads=[e.name], writes=[("xbcD", b, r)], tag=tag + "sx")
        else:
            prog.op("dve", lambda eng, ps=ps, s=s: eng.tensor_copy(
                dtr[:, s * NB:(s + 1) * NB], ps[0:32, 0:NB]),
                reads=[ps], writes=[dtr], tag=tag + "ed")

    mt = [(r * 128, (r + 1) * 128, ("z", r)) for r in range(16)] + \
         [(2048 + r * 128, 2048 + (r + 1) * 128, ("x", r)) for r in range(18)] + \
         [(4352, 4384, ("d", 0))]
    emit_proj_tiles(ctx, w_in, 4384, h_bf, tag + "_hk", NCH, T, mt, evac_zx,
                    tag=tag + "pj")
    prog.barrier()
    esA2.close()
    ctx.es = esA
    prog.barrier()
    esA.close()
    ctx.es = es0

    Bfb = ctx.sbuf(tag + "Bfb", (128, T), BF16)
    Cfb = ctx.sbuf(tag + "Cfb", (128, T), BF16)
    y_bf = ctx.sbuf(tag + "ybf", (128, 16 * T), BF16)
    esM = ExitStack()
    ctx.es = esM
    dtbc = ctx.feat_cols(dtb_d, 32, tag=tag + "db")
    acol = ctx.feat_cols(Alog_d, 32, tag=tag + "al")
    prog.op("act", lambda eng: eng.activation(acol[:, :], acol[:, :], AF.Exp),
            reads=[acol], writes=[acol], tag=tag + "ae")
    prog.op("dve", lambda eng: eng.tensor_scalar(dtr[:, :], dtr[:, :], dtbc[:, 0:1],
                                                 None, ALU.add),
            reads=[dtr, dtbc], writes=[dtr], tag=tag + "d1")
    prog.op("act", lambda eng: eng.activation(dtr[:, :], dtr[:, :], AF.Exp),
            reads=[dtr], writes=[dtr], tag=tag + "d2")
    prog.op("act", lambda eng: eng.activation(dtr[:, :], dtr[:, :], AF.Ln,
                                              bias=scal_ap(ctx, 1.0, 32)),
            reads=[dtr, "c_scal"], writes=[dtr], tag=tag + "d3")
    la = ctx.sbuf(tag + "la", (32, T))
    prog.op("dve", lambda eng: eng.tensor_scalar(la[:, :], dtr[:, :], acol[:, 0:1],
                                                 None, ALU.mult),
            reads=[dtr, acol], writes=[la], tag=tag + "la")
    prog.op("dve", lambda eng: eng.tensor_scalar(la[:, :], la[:, :], -1.0, None,
                                                 ALU.mult),
            reads=[la], writes=[la], tag=tag + "lan")
    ones32 = ctx.sbuf(tag + "o32", (32, 128))
    prog.op("dve", lambda eng: eng.memset(ones32[:, :], 1.0),
            reads=[], writes=[ones32], tag=tag + "o1")
    cs = ctx.sbuf(tag + "cs", (32, T))
    for tc in range(16):
        prog.op("dve", lambda eng, tc=tc: eng.tensor_tensor_scan(
            cs[:, tc * 128:(tc + 1) * 128], ones32[:, :],
            la[:, tc * 128:(tc + 1) * 128], 0.0, ALU.mult, ALU.add),
            reads=[ones32, la], writes=[cs], tag=tag + "cum")
    biasr = ctx.sbuf(tag + "bi", (32, T))
    prog.op("act", lambda eng: eng.activation(biasr[:, :], dtr[:, :], AF.Ln),
            reads=[dtr], writes=[biasr], tag=tag + "ld")
    prog.op("dve", lambda eng: eng.tensor_tensor(biasr[:, :], biasr[:, :], cs[:, :],
                                                 ALU.subtract),
            reads=[biasr, cs], writes=[biasr], tag=tag + "b2")
    biasT = ctx.sbuf(tag + "bT", (128, 16 * 32))
    biasT2 = ctx.sbuf(tag + "bT2", (128, 16 * 32))
    b2row = ctx.sbuf(tag + "b2r", (32, 128))
    for tc in range(16):
        prog.op("dve", lambda eng, tc=tc: eng.tensor_scalar(
            b2row[:, :], biasr[:, tc * 128:(tc + 1) * 128],
            cs[:, tc * 128 + 127: tc * 128 + 128], None, ALU.add),
            reads=[biasr, cs], writes=[b2row], tag=tag + "b2a")
        for src_, dst in ((biasr, biasT), (b2row, biasT2)):
            ps = ctx.pp.get()
            if src_ is biasr:
                prog.op("pe", lambda eng, tc=tc, ps=ps, src_=src_: eng.transpose(
                    ps[0:128, 0:32], src_[:, tc * 128:(tc + 1) * 128],
                    ctx.c_ident[0:32, 0:32]),
                    reads=[src_, "c_ident"], writes=[ps], tag=tag + "bt")
            else:
                prog.op("pe", lambda eng, ps=ps, src_=src_: eng.transpose(
                    ps[0:128, 0:32], src_[:, :],
                    ctx.c_ident[0:32, 0:32]),
                    reads=[src_, "c_ident"], writes=[ps], tag=tag + "bt2")
            prog.op("dve", lambda eng, tc=tc, ps=ps, dst=dst: eng.tensor_copy(
                dst[:, tc * 32:(tc + 1) * 32], ps[0:128, 0:32]),
                reads=[ps], writes=[dst], tag=tag + "be")
    ebias = ctx.sbuf(tag + "eb", (128, 16 * 32))
    prog.op("act", lambda eng: eng.activation(ebias[:, :], biasT2[:, :], AF.Exp),
            reads=[biasT2], writes=[ebias], tag=tag + "ebx")
    gam = ctx.sbuf(tag + "gm", (32, 16))
    for tc in range(16):
        prog.op("dve", lambda eng, tc=tc: eng.tensor_copy(
            gam[:, tc:tc + 1], cs[:, tc * 128 + 127: tc * 128 + 128]),
            reads=[cs], writes=[gam], tag=tag + "g1")
    prog.op("act", lambda eng: eng.activation(gam[:, :], gam[:, :], AF.Exp),
            reads=[gam], writes=[gam], tag=tag + "g2")
    gamT = ctx.sbuf(tag + "gT", (128, 16 * 32))
    psg = ctx.pp.get()
    prog.op("pe", lambda eng: eng.transpose(psg[0:16, 0:32], gam[:, :],
                                            ctx.c_ident[0:32, 0:32]),
            reads=[gam, "c_ident"], writes=[psg], tag=tag + "gt2")
    gamr = ctx.sbuf(tag + "gr", (16, 32))
    prog.op("dve", lambda eng: eng.tensor_copy(gamr[:, :], psg[0:16, 0:32]),
            reads=[psg], writes=[gamr], tag=tag + "ge")
    for tc in range(16):
        ps = ctx.pp.get()
        prog.op("pe", lambda eng, tc=tc, ps=ps: eng.matmul(
            ps[0:128, 0:32], ctx.c_sel16[:, tc * 128: tc * 128 + 128],
            gamr[0:16, 0:32], start=True, stop=True),
            reads=[gamr, "c_sel16"], writes=[ps], tag=tag + "gb")
        prog.op("dve", lambda eng, tc=tc, ps=ps: eng.tensor_copy(
            gamT[:, tc * 32:(tc + 1) * 32], ps[0:128, 0:32]),
            reads=[ps], writes=[gamT], tag=tag + "gbe")

    esB = ExitStack()
    ctx.es = esB
    cw = [ctx.feat_cols(convw_d[k, :], 2304, tag=tag + f"cw{k}") for k in range(4)]
    cb = ctx.feat_cols(convb_d, 2304, tag=tag + "cb")
    Bf = ctx.sbuf(tag + "Bf", (128, T))
    Cf = ctx.sbuf(tag + "Cf", (128, T))
    xcb = [ctx.sbuf(tag + "xc", (128, T)) for _ in range(2)]
    ycb = [ctx.sbuf(tag + "yc", (128, T)) for _ in range(2)]
    ycbb = [ctx.sbuf(tag + "yb", (128, T), BF16) for _ in range(2)]
    for r in range(18):
        xc = xcb[r % 2]
        yc = ycb[r % 2]
        dma(prog, "sp", xc[:, :], scr["xbcD"][b, r * 128:(r + 1) * 128, :],
            reads=[("xbcD", b, r)], writes=[xc.name], tag=tag + "lc")
        prog.op("dve", lambda eng, r=r, xc=xc, yc=yc: eng.tensor_scalar(
            yc[:, :], xc[:, :], cw[3][:, r:r + 1], cb[:, r:r + 1], ALU.mult,
            ALU.add),
            reads=[xc.name, cw[3], cb], writes=[yc.name], tag=tag + "c3")
        for k in range(3):
            sh = 3 - k
            prog.op("dve", lambda eng, r=r, k=k, sh=sh, xc=xc, yc=yc: eng.scalar_tensor_tensor(
                yc[:, sh:T], xc[:, 0:T - sh], cw[k][:, r:r + 1], yc[:, sh:T],
                ALU.mult, ALU.add),
                reads=[xc.name, cw[k], yc.name], writes=[yc.name], tag=tag + "ck")
        if r < 16:
            yb = ycbb[r % 2]
            prog.op("act", lambda eng, yc=yc, yb=yb: eng.activation(
                yb[:, :], yc[:, :], AF.Silu),
                reads=[yc.name], writes=[yb.name], tag=tag + "si")
            dma(prog, "sp", scr["xmD"][b, r * 128:(r + 1) * 128, :], yb[:, :],
                reads=[yb.name], writes=[("xmD", b, r)], tag=tag + "sm")
        else:
            dst = Bf if r == 16 else Cf
            prog.op("act", lambda eng, yc=yc, dst=dst: eng.activation(
                dst[:, :], yc[:, :], AF.Silu),
                reads=[yc.name], writes=[dst], tag=tag + "sB")
    prog.op("dve", lambda eng: eng.tensor_copy(Bfb[:, :], Bf[:, :]),
            reads=[Bf], writes=[Bfb], tag=tag + "bc1")
    prog.op("dve", lambda eng: eng.tensor_copy(Cfb[:, :], Cf[:, :]),
            reads=[Cf], writes=[Cfb], tag=tag + "bc2")

    dma(prog, "sp", scr["BD"][b, :, :], Bfb[:, :], reads=[Bfb], writes=[("BD", b)],
        tag=tag + "sB2")
    dma(prog, "sp", scr["CD"][b, :, :], Cfb[:, :], reads=[Cfb], writes=[("CD", b)],
        tag=tag + "sC2")
    prog.barrier()
    esB.close()

    esC = ExitStack()
    ctx.es = esC
    S_f = ctx.sbuf(tag + "S", (128, NH * 64))
    S_b = ctx.sbuf(tag + "Sb", (128, NH * 64), BF16)
    prog.op("dve", lambda eng: eng.memset(S_f[:, :], 0.0), reads=[], writes=[S_f],
            tag=tag + "s0")
    prog.op("dve", lambda eng: eng.memset(S_b[:, :], 0.0), reads=[], writes=[S_b],
            tag=tag + "s0b")
    Xt = [ctx.sbuf(tag + "Xt", (128, DI), BF16) for _ in range(2)]
    Bt_ = [ctx.sbuf(tag + "Bt", (128, 128), BF16) for _ in range(2)]
    NROT = 3
    CSb_r = [ctx.sbuf(tag + "CSb", (128, 128)) for _ in range(NROT)]
    Wm_r = [ctx.sbuf(tag + "Wm", (128, 128), BF16) for _ in range(NROT)]
    CDt_r = [ctx.sbuf(tag + "CD", (128, 128), BF16) for _ in range(NROT)]
    Bd_r = [ctx.sbuf(tag + "Bd", (128, 128), BF16) for _ in range(NROT)]
    dec3_r = [ctx.sbuf(tag + "d3", (128, 128), BF16) for _ in range(NROT)]
    sup = [ctx.sbuf(tag + "su", (128, 64)) for _ in range(3)]
    for tc in range(16):
        Xtile = Xt[tc % 2]
        Btile = Bt_[tc % 2]
        dma(prog, "sp", Xtile[:, :],
            scr["xmD"][b, :, tc * 128:(tc + 1) * 128], transpose=True,
            reads=[("xmD", b, r) for r in range(16)], writes=[Xtile.name],
            tag=tag + "lxT")
        dma(prog, "sp", Btile[:, :],
            scr["BD"][b, :, tc * 128:(tc + 1) * 128], transpose=True,
            reads=[("BD", b)], writes=[Btile.name], tag=tag + "lbT")
        psG = ctx.pp.tiles[3]
        prog.op("pe", lambda eng, tc=tc, psG=psG: eng.matmul(
            psG[0:128, 0:128], Bfb[:, tc * 128:(tc + 1) * 128],
            Cfb[:, tc * 128:(tc + 1) * 128], start=True, stop=True),
            reads=[Bfb, Cfb], writes=[psG], tag=tag + "gg2")
        def stageA(h, tc=tc):
            CSb = CSb_r[h % 3]
            Wm = Wm_r[h % 3]
            CDt = CDt_r[h % 3]
            Bd = Bd_r[h % 3]
            dec3 = dec3_r[h % 3]
            Btile = Bt_[tc % 2]
            psc = ctx.pp.tiles[6 + (h % 2)]
            prog.op("pe", lambda eng, tc=tc, h=h, psc=psc: eng.matmul(
                psc[0:128, 0:128], ctx.c_sel32[:, h * 128:(h + 1) * 128],
                cs[0:32, tc * 128:(tc + 1) * 128], start=True, stop=True),
                reads=[cs, "c_sel32"], writes=[psc], tag=tag + "cb2")
            prog.op("dve", lambda eng, psc=psc, CSb=CSb: eng.tensor_tensor(
                CSb[:, :], psc[0:128, 0:128], ctx.c_trimask, ALU.add),
                reads=[psc, "c_trimask"], writes=[CSb], tag=tag + "msk")
            prog.op("act", lambda eng, tc=tc, h=h, Wm=Wm, CSb=CSb: eng.activation(
                Wm[:, :], CSb[:, :], AF.Exp, bias=biasT[:, tc * 32 + h: tc * 32 + h + 1]),
                reads=[CSb, biasT], writes=[Wm], tag=tag + "wx")
            prog.op("act", lambda eng, psc=psc, dec3=dec3: eng.activation(
                dec3[:, :], psc[0:128, 0:128], AF.Exp),
                reads=[psc], writes=[dec3], tag=tag + "dc3")
            prog.op("dve", lambda eng, tc=tc, h=h, Btile=Btile, Bd=Bd: eng.tensor_scalar(
                Bd[:, :], Btile[:, :], ebias[:, tc * 32 + h: tc * 32 + h + 1],
                None, ALU.mult),
                reads=[Btile.name, ebias], writes=[Bd], tag=tag + "bd")
            prog.op("dve", lambda eng, psG=psG, Wm=Wm: eng.tensor_tensor(
                Wm[:, :], Wm[:, :], psG[0:128, 0:128], ALU.mult),
                reads=[Wm, psG], writes=[Wm], tag=tag + "mG")
            prog.op("dve", lambda eng, tc=tc, CDt=CDt, dec3=dec3: eng.tensor_tensor(
                CDt[:, :], Cfb[:, tc * 128:(tc + 1) * 128], dec3[:, :], ALU.mult),
                reads=[Cfb, dec3], writes=[CDt], tag=tag + "cd")

        def stageB(h, tc=tc):
            Wm = Wm_r[h % 3]
            CDt = CDt_r[h % 3]
            Bd = Bd_r[h % 3]
            Xtile = Xt[tc % 2]
            psY = ctx.pp.tiles[5 if h % 2 == 0 else 0]
            prog.op("pe", lambda eng, h=h, psY=psY, Xtile=Xtile, Wm=Wm: eng.matmul(
                psY[0:64, 0:128], Xtile[:, h * 64:(h + 1) * 64], Wm[:, :],
                start=True, stop=False),
                reads=[Xtile.name, Wm], writes=[psY], tag=tag + "yi")
            prog.op("pe", lambda eng, h=h, psY=psY, CDt=CDt: eng.matmul(
                psY[0:64, 0:128], S_b[:, h * 64:(h + 1) * 64], CDt[:, :],
                start=False, stop=True),
                reads=[(S_b.name, h), CDt], writes=[psY], tag=tag + "ycr")
            psS = ctx.pp.tiles[4 if h % 2 == 0 else 1]
            prog.op("pe", lambda eng, h=h, psS=psS, Xtile=Xtile, Bd=Bd: eng.matmul(
                psS[0:128, 0:64], Bd[:, :], Xtile[:, h * 64:(h + 1) * 64],
                start=True, stop=True),
                reads=[Bd, Xtile.name], writes=[psS], tag=tag + "su")
            s_ = sup[h % 3]
            prog.op("dve", lambda eng, tc=tc, h=h, psS=psS, s_=s_: eng.scalar_tensor_tensor(
                s_[:, :], S_f[:, h * 64:(h + 1) * 64],
                gamT[:, tc * 32 + h: tc * 32 + h + 1], psS[0:128, 0:64],
                ALU.mult, ALU.add),
                reads=[psS, (S_f.name, h), gamT], writes=[s_.name], tag=tag + "sa")
            prog.op("dve", lambda eng, h=h, s_=s_: eng.tensor_copy(
                S_f[:, h * 64:(h + 1) * 64], s_[:, :]),
                reads=[s_.name], writes=[(S_f.name, h)], tag=tag + "sg")
            prog.op("dve", lambda eng, h=h: eng.tensor_copy(
                S_b[:, h * 64:(h + 1) * 64], S_f[:, h * 64:(h + 1) * 64]),
                reads=[(S_f.name, h)], writes=[(S_b.name, h)], tag=tag + "sb")
            hc = h // 2
            pb = (h % 2) * 64
            prog.op("dve", lambda eng, psY=psY, hc=hc, pb=pb, tc=tc: eng.tensor_copy(
                y_bf[pb:pb + 64, hc * T + tc * 128: hc * T + (tc + 1) * 128],
                psY[0:64, 0:128]),
                reads=[psY], writes=[(y_bf.name, hc)], tag=tag + "ye")

        stageA(0)
        for h in range(NH):
            if h + 1 < NH:
                stageA(h + 1)
            stageB(h)
    prog.barrier()
    esC.close()
    esM.close()
    ctx.es = es0

    esD = ExitStack()
    ctx.es = esD
    dmrow = load_row(ctx, Dm_d, 32, tag=tag + "dm")
    psd = ctx.pp.get()
    prog.op("pe", lambda eng: eng.transpose(psd[0:32, 0:1], dmrow[0:1, 0:32],
                                            ctx.c_ident[0:1, 0:1]),
            reads=[dmrow, "c_ident"], writes=[psd], tag=tag + "dmt")
    dmcol = ctx.sbuf(tag + "dmo", (32, 1))
    prog.op("dve", lambda eng: eng.tensor_copy(dmcol[:, :], psd[0:32, 0:1]),
            reads=[psd], writes=[dmcol], tag=tag + "dme")
    r1 = ctx.sbuf(tag + "r1", (32, 16))
    r2 = ctx.sbuf(tag + "r2", (32, 16))
    prog.op("dve", lambda eng: eng.tensor_scalar(r1[:, :], ctx.c_dm_m1[0:32, :],
                                                 dmcol[:, 0:1], None, ALU.mult),
            reads=["c_dm_m1", dmcol], writes=[r1], tag=tag + "r1")
    prog.op("dve", lambda eng: eng.tensor_scalar(r2[:, :], ctx.c_dm_m2[0:32, :],
                                                 dmcol[:, 0:1], None, ALU.mult),
            reads=["c_dm_m2", dmcol], writes=[r2], tag=tag + "r2")
    psd2 = ctx.pp.get()
    prog.op("pe", lambda eng: eng.matmul(psd2[0:128, 0:16], ctx.c_dm_l1,
                                         r1[:, :], start=True, stop=False),
            reads=["c_dm_l1", r1], writes=[psd2], tag=tag + "dm1")
    prog.op("pe", lambda eng: eng.matmul(psd2[0:128, 0:16], ctx.c_dm_l2,
                                         r2[:, :], start=False, stop=True),
            reads=["c_dm_l2", r2], writes=[psd2], tag=tag + "dm2")
    dmdim = ctx.sbuf(tag + "dmd", (128, 16))
    prog.op("dve", lambda eng: eng.tensor_copy(dmdim[:, :], psd2[0:128, 0:16]),
            reads=[psd2], writes=[dmdim], tag=tag + "dm3")
    mnc = ctx.feat_cols(mnw_d, DI, tag=tag + "mn")
    zcb = [ctx.sbuf(tag + "zc", (128, T), BF16)] * 2
    xmc = [ctx.sbuf(tag + "xm", (128, T), BF16)] * 2
    y2c = [ctx.sbuf(tag + "y2", (128, T))] * 2
    sqt = [ctx.sbuf(tag + "sq", (128, T))] * 2
    y2b = y_bf
    ps_q = [ctx.pp.get() for _ in range(4)]

    def compute_y2(c, zc, xc, y2):
        dma(prog, "sp", zc[:, :], scr["zD"][b, c * 128:(c + 1) * 128, :],
            reads=[("zD", b, c)], writes=[zc.name], tag=tag + "lz")
        dma(prog, "pool", xc[:, :], scr["xmD"][b, c * 128:(c + 1) * 128, :],
            reads=[("xmD", b, c)], writes=[xc.name], tag=tag + "lm")
        prog.op("act", lambda eng, zc=zc: eng.activation(zc[:, :], zc[:, :],
                                                         AF.Silu),
                reads=[zc.name], writes=[zc.name], tag=tag + "sz2")
        prog.op("dve", lambda eng, c=c, xc=xc, y2=y2: eng.tensor_scalar(
            y2[:, :], xc[:, :], dmdim[:, c:c + 1], None, ALU.mult),
            reads=[xc.name, dmdim], writes=[y2.name], tag=tag + "dx")
        prog.op("dve", lambda eng, c=c, y2=y2: eng.tensor_tensor(
            y2[:, :], y2[:, :], y_bf[:, c * T:(c + 1) * T], ALU.add),
            reads=[y2.name, (y_bf.name, c)], writes=[y2.name], tag=tag + "ya")
        prog.op("dve", lambda eng, y2=y2, zc=zc: eng.tensor_tensor(
            y2[:, :], y2[:, :], zc[:, :], ALU.mult),
            reads=[y2.name, zc.name], writes=[y2.name], tag=tag + "yz")

    for c in range(16):
        zc = zcb[c % 2]
        xc = xmc[c % 2]
        y2 = y2c[c % 2]
        s_ = sqt[c % 2]
        compute_y2(c, zc, xc, y2)
        prog.op("dve", lambda eng, c=c, y2=y2: eng.tensor_copy(
            y2b[:, c * T:(c + 1) * T], y2[:, :]),
            reads=[y2.name], writes=[(y2b.name, c)], tag=tag + "yb2")
        prog.op("act", lambda eng, y2=y2, s_=s_: eng.activation(
            s_[:, :], y2[:, :], AF.Square),
            reads=[y2.name], writes=[s_.name], tag=tag + "q1")
        for s in range(NS):
            prog.op("pe", lambda eng, c=c, s=s, s_=s_: eng.matmul(
                ps_q[s][0:1, 0:NB], ctx.c_ones_red, s_[:, s * NB:(s + 1) * NB],
                start=(c == 0), stop=(c == 15)),
                reads=[s_.name, "c_ones_red"], writes=[ps_q[s]], tag=tag + "q2")
    rrow = ctx.sbuf(tag + "rr", (1, T))
    for s in range(NS):
        prog.op("dve", lambda eng, s=s: eng.tensor_copy(
            rrow[0:1, s * NB:(s + 1) * NB], ps_q[s][0:1, 0:NB]),
            reads=[ps_q[s]], writes=[rrow], tag=tag + "re")
    prog.op("dve", lambda eng: eng.tensor_scalar(rrow[:, :], rrow[:, :], 1.0 / DI,
                                                 None, ALU.mult),
            reads=[rrow], writes=[rrow], tag=tag + "rv")
    prog.op("act", lambda eng: eng.activation(rrow[:, :], rrow[:, :], AF.Ln,
                                              bias=scal_ap(ctx, 1e-5, 1)),
            reads=[rrow, "c_scal"], writes=[rrow], tag=tag + "rl")
    prog.op("act", lambda eng: eng.activation(rrow[:, :], rrow[:, :], AF.Exp,
                                              scale=-0.5),
            reads=[rrow], writes=[rrow], tag=tag + "rx")
    for s in range(NS):
        pR = ctx.pp.get()
        bcast_row(ctx.prog, ctx.c_ones_bc, rrow, pR, s * NB, (s + 1) * NB,
                  rrow.name, pR.name, tag=tag + "rb")
        for c in range(16):
            lo = c * T + s * NB
            hi = c * T + (s + 1) * NB
            prog.op("dve", lambda eng, lo=lo, hi=hi, pR=pR, c=c: eng.tensor_tensor(
                sqt[0][:, 0:NB], y2b[:, lo:hi], pR[:, 0:NB], ALU.mult),
                reads=[(y2b.name, c), pR], writes=[sqt[0]], tag=tag + "ap1")
            prog.op("dve", lambda eng, lo=lo, hi=hi, c=c: eng.tensor_scalar(
                y2b[:, lo:hi], sqt[0][:, 0:NB], mnc[:, c:c + 1], None, ALU.mult),
                reads=[sqt[0], mnc], writes=[(y2b.name, c)], tag=tag + "ap2")
    prog.barrier()
    esD.close()
    emit_out_proj(ctx, b, Wout_d, None, y2b, y2b.name, xin, xout, 16, T,
                  tag=tag + "o")
    ctx.es = old
    prog.barrier()
    es0.close()


def emit_m2attn(ctx, b, xin, xout, Wqkv_d, Wcp_d, qcw_d, qcb_d, kcw_d, kcb_d,
                vcw_d, vcb_d, tag="ma"):
    nc, prog = ctx.nc, ctx.prog
    from contextlib import ExitStack
    T = SEQ
    NS = T // NB
    NCH = 8
    NH = 16
    old = ctx.es
    es0 = ExitStack()
    ctx.es = es0
    qf = ctx.sbuf(tag + "q", (128, NCH * T), BF16)
    kf = ctx.sbuf(tag + "k", (64, T))
    vf = ctx.sbuf(tag + "v", (64, T))
    yt = ctx.sbuf(tag + "y", (128, NCH * T), BF16)
    convcols = {}
    for nm, wd, bd in (("q", qcw_d, qcb_d), ("k", kcw_d, kcb_d), ("v", vcw_d, vcb_d)):
        cols = []
        for k in range(3):
            c64 = ctx.feat_cols(wd[k, :], 64, tag=tag + f"cw{nm}{k}")
            c128 = ctx.sbuf(tag + f"cw2{nm}{k}", (128, 1))
            for pb in (0, 64):
                prog.op("dve", lambda eng, pb=pb, c64=c64, c128=c128: eng.tensor_copy(
                    c128[pb:pb + 64, :], c64[:, :]),
                    reads=[c64], writes=[c128], tag=tag + "cd")
            cols.append(c128)
        b64 = ctx.feat_cols(bd, 64, tag=tag + f"cb{nm}")
        b128 = ctx.sbuf(tag + f"cb2{nm}", (128, 1))
        for pb in (0, 64):
            prog.op("dve", lambda eng, pb=pb, b64=b64, b128=b128: eng.tensor_copy(
                b128[pb:pb + 64, :], b64[:, :]),
                reads=[b64], writes=[b128], tag=tag + "cd2")
        convcols[nm] = (cols, b128)

    es1 = ExitStack()
    ctx.es = es1
    h_bf = ctx.sbuf(tag + "h", (128, NCH * T), BF16)
    es1b = ExitStack()
    ctx.es = es1b
    loader = dram_chunk_loader(
        ctx, lambda c: (xin[b, c * 128:(c + 1) * 128, :], (xin.tensor.name, b, c)),
        T, tag=tag + "xl", nbuf=1)
    emit_ln_fm(ctx, loader, T, NCH, h_bf, tag + "_hk", tag=tag + f"ln{b}", nsq=1,
               rms=True, eps=1e-6)
    prog.barrier()
    es1b.close()
    es2 = ExitStack()
    ctx.es = es2
    wq = ctx.load_w_bf16(Wqkv_d, DM, 1152, tag=tag + "wq")
    cvd = [ctx.sbuf(tag + "cvd", (128, T)) for _ in range(2)]
    qcv = ctx.sbuf(tag + "qcv", (128, T))
    kvraw = ctx.sbuf(tag + "kvr", (128, T))
    kvc = qcv

    def evac_q(info, m0, m1, s, ps):
        r = info
        if r < 8:
            xc = cvd[r % 2]
            prog.op("dve", lambda eng, s=s, ps=ps, xc=xc: eng.tensor_copy(
                xc[:, s * NB:(s + 1) * NB], ps[0:128, 0:NB]),
                reads=[ps], writes=[xc.name], tag=tag + "eq")
            if s == NS - 1:
                cols, bcol = convcols["q"]
                prog.op("dve", lambda eng, xc=xc: eng.tensor_scalar(
                    qcv[:, :], xc[:, :], cols[2][:, 0:1], bcol[:, 0:1], ALU.mult,
                    ALU.add),
                    reads=[xc.name, cols[2], bcol], writes=[qcv.name],
                    tag=tag + "q3")
                for k in range(2):
                    sh = 2 - k
                    prog.op("dve", lambda eng, k=k, sh=sh, xc=xc: eng.scalar_tensor_tensor(
                        qcv[:, sh:T], xc[:, 0:T - sh],
                        cols[k][:, 0:1], qcv[:, sh:T], ALU.mult, ALU.add),
                        reads=[xc.name, cols[k], qcv.name],
                        writes=[qcv.name], tag=tag + "qk")
                prog.op("dve", lambda eng, r=r: eng.tensor_copy(
                    qf[:, r * T:(r + 1) * T], qcv[:, :]),
                    reads=[qcv.name], writes=[(qf.name, r)], tag=tag + "qf")
        else:
            pb = (r - 8) * 64
            prog.op("dve", lambda eng, pb=pb, s=s, ps=ps: eng.tensor_copy(
                kvraw[pb:pb + 64, s * NB:(s + 1) * NB], ps[0:64, 0:NB]),
                reads=[ps], writes=[(kvraw.name, pb)], tag=tag + "ekv")

    mt = [(r * 128, (r + 1) * 128, r) for r in range(8)] + \
         [(1024, 1088, 8), (1088, 1152, 9)]
    emit_proj_tiles(ctx, wq, 1152, h_bf, tag + "_hk", NCH, T, mt, evac_q,
                    tag=tag + "pq")
    for nm, pb, dst in (("k", 0, kf), ("v", 64, vf)):
        cols, bcol = convcols[nm]
        prog.op("dve", lambda eng, pb=pb, cols=cols, bcol=bcol: eng.tensor_scalar(
            kvc[pb:pb + 64, :], kvraw[pb:pb + 64, :], cols[2][pb:pb + 64, 0:1],
            bcol[pb:pb + 64, 0:1], ALU.mult, ALU.add),
            reads=[(kvraw.name, pb), cols[2], bcol], writes=[kvc.name],
            tag=tag + nm + "3")
        for k in range(2):
            sh = 2 - k
            prog.op("dve", lambda eng, pb=pb, k=k, sh=sh, cols=cols: eng.scalar_tensor_tensor(
                kvc[pb:pb + 64, sh:T], kvraw[pb:pb + 64, 0:T - sh],
                cols[k][pb:pb + 64, 0:1], kvc[pb:pb + 64, sh:T], ALU.mult, ALU.add),
                reads=[(kvraw.name, pb), cols[k], kvc.name],
                writes=[kvc.name], tag=tag + nm + "k")
        prog.op("dve", lambda eng, pb=pb, dst=dst: eng.tensor_copy(
            dst[:, :], kvc[pb:pb + 64, :]),
            reads=[kvc.name], writes=[dst], tag=tag + nm + "f")
    prog.barrier()
    es2.close()
    es1.close()

    es3 = ExitStack()
    ctx.es = es3
    kfb = ctx.sbuf(tag + "kfb", (128, T), BF16)
    for pb in (0, 64):
        prog.op("dve", lambda eng, pb=pb: eng.tensor_copy(
            kfb[pb:pb + 64, :], kf[:, :]),
            reads=[kf], writes=[kfb], tag=tag + "kb")
    vfb = ctx.sbuf(tag + "vfb", (64, T), BF16)
    prog.op("dve", lambda eng: eng.tensor_copy(vfb[:, :], vf[:, :]),
            reads=[vf], writes=[vfb], tag=tag + "vb")
    vtok = ctx.sbuf(tag + "vt", (128, 16 * 64), BF16)
    for tt in range(16):
        ps = ctx.pp.get()
        prog.op("pe", lambda eng, tt=tt, ps=ps: eng.transpose(
            ps[0:128, 0:64], vf[0:64, tt * 128:(tt + 1) * 128],
            ctx.c_ident[0:64, 0:64]),
            reads=[vf, "c_ident"], writes=[ps], tag=tag + "vt")
        prog.op("dve", lambda eng, tt=tt, ps=ps: eng.tensor_copy(
            vtok[:, tt * 64:(tt + 1) * 64], ps[0:128, 0:64]),
            reads=[ps], writes=[(vtok.name, tt)], tag=tag + "ve")
    Pt = ctx.sbuf(tag + "P", (128, 16 * NB), BF16)
    srow = ctx.sbuf(tag + "sr", (1, NB))
    rrow = ctx.sbuf(tag + "rr", (1, NB))
    yev = ctx.sbuf(tag + "ye", (64, NB))
    msks = ctx.sbuf(tag + "mk", (128, NB))
    qk_tiles = ctx.pp.tiles[0:5]
    t_sum = ctx.pp.tiles[5]
    t_r = ctx.pp.tiles[6]
    t_y = ctx.pp.tiles[7]
    qkc = [0]
    for h in range(NH):
        hb = (h % 2) * 64
        hc = h // 2
        for qb in range(NS):
            nkt = 4 * qb + 4
            for kt in range(nkt):
                ps = qk_tiles[qkc[0] % 5]
                qkc[0] += 1
                prog.op("pe", lambda eng, h=h, qb=qb, kt=kt, ps=ps, hb=hb, hc=hc: eng.matmul(
                    ps[0:128, 0:NB],
                    kfb[hb:hb + 64, kt * 128:(kt + 1) * 128],
                    qf[hb:hb + 64, hc * T + qb * NB: hc * T + (qb + 1) * NB],
                    start=True, stop=True),
                    reads=[kfb, (qf.name, hc)], writes=[ps], tag=tag + "qk")
                if kt >= 4 * qb:
                    i = kt - 4 * qb
                    prog.op("dve", lambda eng, i=i, ps=ps: eng.tensor_tensor(
                        msks[:, :], ps[0:128, 0:NB],
                        ctx.c_cmask[:, i * NB:(i + 1) * NB], ALU.add),
                        reads=[ps, "c_cmask"], writes=[msks], tag=tag + "cm")
                    prog.op("act", lambda eng, kt=kt: eng.activation(
                        Pt[:, kt * NB:(kt + 1) * NB], msks[:, :], AF.Exp,
                        scale=0.125),
                        reads=[msks], writes=[(Pt.name, kt)], tag=tag + "ex")
                else:
                    prog.op("act", lambda eng, kt=kt, ps=ps: eng.activation(
                        Pt[:, kt * NB:(kt + 1) * NB], ps[0:128, 0:NB], AF.Exp,
                        scale=0.125),
                        reads=[ps], writes=[(Pt.name, kt)], tag=tag + "ex2")
            for kt in range(nkt):
                prog.op("pe", lambda eng, kt=kt, nkt=nkt: eng.matmul(
                    t_sum[0:1, 0:NB], ctx.c_ones_red_bf,
                    Pt[:, kt * NB:(kt + 1) * NB],
                    start=(kt == 0), stop=(kt == nkt - 1)),
                    reads=[(Pt.name, kt), "c_ones_red_bf"], writes=[t_sum],
                    tag=tag + "sm")
            prog.op("dve", lambda eng: eng.tensor_copy(srow[:, :], t_sum[0:1, 0:NB]),
                    reads=[t_sum], writes=[srow], tag=tag + "se")
            prog.op("dve", lambda eng: eng.reciprocal(rrow[:, :], srow[:, :]),
                    reads=[srow], writes=[rrow], tag=tag + "rc")
            prog.op("pe", lambda eng: eng.matmul(
                t_r[:, 0:NB], ctx.c_ones_bc, rrow[0:1, 0:NB], start=True, stop=True),
                reads=[rrow, "c_ones_bc"], writes=[t_r], tag=tag + "rb")
            for kt in range(nkt):
                prog.op("pe", lambda eng, kt=kt, nkt=nkt: eng.matmul(
                    t_y[0:64, 0:NB], vtok[:, kt * 64:(kt + 1) * 64],
                    Pt[:, kt * NB:(kt + 1) * NB],
                    start=(kt == 0), stop=(kt == nkt - 1)),
                    reads=[(vtok.name, kt), (Pt.name, kt)], writes=[t_y],
                    tag=tag + "pv")
            prog.op("dve", lambda eng: eng.tensor_copy(yev[:, :], t_y[0:64, 0:NB]),
                    reads=[t_y], writes=[yev], tag=tag + "ye2")
            prog.op("dve", lambda eng, h=h, qb=qb, hb=hb, hc=hc: eng.tensor_tensor(
                yt[hb:hb + 64, hc * T + qb * NB: hc * T + (qb + 1) * NB],
                yev[:, :], t_r[0:64, 0:NB], ALU.mult),
                reads=[yev, t_r], writes=[(yt.name, hc)], tag=tag + "yw")
    prog.barrier()
    es3.close()
    emit_out_proj(ctx, b, Wcp_d, None, yt, yt.name, xin, xout, NCH, T, tag=tag + "o")
    ctx.es = old
    prog.barrier()
    es0.close()


def emit_m2rwkv(ctx, b, xin, xout, maak_d, maar_d, Wkey_d, Wrec_d, Wval_d, scr,
                tag="rw"):
    nc, prog = ctx.nc, ctx.prog
    from contextlib import ExitStack
    T = SEQ
    NS = T // NB
    NCH = 8
    FFN = 4096
    old = ctx.es
    es0 = ExitStack()
    ctx.es = es0
    sg_bf = ctx.sbuf(tag + "sg", (128, NCH * T), BF16)
    mkc = ctx.feat_cols(maak_d, DM, tag=tag + "mk")
    mrc = ctx.feat_cols(maar_d, DM, tag=tag + "mr")
    esX = ExitStack()
    ctx.es = esX
    xk_bf = ctx.sbuf(tag + "xk", (128, NCH * T), BF16)
    xr_bf = ctx.sbuf(tag + "xr", (128, NCH * T), BF16)
    es1 = ExitStack()
    ctx.es = es1
    h_f = ctx.sbuf(tag + "h", (128, NCH * T), BF16)
    es1b = ExitStack()
    ctx.es = es1b
    loader = dram_chunk_loader(
        ctx, lambda c: (xin[b, c * 128:(c + 1) * 128, :], (xin.tensor.name, b, c)),
        T, tag=tag + "xl", nbuf=1)
    emit_ln_fm(ctx, loader, T, NCH, h_f, tag + "_hk", tag=tag + f"ln{b}", nsq=1,
               rms=True, eps=1e-6)
    prog.barrier()
    es1b.close()
    ctx.es = es1
    xxb = ctx.sbuf(tag + "xx", (128, T), BF16)
    for c in range(NCH):
        prog.op("dve", lambda eng, c=c: eng.tensor_tensor(
            xxb[:, 1:T], h_f[:, c * T: c * T + T - 1], h_f[:, c * T + 1:(c + 1) * T],
            ALU.subtract),
            reads=[(tag + "_hk", c)], writes=[xxb], tag=tag + "x1")
        prog.op("dve", lambda eng, c=c: eng.tensor_scalar(
            xxb[:, 0:1], h_f[:, c * T: c * T + 1], -1.0, None, ALU.mult),
            reads=[(tag + "_hk", c)], writes=[xxb], tag=tag + "x2")
        prog.op("dve", lambda eng, c=c: eng.scalar_tensor_tensor(
            xk_bf[:, c * T:(c + 1) * T], xxb[:, :], mkc[:, c:c + 1],
            h_f[:, c * T:(c + 1) * T], ALU.mult, ALU.add),
            reads=[xxb, mkc, (tag + "_hk", c)], writes=[(xk_bf.name, c)],
            tag=tag + "x3")
        prog.op("dve", lambda eng, c=c: eng.scalar_tensor_tensor(
            xr_bf[:, c * T:(c + 1) * T], xxb[:, :], mrc[:, c:c + 1],
            h_f[:, c * T:(c + 1) * T], ALU.mult, ALU.add),
            reads=[xxb, mrc, (tag + "_hk", c)], writes=[(xr_bf.name, c)],
            tag=tag + "x4")
    prog.barrier()
    es1.close()
    es2 = ExitStack()
    ctx.es = es2
    wk = ctx.load_w_bf16(Wkey_d, DM, FFN, tag=tag + "wk")
    t1 = [ctx.sbuf(tag + "t1", (128, NB)) for _ in range(2)]
    t2 = [ctx.sbuf(tag + "t2", (128, NB), BF16) for _ in range(2)]

    def evac_kf(info, m0, m1, s, ps):
        r = info
        a = t1[(r * NS + s) % 2]
        bb = t2[(r * NS + s) % 2]
        prog.op("act", lambda eng, ps=ps, a=a: eng.activation(
            a[:, :], ps[0:128, 0:NB], AF.Relu),
            reads=[ps], writes=[a.name], tag=tag + "re")
        prog.op("act", lambda eng, a=a, bb=bb: eng.activation(
            bb[:, :], a[:, :], AF.Square),
            reads=[a.name], writes=[bb.name], tag=tag + "sq")
        dma(prog, "sp", scr["kfD"][b, m0:m1, s * NB:(s + 1) * NB], bb[:, :],
            reads=[bb.name], writes=[("kfD", b, r)], tag=tag + "skf")

    mt = [(r * 128, (r + 1) * 128, r) for r in range(32)]
    emit_proj_tiles(ctx, wk, FFN, xk_bf, xk_bf.name, NCH, T, mt, evac_kf,
                    tag=tag + "pk")
    prog.barrier()
    es2.close()
    es3 = ExitStack()
    ctx.es = es3
    wr = ctx.load_w_bf16(Wrec_d, DM, DM, tag=tag + "wr")

    def evac_sg(info, m0, m1, s, ps):
        r = info
        prog.op("act", lambda eng, ps=ps, r=r, s=s: eng.activation(
            sg_bf[:, r * T + s * NB: r * T + (s + 1) * NB], ps[0:128, 0:NB],
            AF.Sigmoid),
            reads=[ps], writes=[(sg_bf.name, r)], tag=tag + "sg")

    emit_proj_tiles(ctx, wr, DM, xr_bf, xr_bf.name, NCH, T,
                    [(r * 128, (r + 1) * 128, r) for r in range(8)], evac_sg,
                    tag=tag + "pr")
    prog.barrier()
    es3.close()
    esX.close()
    ctx.es = es0
    es4 = ExitStack()
    ctx.es = es4
    wv = ctx.load_w_bf16(Wval_d, FFN, DM, tag=tag + "wv")
    kfc = ctx.sbuf(tag + "kfc", (128, 32 * NB), BF16)
    xr2 = [ctx.sbuf(tag + "x2r", (128, NB)) for _ in range(2)]
    xo2 = [ctx.sbuf(tag + "x2o", (128, NB)) for _ in range(2)]
    for s in range(NS):
        for r in range(32):
            dma(prog, "pool", kfc[:, r * NB:(r + 1) * NB],
                scr["kfD"][b, r * 128:(r + 1) * 128, s * NB:(s + 1) * NB],
                reads=[("kfD", b, r)], writes=[(kfc.name, r)], tag=tag + "lkf")
        for m in range(8):
            ps = ctx.pp.get()
            for k in range(32):
                prog.op("pe", lambda eng, m=m, s=s, k=k, ps=ps: eng.matmul(
                    ps[0:128, 0:NB], wv[:, k * DM + m * 128: k * DM + (m + 1) * 128],
                    kfc[:, k * NB:(k + 1) * NB],
                    start=(k == 0), stop=(k == 31)),
                    reads=[(wv.name, k), (kfc.name, k)], writes=[ps], tag=tag + "mv")
            xr_ = xr2[m % 2]
            xw_ = xo2[m % 2]
            dma(prog, "sp", xr_[:, :], xin[b, m * 128:(m + 1) * 128, s * NB:(s + 1) * NB],
                reads=[(xin.tensor.name, b, m)], writes=[xr_.name], tag=tag + "lx")
            prog.op("dve", lambda eng, ps=ps, m=m, s=s: eng.tensor_tensor(
                ps[0:128, 0:NB], ps[0:128, 0:NB],
                sg_bf[:, m * T + s * NB: m * T + (s + 1) * NB], ALU.mult),
                reads=[ps, (sg_bf.name, m)], writes=[ps], tag=tag + "gv")
            prog.op("dve", lambda eng, ps=ps, xr_=xr_, xw_=xw_: eng.scalar_tensor_tensor(
                xw_[:, :], ps[0:128, 0:NB], 1.0, xr_[:, :], ALU.mult, ALU.add),
                reads=[ps, xr_.name], writes=[xw_.name], tag=tag + "ad")
            dma(prog, "sp", xout[b, m * 128:(m + 1) * 128, s * NB:(s + 1) * NB],
                xw_[:, :], reads=[xw_.name], writes=[(xout.tensor.name, b, m)],
                tag=tag + "so")
    prog.barrier()
    es4.close()
    ctx.es = old
    es0.close()



SLOTS0 = 2048
SLOTS1 = 5120


def _mk_progfn_sub(emit, weights, extra_consts=()):
    def progfn(nc, prog, es):
        ctx = Ctx(nc, prog, es)
        xin = nc.dram_tensor("xin", [BS, DM, SEQ], F32, kind="ExternalInput").ap()
        xout = nc.dram_tensor("xout", [BS, DM, SEQ], F32, kind="ExternalOutput").ap()
        dr = [nc.dram_tensor(f"w{i}", list(np.asarray(w).shape), F32,
                             kind="ExternalInput").ap() for i, w in enumerate(weights)]
        declare_consts(ctx, extra=extra_consts)
        for b in range(BS):
            emit(ctx, b, xin, xout, *dr)
    return progfn


def _base_inmap(ws, extra=()):
    m = {}
    for i, w in enumerate(ws):
        m[f"w{i}"] = np.ascontiguousarray(w)
    want = set(BASE_CONSTS) | set(extra)
    for k, v in const_arrays().items():
        if k in want:
            m[k + "_d"] = v
    return m


def _run_sub(emit, ws, xs, extra_consts=()):
    progfn = _mk_progfn_sub(emit, ws, extra_consts)
    base = _base_inmap(ws, extra_consts)
    in_maps = [dict(base, xin=xs[i]) for i in range(NCORE)]
    res = launch(progfn, in_maps)
    return [np.asarray(res.results[i]["xout"]) for i in range(NCORE)]


def _flatten_tok(per_core_arrays, tr=False):
    out = []
    for Bg in range(BATCH):
        a = per_core_arrays[Bg % 8][Bg // 8]
        out.append(a.T if tr else a)
    return np.concatenate(out, axis=0)


def _routing(idx_all, cap):
    order = np.argsort(idx_all, kind="stable")
    sorted_idx = idx_all[order]
    sel = np.zeros((8, cap), np.int64)
    valid = np.zeros((8, cap), bool)
    keep = np.zeros(idx_all.shape[0], bool)
    slot = np.zeros(idx_all.shape[0], np.int64)
    for e in range(8):
        toks = order[sorted_idx == e]
        n = min(len(toks), cap)
        sel[e, :n] = toks[:n]
        valid[e, :n] = True
        keep[toks[:n]] = True
        slot[toks[:n]] = e * cap + np.arange(n)
    return sel, valid, keep, slot


def _gather_fm_bf16(h_all, sel_row):
    return np.ascontiguousarray(h_all[sel_row].T)


def _per_core_rows(vec, one_col=False):
    out = []
    for i in range(NCORE):
        rows = []
        for j in range(BS):
            Bg = i + 8 * j
            rows.append(vec[Bg * SEQ:(Bg + 1) * SEQ][None, :])
        out.append(np.stack(rows, axis=0))
    return out


def _per_core_fm(mat):
    out = []
    for i in range(NCORE):
        sl = []
        for j in range(BS):
            Bg = i + 8 * j
            sl.append(np.ascontiguousarray(mat[Bg * SEQ:(Bg + 1) * SEQ].T))
        out.append(np.stack(sl, axis=0))
    return out


def _moe_layer(xs, kind, rw, rb, expert_ws, h_dt):
    def progA(nc, prog, es):
        ctx = Ctx(nc, prog, es)
        xin = nc.dram_tensor("xin", [BS, DM, SEQ], F32, kind="ExternalInput").ap()
        hout = nc.dram_tensor("hout", [BS, DM, SEQ], BF16, kind="ExternalOutput").ap()
        gate = nc.dram_tensor("gate", [BS, 16, 128], F32, kind="ExternalOutput").ap()
        idx = nc.dram_tensor("idx", [BS, 16, 128], F32, kind="ExternalOutput").ap()
        rwd = nc.dram_tensor("rw", [DM, 8], F32, kind="ExternalInput").ap()
        rbd = nc.dram_tensor("rb", [8], F32, kind="ExternalInput").ap()
        declare_consts(ctx, extra=("c_rev8",))
        for b in range(BS):
            emit_router(ctx, b, xin, hout, rwd, rbd, gate, idx)
    base = {"rw": np.ascontiguousarray(rw), "rb": np.ascontiguousarray(rb)}
    want = set(BASE_CONSTS) | {"c_rev8"}
    for k, v in const_arrays().items():
        if k in want:
            base[k + "_d"] = v
    res = launch(progA, [dict(base, xin=xs[i]) for i in range(NCORE)])
    h_pc = [np.asarray(res.results[i]["hout"]) for i in range(NCORE)]
    gate_pc = [np.asarray(res.results[i]["gate"]) for i in range(NCORE)]
    idx_pc = [np.asarray(res.results[i]["idx"]) for i in range(NCORE)]
    h_all = _flatten_tok(h_pc, tr=True)
    idx_all = _flatten_tok([a.reshape(BS, SEQ) for a in idx_pc]).reshape(-1)
    idx_all = idx_all.astype(np.int64)
    gate_all = _flatten_tok([a.reshape(BS, SEQ) for a in gate_pc]).reshape(-1)
    cap = SLOTS0 if kind == "keep" else SLOTS1
    sel, valid, keep, slot = _routing(idx_all, cap)
    if kind == "keep":
        w1, b1, w2, b2 = expert_ws

        def progB(nc, prog, es):
            ctx = Ctx(nc, prog, es)
            Xg = nc.dram_tensor("Xg", [DM, cap], BF16, kind="ExternalInput").ap()
            Yg = nc.dram_tensor("Yg", [DM, cap], F32, kind="ExternalOutput").ap()
            w1d = nc.dram_tensor("w1", [DM, 4096], F32, kind="ExternalInput").ap()
            b1d = nc.dram_tensor("b1", [4096], F32, kind="ExternalInput").ap()
            w2d = nc.dram_tensor("w2", [4096, DM], F32, kind="ExternalInput").ap()
            b2d = nc.dram_tensor("b2", [DM], F32, kind="ExternalInput").ap()
            declare_consts(ctx)
            emit_expert_l0(ctx, Xg, w1d, b1d, w2d, b2d, Yg, cap)
        cb = {}
        for k, v in const_arrays().items():
            if k in BASE_CONSTS:
                cb[k + "_d"] = v
        in_maps = [dict(cb, Xg=_gather_fm_bf16(h_all, sel[e]),
                        w1=np.ascontiguousarray(w1[e]), b1=np.ascontiguousarray(b1[e]),
                        w2=np.ascontiguousarray(w2[e]), b2=np.ascontiguousarray(b2[e]))
                   for e in range(8)]
        res = launch(progB, in_maps)
    else:
        ew, eb = expert_ws

        def progB(nc, prog, es):
            ctx = Ctx(nc, prog, es)
            Xg = nc.dram_tensor("Xg", [DM, cap], BF16, kind="ExternalInput").ap()
            Yg = nc.dram_tensor("Yg", [DM, cap], F32, kind="ExternalOutput").ap()
            wd = nc.dram_tensor("we", [DM, DM], F32, kind="ExternalInput").ap()
            bd = nc.dram_tensor("be", [DM], F32, kind="ExternalInput").ap()
            declare_consts(ctx)
            emit_expert_l1(ctx, Xg, wd, bd, Yg, cap)
        cb = {}
        for k, v in const_arrays().items():
            if k in BASE_CONSTS:
                cb[k + "_d"] = v
        in_maps = [dict(cb, Xg=_gather_fm_bf16(h_all, sel[e]),
                        we=np.ascontiguousarray(ew[e]), be=np.ascontiguousarray(eb[e]))
                   for e in range(8)]
        res = launch(progB, in_maps)
    Yg_pc = [np.asarray(res.results[e]["Yg"]) for e in range(8)]
    Ysel = np.zeros((NTOK, DM), np.float32)
    for e in range(8):
        v = valid[e]
        Ysel[sel[e][v]] = Yg_pc[e].T[v]
    gk = keep.astype(np.float32) if kind == "keep" else gate_all.astype(np.float32)
    gk_pc = _per_core_rows(gk)
    Ysel_pc = _per_core_fm(Ysel)

    def progC(nc, prog, es):
        ctx = Ctx(nc, prog, es)
        xin = nc.dram_tensor("xin", [BS, DM, SEQ], F32, kind="ExternalInput").ap()
        xout = nc.dram_tensor("xout", [BS, DM, SEQ], F32, kind="ExternalOutput").ap()
        Ys = nc.dram_tensor("Ys", [BS, DM, SEQ], F32, kind="ExternalInput").ap()
        gkd = nc.dram_tensor("gk", [BS, 1, SEQ], F32, kind="ExternalInput").ap()
        declare_consts(ctx)
        for b in range(BS):
            emit_combine(ctx, b, xin, xout, Ys, gkd, kind)
    cb = {}
    for k, v in const_arrays().items():
        if k in BASE_CONSTS:
            cb[k + "_d"] = v
    res = launch(progC, [dict(cb, xin=xs[i], Ys=Ysel_pc[i], gk=gk_pc[i])
                         for i in range(NCORE)])
    return [np.asarray(res.results[i]["xout"]) for i in range(NCORE)]


def _dump_stage(xs, name):
    import os
    if os.environ.get("DUMPSTAGE"):
        xfull = np.zeros((BATCH, SEQ, DM), np.float32)
        for i in range(NCORE):
            for j in range(BS):
                xfull[i + 8 * j] = np.asarray(xs[i][j]).T
        np.save(f"/tmp/stage_{name}.npy", xfull)


def kernel(**inp):
    g = lambda k: np.asarray(inp[k])
    x0 = g("L0_attn_a_x")
    xs = []
    for i in range(NCORE):
        xs.append(np.ascontiguousarray(
            x0[[i, i + 8]].transpose(0, 2, 1)).astype(np.float32))

    xs = _run_sub(
        lambda ctx, b, xin, xout, WxT, WoutT, Ad, Dd: emit_mamba1(
            ctx, b, xin, xout, WxT, WoutT, Ad, Dd),
        [g("L0_attn_a_Wx").T, g("L0_attn_a_Wout").T, g("L0_attn_a_A_log"),
         g("L0_attn_a_D")], xs, extra_consts=("c_sel16",))
    _dump_stage(xs, "1m1")

    l0w = (g("L0_ffn_w1"), g("L0_ffn_b1"), g("L0_ffn_w2"), g("L0_ffn_b2"))
    xs = _moe_layer(xs, "keep", g("L0_ffn_w_switch"), g("L0_ffn_b_switch"), l0w,
                    None)
    _dump_stage(xs, "2ffn")

    xs = _run_sub(
        lambda ctx, b, xin, xout, wq, bq, wk, bk, wv, bv, wo, bo: emit_attn0(
            ctx, b, xin, xout, wq, bq, wk, bk, wv, bv, wo, bo),
        [g("L0_attn_b_wq").T, g("L0_attn_b_bq"), g("L0_attn_b_wk").T,
         g("L0_attn_b_bk"), g("L0_attn_b_wv").T, g("L0_attn_b_bv"),
         g("L0_attn_b_wo").T, g("L0_attn_b_bo")], xs,
        extra_consts=("c_ones_red_bf",))
    _dump_stage(xs, "3attn")

    xs = _moe_layer(xs, "keep", g("L0_ffn_w_switch"), g("L0_ffn_b_switch"), l0w,
                    None)
    _dump_stage(xs, "4ffn")

    xfull = np.zeros((BATCH, DM, SEQ), np.float32)
    for i in range(NCORE):
        for j in range(BS):
            xfull[i + 8 * j] = xs[i][j]
    SL = SEQ // NCORE
    Vc = g("L1_attn_a_V")
    Wc = g("L1_attn_a_Vinv")

    def progS(nc, prog, es):
        ctx = Ctx(nc, prog, es)
        xin = nc.dram_tensor("xin", [BATCH, DM, SL], F32, kind="ExternalInput").ap()
        xout = nc.dram_tensor("xout", [BATCH, DM, SL], F32, kind="ExternalOutput").ap()
        dr = {}
        for nm, shp in [("logdt", (64,)), ("logLre", (64,)), ("Lim", (64,)),
                        ("Vre", (64, 64)), ("Vim", (64, 64)), ("Wre", (64, 64)),
                        ("Wim", (64, 64))]:
            dr[nm] = nc.dram_tensor(nm, list(shp), F32, kind="ExternalInput").ap()
        declare_consts(ctx)
        emit_s5(ctx, xin, xout, dr["logdt"], dr["logLre"], dr["Lim"], dr["Vre"],
                dr["Vim"], dr["Wre"], dr["Wim"], SL)
    base = {"logdt": g("L1_attn_a_log_dt"), "logLre": g("L1_attn_a_log_Lambda_real"),
            "Lim": g("L1_attn_a_Lambda_imag"),
            "Vre": np.ascontiguousarray(Vc.real), "Vim": np.ascontiguousarray(Vc.imag),
            "Wre": np.ascontiguousarray(Wc.real), "Wim": np.ascontiguousarray(Wc.imag)}
    for k, v in const_arrays().items():
        if k in BASE_CONSTS:
            base[k + "_d"] = v
    res = launch(progS, [dict(base, xin=np.ascontiguousarray(
        xfull[:, :, i * SL:(i + 1) * SL])) for i in range(NCORE)])
    for i in range(NCORE):
        xfull[:, :, i * SL:(i + 1) * SL] = np.asarray(res.results[i]["xout"])
    xs = [np.ascontiguousarray(xfull[[i, i + 8]]) for i in range(NCORE)]
    _dump_stage(xs, "5s5")

    l1w = (g("L1_ffn_expert_w"), g("L1_ffn_expert_b"))
    xs = _moe_layer(xs, "gate", g("L1_ffn_router_w"), g("L1_ffn_router_b"), l1w,
                    None)
    _dump_stage(xs, "6ffn")

    def progM2(nc, prog, es):
        ctx = Ctx(nc, prog, es)
        xin = nc.dram_tensor("xin", [BS, DM, SEQ], F32, kind="ExternalInput").ap()
        xout = nc.dram_tensor("xout", [BS, DM, SEQ], F32, kind="ExternalOutput").ap()
        xu = nc.dram_tensor("xu", [BS, DM, SEQ], F32).ap()
        xa = nc.dram_tensor("xa", [BS, DM, SEQ], F32).ap()
        xb = nc.dram_tensor("xb", [BS, DM, SEQ], F32).ap()
        xc = nc.dram_tensor("xc", [BS, DM, SEQ], F32).ap()
        nm_shapes = [("W_in", (DM, 4384)), ("conv_w", (4, 2304)), ("conv_b", (2304,)),
                     ("A_log", (32,)), ("Dm", (32,)), ("dt_bias", (32,)),
                     ("mnorm_w", (2048,)), ("W_out", (2048, DM)),
                     ("W_qkv", (DM, 1152)), ("W_cproj", (DM, DM)),
                     ("qconv_w", (3, 64)), ("qconv_b", (64,)), ("kconv_w", (3, 64)),
                     ("kconv_b", (64,)), ("vconv_w", (3, 64)), ("vconv_b", (64,)),
                     ("maa_k", (DM,)), ("maa_r", (DM,)), ("W_key", (DM, 4096)),
                     ("W_rec", (DM, DM)), ("W_val", (4096, DM))]
        dr = {nm: nc.dram_tensor(nm, list(shp), F32, kind="ExternalInput").ap()
              for nm, shp in nm_shapes}
        scr = {
            "zD": nc.dram_tensor("zD", [BS, 2048, SEQ], BF16).ap(),
            "xbcD": nc.dram_tensor("xbcD", [BS, 2304, SEQ], F32).ap(),
            "xmD": nc.dram_tensor("xmD", [BS, 2048, SEQ], BF16).ap(),
            "BD": nc.dram_tensor("BD", [BS, 128, SEQ], BF16).ap(),
            "CD": nc.dram_tensor("CD", [BS, 128, SEQ], BF16).ap(),
            "kfD": nc.dram_tensor("kfD", [BS, 4096, SEQ], BF16).ap(),
        }
        declare_consts(ctx, extra=("c_sel16", "c_sel32", "c_trimask", "c_cmask", "c_dm_l1", "c_dm_l2", "c_dm_m1", "c_dm_m2", "c_ones_red_bf"))
        for b in range(BS):
            emit_ln_to_dram(ctx, b, xin, xu)
            emit_m2core(ctx, b, xu, xa, dr["W_in"], dr["conv_w"], dr["conv_b"],
                        dr["A_log"], dr["Dm"], dr["dt_bias"], dr["mnorm_w"],
                        dr["W_out"], scr)
            emit_m2attn(ctx, b, xa, xb, dr["W_qkv"], dr["W_cproj"], dr["qconv_w"],
                        dr["qconv_b"], dr["kconv_w"], dr["kconv_b"], dr["vconv_w"],
                        dr["vconv_b"])
            emit_m2rwkv(ctx, b, xb, xc, dr["maa_k"], dr["maa_r"], dr["W_key"],
                        dr["W_rec"], dr["W_val"], scr)
            emit_add_res(ctx, b, xin, xc, xout)
    base = {"W_in": g("L1_attn_b_W_in"),
            "conv_w": np.ascontiguousarray(g("L1_attn_b_conv_w").T),
            "conv_b": g("L1_attn_b_conv_b"), "A_log": g("L1_attn_b_A_log"),
            "Dm": g("L1_attn_b_Dm"), "dt_bias": g("L1_attn_b_dt_bias"),
            "mnorm_w": g("L1_attn_b_mnorm_w"), "W_out": g("L1_attn_b_W_out"),
            "W_qkv": g("L1_attn_b_W_qkv"), "W_cproj": g("L1_attn_b_W_cproj"),
            "qconv_w": np.ascontiguousarray(g("L1_attn_b_qconv_w").T),
            "qconv_b": g("L1_attn_b_qconv_b"),
            "kconv_w": np.ascontiguousarray(g("L1_attn_b_kconv_w").T),
            "kconv_b": g("L1_attn_b_kconv_b"),
            "vconv_w": np.ascontiguousarray(g("L1_attn_b_vconv_w").T),
            "vconv_b": g("L1_attn_b_vconv_b"), "maa_k": g("L1_attn_b_time_maa_k"),
            "maa_r": g("L1_attn_b_time_maa_r"), "W_key": g("L1_attn_b_W_key"),
            "W_rec": g("L1_attn_b_W_rec"), "W_val": g("L1_attn_b_W_val")}
    want = set(BASE_CONSTS) | set(("c_sel16", "c_sel32", "c_trimask", "c_cmask", "c_dm_l1", "c_dm_l2", "c_dm_m1", "c_dm_m2", "c_ones_red_bf"))
    for k, v in const_arrays().items():
        if k in want:
            base[k + "_d"] = v
    res = launch(progM2, [dict(base, xin=xs[i]) for i in range(NCORE)])
    xs = [np.asarray(res.results[i]["xout"]) for i in range(NCORE)]
    _dump_stage(xs, "7m2")

    xs = _moe_layer(xs, "gate", g("L1_ffn_router_w"), g("L1_ffn_router_b"), l1w,
                    None)

    out = np.zeros((BATCH, SEQ, DM), np.float32)
    for i in range(NCORE):
        for j in range(BS):
            out[i + 8 * j] = xs[i][j].T
    return out


def emit_ln_to_dram(ctx, b, xin, uout, tag="l2d", eps=1e-5, rms=False):
    nc, prog = ctx.nc, ctx.prog
    from contextlib import ExitStack
    T = SEQ
    NCH = 8
    old = ctx.es
    es = ExitStack()
    ctx.es = es
    hf = ctx.sbuf(tag + "h", (128, NCH * T))
    loader = dram_chunk_loader(
        ctx, lambda c: (xin[b, c * 128:(c + 1) * 128, :], (xin.tensor.name, b, c)),
        T, tag=tag + "xl", nbuf=1)
    emit_ln_fm(ctx, loader, T, NCH, hf, tag + "_hk", tag=tag + f"ln{b}", nsq=1,
               eps=eps, rms=rms)
    for c in range(NCH):
        dma(prog, "sp", uout[b, c * 128:(c + 1) * 128, :], hf[:, c * T:(c + 1) * T],
            reads=[(tag + "_hk", c)], writes=[(uout.tensor.name, b, c)],
            tag=tag + "st")
    ctx.es = old
    prog.barrier()
    es.close()


def emit_add_res(ctx, b, xin, yin, xout, tag="ar"):
    nc, prog = ctx.nc, ctx.prog
    from contextlib import ExitStack
    T = SEQ
    old = ctx.es
    es = ExitStack()
    ctx.es = es
    a = [ctx.sbuf(tag + "a", (128, T)) for _ in range(2)]
    bb = [ctx.sbuf(tag + "b", (128, T)) for _ in range(2)]
    for c in range(8):
        a_ = a[c % 2]
        b_ = bb[c % 2]
        dma(prog, "sp", a_[:, :], xin[b, c * 128:(c + 1) * 128, :],
            reads=[(xin.tensor.name, b, c)], writes=[a_.name], tag=tag + "la")
        dma(prog, "pool", b_[:, :], yin[b, c * 128:(c + 1) * 128, :],
            reads=[(yin.tensor.name, b, c)], writes=[b_.name], tag=tag + "lb")
        prog.op("dve", lambda eng, a_=a_, b_=b_: eng.tensor_tensor(
            a_[:, :], a_[:, :], b_[:, :], ALU.add),
            reads=[a_.name, b_.name], writes=[a_.name], tag=tag + "ad")
        dma(prog, "sp", xout[b, c * 128:(c + 1) * 128, :], a_[:, :],
            reads=[a_.name], writes=[(xout.tensor.name, b, c)], tag=tag + "st")
    ctx.es = old
    prog.barrier()
    es.close()
